# version 08
# baseline (speedup 1.0000x reference)
import numpy as np
import ml_dtypes
from contextlib import ExitStack

import concourse.bass as bass
import concourse.mybir as mybir
from concourse.bass_utils import run_bass_kernel_spmd

F32 = mybir.dt.float32
BF16 = mybir.dt.bfloat16
I8 = mybir.dt.int8
AF = mybir.ActivationFunctionType
ALU = mybir.AluOpType
AX = mybir.AxisListType.X

D = 1024
TOK = 2048
TFULL = 4096
NH = 8
HD = 64
II = 2816
NIC = II // 128
CH = 64
G = 512
NG = TFULL // G
QMAX = 127.0

ENGINES = ("te", "dve", "act", "pool", "sp")
SELF_FENCE = True


class Sched:

    def __init__(self):
        self.steps = []
        self.eng_steps = {e: [] for e in ENGINES}
        self.writers = {}
        self.readers = {}

    def step(self, eng, fn, reads=(), writes=(), dma=False, tag=""):
        idx = len(self.steps)
        deps = set()
        for k in reads:
            w = self.writers.get(k)
            if w is not None:
                deps.add(w)
            self.readers.setdefault(k, []).append(idx)
        for k in writes:
            w = self.writers.get(k)
            if w is not None:
                deps.add(w)
            for r in self.readers.get(k, ()):
                deps.add(r)
            self.readers[k] = []
            self.writers[k] = idx
        deps.discard(idx)
        st = dict(idx=idx, eng=eng, fn=fn, deps=deps, dma=dma, tag=tag,
                  acc=False)
        self.steps.append(st)
        self.eng_steps[eng].append(st)
        return idx

    _bar_n = 0

    def barrier(self):
        n = Sched._bar_n = Sched._bar_n + 1
        for e in ENGINES:
            self.step(e, lambda en: en.nop(), reads=(),
                      writes=[("__barA__", n, e)], tag=f"barA{n}")
        for e in ENGINES:
            self.step(e, lambda en: en.nop(),
                      reads=[("__barA__", n, x) for x in ENGINES],
                      writes=[("__barB__", n, e)], tag=f"barB{n}")

    def emit(self, block, sems, dma_sems, cc_sems):
        lastmm = {}
        for st in self.steps:
            bank = st.get("mm_bank")
            if bank is None:
                continue
            prev = lastmm.get(bank)
            if prev is not None and prev["mm_rg"] != st["mm_rg"]:
                st["deps"].add(prev["idx"])
                st.setdefault("rgdeps", set()).add(prev["idx"])
            lastmm[bank] = st
        NP = len(dma_sems)
        ncc = 0
        for st in self.steps:
            st["sig"] = False
            if st["acc"]:
                st["cc_sem"] = ncc
                ncc += 1
        assert ncc <= len(cc_sems)
        for e in ENGINES:
            for i, st in enumerate(self.eng_steps[e]):
                st["epos"] = i
        for st in self.steps:
            rgd = st.get("rgdeps", ())
            best = {}
            for d in st["deps"]:
                dep = self.steps[d]
                if dep["dma"] or dep["acc"]:
                    continue
                if dep["eng"] == st["eng"] == "te" and d not in rgd:
                    continue
                b = best.get(dep["eng"])
                if b is None or dep["epos"] > b["epos"]:
                    best[dep["eng"]] = dep
            st["best"] = best
            for dep in best.values():
                dep["sig"] = True
        use = [0] * NP
        nsw = NP * 3 // 4
        ndma = {"sp": 0, "act": 0, "pool": 0, "te": 0, "dve": 0}
        for st in self.steps:
            if st["dma"]:
                if st["eng"] == "pool":
                    p = nsw + ndma["pool"] % (NP - nsw)
                else:
                    p = ndma[st["eng"]] % nsw
                ndma[st["eng"]] += 1
                st["dma_sem"] = p
                st["dma_fence"] = use[p] * 16 if use[p] else None
                use[p] += 1
                st["dma_val"] = use[p] * 16
                st["sig_val"] = None
        for e in ENGINES:
            n = 0
            for st in self.eng_steps[e]:
                if st["dma"] or st["acc"]:
                    st["sig_val"] = None
                    continue
                st["sig_val"] = (n := n + 1) if st["sig"] else None
        for e in ENGINES:
            know = {x: 0 for x in ENGINES}
            know_d = [0] * NP
            for st in self.eng_steps[e]:
                waits, dwaits, cwaits = {}, {}, {}
                for d in sorted(st["deps"]):
                    dep = self.steps[d]
                    if dep["dma"]:
                        p = dep["dma_sem"]
                        if dep["dma_val"] > know_d[p]:
                            dwaits[p] = max(dwaits.get(p, 0), dep["dma_val"])
                    elif dep["acc"]:
                        cwaits[dep["cc_sem"]] = 1
                for de, dep in st["best"].items():
                    if dep["sig_val"] > know[de]:
                        waits[de] = max(waits.get(de, 0), dep["sig_val"])
                st["waits"] = waits
                st["dwaits"] = dwaits
                st["cwaits"] = cwaits
                for de, v in waits.items():
                    know[de] = max(know[de], v)
                for p, v in dwaits.items():
                    know_d[p] = max(know_d[p], v)
                for d in st["deps"]:
                    dep = self.steps[d]
                    if dep["eng"] == e or dep["dma"]:
                        continue
                    for de, v in dep.get("waits", {}).items():
                        if de != e:
                            know[de] = max(know[de], v)
                    for p, v in dep.get("dwaits", {}).items():
                        know_d[p] = max(know_d[p], v)

        def emit_engine(e, eng):
            for st in self.eng_steps[e]:
                if (SELF_FENCE and st["sig"] and st["sig_val"]
                        and st["sig_val"] > 1):
                    eng.wait_ge(sems[e], st["sig_val"] - 1)
                for de, v in sorted(st["waits"].items()):
                    eng.wait_ge(sems[de], v)
                for p, v in sorted(st["dwaits"].items()):
                    eng.wait_ge(dma_sems[p], v)
                for p, v in sorted(st.get("cwaits", {}).items()):
                    eng.wait_ge(cc_sems[p], v)
                if st["dma"] and st["dma_fence"] is not None:
                    eng.wait_ge(dma_sems[st["dma_sem"]], st["dma_fence"])
                inst = st["fn"](eng)
                if st["dma"]:
                    assert inst is not None, st["tag"]
                    inst.then_inc(dma_sems[st["dma_sem"]], 16)
                elif st["acc"]:
                    assert inst is not None, st["tag"]
                    inst.then_inc(cc_sems[st["cc_sem"]], 1)
                elif st["sig"]:
                    if inst is None:
                        inst = eng.nop()
                    inst.then_inc(sems[e], 1)

        @block.tensor
        def _(eng):
            emit_engine("te", eng)

        @block.vector
        def _(eng):
            emit_engine("dve", eng)

        @block.scalar
        def _(eng):
            emit_engine("act", eng)

        @block.gpsimd
        def _(eng):
            emit_engine("pool", eng)

        @block.sync
        def _(eng):
            emit_engine("sp", eng)


def _bind(fn, *a):
    return lambda eng: fn(eng, *a)


ARF_COLS = 11264
ARF = dict(mu=0, var=512, rstd=1024, mu2=1536, lnt0=2048, lnt1=2560,
           sg0=3072, sg1=3584, gg0=4096, gg1=4608, gld0=5120, gld1=5632,
           natb0=6144, natb1=7168, dnat=8192)
ARB_COLS = 44544
ARB = dict(xq=0,
           ws=16384,
           qb0=20480, qb1=21504,
           dwl0=22528, dwl1=23040,
           hq=23552,
           xb0=34816, xb1=35328, x20=35840, x21=36352,
           free=36864)
ATB = dict(aw=0,
           awf=8192,
           wv=9344,
           lng=13440,
           q=17536,
           k=19584,
           kt=21632,
           ab=23680,
           qb=25728,
           gr=27776,
           pw=29824,
           kht=32896,
           u=34944,
           uc=35456,
           g8=35584,
           s=36096,
           og=36352,
           pw2=38400,
           pw3=41472,
           end=44544)
OTB = dict(wo=0,
           ot0=8192, ot1=12288,
           lnxo0=16384,
           lnxo1=20480,
           ogt=24576,
           oln=25088)
ATF = dict(zf=0,
           P=2048,
           Pinv=4096,
           vg=6144,
           tmp=10240)
OTF = dict(ov=0,
           mu=4096, var=4608, rstd=5120, mu2=5632)


class Builder:
    def __init__(self, phases=("ffn1", "attn", "ffn2"), dumps=(), attn_cut=99,
                 gcut=99):
        self.attn_cut = attn_cut
        self.gcut = gcut
        self.phases = phases
        self.dumps = list(dumps)
        self.nc = bass.Bass("TRN2", target_bir_lowering=False, debug=False)
        self.S = Sched()
        self.ctx = ExitStack()
        self.sb = {}
        self.ps = {}
        self.dram = {}
        self.const_arrays = {}

    def sbuf(self, name, shape, dtype):
        t = self.ctx.enter_context(self.nc.sbuf_tensor(name, shape, dtype))
        self.sb[name] = t
        return t

    def psum(self, name, shape, dtype):
        t = self.ctx.enter_context(self.nc.psum_tensor(name, shape, dtype))
        self.ps[name] = t
        return t

    def din(self, name, shape, dtype):
        t = self.nc.dram_tensor(name, shape, dtype, kind="ExternalInput")
        self.dram[name] = t
        return t

    def dout(self, name, shape, dtype):
        t = self.nc.dram_tensor(name, shape, dtype, kind="ExternalOutput")
        self.dram[name] = t
        return t

    def dtmp(self, name, shape, dtype):
        t = self.nc.dram_tensor(name, shape, dtype)
        self.dram[name] = t
        return t

    def const(self, name, arr):
        if name not in self.const_arrays:
            dt = {np.dtype(np.float32): F32,
                  np.dtype(ml_dtypes.bfloat16): BF16}[arr.dtype]
            self.din("c_" + name, list(arr.shape), dt)
            self.const_arrays["c_" + name] = np.ascontiguousarray(arr)
            self.sbuf("cs_" + name, list(arr.shape), dt)
            self.DMA(self.sb["cs_" + name][:], self.dram["c_" + name].ap(),
                     [], [("c", name)], tag="ldc_" + name)
        return self.sb["cs_" + name]

    def rf(self, reg, width=512, off=0):
        c = ARF[reg] + off
        return self.sb["arf"][:, c:c + width]

    def rb(self, reg, width, off=0):
        c = ARB[reg] + off
        return self.sb["arb"][:, c:c + width]

    def TT(self, out, a, b, op, reads, writes, tag=""):
        self.S.step("dve", _bind(
            lambda e, out, a, b: e.tensor_tensor(out=out, in0=a, in1=b, op=op),
            out, a, b), reads, writes, tag=tag)

    def TS(self, out, a, s1, s2, op0, op1, reads, writes, tag="", eng="dve"):
        if s2 is None:
            self.S.step(eng, _bind(
                lambda e, out, a, s1: e.tensor_scalar(
                    out=out, in0=a, scalar1=s1, scalar2=None, op0=op0),
                out, a, s1), reads, writes, tag=tag)
        else:
            self.S.step(eng, _bind(
                lambda e, out, a, s1, s2: e.tensor_scalar(
                    out=out, in0=a, scalar1=s1, scalar2=s2, op0=op0, op1=op1),
                out, a, s1, s2), reads, writes, tag=tag)

    def STT(self, out, a, s, b, op0, op1, reads, writes, tag="", eng="dve"):
        self.S.step(eng, _bind(
            lambda e, out, a, s, b: e.scalar_tensor_tensor(
                out=out, in0=a, scalar=s, in1=b, op0=op0, op1=op1),
            out, a, s, b), reads, writes, tag=tag)

    def COPY(self, out, a, reads, writes, tag="", eng="dve"):
        self.S.step(eng, _bind(
            lambda e, out, a: e.tensor_copy(out=out, in_=a), out, a),
            reads, writes, tag=tag)

    def MEMSET(self, ap, v, writes, tag="", eng="dve"):
        self.S.step(eng, _bind(lambda e, ap, v: e.memset(ap, v), ap, v),
                    (), writes, tag=tag)

    def RED(self, out, a, op, reads, writes, tag="", absval=False):
        self.S.step("dve", _bind(
            lambda e, out, a: e.tensor_reduce(
                out=out, in_=a, axis=AX, op=op, apply_absolute_value=absval),
            out, a), reads, writes, tag=tag)

    def RECIP(self, out, a, reads, writes, tag=""):
        self.S.step("dve", _bind(
            lambda e, out, a: e.reciprocal(out=out, in_=a), out, a),
            reads, writes, tag=tag)

    def SCAN(self, out, d0, d1, init, op0, op1, reads, writes, tag=""):
        self.S.step("dve", _bind(
            lambda e, out, d0, d1: e.tensor_tensor_scan(
                out=out, data0=d0, data1=d1, initial=init, op0=op0, op1=op1),
            out, d0, d1), reads, writes, tag=tag)

    def ACTV(self, out, a, func, reads, writes, tag="", bias=0.0, scale=1.0):
        self.S.step("act", _bind(
            lambda e, out, a, bias, scale: e.activation(
                out=out, in_=a, func=func, bias=bias, scale=scale),
            out, a, bias, scale), reads, writes, tag=tag)

    def MM(self, out, lhsT, rhs, start, stop, reads, writes, tag="", tp=None):
        i = self.S.step("te", _bind(
            lambda e, out, lhsT, rhs: e.matmul(
                out, lhsT, rhs, start=start, stop=stop, tile_position=tp),
            out, lhsT, rhs), reads, writes, tag=tag)
        st = self.S.steps[i]
        st["mm_bank"] = out.tensor.name
        st["mm_rg"] = (tp[0] if tp else 0, lhsT.shape[0])

    def TPO(self, out, a, reads, writes, tag="", f32=False):
        nm = "identf" if f32 else "ident"
        ident = self.sb["cs_" + nm][:, :]
        i = self.S.step("te", _bind(
            lambda e, out, a, ident: e.transpose(out, a, ident),
            out, a, ident), list(reads) + [("c", nm)], writes, tag=tag)
        st = self.S.steps[i]
        st["mm_bank"] = out.tensor.name
        st["mm_rg"] = (0, a.shape[0])

    def DMA(self, out, a, reads, writes, tag="", eng="sp"):
        self.S.step(eng, _bind(
            lambda e, out, a: e.dma_start(out=out, in_=a), out, a),
            reads, writes, dma=True, tag=tag)

    def CC(self, kind, op, groups, ins, outs, reads, writes, tag=""):
        i = self.S.step("pool", _bind(
            lambda e, ins, outs: e.collective_compute(
                kind, op, replica_groups=groups, ins=ins, outs=outs),
            ins, outs), reads, writes, tag=tag)
        self.S.steps[i]["acc"] = True

    def build(self):
        self.din("x_fm", [D, TOK], F32)
        self.din("wg_nat", [II, D], F32)
        self.din("wu_nat", [II, D], F32)
        self.din("wd_nat", [D, II], F32)
        self.din("clip_x", [1, 1], F32)
        self.din("clip_down", [1, 1], F32)
        self.din("Wq", [D, D // 2], F32)
        self.din("Wk", [D, D // 2], F32)
        self.din("Wv", [D, D // 2], F32)
        self.din("Wgam", [D, NH], F32)
        self.din("Wf1", [D, HD], F32)
        self.din("Wf2", [HD, D // 2], F32)
        self.din("Wog1", [D, HD], F32)
        self.din("Wog2", [HD, D], F32)
        self.din("Wo", [D, D], F32)
        self.din("norm_w", [128, 8], F32)
        self.din("hgmask", [128, 1], F32)
        self.sbuf("hgm", [128, 1], F32)
        self.dout("out_fm", [D, TOK], F32)

        self.dtmp("ln_spill", [D, TOK], F32)
        self.dtmp("g_spill", [II, TOK], F32)
        self.dtmp("x1_spill", [D, TOK], F32)
        self.dtmp("wq_spill", [3 * NIC, 128, 1024], BF16)
        self.dtmp("o_spill", [D // 2, TFULL], BF16)
        self.dtmp("ar_in", [1, 128], F32)
        self.dtmp("ar_out", [1, 128], F32)
        self.dtmp("ag_lnx_in", [D, TOK], BF16)
        self.dtmp("ag_lnx_out", [2 * D, TOK], BF16)
        self.dtmp("ag_o_out", [D, TFULL], BF16)

        self.sbuf("xs", [128, 8 * TOK], F32)
        self.sbuf("arf", [128, ARF_COLS], F32)
        self.sbuf("arb", [128, ARB_COLS], BF16)
        self.sbuf("stats", [128, 256], F32)
        self.sbuf("small", [128, 64], F32)
        self.sbuf("srow", [1, 512], F32)

        for i in range(6):
            self.psum(f"pb{i}", [128, 512], F32)
        self.psum("ptb", [128, 1024], BF16)
        self.psum("pb7", [128, 512], F32)

        self.const("ones128", np.ones((128, 128), dtype=ml_dtypes.bfloat16))
        bo = np.zeros((128, 128), dtype=np.float32)
        bo[0:64, 0:64] = 1.0
        bo[64:128, 64:128] = 1.0
        self.const("blockones", bo.astype(ml_dtypes.bfloat16))
        sel = np.zeros((8, 512), dtype=np.float32)
        for h in range(8):
            for pcol in range(512):
                if h == 2 * (pcol // 128) + (pcol % 128) // 64:
                    sel[h, pcol] = 1.0
        self.const("sel8", sel.astype(ml_dtypes.bfloat16))
        pp, ff = np.meshgrid(np.arange(128), np.arange(512), indexing="ij")
        self.const("mstril", ((ff % 64) < (pp % 64)).astype(ml_dtypes.bfloat16))
        self.const("mstriu", ((ff % 64) > (pp % 64)).astype(ml_dtypes.bfloat16))
        self.const("mtriuincl", ((ff % 64) >= (pp % 64)).astype(ml_dtypes.bfloat16))
        self.const("identblk", ((ff % 64) == (pp % 64)).astype(ml_dtypes.bfloat16))
        self.const("zero64", np.zeros((128, 64), dtype=np.float32))
        self.const("ones1f", np.ones((1, 128), dtype=np.float32))
        self.const("ident", np.eye(128, dtype=np.float32).astype(ml_dtypes.bfloat16))
        self.const("identf", np.eye(128, dtype=np.float32))
        self.const("eps", np.full((128, 1), 1e-5, dtype=np.float32))

        xs = self.sb["xs"]
        xin = self.dram["x_fm"].ap()
        self.DMA(self.sb["hgm"][:], self.dram["hgmask"].ap(), [], [("hgm",)],
                 tag="ldhgm")
        for f in range(8):
            self.DMA(xs[:, f * TOK:(f + 1) * TOK], xin[f * 128:(f + 1) * 128, :],
                     [], [("x", f)], tag=f"ldx{f}")

        if "ffn1" in self.phases:
            self.emit_ffn(0)
        if "attn" in self.phases:
            self.emit_attn()
        if "ffn2" in self.phases:
            self.emit_ffn(1)

        outap = self.dram["out_fm"].ap()
        for f in range(8):
            self.DMA(outap[f * 128:(f + 1) * 128, :], xs[:, f * TOK:(f + 1) * TOK],
                     [("x", f)], [("ost", f)], tag=f"stx{f}")

        for name, src, reads in self.dumps:
            t = self.dout("dump_" + name, [src.shape[0], src.shape[1]], src.dtype)
            self.DMA(t.ap(), src, list(reads), [("dump", name)],
                     tag="dump_" + name)
        self.S.barrier()
        self.S.step("sp", lambda e: e.nop(),
                    reads=[("ost", f) for f in range(8)] +
                          [("dump", n) for n, _, _ in self.dumps],
                    tag="final_wait")
        return self

    def _scales_from_max(self, P, clip_ap):
        sm = self.sb["small"]
        self.TPO(self.ps["pb7"][0:1, 0:128], sm[:, 8:9], [("sm", 8)],
                 [("psum", 7)], tag=P + "mxT", f32=True)
        self.COPY(self.sb["srow"][0:1, 0:128], self.ps["pb7"][0:1, 0:128],
                  [("psum", 7)], [("srow",), ("psum", 7)], tag=P + "mxc")
        self.RED(sm[0:1, 10:11], self.sb["srow"][0:1, 0:128], ALU.max,
                 [("srow",)], [("sm", 10)], tag=P + "mxr")
        self.TS(self.sb["srow"][0:1, 128:256], self.sb["cs_ones1f"][0:1, :],
                sm[0:1, 10:11], None, ALU.mult, None,
                [("sm", 10), ("c", "ones1f")], [("srow2",)], tag=P + "mxbc")
        self.DMA(self.dram["ar_in"].ap()[0:1, :], self.sb["srow"][0:1, 128:256],
                 [("srow2",)], [("ar",)], tag=P + "arst")
        self.CC("AllReduce", ALU.max, [list(range(8))],
                [self.dram["ar_in"].ap()], [self.dram["ar_out"].ap()],
                [("ar",)], [("ar",)], tag=P + "ar")
        self.DMA(sm[0:1, 11:12], self.dram["ar_out"].ap()[0:1, 0:1],
                 [("ar",)], [("sm", 11)], tag=P + "arld")
        self.DMA(sm[0:1, 12:13], clip_ap, [], [("sm", 12)], tag=P + "clipld")
        self.ACTV(sm[0:1, 13:14], sm[0:1, 12:13], AF.Tanh,
                  [("sm", 12)], [("sm", 13)], tag=P + "tanh", scale=0.5)
        self.TS(sm[0:1, 13:14], sm[0:1, 13:14], 0.5, 0.5, ALU.mult, ALU.add,
                [("sm", 13)], [("sm", 13)], tag=P + "sig")
        self.TT(sm[0:1, 14:15], sm[0:1, 13:14], sm[0:1, 11:12], ALU.mult,
                [("sm", 13), ("sm", 11)], [("sm", 14)], tag=P + "sM")
        self.TS(sm[0:1, 15:16], sm[0:1, 14:15], 1.0 / QMAX, None, ALU.mult, None,
                [("sm", 14)], [("sm", 15)], tag=P + "s")
        self.RECIP(sm[0:1, 16:17], sm[0:1, 14:15], [("sm", 14)], [("sm", 16)],
                   tag=P + "rinv")
        self.TS(sm[0:1, 17:18], sm[0:1, 16:17], QMAX, None, ALU.mult, None,
                [("sm", 16)], [("sm", 17)], tag=P + "r")
        ones1f = self.sb["cs_ones1f"]
        self.MM(self.ps["pb7"][:, 0:1], ones1f[0:1, :], sm[0:1, 17:18],
                True, True, [("sm", 17), ("c", "ones1f"), ("psum", 7)],
                [("psum", 7)], tag=P + "bcr")
        self.COPY(sm[:, 0:1], self.ps["pb7"][:, 0:1], [("psum", 7)],
                  [("sm", 0), ("psum", 7)], tag=P + "bcrc")
        self.MM(self.ps["pb7"][:, 0:1], ones1f[0:1, :], sm[0:1, 15:16],
                True, True, [("sm", 15), ("c", "ones1f"), ("psum", 7)],
                [("psum", 7)], tag=P + "bcs")
        self.COPY(sm[:, 5:6], self.ps["pb7"][:, 0:1], [("psum", 7)],
                  [("sm", 5), ("psum", 7)], tag=P + "bcsc")

    def quant_round(self, src, rcol, rk, out, srck, outk, tag, fbuf=None,
                    fkey=None):
        if fbuf is None:
            fbuf = self.rf("var", src.shape[1])
            fkey = ("arf", "var")
        MAGIC = 12582912.0
        self.TS(src, src, rcol, MAGIC, ALU.mult, ALU.add, [srck, rk], [srck],
                tag=tag + "a")
        self.TS(src, src, -MAGIC, None, ALU.add, None, [srck], [srck],
                tag=tag + "b")
        self.TS(out, src, 127.0, -128.0, ALU.min, ALU.max, [srck], [outk],
                tag=tag + "d")

    def emit_ffn(self, kk):
        xs = self.sb["xs"]
        arb = self.sb["arb"]
        stats = self.sb["stats"]
        sm = self.sb["small"]
        pb = [self.ps[f"pb{i}"] for i in range(6)]
        ptb = self.ps["ptb"]
        ones128 = self.sb["cs_ones128"]
        P = f"f{kk}"
        wsp = self.dram["wq_spill"].ap()

        if kk == 0:
            for widx, nm in ((0, "wg_nat"), (1, "wu_nat")):
                nat = self.dram[nm].ap()
                for ic in range(NIC):
                    par = ic % 2
                    natb = self.rf("natb0" if par == 0 else "natb1", 1024)
                    natk = ("arf", "natb", par)
                    self.DMA(natb, nat[ic * 128:(ic + 1) * 128, :],
                             [natk], [natk], tag=f"natld{widx}_{ic}")
                    self.RED(sm[:, 20:21], natb, ALU.max, [natk], [("sm", 20)],
                             tag=f"wmax{widx}_{ic}", absval=True)
                    self.TS(stats[:, widx * 32 + ic: widx * 32 + ic + 1],
                            sm[:, 20:21], 1.0 / QMAX, None, ALU.mult, None,
                            [("sm", 20)], [("stats", widx * 32 + ic)],
                            tag=f"wsc{widx}_{ic}")
                    self.RECIP(sm[:, 21:22], sm[:, 20:21], [("sm", 20)],
                               [("sm", 21)], tag=f"wrc{widx}_{ic}")
                    self.TS(sm[:, 22:23], sm[:, 21:22], QMAX, None, ALU.mult,
                            None, [("sm", 21)], [("sm", 22)],
                            tag=f"wr2{widx}_{ic}")
                    qb = self.rb("qb0" if par == 0 else "qb1", 1024)
                    qbk = ("arb", "qb", par)
                    self.quant_round(natb, sm[:, 22:23], ("sm", 22), qb,
                                     natk, qbk, f"wq{widx}_{ic}",
                                     fbuf=self.rf("dnat", 1024),
                                     fkey=("arf", "dnat"))
                    for f in range(8):
                        self.TPO(ptb[:, f * 128:(f + 1) * 128],
                                 qb[:, f * 128:(f + 1) * 128],
                                 [qbk], [("ptb", f)], tag=f"wt{widx}_{ic}_{f}")
                    stt = self.rb("ws", 1024, par * 2048)
                    sttk = ("arb", "ws", par, 0)
                    self.COPY(stt, ptb[:, :], [("ptb", f) for f in range(8)],
                              [sttk] + [("ptb", f) for f in range(8)],
                              tag=f"wte{widx}_{ic}")
                    self.DMA(wsp[widx * NIC + ic, :, :], stt, [sttk],
                             [("wq", widx, ic)], tag=f"wsp{widx}_{ic}")
            nat = self.dram["wd_nat"].ap()
            for oc in range(8):
                natb = self.rf("dnat", 2816)
                natk = ("arf", "dnat")
                self.DMA(natb, nat[oc * 128:(oc + 1) * 128, :], [natk], [natk],
                         tag=f"dnatld{oc}")
                self.RED(sm[:, 20:21], natb, ALU.max, [natk], [("sm", 20)],
                         tag=f"dmax{oc}", absval=True)
                self.TS(stats[:, 64 + oc: 64 + oc + 1], sm[:, 20:21],
                        1.0 / QMAX, None, ALU.mult, None, [("sm", 20)],
                        [("stats", 64 + oc)], tag=f"dsc{oc}")
                self.RECIP(sm[:, 21:22], sm[:, 20:21], [("sm", 20)],
                           [("sm", 21)], tag=f"drc{oc}")
                self.TS(sm[:, 22:23], sm[:, 21:22], QMAX, None, ALU.mult, None,
                        [("sm", 21)], [("sm", 22)], tag=f"dr2{oc}")
                for part, w in ((0, 1024), (1024, 1024), (2048, 768)):
                    qb = self.rb("qb0", 1024)
                    qbk = ("arb", "qb", 0)
                    self.quant_round(natb[:, part:part + w], sm[:, 22:23],
                                     ("sm", 22), qb[:, 0:w], natk, qbk,
                                     f"dq{oc}_{part}",
                                     fbuf=self.rf("natb0", w),
                                     fkey=("arf", "natb", 0))
                    for j in range(w // 128):
                        ic = (part + j * 128) // 128
                        self.TPO(ptb[:, j * 128:(j + 1) * 128],
                                 qb[:, j * 128:(j + 1) * 128],
                                 [qbk], [("ptb", j)], tag=f"dt{oc}_{ic}")
                    stt = self.rb("ws", 1024, 0)
                    sttk = ("arb", "ws", 0, 0)
                    self.COPY(stt[:, 0:w], ptb[:, 0:w],
                              [("ptb", j) for j in range(w // 128)],
                              [sttk] + [("ptb", j) for j in range(w // 128)],
                              tag=f"dte{oc}_{part}")
                    for j in range(w // 128):
                        ic = (part + j * 128) // 128
                        self.DMA(wsp[2 * NIC + ic, :, oc * 128:(oc + 1) * 128],
                                 stt[:, j * 128:(j + 1) * 128], [sttk],
                                 [("wqd", ic, oc)], tag=f"dsp{oc}_{ic}")

        self.MEMSET(sm[:, 8:9], 0.0, [("sm", 8)], tag=P + "mx0")
        for tt in range(4):
            for f in range(8):
                xsl = xs[:, f * TOK + tt * 512: f * TOK + tt * 512 + 512]
                xb = self.rb("xb0" if f % 2 == 0 else "xb1", 512)
                x2 = self.rb("x20" if f % 2 == 0 else "x21", 512)
                xbk = ("arb", "xb", f % 2)
                x2k = ("arb", "x2", f % 2)
                self.COPY(xb, xsl, [("x", f)], [xbk], tag=P + f"cast{tt}_{f}")
                self.TT(x2, xb, xb, ALU.mult, [xbk], [x2k], tag=P + f"sq{tt}_{f}")
                self.MM(pb[0][:, :], ones128[:], xb, f == 0, f == 7,
                        [xbk, ("c", "ones128"), ("psum", 0)], [("psum", 0)],
                        tag=P + f"sums{tt}_{f}")
                self.MM(pb[1][:, :], ones128[:], x2, f == 0, f == 7,
                        [x2k, ("c", "ones128"), ("psum", 1)], [("psum", 1)],
                        tag=P + f"sumsq{tt}_{f}")
            mu, var, rstd, mu2 = (self.rf(r) for r in ("mu", "var", "rstd", "mu2"))
            self.TS(mu, pb[0][:, :], 1.0 / D, None, ALU.mult, None,
                    [("psum", 0)], [("arf", "mu"), ("psum", 0)], tag=P + f"mu{tt}")
            self.TS(var, pb[1][:, :], 1.0 / D, None, ALU.mult, None,
                    [("psum", 1)], [("arf", "var"), ("psum", 1)], tag=P + f"msq{tt}")
            self.TT(mu2, mu, mu, ALU.mult, [("arf", "mu")], [("arf", "mu2")],
                    tag=P + f"mu2{tt}")
            self.TT(var, var, mu2, ALU.subtract,
                    [("arf", "var"), ("arf", "mu2")], [("arf", "var")],
                    tag=P + f"var{tt}")
            self.ACTV(rstd, var, AF.Sqrt, [("arf", "var"), ("c", "eps")],
                      [("arf", "rstd")], tag=P + f"sd{tt}",
                      bias=self.sb["cs_eps"][:, 0:1])
            self.RECIP(rstd, rstd, [("arf", "rstd")], [("arf", "rstd")],
                       tag=P + f"rstd{tt}")
            for f in range(8):
                xsl = xs[:, f * TOK + tt * 512: f * TOK + tt * 512 + 512]
                lnt = self.rf("lnt0" if f % 2 == 0 else "lnt1")
                lk = ("arf", "lnt", f % 2)
                self.TT(lnt, xsl, mu, ALU.subtract, [("x", f), ("arf", "mu")],
                        [lk], tag=P + f"lsub{tt}_{f}")
                self.TT(lnt, lnt, rstd, ALU.mult, [lk, ("arf", "rstd")], [lk],
                        tag=P + f"lmul{tt}_{f}")
                self.DMA(self.dram["ln_spill"].ap()[f * 128:(f + 1) * 128,
                                                    tt * 512:(tt + 1) * 512],
                         lnt, [lk], [("lnsp", tt, f)], tag=P + f"lnsp{tt}_{f}")
                self.RED(sm[:, 9:10], lnt, ALU.max, [lk], [("sm", 9)],
                         tag=P + f"lmax{tt}_{f}", absval=True)
                self.TT(sm[:, 8:9], sm[:, 8:9], sm[:, 9:10], ALU.max,
                        [("sm", 9), ("sm", 8)], [("sm", 8)],
                        tag=P + f"mxa{tt}_{f}")

        self._scales_from_max(P, self.dram["clip_x"].ap())

        for f in range(8):
            for tt in range(4):
                lnt = self.rf("lnt0" if (f + tt) % 2 == 0 else "lnt1")
                lk = ("arf", "lnt", (f + tt) % 2)
                self.DMA(lnt, self.dram["ln_spill"].ap()[f * 128:(f + 1) * 128,
                                                         tt * 512:(tt + 1) * 512],
                         [("lnsp", tt, f), lk], [lk], tag=P + f"lnld{tt}_{f}")
                self.quant_round(lnt, sm[:, 0:1], ("sm", 0),
                                 arb[:, f * TOK + tt * 512: f * TOK + tt * 512 + 512],
                                 lk, ("xq", f, tt), P + f"q{tt}_{f}")

        for widx in range(2):
            for ic in range(NIC):
                self.TT(stats[:, 128 + widx * 32 + ic: 128 + widx * 32 + ic + 1],
                        stats[:, widx * 32 + ic: widx * 32 + ic + 1],
                        sm[:, 5:6], ALU.mult,
                        [("stats", widx * 32 + ic), ("sm", 5)],
                        [("stats", 128 + widx * 32 + ic)],
                        tag=P + f"esc{widx}_{ic}")

        self.MEMSET(sm[:, 8:9], 0.0, [("sm", 8)], tag=P + "gmx0")
        for ic in range(NIC):
            par = ic % 2
            gst = self.rb("ws", 1024, par * 2048)
            ust = self.rb("ws", 1024, par * 2048 + 1024)
            gstk = ("arb", "ws", par, 0)
            ustk = ("arb", "ws", par, 1)
            self.DMA(gst, wsp[ic, :, :], [gstk, ("wq", 0, ic)], [gstk],
                     tag=P + f"gwld{ic}")
            self.DMA(ust, wsp[NIC + ic, :, :], [ustk, ("wq", 1, ic)], [ustk],
                     tag=P + f"uwld{ic}")
            for tt in range(4):
                bg = 2 + (tt % 2)
                bu = 4 + (tt % 2)
                for f in range(8):
                    self.MM(pb[bg][:, :], gst[:, f * 128:(f + 1) * 128],
                            arb[:, f * TOK + tt * 512: f * TOK + tt * 512 + 512],
                            f == 0, f == 7,
                            [("xq", f, tt), gstk, ("psum", bg)], [("psum", bg)],
                            tag=P + f"gm{ic}_{tt}_{f}")
                pbu = pb[bu] if bu < 6 else self.ps["pb7"]
                for f in range(8):
                    self.MM(pbu[:, :], ust[:, f * 128:(f + 1) * 128],
                            arb[:, f * TOK + tt * 512: f * TOK + tt * 512 + 512],
                            f == 0, f == 7,
                            [("xq", f, tt), ustk, ("psum", bu)], [("psum", bu)],
                            tag=P + f"um{ic}_{tt}_{f}")
                pbu = pb[bu] if bu < 6 else self.ps["pb7"]
                sg = self.rf("sg0" if tt % 2 == 0 else "sg1")
                gg = self.rf("gg0" if tt % 2 == 0 else "gg1")
                sgk = ("arf", "sg", tt % 2)
                ggk = ("arf", "gg", tt % 2)
                self.ACTV(sg, pb[bg][:, :], AF.Silu,
                          [("psum", bg), ("stats", 128 + ic)],
                          [sgk, ("psum", bg)], tag=P + f"sge{ic}_{tt}",
                          scale=stats[:, 128 + ic: 128 + ic + 1])
                self.STT(gg, pbu[:, :], stats[:, 160 + ic:160 + ic + 1], sg,
                         ALU.mult, ALU.mult,
                         [("psum", bu), ("stats", 160 + ic), sgk],
                         [ggk, ("psum", bu)], tag=P + f"ge{ic}_{tt}")
                self.DMA(self.dram["g_spill"].ap()[ic * 128:(ic + 1) * 128,
                                                   tt * 512:(tt + 1) * 512],
                         gg, [ggk], [("gsp", ic, tt)], tag=P + f"gsp{ic}_{tt}")
                self.RED(sm[:, 9:10], gg, ALU.max, [ggk], [("sm", 9)],
                         tag=P + f"gmx{ic}_{tt}", absval=True)
                self.TT(sm[:, 8:9], sm[:, 8:9], sm[:, 9:10], ALU.max,
                        [("sm", 9), ("sm", 8)], [("sm", 8)],
                        tag=P + f"gma{ic}_{tt}")

        self._scales_from_max(P + "h", self.dram["clip_down"].ap())

        for oc in range(8):
            self.TT(stats[:, 192 + oc:192 + oc + 1],
                    stats[:, 64 + oc:64 + oc + 1], sm[:, 5:6], ALU.mult,
                    [("stats", 64 + oc), ("sm", 5)], [("stats", 192 + oc)],
                    tag=P + f"descA{oc}")
            self.TS(stats[:, 192 + oc:192 + oc + 1],
                    stats[:, 192 + oc:192 + oc + 1], 0.5, None, ALU.mult, None,
                    [("stats", 192 + oc)], [("stats", 192 + oc)],
                    tag=P + f"descB{oc}")

        for tt in range(4):
            for ic in range(NIC):
                par = ic % 2
                gg = self.rf("gld0" if par == 0 else "gld1")
                ggk = ("arf", "gld", par)
                self.DMA(gg, self.dram["g_spill"].ap()[ic * 128:(ic + 1) * 128,
                                                       tt * 512:(tt + 1) * 512],
                         [ggk, ("gsp", ic, tt)], [ggk], tag=P + f"gld{tt}_{ic}")
                self.quant_round(gg, sm[:, 0:1], ("sm", 0),
                                 self.rb("hq", 512, ic * 512),
                                 ggk, ("hq", ic), P + f"hq{tt}_{ic}")
            for ph in range(2):
                bks = [0, 1, 2, 3] if ph == 0 else [4, 5, 7, 2]
                for ic in range(NIC):
                    par = ic % 2
                    dwl = self.rb("dwl0" if par == 0 else "dwl1", 512)
                    dwlk = ("arb", "dwl", par)
                    self.DMA(dwl, wsp[2 * NIC + ic, :, ph * 512:(ph + 1) * 512],
                             [dwlk] + [("wqd", ic, ph * 4 + j) for j in range(4)],
                             [dwlk], tag=P + f"dwld{tt}_{ph}_{ic}")
                    for j in range(4):
                        bk = bks[j]
                        pbj = pb[bk] if bk < 6 else self.ps["pb7"]
                        self.MM(pbj[:, :], dwl[:, j * 128:(j + 1) * 128],
                                self.rb("hq", 512, ic * 512),
                                ic == 0, ic == NIC - 1,
                                [("hq", ic), dwlk, ("psum", bk)], [("psum", bk)],
                                tag=P + f"dm{tt}_{ph}_{ic}_{j}")
                for j in range(4):
                    bk = bks[j]
                    pbj = pb[bk] if bk < 6 else self.ps["pb7"]
                    oc = ph * 4 + j
                    xsl = xs[:, oc * TOK + tt * 512: oc * TOK + tt * 512 + 512]
                    self.STT(xsl, pbj[:, :], stats[:, 192 + oc:192 + oc + 1],
                             xsl, ALU.mult, ALU.add,
                             [("psum", bk), ("stats", 192 + oc), ("x", oc)],
                             [("x", oc), ("psum", bk)], tag=P + f"dres{tt}_{oc}")



    def emit_attn_group(self, g):
        P = f"ag{g}"
        pb = [self.ps[f"pb{i}"] for i in range(6)]
        pb7 = self.ps["pb7"]
        ptb = self.ps["ptb"]
        lng = self.ab_("lng", 4096)
        lngk = ("arb", "lng")
        hgb = g // 4

        self.DMA(lng.rearrange("p (a m) -> p a m", a=8),
                 self.dram["ag_lnx_out"].ap()[hgb * D:(hgb + 1) * D,
                                              (g % 4) * 512:(g % 4) * 512 + 512]
                 .rearrange("(a p) m -> p a m", p=128),
                 [lngk, ("aglnx",)], [lngk], tag=P + "lngld")

        for which, kbase in (("q", 0), ("k", 4096)):
            for oc in range(4):
                bnk = 2 + (oc % 2)
                for kc in range(8):
                    self.MM(pb[bnk][:, :],
                            self.ab_("aw", 128, kbase + kc * 512 + oc * 128),
                            lng[:, kc * 512:(kc + 1) * 512],
                            kc == 0, kc == 7,
                            [lngk, ("aw", which, kc), ("psum", bnk)],
                            [("psum", bnk)], tag=P + f"{which}m{oc}_{kc}")
                t = self.af_("tmp", 512)
                tk = ("arf", "tmp")
                self.ACTV(t, pb[bnk][:, :], AF.Tanh,
                          [("psum", bnk)], [tk], tag=P + f"{which}th{oc}",
                          scale=0.5)
                self.TS(t, t, 0.5, 0.5, ALU.mult, ALU.add, [tk], [tk],
                        tag=P + f"{which}sg{oc}")
                dst = self.ab_(which, 512, oc * 512)
                self.TT(dst, t, pb[bnk][:, :], ALU.mult,
                        [tk, ("psum", bnk)],
                        [("arb", which, oc), ("psum", bnk)],
                        tag=P + f"{which}sl{oc}")
        if self.gcut < 2:
            return
        bo = self.sb["cs_blockones"]
        for oc in range(4):
            k2 = self.af_("tmp", 512)
            tk = ("arf", "tmp")
            kt = self.ab_("k", 512, oc * 512)
            k2b = self.rb("xb0", 512)
            self.TT(k2b, kt, kt, ALU.mult, [("arb", "k", oc)],
                    [("arb", "xb", 0)], tag=P + f"k2_{oc}")
            self.MM(pb7[:, :], bo[:], k2b, True, True,
                    [("arb", "xb", 0), ("c", "blockones"), ("psum", 7)],
                    [("psum", 7)], tag=P + f"k2m{oc}")
            self.ACTV(k2, pb7[:, :], AF.Sqrt, [("psum", 7)],
                      [tk, ("psum", 7)], tag=P + f"knrm{oc}")
            self.TS(k2, k2, 1e-12, None, ALU.max, None, [tk], [tk],
                    tag=P + f"knmx{oc}")
            self.RECIP(k2, k2, [tk], [tk], tag=P + f"knr{oc}")
            self.TT(kt, kt, k2, ALU.mult, [("arb", "k", oc), tk],
                    [("arb", "k", oc)], tag=P + f"knd{oc}")

        if self.gcut < 3:
            return
        for kc in range(8):
            self.DMA(self.ab_("wv", 512, kc * 512),
                     self.dram["Wv"].ap()[kc * 128:(kc + 1) * 128, :],
                     [("arb", "wv", kc)], [("arb", "wv", kc)],
                     tag=P + f"wvl{kc}", eng="pool")
        for c in range(8):
            bnk = 4 + (c % 2)
            for kc in range(8):
                self.MM(pb[bnk][0:64, :],
                        lng[:, kc * 512 + c * 64: kc * 512 + c * 64 + 64],
                        self.ab_("wv", 512, kc * 512), kc == 0, kc == 7,
                        [lngk, ("arb", "wv", kc), ("psum", bnk)],
                        [("psum", bnk)], tag=P + f"vm{c}_{kc}")
            self.COPY(self.af_("vg", 512, c * 512)[0:64, :], pb[bnk][0:64, :],
                      [("psum", bnk)], [("arf", "vg", c), ("psum", bnk)],
                      tag=P + f"ve{c}")
        self.DMA(self.sb["arf"][64:128, ATF["vg"]:ATF["vg"] + 4096],
                 self.sb["arf"][0:64, ATF["vg"]:ATF["vg"] + 4096],
                 [("arf", "vg", c) for c in range(8)] + [("arf", "vgd")],
                 [("arf", "vgd")], tag=P + "vdup")

        if self.gcut < 4:
            return
        for kc in range(8):
            self.MM(pb7[0:64, :], self.ab_("awf", 64, kc * 64),
                    lng[:, kc * 512:(kc + 1) * 512], kc == 0, kc == 7,
                    [lngk, ("aw", "f1", kc), ("psum", 7)], [("psum", 7)],
                    tag=P + f"f1m{kc}")
        xf1 = self.sb["arb"][0:64, ATB["u"]:ATB["u"] + 512]
        self.COPY(xf1, pb7[0:64, :], [("psum", 7)],
                  [("arb", "xf1"), ("psum", 7)], tag=P + "f1e")
        for oc in range(4):
            bnk = 2 + (oc % 2)
            self.MM(pb[bnk][:, :],
                    self.sb["arb"][0:64, ATB["awf"] + 512 + oc * 128:
                                   ATB["awf"] + 512 + (oc + 1) * 128],
                    xf1, True, True,
                    [("arb", "xf1"), ("aw", "f2"), ("psum", bnk)],
                    [("psum", bnk)], tag=P + f"f2m{oc}")
            lam = self.af_("zf", 512, oc * 512)
            self.ACTV(lam, pb[bnk][:, :], AF.Tanh, [("psum", bnk)],
                      [("arf", "zf", oc), ("psum", bnk)], tag=P + f"lth{oc}",
                      scale=0.5)
            self.TS(lam, lam, 0.5, 0.5, ALU.mult, ALU.add,
                    [("arf", "zf", oc)], [("arf", "zf", oc)], tag=P + f"lsg{oc}")

        for kc in range(8):
            self.MM(pb7[0:8, :], self.ab_("awf", 8, 1024 + kc * 8),
                    lng[:, kc * 512:(kc + 1) * 512], kc == 0, kc == 7,
                    [lngk, ("aw", "g", kc), ("psum", 7)], [("psum", 7)],
                    tag=P + f"gm{kc}")
        gamt = self.sb["arf"][0:8, ATF["tmp"]:ATF["tmp"] + 512]
        self.ACTV(gamt, pb7[0:8, :], AF.Tanh, [("psum", 7)],
                  [("arf", "tmp"), ("psum", 7)], tag=P + "gth", scale=0.5)
        self.TS(gamt, gamt, -0.5, -0.5, ALU.mult, ALU.add, [("arf", "tmp")],
                [("arf", "tmp")], tag=P + "gsg")
        gamb = self.sb["arb"][0:8, ATB["g8"]:ATB["g8"] + 512]
        self.COPY(gamb, gamt, [("arf", "tmp")], [("arb", "g8")], tag=P + "gbc")
        sel = self.sb["cs_sel8"]
        for pt in range(4):
            self.MM(pb7[:, :], sel[:, pt * 128:(pt + 1) * 128], gamb,
                    True, True,
                    [("arb", "g8"), ("c", "sel8"), ("psum", 7)],
                    [("psum", 7)], tag=P + f"grm{pt}")
            self.COPY(self.ab_("gr", 512, pt * 512), pb7[:, :], [("psum", 7)],
                      [("arb", "gr", pt), ("psum", 7)], tag=P + f"gre{pt}")

        if self.gcut < 5:
            return
        z64 = self.sb["cs_zero64"]
        for pt in range(4):
            for c in range(8):
                off = ATF["P"] + pt * 512 + c * 64
                zoff = ATF["zf"] + pt * 512 + c * 64
                self.SCAN(self.sb["arf"][:, off:off + 64],
                          self.sb["arf"][:, zoff:zoff + 64],
                          z64[:, 0:64], 1.0, ALU.mult, ALU.add,
                          [("arf", "zf", pt), ("c", "zero64")],
                          [("arf", "P", pt)], tag=P + f"scan{pt}_{c}")
            self.RECIP(self.af_("Pinv", 512, pt * 512),
                       self.af_("P", 512, pt * 512),
                       [("arf", "P", pt)], [("arf", "Pinv", pt)],
                       tag=P + f"pinv{pt}")

        if self.gcut < 6:
            return
        for pt in range(4):
            q = self.ab_("q", 512, pt * 512)
            k = self.ab_("k", 512, pt * 512)
            Pt = self.af_("P", 512, pt * 512)
            Pi = self.af_("Pinv", 512, pt * 512)
            lam = self.af_("zf", 512, pt * 512)
            t = self.af_("tmp", 512)
            tk = ("arf", "tmp")
            self.TT(self.ab_("qb", 512, pt * 512), q, Pt, ALU.mult,
                    [("arb", "q", pt), ("arf", "P", pt)], [("arb", "qb", pt)],
                    tag=P + f"qb{pt}")
            self.TT(self.ab_("kt", 512, pt * 512), k, Pi, ALU.mult,
                    [("arb", "k", pt), ("arf", "Pinv", pt)], [("arb", "kt", pt)],
                    tag=P + f"ktb{pt}")
            self.TT(t, lam, Pt, ALU.mult,
                    [("arf", "zf", pt), ("arf", "P", pt)], [tk],
                    tag=P + f"lp{pt}")
            self.TT(t, t, self.ab_("gr", 512, pt * 512), ALU.mult,
                    [tk, ("arb", "gr", pt)], [tk], tag=P + f"lpg{pt}")
            self.TT(self.ab_("ab", 512, pt * 512), k, t, ALU.mult,
                    [("arb", "k", pt), tk], [("arb", "ab", pt)],
                    tag=P + f"abt{pt}")
            for c in range(8):
                self.TS(self.ab_("k", 64, pt * 512 + c * 64),
                        self.ab_("kt", 64, pt * 512 + c * 64),
                        self.sb["arf"][:, ATF["P"] + pt * 512 + c * 64 + 63:
                                       ATF["P"] + pt * 512 + c * 64 + 64],
                        None, ALU.mult, None,
                        [("arb", "kt", pt), ("arf", "P", pt)],
                        [("arb", "k", pt)], tag=P + f"khf{pt}_{c}")

        if self.gcut < 7:
            return
        for pt in range(4):
            for cp in range(4):
                self.TPO(ptb[:, cp * 128:(cp + 1) * 128],
                         self.ab_("k", 128, pt * 512 + cp * 128),
                         [("arb", "k", pt)], [("ptb", cp)],
                         tag=P + f"khT{pt}_{cp}")
            self.COPY(self.ab_("kht", 512, pt * 512), ptb[:, 0:512],
                      [("ptb", cp) for cp in range(4)],
                      [("arb", "kht", pt)] + [("ptb", cp) for cp in range(4)],
                      tag=P + f"khE{pt}")

        if self.gcut < 8:
            return
        self.run_pairs(g)

        for pt in range(4):
            self.DMA(self.dram["o_spill"].ap()[pt * 128:(pt + 1) * 128,
                                               g * 512:(g + 1) * 512],
                     self.ab_("og", 512, pt * 512),
                     [("arb", "og", pt, c) for c in range(8)],
                     [("osp", g, pt)], tag=P + f"osp{pt}")

    PW = dict(T=0, W=512, sT=1024, sW=1536, N=2048, AT=2560)

    def pwsl(self, slot, j, c, pp):
        base = ATB[("pw", "pw2", "pw3")[pp]] + self.PW[slot] + 256 * j + 64 * (c // 2)
        r = 64 * (c % 2)
        return self.sb["arb"][r:r + 64, base:base + 64]

    def pwt(self, slot, pp):
        return self.ab_(("pw", "pw2", "pw3")[pp], 512, self.PW[slot])

    def _gsl(self, reg, h, c):
        pt = h // 2
        base = ATB[reg] + pt * 512 + c * 64
        r = 64 * (h % 2)
        return self.sb["arb"][r:r + 64, base:base + 64]

    def _vsl(self, h, c):
        base = ATF["vg"] + c * 512 + h * 64
        r = 64 * (c % 2)
        return self.sb["arf"][r:r + 64, base:base + 64]

    def _khsl(self, h, c):
        base = ATB["kht"] + (h // 2) * 512 + (c // 2) * 128 + 64 * (h % 2)
        r = 64 * (c % 2)
        return self.sb["arf"] if False else self.sb["arb"][r:r + 64, base:base + 64]

    def _ssl(self, h):
        base = ATB["s"] + 64 * (h // 2)
        r = 64 * (h % 2)
        return self.sb["arb"][r:r + 64, base:base + 64]

    def gen_precompute(self, g, hp):
        P = f"ag{g}p{hp}"
        pp = hp % 3
        pb = [self.ps[f"pb{i}"] for i in range(6)]
        pb7 = self.ps["pb7"]
        if hp % 2 == 0:
            bt, bw, kt_, kw_ = pb[2], pb[3], 2, 3
        else:
            bt, bw, kt_, kw_ = pb[1], pb7, 1, 7
        kkey = ("arb", "pw", pp)
        msl = self.sb["cs_mstril"]
        msu = self.sb["cs_mstriu"]
        mti = self.sb["cs_mtriuincl"]
        idb = self.sb["cs_identblk"]
        heads = (2 * hp, 2 * hp + 1)
        gsl = self._gsl
        arb = self.sb["arb"]

        def osl(j, c, bank):
            return bank[64 * (c % 2):64 * (c % 2) + 64,
                        256 * j + 64 * (c // 2):256 * j + 64 * (c // 2) + 64]

        def pwh(slot, r0, c0, n):
            base = ATB[("pw", "pw2", "pw3")[pp]] + self.PW[slot] + c0
            return arb[r0:r0 + 64, base:base + n]

        for j, h in enumerate(heads):
            for c in range(8):
                tp = (64 * (h % 2), 64 * (c % 2))
                self.MM(osl(j, c, bt), gsl("ab", h, c), gsl("kt", h, c),
                        True, True,
                        [("arb", "ab", h // 2), ("arb", "kt", h // 2),
                         ("psum", kt_)], [("psum", kt_)],
                        tag=P + f"Tb{j}_{c}", tp=tp)
                self.MM(osl(j, c, bw), gsl("kt", h, c), gsl("ab", h, c),
                        True, True,
                        [("arb", "ab", h // 2), ("arb", "kt", h // 2),
                         ("psum", kw_)], [("psum", kw_)],
                        tag=P + f"Wb{j}_{c}", tp=tp)
                if c % 4 == 3:
                    yield
        self.TT(self.pwt("T", pp), bt[:, :], msl[:], ALU.mult,
                [("psum", kt_), ("c", "mstril")],
                [kkey + ("T",), ("psum", kt_)], tag=P + "Te")
        self.TT(self.pwt("W", pp), bw[:, :], msu[:], ALU.mult,
                [("psum", kw_), ("c", "mstriu")],
                [kkey + ("W",), ("psum", kw_)], tag=P + "We")
        yield
        for j, h in enumerate(heads):
            for c in range(8):
                tp = (64 * (h % 2), 64 * (c % 2))
                self.MM(osl(j, c, bt), gsl("kt", h, c), gsl("qb", h, c),
                        True, True,
                        [("arb", "qb", h // 2), ("arb", "kt", h // 2),
                         ("psum", kt_)], [("psum", kt_)],
                        tag=P + f"ATb{j}_{c}", tp=tp)
                if c % 4 == 3:
                    yield
        self.TT(self.pwt("AT", pp), bt[:, :], mti[:], ALU.mult,
                [("psum", kt_), ("c", "mtriuincl")],
                [kkey + ("AT",), ("psum", kt_)], tag=P + "ATe")
        self.TT(self.pwt("N", pp), self.pwt("W", pp), idb[:], ALU.add,
                [kkey + ("W",), ("c", "identblk")], [kkey + ("N",)],
                tag=P + "Ni")
        yield

        cur_t, cur_w, nxt_t, nxt_w = "T", "W", "sT", "sW"
        corder = [0, 2, 4, 6, 1, 3, 5, 7]
        for lvl in range(4):
            for j, h in enumerate(heads):
                for c in corder:
                    tpp = (64 * (c % 2), 64 * (c % 2))
                    self.MM(osl(j, c, bt), self.pwsl(cur_w, j, c, pp),
                            self.pwsl(cur_t, j, c, pp), True, True,
                            [kkey + (cur_w,), kkey + (cur_t,),
                             ("psum", kt_)], [("psum", kt_)],
                            tag=P + f"sq{lvl}t{j}_{c}", tp=tpp)
                    self.MM(osl(j, c, bw), self.pwsl(cur_t, j, c, pp),
                            self.pwsl(cur_w, j, c, pp), True, True,
                            [kkey + (cur_w,), kkey + (cur_t,),
                             ("psum", kw_)], [("psum", kw_)],
                            tag=P + f"sq{lvl}w{j}_{c}", tp=tpp)
                    if c in (6, 7):
                        yield
            self.ACTV(self.pwt(nxt_t, pp), bt[:, :], AF.Copy,
                      [("psum", kt_)], [kkey + (nxt_t,), ("psum", kt_)],
                      tag=P + f"sqe{lvl}t")
            self.ACTV(self.pwt(nxt_w, pp), bw[:, :], AF.Copy,
                      [("psum", kw_)], [kkey + (nxt_w,), ("psum", kw_)],
                      tag=P + f"sqe{lvl}w")
            yield
            for j, h in enumerate(heads):
                for c in corder:
                    tpp = (64 * (c % 2), 64 * (c % 2))
                    self.MM(osl(j, c, bt), self.pwsl(nxt_t, j, c, pp),
                            self.pwsl("N", j, c, pp), True, True,
                            [kkey + (nxt_t,), kkey + ("N",),
                             ("psum", kt_)], [("psum", kt_)],
                            tag=P + f"nu{lvl}_{j}_{c}", tp=tpp)
                    if c in (6, 7):
                        yield
            self.TT(self.pwt("N", pp), bt[:, :], self.pwt("N", pp), ALU.add,
                    [("psum", kt_), kkey + ("N",)],
                    [kkey + ("N",), ("psum", kt_)], tag=P + f"nue{lvl}")
            yield
            cur_t, cur_w, nxt_t, nxt_w = nxt_t, nxt_w, cur_t, cur_w
        for j, h in enumerate(heads):
            for c in corder:
                tpp = (64 * (c % 2), 64 * (c % 2))
                self.MM(osl(j, c, bt), self.pwsl(cur_w, j, c, pp),
                        self.pwsl(cur_t, j, c, pp), True, True,
                        [kkey + (cur_w,), kkey + (cur_t,),
                         ("psum", kt_)], [("psum", kt_)],
                        tag=P + f"sq32t{j}_{c}", tp=tpp)
                if c in (6, 7):
                    yield
        self.ACTV(self.pwt(nxt_t, pp), bt[:, :], AF.Copy, [("psum", kt_)],
                  [kkey + (nxt_t,), ("psum", kt_)], tag=P + "sqe32t")
        yield
        for j, h in enumerate(heads):
            for c in corder:
                tpp = (64 * (c % 2), 64 * (c % 2))
                self.MM(osl(j, c, bt), self.pwsl(nxt_t, j, c, pp),
                        self.pwsl("N", j, c, pp), True, True,
                        [kkey + (nxt_t,), kkey + ("N",),
                         ("psum", kt_)], [("psum", kt_)],
                        tag=P + f"nu32_{j}_{c}", tp=tpp)
                if c in (6, 7):
                    yield
        self.TT(self.pwt("N", pp), bt[:, :], self.pwt("N", pp), ALU.add,
                [("psum", kt_), kkey + ("N",)], [kkey + ("N",), ("psum", kt_)],
                tag=P + "nue32")
        yield

    def gen_chain(self, g, hp):
        P = f"ag{g}c{hp}"
        pp = hp % 3
        pb = [self.ps[f"pb{i}"] for i in range(6)]
        kkey = ("arb", "pw", pp)
        heads = (2 * hp, 2 * hp + 1)
        gsl, vsl, khsl, ssl = self._gsl, self._vsl, self._khsl, self._ssl
        ucR = self.sb["arb"][:, ATB["uc"]:ATB["uc"] + 64]
        ucU = self.sb["arb"][:, ATB["uc"] + 64:ATB["uc"] + 128]
        for c in range(8):
            for j, h in enumerate(heads):
                cpar = 64 * (c % 2)
                hpar = 64 * (h % 2)
                self.MM(pb[5][hpar:hpar + 64, 0:64], ssl(h), gsl("qb", h, c),
                        True, True,
                        [("arb", "s", h), ("arb", "qb", h // 2), ("psum", 5)],
                        [("psum", 5)], tag=P + f"cO1{c}_{j}", tp=(hpar, hpar))
                self.MM(pb[0][cpar:cpar + 64, 0:64], gsl("ab", h, c), ssl(h),
                        True, True,
                        [("arb", "ab", h // 2), ("arb", "s", h), ("psum", 0)],
                        [("psum", 0)], tag=P + f"cR{c}_{j}", tp=(hpar, cpar))
                self.TT(ucR[cpar:cpar + 64, :], pb[0][cpar:cpar + 64, 0:64],
                        vsl(h, c), ALU.add,
                        [("psum", 0), ("arf", "vgd")],
                        [("arb", "ucR"), ("psum", 0)], tag=P + f"cRe{c}_{j}")
                self.MM(pb[4][cpar:cpar + 64, 64:128], self.pwsl("N", j, c, pp),
                        ucR[cpar:cpar + 64, :], True, True,
                        [kkey + ("N",), ("arb", "ucR"), ("psum", 4)],
                        [("psum", 4)], tag=P + f"cU{c}_{j}", tp=(cpar, cpar))
                self.ACTV(ucU[cpar:cpar + 64, :], pb[4][cpar:cpar + 64, 64:128],
                          AF.Copy, [("psum", 4)], [("arb", "ucU"), ("psum", 4)],
                          tag=P + f"cUe{c}_{j}")
                self.MM(pb[5][hpar:hpar + 64, 64:128], ucU[cpar:cpar + 64, :],
                        self.pwsl("AT", j, c, pp), True, True,
                        [("arb", "ucU"), kkey + ("AT",), ("psum", 5)],
                        [("psum", 5)], tag=P + f"cO2{c}_{j}", tp=(cpar, hpar))
                self.MM(pb[4][hpar:hpar + 64, 0:64], khsl(h, c),
                        ucU[cpar:cpar + 64, :], True, True,
                        [("arb", "kht", h // 2), ("arb", "ucU"), ("psum", 4)],
                        [("psum", 4)], tag=P + f"cK{c}_{j}", tp=(cpar, hpar))
                pcol = self.sb["arf"][hpar:hpar + 64,
                                      ATF["P"] + (h // 2) * 512 + c * 64 + 63:
                                      ATF["P"] + (h // 2) * 512 + c * 64 + 64]
                self.STT(ssl(h), ssl(h), pcol, pb[4][hpar:hpar + 64, 0:64],
                         ALU.mult, ALU.add,
                         [("arb", "s", h), ("arf", "P", h // 2), ("psum", 4)],
                         [("arb", "s", h), ("psum", 4)], tag=P + f"cS{c}_{j}")
                yield
            self.COPY(self.ab_("og", 64, hp * 512 + c * 64), pb[5][:, 0:64],
                      [("psum", 5)], [("arb", "og", hp, c)], tag=P + f"cOc{c}")
            self.TT(self.ab_("og", 64, hp * 512 + c * 64),
                    self.ab_("og", 64, hp * 512 + c * 64), pb[5][:, 64:128],
                    ALU.add, [("arb", "og", hp, c), ("psum", 5)],
                    [("arb", "og", hp, c), ("psum", 5)], tag=P + f"cOe{c}")
            yield

    def run_pairs(self, g):
        pre = {hp: self.gen_precompute(g, hp) for hp in range(4)}
        chains = {}
        pre_done, ch_done = set(), set()
        active, next_pre = [], 0
        while len(ch_done) < 4:
            while (next_pre < 4 and len(active) < 2
                   and (next_pre < 3 or (next_pre - 3) in ch_done)):
                active.append(next_pre)
                next_pre += 1
            achain = None
            for hp in range(4):
                if hp in pre_done and hp not in ch_done:
                    achain = hp
                    break
            if achain is not None:
                gen = chains.get(achain)
                if gen is None:
                    gen = chains[achain] = self.gen_chain(g, achain)
                try:
                    next(gen)
                except StopIteration:
                    ch_done.add(achain)
            for hp in list(active):
                try:
                    next(pre[hp])
                    next(pre[hp])
                except StopIteration:
                    pre_done.add(hp)
                    active.remove(hp)
            if achain is None and not active and next_pre >= 4:
                break

    def emit_attn_output(self):
        P = "ao"
        pb = [self.ps[f"pb{i}"] for i in range(6)]
        pb7 = self.ps["pb7"]
        xs = self.sb["xs"]
        ones128 = self.sb["cs_ones128"]
        hgsel = ("hgsel",)

        for kc in range(8):
            self.DMA(self.ob_("wo", 1024, kc * 1024),
                     self.dram["Wo"].ap()[kc * 128:(kc + 1) * 128, :],
                     [], [("wo", kc)], tag=P + f"wol{kc}", eng="pool")
            self.DMA(self.ob_("lnxo1", 64, kc * 64),
                     self.dram["Wog1"].ap()[kc * 128:(kc + 1) * 128, :],
                     [], [("wog1", kc)], tag=P + f"wog1l{kc}", eng="pool")
        self.DMA(self.sb["arb"][0:64, OTB["lnxo1"] + 512:OTB["lnxo1"] + 1536],
                 self.dram["Wog2"].ap()[:, :], [], [("wog2",)],
                 tag=P + "wog2l", eng="pool")
        self.DMA(self.sb["small"][:, 24:32], self.dram["norm_w"].ap(),
                 [("sm", 24)], [("sm", 24)], tag=P + "nwl")

        for tt in range(4):
            par = tt % 2
            ot = self.ob_("ot0" if par == 0 else "ot1", 4096)
            otk = ("arb", "ot", par)
            lnxo = self.ob_("lnxo0", 4096)
            lnk = ("arb", "lnxo")
            own0 = (tt * 512)
            oth = self.ob_("ot0" if par == 1 else "ot1", 4096)
            othk = ("arb", "ot", 1 - par)
            self.DMA(ot.rearrange("p (a m) -> p a m", a=8),
                     self.dram["ag_o_out"].ap()[:, own0:own0 + 512]
                     .rearrange("(a p) m -> p a m", p=128),
                     [otk, ("agoout",)], [otk], tag=P + f"otl{tt}")
            self.DMA(oth.rearrange("p (a m) -> p a m", a=8),
                     self.dram["ag_o_out"].ap()[:, 2048 + own0:2048 + own0 + 512]
                     .rearrange("(a p) m -> p a m", p=128),
                     [othk, ("agoout",)], [othk], tag=P + f"oth{tt}")
            self.TT(oth, oth, ot, ALU.subtract, [otk, othk], [othk],
                    tag=P + f"osub{tt}")
            self.STT(ot, oth, self.sb["hgm"][:, 0:1], ot, ALU.mult, ALU.add,
                     [othk, ("hgm",), otk], [otk], tag=P + f"oblend{tt}")
            self.DMA(lnxo.rearrange("p (a m) -> p a m", a=8),
                     self.dram["ag_lnx_in"].ap()[:, own0:own0 + 512]
                     .rearrange("(a p) m -> p a m", p=128),
                     [lnk] + [("agin", tt, f) for f in range(8)], [lnk],
                     tag=P + f"lnxl{tt}")
            for kc in range(8):
                self.MM(pb7[0:64, :], self.ob_("lnxo1", 64, kc * 64),
                        lnxo[:, kc * 512:(kc + 1) * 512], kc == 0, kc == 7,
                        [lnk, ("wog1", kc), ("psum", 7)], [("psum", 7)],
                        tag=P + f"og1m{tt}_{kc}")
            og1 = self.sb["arb"][0:64, OTB["ogt"]:OTB["ogt"] + 512]
            self.COPY(og1, pb7[0:64, :], [("psum", 7)],
                      [("arb", "ogt"), ("psum", 7)], tag=P + f"og1e{tt}")
            for oc in range(8):
                bnk = 2 + (oc % 2)
                self.MM(pb[bnk][:, :],
                        self.sb["arb"][0:64, OTB["lnxo1"] + 512 + oc * 128:
                                       OTB["lnxo1"] + 512 + (oc + 1) * 128],
                        og1, True, True,
                        [("arb", "ogt"), ("wog2",), ("psum", bnk)],
                        [("psum", bnk)], tag=P + f"og2m{tt}_{oc}")
                t = self.af_("tmp", 512) if False else self.of_("mu", 512)
                tk = ("arf", "mu")
                self.ACTV(t, pb[bnk][:, :], AF.Tanh, [("psum", bnk)], [tk],
                          tag=P + f"ogth{tt}_{oc}", scale=0.5)
                self.TS(t, t, 0.5, 0.5, ALU.mult, ALU.add, [tk], [tk],
                        tag=P + f"ogsg{tt}_{oc}")
                ov = self.of_("ov", 512, oc * 512)
                self.TT(ov, ot[:, oc * 512:(oc + 1) * 512], t, ALU.mult,
                        [otk, tk], [("arf", "ov", oc)], tag=P + f"ov{tt}_{oc}")
            for oc in range(8):
                ovb = self.rb("xb0" if oc % 2 == 0 else "xb1", 512)
                ovbk = ("arb", "xb", oc % 2)
                ov = self.of_("ov", 512, oc * 512)
                self.COPY(ovb, ov, [("arf", "ov", oc)], [ovbk],
                          tag=P + f"ovb{tt}_{oc}")
                x2 = self.rb("x20" if oc % 2 == 0 else "x21", 512)
                x2k = ("arb", "x2", oc % 2)
                self.TT(x2, ovb, ovb, ALU.mult, [ovbk], [x2k],
                        tag=P + f"ovsq{tt}_{oc}")
                self.MM(pb[0][:, :], ones128[:], ovb, oc == 0, oc == 7,
                        [ovbk, ("c", "ones128"), ("psum", 0)], [("psum", 0)],
                        tag=P + f"osum{tt}_{oc}")
                self.MM(pb[1][:, :], ones128[:], x2, oc == 0, oc == 7,
                        [x2k, ("c", "ones128"), ("psum", 1)], [("psum", 1)],
                        tag=P + f"osumsq{tt}_{oc}")
            mu = self.of_("mu", 512)
            var = self.of_("var", 512)
            rstd = self.of_("rstd", 512)
            mu2 = self.of_("mu2", 512)
            self.TS(mu, pb[0][:, :], 1.0 / D, None, ALU.mult, None,
                    [("psum", 0)], [("arf", "mu"), ("psum", 0)], tag=P + f"mu{tt}")
            self.TS(var, pb[1][:, :], 1.0 / D, None, ALU.mult, None,
                    [("psum", 1)], [("arf", "var"), ("psum", 1)],
                    tag=P + f"msq{tt}")
            self.TT(mu2, mu, mu, ALU.mult, [("arf", "mu")], [("arf", "mu2")],
                    tag=P + f"mu2{tt}")
            self.TT(var, var, mu2, ALU.subtract,
                    [("arf", "var"), ("arf", "mu2")], [("arf", "var")],
                    tag=P + f"var{tt}")
            self.ACTV(rstd, var, AF.Sqrt, [("arf", "var"), ("c", "eps")],
                      [("arf", "rstd")], tag=P + f"sd{tt}",
                      bias=self.sb["cs_eps"][:, 0:1])
            self.RECIP(rstd, rstd, [("arf", "rstd")], [("arf", "rstd")],
                       tag=P + f"rstd{tt}")
            for oc in range(8):
                ov = self.of_("ov", 512, oc * 512)
                self.TT(ov, ov, mu, ALU.subtract,
                        [("arf", "ov", oc), ("arf", "mu")], [("arf", "ov", oc)],
                        tag=P + f"on1{tt}_{oc}")
                self.TT(ov, ov, rstd, ALU.mult,
                        [("arf", "ov", oc), ("arf", "rstd")],
                        [("arf", "ov", oc)], tag=P + f"on2{tt}_{oc}")
                oln = self.ob_("oln", 512, oc * 512)
                self.ACTV(oln, ov, AF.Copy,
                          [("arf", "ov", oc), ("sm", 24)],
                          [("arb", "oln", oc)], tag=P + f"onw{tt}_{oc}",
                          scale=self.sb["small"][:, 24 + oc:24 + oc + 1])
            for oc2 in range(8):
                bnk = 2 + (oc2 % 2)
                for kc in range(8):
                    self.MM(pb[bnk][:, :],
                            self.ob_("wo", 128, kc * 1024 + oc2 * 128),
                            self.ob_("oln", 512, kc * 512), kc == 0, kc == 7,
                            [("arb", "oln", kc), ("wo", kc), ("psum", bnk)],
                            [("psum", bnk)], tag=P + f"wom{tt}_{oc2}_{kc}")
                xsl = xs[:, oc2 * TOK + tt * 512: oc2 * TOK + tt * 512 + 512]
                self.TT(xsl, pb[bnk][:, :], xsl, ALU.add,
                        [("psum", bnk), ("x", oc2)],
                        [("x", oc2), ("psum", bnk)], tag=P + f"ores{tt}_{oc2}")
    def ab_(self, reg, width, off=0):
        return self.sb["arb"][:, ATB[reg] + off: ATB[reg] + off + width]

    def ob_(self, reg, width, off=0):
        return self.sb["arb"][:, OTB[reg] + off: OTB[reg] + off + width]

    def af_(self, reg, width, off=0):
        return self.sb["arf"][:, ATF[reg] + off: ATF[reg] + off + width]

    def of_(self, reg, width, off=0):
        return self.sb["arf"][:, OTF[reg] + off: OTF[reg] + off + width]

    def emit_attn(self):
        S = self.S
        xs = self.sb["xs"]
        arb = self.sb["arb"]
        arf = self.sb["arf"]
        sm = self.sb["small"]
        pb = [self.ps[f"pb{i}"] for i in range(6)]
        pb7 = self.ps["pb7"]
        ptb = self.ps["ptb"]
        ones128 = self.sb["cs_ones128"]
        hg_tokens = TOK

        S.barrier()
        P = "at"

        for tt in range(4):
            for f in range(8):
                xsl = xs[:, f * TOK + tt * 512: f * TOK + tt * 512 + 512]
                xb = self.rb("xb0" if f % 2 == 0 else "xb1", 512)
                x2 = self.rb("x20" if f % 2 == 0 else "x21", 512)
                xbk = ("arb", "xb", f % 2)
                x2k = ("arb", "x2", f % 2)
                self.COPY(xb, xsl, [("x", f)], [xbk], tag=P + f"cast{tt}_{f}")
                self.TT(x2, xb, xb, ALU.mult, [xbk], [x2k], tag=P + f"sq{tt}_{f}")
                self.MM(pb[0][:, :], ones128[:], xb, f == 0, f == 7,
                        [xbk, ("c", "ones128"), ("psum", 0)], [("psum", 0)],
                        tag=P + f"sums{tt}_{f}")
                self.MM(pb[1][:, :], ones128[:], x2, f == 0, f == 7,
                        [x2k, ("c", "ones128"), ("psum", 1)], [("psum", 1)],
                        tag=P + f"sumsq{tt}_{f}")
            mu, var, rstd, mu2 = (self.rf(r) for r in ("mu", "var", "rstd", "mu2"))
            self.TS(mu, pb[0][:, :], 1.0 / D, None, ALU.mult, None,
                    [("psum", 0)], [("arf", "mu"), ("psum", 0)], tag=P + f"mu{tt}")
            self.TS(var, pb[1][:, :], 1.0 / D, None, ALU.mult, None,
                    [("psum", 1)], [("arf", "var"), ("psum", 1)], tag=P + f"msq{tt}")
            self.TT(mu2, mu, mu, ALU.mult, [("arf", "mu")], [("arf", "mu2")],
                    tag=P + f"mu2{tt}")
            self.TT(var, var, mu2, ALU.subtract,
                    [("arf", "var"), ("arf", "mu2")], [("arf", "var")],
                    tag=P + f"var{tt}")
            self.ACTV(rstd, var, AF.Sqrt, [("arf", "var"), ("c", "eps")],
                      [("arf", "rstd")], tag=P + f"sd{tt}",
                      bias=self.sb["cs_eps"][:, 0:1])
            self.RECIP(rstd, rstd, [("arf", "rstd")], [("arf", "rstd")],
                       tag=P + f"rstd{tt}")
            for f in range(8):
                xsl = xs[:, f * TOK + tt * 512: f * TOK + tt * 512 + 512]
                lnt = self.rf("lnt0" if f % 2 == 0 else "lnt1")
                lk = ("arf", "lnt", f % 2)
                self.TT(lnt, xsl, mu, ALU.subtract, [("x", f), ("arf", "mu")],
                        [lk], tag=P + f"lsub{tt}_{f}")
                self.TT(lnt, lnt, rstd, ALU.mult, [lk, ("arf", "rstd")], [lk],
                        tag=P + f"lmul{tt}_{f}")
                self.DMA(self.dram["ag_lnx_in"].ap()[f * 128:(f + 1) * 128,
                                                     tt * 512:(tt + 1) * 512],
                         lnt, [lk], [("agin", tt, f), lk],
                         tag=P + f"lnst{tt}_{f}", eng="pool")

        if self.attn_cut < 1:
            return
        self.CC("AllGather", ALU.bypass, [[0, 1], [2, 3], [4, 5], [6, 7]],
                [self.dram["ag_lnx_in"].ap()], [self.dram["ag_lnx_out"].ap()],
                [("agin", tt, f) for tt in range(4) for f in range(8)],
                [("aglnx",)], tag=P + "aglnx")
        if self.attn_cut < 2:
            return

        for kc in range(8):
            self.DMA(self.ab_("aw", 512, kc * 512),
                     self.dram["Wq"].ap()[kc * 128:(kc + 1) * 128, :],
                     [], [("aw", "q", kc)], tag=P + f"wql{kc}", eng="pool")
            self.DMA(self.ab_("aw", 512, 4096 + kc * 512),
                     self.dram["Wk"].ap()[kc * 128:(kc + 1) * 128, :],
                     [], [("aw", "k", kc)], tag=P + f"wkl{kc}", eng="pool")
            self.DMA(self.ab_("awf", 64, kc * 64),
                     self.dram["Wf1"].ap()[kc * 128:(kc + 1) * 128, :],
                     [], [("aw", "f1", kc)], tag=P + f"wf1l{kc}", eng="pool")
            self.DMA(self.ab_("awf", 8, 1024 + kc * 8),
                     self.dram["Wgam"].ap()[kc * 128:(kc + 1) * 128, :],
                     [], [("aw", "g", kc)], tag=P + f"wgl{kc}", eng="pool")
        self.DMA(self.sb["arb"][0:64, ATB["awf"] + 512: ATB["awf"] + 1024],
                 self.dram["Wf2"].ap()[:, :], [], [("aw", "f2")],
                 tag=P + "wf2l", eng="pool")

        self.MEMSET(self.ab_("s", 256), 0.0,
                    [("arb", "s", h) for h in range(8)], tag=P + "s0")

        if self.attn_cut < 3:
            return
        for g in range(min(NG, self.attn_cut - 2)):
            self.emit_attn_group(g)
        if self.attn_cut < 11:
            return

        self.CC("AllGather", ALU.bypass, [[0, 1], [2, 3], [4, 5], [6, 7]],
                [self.dram["o_spill"].ap()], [self.dram["ag_o_out"].ap()],
                [("osp", g, pt) for g in range(NG) for pt in range(4)],
                [("agoout",)], tag=P + "ago")

        if self.attn_cut < 12:
            return
        S.barrier()
        self.emit_attn_output()
        S.barrier()



def _prep_core_inputs(c, inputs, const_arrays):
    b, hg = c // 2, c % 2
    x = inputs["L0_attn_x"]
    xslice = np.asarray(x[b, hg * TOK:(hg + 1) * TOK, :], dtype=np.float32)
    mycols = slice(hg * 512, hg * 512 + 512)
    m = {
        "x_fm": np.ascontiguousarray(xslice.T),
        "wg_nat": np.ascontiguousarray(inputs["L0_ffn_w_gate"], dtype=np.float32),
        "wu_nat": np.ascontiguousarray(inputs["L0_ffn_w_up"], dtype=np.float32),
        "wd_nat": np.ascontiguousarray(inputs["L0_ffn_w_down"], dtype=np.float32),
        "clip_x": np.asarray(inputs["L0_ffn_clip_x"], dtype=np.float32).reshape(1, 1),
        "clip_down": np.asarray(inputs["L0_ffn_clip_down"], dtype=np.float32).reshape(1, 1),
        "Wq": np.ascontiguousarray(np.asarray(inputs["L0_attn_Wq"])[:, mycols], dtype=np.float32),
        "Wk": np.ascontiguousarray(np.asarray(inputs["L0_attn_Wk"])[:, mycols], dtype=np.float32),
        "Wv": np.ascontiguousarray(np.asarray(inputs["L0_attn_Wv"])[:, mycols], dtype=np.float32),
        "Wgam": np.ascontiguousarray(np.asarray(inputs["L0_attn_Wgamma"])[:, hg * NH:(hg + 1) * NH], dtype=np.float32),
        "Wf1": np.ascontiguousarray(inputs["L0_attn_Wf1"], dtype=np.float32),
        "Wf2": np.ascontiguousarray(np.asarray(inputs["L0_attn_Wf2"])[:, mycols], dtype=np.float32),
        "Wog1": np.ascontiguousarray(inputs["L0_attn_Wog1"], dtype=np.float32),
        "Wog2": np.ascontiguousarray(inputs["L0_attn_Wog2"], dtype=np.float32),
        "Wo": np.ascontiguousarray(inputs["L0_attn_Wo"], dtype=np.float32),
        "norm_w": np.ascontiguousarray(np.asarray(inputs["L0_attn_norm_w"], dtype=np.float32).reshape(8, 128).T),
        "hgmask": np.full((128, 1), float(hg), dtype=np.float32),
    }
    m.update(const_arrays)
    return m


_BUILD_CACHE = {}


def _get_built(phases, dumps=(), attn_cut=99, gcut=99):
    key = (tuple(phases), tuple(n for n, _, _ in dumps), attn_cut, gcut)
    if key not in _BUILD_CACHE:
        b = Builder(phases=phases, dumps=dumps, attn_cut=attn_cut, gcut=gcut)
        b.build()
        with (b.ctx, b.nc.Block() as block):
            sems = {}
            es = ExitStack()
            for e in ENGINES:
                sems[e] = es.enter_context(b.nc.semaphore("s_" + e))
            dsems = [es.enter_context(b.nc.semaphore(f"dp{i}"))
                     for i in range(64)]
            csems = [es.enter_context(b.nc.semaphore(f"cc{i}"))
                     for i in range(8)]
            b.S.emit(block, sems, dsems, csems)
        _BUILD_CACHE[key] = b
    return _BUILD_CACHE[key]


def kernel(**inputs):
    b = _get_built(("ffn1", "attn", "ffn2"))
    in_maps = [_prep_core_inputs(c, inputs, b.const_arrays) for c in range(8)]
    res = run_bass_kernel_spmd(b.nc, in_maps, core_ids=list(range(8)))
    out = np.empty((4, TFULL, D), dtype=np.float32)
    for c in range(8):
        bb, hg = c // 2, c % 2
        out[bb, hg * TOK:(hg + 1) * TOK, :] = res.results[c]["out_fm"].T
    return out


# version 10
# speedup vs baseline: 1.0400x; 1.0400x over previous
import numpy as np
import ml_dtypes
from contextlib import ExitStack

import concourse.bass as bass
import concourse.mybir as mybir
from concourse.bass_utils import run_bass_kernel_spmd

F32 = mybir.dt.float32
BF16 = mybir.dt.bfloat16
I8 = mybir.dt.int8
AF = mybir.ActivationFunctionType
ALU = mybir.AluOpType
AX = mybir.AxisListType.X

D = 1024
TOK = 2048
TFULL = 4096
NH = 8
HD = 64
II = 2816
NIC = II // 128
CH = 64
G = 512
NG = TFULL // G
QMAX = 127.0

ENGINES = ("te", "dve", "act", "pool", "sp")
SELF_FENCE = True


class Sched:

    def __init__(self):
        self.steps = []
        self.eng_steps = {e: [] for e in ENGINES}
        self.writers = {}
        self.readers = {}

    def step(self, eng, fn, reads=(), writes=(), dma=False, tag=""):
        idx = len(self.steps)
        deps = set()
        for k in reads:
            w = self.writers.get(k)
            if w is not None:
                deps.add(w)
            self.readers.setdefault(k, []).append(idx)
        for k in writes:
            w = self.writers.get(k)
            if w is not None:
                deps.add(w)
            for r in self.readers.get(k, ()):
                deps.add(r)
            self.readers[k] = []
            self.writers[k] = idx
        deps.discard(idx)
        st = dict(idx=idx, eng=eng, fn=fn, deps=deps, dma=dma, tag=tag,
                  acc=False)
        self.steps.append(st)
        self.eng_steps[eng].append(st)
        return idx

    _bar_n = 0

    def barrier(self):
        n = Sched._bar_n = Sched._bar_n + 1
        for e in ENGINES:
            self.step(e, lambda en: en.nop(), reads=(),
                      writes=[("__barA__", n, e)], tag=f"barA{n}")
        for e in ENGINES:
            self.step(e, lambda en: en.nop(),
                      reads=[("__barA__", n, x) for x in ENGINES],
                      writes=[("__barB__", n, e)], tag=f"barB{n}")

    def emit(self, block, sems, dma_sems, cc_sems):
        lastmm = {}
        for st in self.steps:
            bank = st.get("mm_bank")
            if bank is None:
                continue
            prev = lastmm.get(bank)
            if prev is not None and prev["mm_rg"] != st["mm_rg"]:
                st["deps"].add(prev["idx"])
                st.setdefault("rgdeps", set()).add(prev["idx"])
            lastmm[bank] = st
        NP = len(dma_sems)
        ncc = 0
        for st in self.steps:
            st["sig"] = False
            if st["acc"]:
                st["cc_sem"] = ncc
                ncc += 1
        assert ncc <= len(cc_sems)
        for e in ENGINES:
            for i, st in enumerate(self.eng_steps[e]):
                st["epos"] = i
        for st in self.steps:
            rgd = st.get("rgdeps", ())
            best = {}
            for d in st["deps"]:
                dep = self.steps[d]
                if dep["dma"] or dep["acc"]:
                    continue
                if dep["eng"] == st["eng"] == "te" and d not in rgd:
                    continue
                b = best.get(dep["eng"])
                if b is None or dep["epos"] > b["epos"]:
                    best[dep["eng"]] = dep
            st["best"] = best
            for dep in best.values():
                dep["sig"] = True
        use = [0] * NP
        nsw = NP * 3 // 4
        ndma = {"sp": 0, "act": 0, "pool": 0, "te": 0, "dve": 0}
        for st in self.steps:
            if st["dma"]:
                if st["eng"] == "pool":
                    p = nsw + ndma["pool"] % (NP - nsw)
                else:
                    p = ndma[st["eng"]] % nsw
                ndma[st["eng"]] += 1
                st["dma_sem"] = p
                st["dma_fence"] = use[p] * 16 if use[p] else None
                use[p] += 1
                st["dma_val"] = use[p] * 16
                st["sig_val"] = None
        for e in ENGINES:
            n = 0
            for st in self.eng_steps[e]:
                if st["dma"] or st["acc"]:
                    st["sig_val"] = None
                    continue
                st["sig_val"] = (n := n + 1) if st["sig"] else None
        for e in ENGINES:
            know = {x: 0 for x in ENGINES}
            know_d = [0] * NP
            for st in self.eng_steps[e]:
                waits, dwaits, cwaits = {}, {}, {}
                for d in sorted(st["deps"]):
                    dep = self.steps[d]
                    if dep["dma"]:
                        p = dep["dma_sem"]
                        if dep["dma_val"] > know_d[p]:
                            dwaits[p] = max(dwaits.get(p, 0), dep["dma_val"])
                    elif dep["acc"]:
                        cwaits[dep["cc_sem"]] = 1
                for de, dep in st["best"].items():
                    if dep["sig_val"] > know[de]:
                        waits[de] = max(waits.get(de, 0), dep["sig_val"])
                st["waits"] = waits
                st["dwaits"] = dwaits
                st["cwaits"] = cwaits
                for de, v in waits.items():
                    know[de] = max(know[de], v)
                for p, v in dwaits.items():
                    know_d[p] = max(know_d[p], v)
                for d in st["deps"]:
                    dep = self.steps[d]
                    if dep["eng"] == e or dep["dma"]:
                        continue
                    for de, v in dep.get("waits", {}).items():
                        if de != e:
                            know[de] = max(know[de], v)
                    for p, v in dep.get("dwaits", {}).items():
                        know_d[p] = max(know_d[p], v)

        def emit_engine(e, eng):
            for st in self.eng_steps[e]:
                if (SELF_FENCE and st["sig"] and st["sig_val"]
                        and st["sig_val"] > 1):
                    eng.wait_ge(sems[e], st["sig_val"] - 1)
                for de, v in sorted(st["waits"].items()):
                    eng.wait_ge(sems[de], v)
                for p, v in sorted(st["dwaits"].items()):
                    eng.wait_ge(dma_sems[p], v)
                for p, v in sorted(st.get("cwaits", {}).items()):
                    eng.wait_ge(cc_sems[p], v)
                if st["dma"] and st["dma_fence"] is not None:
                    eng.wait_ge(dma_sems[st["dma_sem"]], st["dma_fence"])
                inst = st["fn"](eng)
                if st["dma"]:
                    assert inst is not None, st["tag"]
                    inst.then_inc(dma_sems[st["dma_sem"]], 16)
                elif st["acc"]:
                    assert inst is not None, st["tag"]
                    inst.then_inc(cc_sems[st["cc_sem"]], 1)
                elif st["sig"]:
                    if inst is None:
                        inst = eng.nop()
                    inst.then_inc(sems[e], 1)

        @block.tensor
        def _(eng):
            emit_engine("te", eng)

        @block.vector
        def _(eng):
            emit_engine("dve", eng)

        @block.scalar
        def _(eng):
            emit_engine("act", eng)

        @block.gpsimd
        def _(eng):
            emit_engine("pool", eng)

        @block.sync
        def _(eng):
            emit_engine("sp", eng)


def _bind(fn, *a):
    return lambda eng: fn(eng, *a)


ARF_COLS = 11264
ARF = dict(mu=0, var=512, rstd=1024, mu2=1536, lnt0=2048, lnt1=2560,
           sg0=3072, sg1=3584, gg0=4096, gg1=4608, gld0=5120, gld1=5632,
           natb0=6144, natb1=7168, dnat=8192)
ARB_COLS = 44800
ARB = dict(xq=0,
           ws=16384,
           qb0=20480, qb1=21504,
           dwl0=22528, dwl1=23040,
           hq=23552,
           xb0=34816, xb1=35328, x20=35840, x21=36352,
           free=36864)
ATB = dict(aw=0,
           awf=8192,
           wv=9344,
           lng=13440,
           q=17536,
           k=19584,
           kt=21632,
           ab=23680,
           qb=25728,
           gr=27776,
           pw=29824,
           kht=32896,
           u=34944,
           uc=35456,
           g8=35584,
           s=36096,
           og=36352,
           pw2=38400,
           pw3=41472,
           uc2=44544,
           end=44800)
OTB = dict(wo=0,
           ot0=8192, ot1=12288,
           lnxo0=16384,
           lnxo1=20480,
           ogt=24576,
           oln=25088)
ATF = dict(zf=0,
           P=2048,
           Pinv=4096,
           vg=6144,
           tmp=10240)
OTF = dict(ov=0,
           mu=4096, var=4608, rstd=5120, mu2=5632)


class Builder:
    def __init__(self, phases=("ffn1", "attn", "ffn2"), dumps=(), attn_cut=99,
                 gcut=99):
        self.attn_cut = attn_cut
        self.gcut = gcut
        self.phases = phases
        self.dumps = list(dumps)
        self.nc = bass.Bass("TRN2", target_bir_lowering=False, debug=False)
        self.S = Sched()
        self.ctx = ExitStack()
        self.sb = {}
        self.ps = {}
        self.dram = {}
        self.const_arrays = {}

    def sbuf(self, name, shape, dtype):
        t = self.ctx.enter_context(self.nc.sbuf_tensor(name, shape, dtype))
        self.sb[name] = t
        return t

    def psum(self, name, shape, dtype):
        t = self.ctx.enter_context(self.nc.psum_tensor(name, shape, dtype))
        self.ps[name] = t
        return t

    def din(self, name, shape, dtype):
        t = self.nc.dram_tensor(name, shape, dtype, kind="ExternalInput")
        self.dram[name] = t
        return t

    def dout(self, name, shape, dtype):
        t = self.nc.dram_tensor(name, shape, dtype, kind="ExternalOutput")
        self.dram[name] = t
        return t

    def dtmp(self, name, shape, dtype):
        t = self.nc.dram_tensor(name, shape, dtype)
        self.dram[name] = t
        return t

    def const(self, name, arr):
        if name not in self.const_arrays:
            dt = {np.dtype(np.float32): F32,
                  np.dtype(ml_dtypes.bfloat16): BF16}[arr.dtype]
            self.din("c_" + name, list(arr.shape), dt)
            self.const_arrays["c_" + name] = np.ascontiguousarray(arr)
            self.sbuf("cs_" + name, list(arr.shape), dt)
            self.DMA(self.sb["cs_" + name][:], self.dram["c_" + name].ap(),
                     [], [("c", name)], tag="ldc_" + name)
        return self.sb["cs_" + name]

    def rf(self, reg, width=512, off=0):
        c = ARF[reg] + off
        return self.sb["arf"][:, c:c + width]

    def rb(self, reg, width, off=0):
        c = ARB[reg] + off
        return self.sb["arb"][:, c:c + width]

    def TT(self, out, a, b, op, reads, writes, tag=""):
        self.S.step("dve", _bind(
            lambda e, out, a, b: e.tensor_tensor(out=out, in0=a, in1=b, op=op),
            out, a, b), reads, writes, tag=tag)

    def TS(self, out, a, s1, s2, op0, op1, reads, writes, tag="", eng="dve"):
        if s2 is None:
            self.S.step(eng, _bind(
                lambda e, out, a, s1: e.tensor_scalar(
                    out=out, in0=a, scalar1=s1, scalar2=None, op0=op0),
                out, a, s1), reads, writes, tag=tag)
        else:
            self.S.step(eng, _bind(
                lambda e, out, a, s1, s2: e.tensor_scalar(
                    out=out, in0=a, scalar1=s1, scalar2=s2, op0=op0, op1=op1),
                out, a, s1, s2), reads, writes, tag=tag)

    def STT(self, out, a, s, b, op0, op1, reads, writes, tag="", eng="dve"):
        self.S.step(eng, _bind(
            lambda e, out, a, s, b: e.scalar_tensor_tensor(
                out=out, in0=a, scalar=s, in1=b, op0=op0, op1=op1),
            out, a, s, b), reads, writes, tag=tag)

    def COPY(self, out, a, reads, writes, tag="", eng="dve"):
        self.S.step(eng, _bind(
            lambda e, out, a: e.tensor_copy(out=out, in_=a), out, a),
            reads, writes, tag=tag)

    def MEMSET(self, ap, v, writes, tag="", eng="dve"):
        self.S.step(eng, _bind(lambda e, ap, v: e.memset(ap, v), ap, v),
                    (), writes, tag=tag)

    def RED(self, out, a, op, reads, writes, tag="", absval=False):
        self.S.step("dve", _bind(
            lambda e, out, a: e.tensor_reduce(
                out=out, in_=a, axis=AX, op=op, apply_absolute_value=absval),
            out, a), reads, writes, tag=tag)

    def RECIP(self, out, a, reads, writes, tag=""):
        self.S.step("dve", _bind(
            lambda e, out, a: e.reciprocal(out=out, in_=a), out, a),
            reads, writes, tag=tag)

    def SCAN(self, out, d0, d1, init, op0, op1, reads, writes, tag=""):
        self.S.step("dve", _bind(
            lambda e, out, d0, d1: e.tensor_tensor_scan(
                out=out, data0=d0, data1=d1, initial=init, op0=op0, op1=op1),
            out, d0, d1), reads, writes, tag=tag)

    def ACTV(self, out, a, func, reads, writes, tag="", bias=0.0, scale=1.0):
        self.S.step("act", _bind(
            lambda e, out, a, bias, scale: e.activation(
                out=out, in_=a, func=func, bias=bias, scale=scale),
            out, a, bias, scale), reads, writes, tag=tag)

    def MM(self, out, lhsT, rhs, start, stop, reads, writes, tag="", tp=None):
        i = self.S.step("te", _bind(
            lambda e, out, lhsT, rhs: e.matmul(
                out, lhsT, rhs, start=start, stop=stop, tile_position=tp),
            out, lhsT, rhs), reads, writes, tag=tag)
        st = self.S.steps[i]
        st["mm_bank"] = out.tensor.name
        st["mm_rg"] = (tp[0] if tp else 0, lhsT.shape[0])

    def TPO(self, out, a, reads, writes, tag="", f32=False):
        nm = "identf" if f32 else "ident"
        ident = self.sb["cs_" + nm][:, :]
        i = self.S.step("te", _bind(
            lambda e, out, a, ident: e.transpose(out, a, ident),
            out, a, ident), list(reads) + [("c", nm)], writes, tag=tag)
        st = self.S.steps[i]
        st["mm_bank"] = out.tensor.name
        st["mm_rg"] = (0, a.shape[0])

    def DMA(self, out, a, reads, writes, tag="", eng="sp"):
        self.S.step(eng, _bind(
            lambda e, out, a: e.dma_start(out=out, in_=a), out, a),
            reads, writes, dma=True, tag=tag)

    def CC(self, kind, op, groups, ins, outs, reads, writes, tag=""):
        i = self.S.step("pool", _bind(
            lambda e, ins, outs: e.collective_compute(
                kind, op, replica_groups=groups, ins=ins, outs=outs),
            ins, outs), reads, writes, tag=tag)
        self.S.steps[i]["acc"] = True

    def build(self):
        self.din("x_fm", [D, TOK], F32)
        self.din("wg_nat", [II, D], F32)
        self.din("wu_nat", [II, D], F32)
        self.din("wd_nat", [D, II], F32)
        self.din("clip_x", [1, 1], F32)
        self.din("clip_down", [1, 1], F32)
        self.din("Wq", [D, D // 2], F32)
        self.din("Wk", [D, D // 2], F32)
        self.din("Wv", [D, D // 2], F32)
        self.din("Wgam", [D, NH], F32)
        self.din("Wf1", [D, HD], F32)
        self.din("Wf2", [HD, D // 2], F32)
        self.din("Wog1", [D, HD], F32)
        self.din("Wog2", [HD, D], F32)
        self.din("Wo", [D, D], F32)
        self.din("norm_w", [128, 8], F32)
        self.din("hgmask", [128, 1], F32)
        self.sbuf("hgm", [128, 1], F32)
        self.dout("out_fm", [D, TOK], F32)

        self.dtmp("ln_spill", [D, TOK], F32)
        self.dtmp("g_spill", [II, TOK], F32)
        self.dtmp("x1_spill", [D, TOK], F32)
        self.dtmp("wq_spill", [3 * NIC, 128, 1024], BF16)
        self.dtmp("o_spill", [D // 2, TFULL], BF16)
        self.dtmp("ar_in", [1, 128], F32)
        self.dtmp("ar_out", [1, 128], F32)
        self.dtmp("ag_lnx_in", [D, TOK], BF16)
        self.dtmp("ag_lnx_out", [2 * D, TOK], BF16)
        self.dtmp("ag_o_out", [D, TFULL], BF16)

        self.sbuf("xs", [128, 8 * TOK], F32)
        self.sbuf("arf", [128, ARF_COLS], F32)
        self.sbuf("arb", [128, ARB_COLS], BF16)
        self.sbuf("stats", [128, 256], F32)
        self.sbuf("small", [128, 64], F32)
        self.sbuf("srow", [1, 512], F32)

        for i in range(6):
            self.psum(f"pb{i}", [128, 512], F32)
        self.psum("ptb", [128, 1024], BF16)
        self.psum("pb7", [128, 512], F32)

        self.const("ones128", np.ones((128, 128), dtype=ml_dtypes.bfloat16))
        bo = np.zeros((128, 128), dtype=np.float32)
        bo[0:64, 0:64] = 1.0
        bo[64:128, 64:128] = 1.0
        self.const("blockones", bo.astype(ml_dtypes.bfloat16))
        sel = np.zeros((8, 512), dtype=np.float32)
        for h in range(8):
            for pcol in range(512):
                if h == 2 * (pcol // 128) + (pcol % 128) // 64:
                    sel[h, pcol] = 1.0
        self.const("sel8", sel.astype(ml_dtypes.bfloat16))
        pp, ff = np.meshgrid(np.arange(128), np.arange(512), indexing="ij")
        self.const("mstril", ((ff % 64) < (pp % 64)).astype(ml_dtypes.bfloat16))
        self.const("mstriu", ((ff % 64) > (pp % 64)).astype(ml_dtypes.bfloat16))
        self.const("mtriuincl", ((ff % 64) >= (pp % 64)).astype(ml_dtypes.bfloat16))
        self.const("identblk", ((ff % 64) == (pp % 64)).astype(ml_dtypes.bfloat16))
        self.const("zero64", np.zeros((128, 64), dtype=np.float32))
        self.const("ones1f", np.ones((1, 128), dtype=np.float32))
        self.const("ident", np.eye(128, dtype=np.float32).astype(ml_dtypes.bfloat16))
        self.const("identf", np.eye(128, dtype=np.float32))
        self.const("eps", np.full((128, 1), 1e-5, dtype=np.float32))

        xs = self.sb["xs"]
        xin = self.dram["x_fm"].ap()
        self.DMA(self.sb["hgm"][:], self.dram["hgmask"].ap(), [], [("hgm",)],
                 tag="ldhgm")
        for f in range(8):
            self.DMA(xs[:, f * TOK:(f + 1) * TOK], xin[f * 128:(f + 1) * 128, :],
                     [], [("x", f)], tag=f"ldx{f}")

        if "ffn1" in self.phases:
            self.emit_ffn(0)
        if "attn" in self.phases:
            self.emit_attn()
        if "ffn2" in self.phases:
            self.emit_ffn(1)

        outap = self.dram["out_fm"].ap()
        for f in range(8):
            self.DMA(outap[f * 128:(f + 1) * 128, :], xs[:, f * TOK:(f + 1) * TOK],
                     [("x", f)], [("ost", f)], tag=f"stx{f}")

        for name, src, reads in self.dumps:
            t = self.dout("dump_" + name, [src.shape[0], src.shape[1]], src.dtype)
            self.DMA(t.ap(), src, list(reads), [("dump", name)],
                     tag="dump_" + name)
        self.S.barrier()
        self.S.step("sp", lambda e: e.nop(),
                    reads=[("ost", f) for f in range(8)] +
                          [("dump", n) for n, _, _ in self.dumps],
                    tag="final_wait")
        return self

    def _scales_from_max(self, P, clip_ap):
        sm = self.sb["small"]
        self.TPO(self.ps["pb7"][0:1, 0:128], sm[:, 8:9], [("sm", 8)],
                 [("psum", 7)], tag=P + "mxT", f32=True)
        self.COPY(self.sb["srow"][0:1, 0:128], self.ps["pb7"][0:1, 0:128],
                  [("psum", 7)], [("srow",), ("psum", 7)], tag=P + "mxc")
        self.RED(sm[0:1, 10:11], self.sb["srow"][0:1, 0:128], ALU.max,
                 [("srow",)], [("sm", 10)], tag=P + "mxr")
        self.TS(self.sb["srow"][0:1, 128:256], self.sb["cs_ones1f"][0:1, :],
                sm[0:1, 10:11], None, ALU.mult, None,
                [("sm", 10), ("c", "ones1f")], [("srow2",)], tag=P + "mxbc")
        self.DMA(self.dram["ar_in"].ap()[0:1, :], self.sb["srow"][0:1, 128:256],
                 [("srow2",)], [("ar",)], tag=P + "arst")
        self.CC("AllReduce", ALU.max, [list(range(8))],
                [self.dram["ar_in"].ap()], [self.dram["ar_out"].ap()],
                [("ar",)], [("ar",)], tag=P + "ar")
        self.DMA(sm[0:1, 11:12], self.dram["ar_out"].ap()[0:1, 0:1],
                 [("ar",)], [("sm", 11)], tag=P + "arld")
        self.DMA(sm[0:1, 12:13], clip_ap, [], [("sm", 12)], tag=P + "clipld")
        self.ACTV(sm[0:1, 13:14], sm[0:1, 12:13], AF.Tanh,
                  [("sm", 12)], [("sm", 13)], tag=P + "tanh", scale=0.5)
        self.TS(sm[0:1, 13:14], sm[0:1, 13:14], 0.5, 0.5, ALU.mult, ALU.add,
                [("sm", 13)], [("sm", 13)], tag=P + "sig")
        self.TT(sm[0:1, 14:15], sm[0:1, 13:14], sm[0:1, 11:12], ALU.mult,
                [("sm", 13), ("sm", 11)], [("sm", 14)], tag=P + "sM")
        self.TS(sm[0:1, 15:16], sm[0:1, 14:15], 1.0 / QMAX, None, ALU.mult, None,
                [("sm", 14)], [("sm", 15)], tag=P + "s")
        self.RECIP(sm[0:1, 16:17], sm[0:1, 14:15], [("sm", 14)], [("sm", 16)],
                   tag=P + "rinv")
        self.TS(sm[0:1, 17:18], sm[0:1, 16:17], QMAX, None, ALU.mult, None,
                [("sm", 16)], [("sm", 17)], tag=P + "r")
        ones1f = self.sb["cs_ones1f"]
        self.MM(self.ps["pb7"][:, 0:1], ones1f[0:1, :], sm[0:1, 17:18],
                True, True, [("sm", 17), ("c", "ones1f"), ("psum", 7)],
                [("psum", 7)], tag=P + "bcr")
        self.COPY(sm[:, 0:1], self.ps["pb7"][:, 0:1], [("psum", 7)],
                  [("sm", 0), ("psum", 7)], tag=P + "bcrc")
        self.MM(self.ps["pb7"][:, 0:1], ones1f[0:1, :], sm[0:1, 15:16],
                True, True, [("sm", 15), ("c", "ones1f"), ("psum", 7)],
                [("psum", 7)], tag=P + "bcs")
        self.COPY(sm[:, 5:6], self.ps["pb7"][:, 0:1], [("psum", 7)],
                  [("sm", 5), ("psum", 7)], tag=P + "bcsc")

    def quant_round(self, src, rcol, rk, out, srck, outk, tag, fbuf=None,
                    fkey=None):
        if fbuf is None:
            fbuf = self.rf("var", src.shape[1])
            fkey = ("arf", "var")
        MAGIC = 12582912.0
        self.TS(src, src, rcol, MAGIC, ALU.mult, ALU.add, [srck, rk], [srck],
                tag=tag + "a")
        self.TS(src, src, -MAGIC, None, ALU.add, None, [srck], [srck],
                tag=tag + "b")
        self.TS(out, src, 127.0, -128.0, ALU.min, ALU.max, [srck], [outk],
                tag=tag + "d")

    def emit_ffn(self, kk):
        xs = self.sb["xs"]
        arb = self.sb["arb"]
        stats = self.sb["stats"]
        sm = self.sb["small"]
        pb = [self.ps[f"pb{i}"] for i in range(6)]
        ptb = self.ps["ptb"]
        ones128 = self.sb["cs_ones128"]
        P = f"f{kk}"
        wsp = self.dram["wq_spill"].ap()

        if kk == 0:
            for widx, nm in ((0, "wg_nat"), (1, "wu_nat")):
                nat = self.dram[nm].ap()
                for ic in range(NIC):
                    par = ic % 2
                    natb = self.rf("natb0" if par == 0 else "natb1", 1024)
                    natk = ("arf", "natb", par)
                    self.DMA(natb, nat[ic * 128:(ic + 1) * 128, :],
                             [natk], [natk], tag=f"natld{widx}_{ic}")
                    self.RED(sm[:, 20:21], natb, ALU.max, [natk], [("sm", 20)],
                             tag=f"wmax{widx}_{ic}", absval=True)
                    self.TS(stats[:, widx * 32 + ic: widx * 32 + ic + 1],
                            sm[:, 20:21], 1.0 / QMAX, None, ALU.mult, None,
                            [("sm", 20)], [("stats", widx * 32 + ic)],
                            tag=f"wsc{widx}_{ic}")
                    self.RECIP(sm[:, 21:22], sm[:, 20:21], [("sm", 20)],
                               [("sm", 21)], tag=f"wrc{widx}_{ic}")
                    self.TS(sm[:, 22:23], sm[:, 21:22], QMAX, None, ALU.mult,
                            None, [("sm", 21)], [("sm", 22)],
                            tag=f"wr2{widx}_{ic}")
                    qb = self.rb("qb0" if par == 0 else "qb1", 1024)
                    qbk = ("arb", "qb", par)
                    self.quant_round(natb, sm[:, 22:23], ("sm", 22), qb,
                                     natk, qbk, f"wq{widx}_{ic}",
                                     fbuf=self.rf("dnat", 1024),
                                     fkey=("arf", "dnat"))
                    for f in range(8):
                        self.TPO(ptb[:, f * 128:(f + 1) * 128],
                                 qb[:, f * 128:(f + 1) * 128],
                                 [qbk], [("ptb", f)], tag=f"wt{widx}_{ic}_{f}")
                    stt = self.rb("ws", 1024, par * 2048)
                    sttk = ("arb", "ws", par, 0)
                    self.COPY(stt, ptb[:, :], [("ptb", f) for f in range(8)],
                              [sttk] + [("ptb", f) for f in range(8)],
                              tag=f"wte{widx}_{ic}")
                    self.DMA(wsp[widx * NIC + ic, :, :], stt, [sttk],
                             [("wq", widx, ic)], tag=f"wsp{widx}_{ic}")
            nat = self.dram["wd_nat"].ap()
            for oc in range(8):
                natb = self.rf("dnat", 2816)
                natk = ("arf", "dnat")
                self.DMA(natb, nat[oc * 128:(oc + 1) * 128, :], [natk], [natk],
                         tag=f"dnatld{oc}")
                self.RED(sm[:, 20:21], natb, ALU.max, [natk], [("sm", 20)],
                         tag=f"dmax{oc}", absval=True)
                self.TS(stats[:, 64 + oc: 64 + oc + 1], sm[:, 20:21],
                        1.0 / QMAX, None, ALU.mult, None, [("sm", 20)],
                        [("stats", 64 + oc)], tag=f"dsc{oc}")
                self.RECIP(sm[:, 21:22], sm[:, 20:21], [("sm", 20)],
                           [("sm", 21)], tag=f"drc{oc}")
                self.TS(sm[:, 22:23], sm[:, 21:22], QMAX, None, ALU.mult, None,
                        [("sm", 21)], [("sm", 22)], tag=f"dr2{oc}")
                for part, w in ((0, 1024), (1024, 1024), (2048, 768)):
                    qb = self.rb("qb0", 1024)
                    qbk = ("arb", "qb", 0)
                    self.quant_round(natb[:, part:part + w], sm[:, 22:23],
                                     ("sm", 22), qb[:, 0:w], natk, qbk,
                                     f"dq{oc}_{part}",
                                     fbuf=self.rf("natb0", w),
                                     fkey=("arf", "natb", 0))
                    for j in range(w // 128):
                        ic = (part + j * 128) // 128
                        self.TPO(ptb[:, j * 128:(j + 1) * 128],
                                 qb[:, j * 128:(j + 1) * 128],
                                 [qbk], [("ptb", j)], tag=f"dt{oc}_{ic}")
                    stt = self.rb("ws", 1024, 0)
                    sttk = ("arb", "ws", 0, 0)
                    self.COPY(stt[:, 0:w], ptb[:, 0:w],
                              [("ptb", j) for j in range(w // 128)],
                              [sttk] + [("ptb", j) for j in range(w // 128)],
                              tag=f"dte{oc}_{part}")
                    for j in range(w // 128):
                        ic = (part + j * 128) // 128
                        self.DMA(wsp[2 * NIC + ic, :, oc * 128:(oc + 1) * 128],
                                 stt[:, j * 128:(j + 1) * 128], [sttk],
                                 [("wqd", ic, oc)], tag=f"dsp{oc}_{ic}")

        self.MEMSET(sm[:, 8:9], 0.0, [("sm", 8)], tag=P + "mx0")
        for tt in range(4):
            for f in range(8):
                xsl = xs[:, f * TOK + tt * 512: f * TOK + tt * 512 + 512]
                xb = self.rb("xb0" if f % 2 == 0 else "xb1", 512)
                x2 = self.rb("x20" if f % 2 == 0 else "x21", 512)
                xbk = ("arb", "xb", f % 2)
                x2k = ("arb", "x2", f % 2)
                self.COPY(xb, xsl, [("x", f)], [xbk], tag=P + f"cast{tt}_{f}")
                self.TT(x2, xb, xb, ALU.mult, [xbk], [x2k], tag=P + f"sq{tt}_{f}")
                self.MM(pb[0][:, :], ones128[:], xb, f == 0, f == 7,
                        [xbk, ("c", "ones128"), ("psum", 0)], [("psum", 0)],
                        tag=P + f"sums{tt}_{f}")
                self.MM(pb[1][:, :], ones128[:], x2, f == 0, f == 7,
                        [x2k, ("c", "ones128"), ("psum", 1)], [("psum", 1)],
                        tag=P + f"sumsq{tt}_{f}")
            mu, var, rstd, mu2 = (self.rf(r) for r in ("mu", "var", "rstd", "mu2"))
            self.TS(mu, pb[0][:, :], 1.0 / D, None, ALU.mult, None,
                    [("psum", 0)], [("arf", "mu"), ("psum", 0)], tag=P + f"mu{tt}")
            self.TS(var, pb[1][:, :], 1.0 / D, None, ALU.mult, None,
                    [("psum", 1)], [("arf", "var"), ("psum", 1)], tag=P + f"msq{tt}")
            self.TT(mu2, mu, mu, ALU.mult, [("arf", "mu")], [("arf", "mu2")],
                    tag=P + f"mu2{tt}")
            self.TT(var, var, mu2, ALU.subtract,
                    [("arf", "var"), ("arf", "mu2")], [("arf", "var")],
                    tag=P + f"var{tt}")
            self.ACTV(rstd, var, AF.Sqrt, [("arf", "var"), ("c", "eps")],
                      [("arf", "rstd")], tag=P + f"sd{tt}",
                      bias=self.sb["cs_eps"][:, 0:1])
            self.RECIP(rstd, rstd, [("arf", "rstd")], [("arf", "rstd")],
                       tag=P + f"rstd{tt}")
            for f in range(8):
                xsl = xs[:, f * TOK + tt * 512: f * TOK + tt * 512 + 512]
                lnt = self.rf("lnt0" if f % 2 == 0 else "lnt1")
                lk = ("arf", "lnt", f % 2)
                self.TT(lnt, xsl, mu, ALU.subtract, [("x", f), ("arf", "mu")],
                        [lk], tag=P + f"lsub{tt}_{f}")
                self.TT(lnt, lnt, rstd, ALU.mult, [lk, ("arf", "rstd")], [lk],
                        tag=P + f"lmul{tt}_{f}")
                self.DMA(self.dram["ln_spill"].ap()[f * 128:(f + 1) * 128,
                                                    tt * 512:(tt + 1) * 512],
                         lnt, [lk], [("lnsp", tt, f)], tag=P + f"lnsp{tt}_{f}")
                self.RED(sm[:, 9:10], lnt, ALU.max, [lk], [("sm", 9)],
                         tag=P + f"lmax{tt}_{f}", absval=True)
                self.TT(sm[:, 8:9], sm[:, 8:9], sm[:, 9:10], ALU.max,
                        [("sm", 9), ("sm", 8)], [("sm", 8)],
                        tag=P + f"mxa{tt}_{f}")

        self._scales_from_max(P, self.dram["clip_x"].ap())

        for f in range(8):
            for tt in range(4):
                lnt = self.rf("lnt0" if (f + tt) % 2 == 0 else "lnt1")
                lk = ("arf", "lnt", (f + tt) % 2)
                self.DMA(lnt, self.dram["ln_spill"].ap()[f * 128:(f + 1) * 128,
                                                         tt * 512:(tt + 1) * 512],
                         [("lnsp", tt, f), lk], [lk], tag=P + f"lnld{tt}_{f}")
                self.quant_round(lnt, sm[:, 0:1], ("sm", 0),
                                 arb[:, f * TOK + tt * 512: f * TOK + tt * 512 + 512],
                                 lk, ("xq", f, tt), P + f"q{tt}_{f}")

        for widx in range(2):
            for ic in range(NIC):
                self.TT(stats[:, 128 + widx * 32 + ic: 128 + widx * 32 + ic + 1],
                        stats[:, widx * 32 + ic: widx * 32 + ic + 1],
                        sm[:, 5:6], ALU.mult,
                        [("stats", widx * 32 + ic), ("sm", 5)],
                        [("stats", 128 + widx * 32 + ic)],
                        tag=P + f"esc{widx}_{ic}")

        self.MEMSET(sm[:, 8:9], 0.0, [("sm", 8)], tag=P + "gmx0")
        for ic in range(NIC):
            par = ic % 2
            gst = self.rb("ws", 1024, par * 2048)
            ust = self.rb("ws", 1024, par * 2048 + 1024)
            gstk = ("arb", "ws", par, 0)
            ustk = ("arb", "ws", par, 1)
            self.DMA(gst, wsp[ic, :, :], [gstk, ("wq", 0, ic)], [gstk],
                     tag=P + f"gwld{ic}")
            self.DMA(ust, wsp[NIC + ic, :, :], [ustk, ("wq", 1, ic)], [ustk],
                     tag=P + f"uwld{ic}")
            for tt in range(4):
                bg = 2 + (tt % 2)
                bu = 4 + (tt % 2)
                for f in range(8):
                    self.MM(pb[bg][:, :], gst[:, f * 128:(f + 1) * 128],
                            arb[:, f * TOK + tt * 512: f * TOK + tt * 512 + 512],
                            f == 0, f == 7,
                            [("xq", f, tt), gstk, ("psum", bg)], [("psum", bg)],
                            tag=P + f"gm{ic}_{tt}_{f}")
                pbu = pb[bu] if bu < 6 else self.ps["pb7"]
                for f in range(8):
                    self.MM(pbu[:, :], ust[:, f * 128:(f + 1) * 128],
                            arb[:, f * TOK + tt * 512: f * TOK + tt * 512 + 512],
                            f == 0, f == 7,
                            [("xq", f, tt), ustk, ("psum", bu)], [("psum", bu)],
                            tag=P + f"um{ic}_{tt}_{f}")
                pbu = pb[bu] if bu < 6 else self.ps["pb7"]
                sg = self.rf("sg0" if tt % 2 == 0 else "sg1")
                gg = self.rf("gg0" if tt % 2 == 0 else "gg1")
                sgk = ("arf", "sg", tt % 2)
                ggk = ("arf", "gg", tt % 2)
                self.ACTV(sg, pb[bg][:, :], AF.Silu,
                          [("psum", bg), ("stats", 128 + ic)],
                          [sgk, ("psum", bg)], tag=P + f"sge{ic}_{tt}",
                          scale=stats[:, 128 + ic: 128 + ic + 1])
                self.STT(gg, pbu[:, :], stats[:, 160 + ic:160 + ic + 1], sg,
                         ALU.mult, ALU.mult,
                         [("psum", bu), ("stats", 160 + ic), sgk],
                         [ggk, ("psum", bu)], tag=P + f"ge{ic}_{tt}")
                self.DMA(self.dram["g_spill"].ap()[ic * 128:(ic + 1) * 128,
                                                   tt * 512:(tt + 1) * 512],
                         gg, [ggk], [("gsp", ic, tt)], tag=P + f"gsp{ic}_{tt}")
                self.RED(sm[:, 9:10], gg, ALU.max, [ggk], [("sm", 9)],
                         tag=P + f"gmx{ic}_{tt}", absval=True)
                self.TT(sm[:, 8:9], sm[:, 8:9], sm[:, 9:10], ALU.max,
                        [("sm", 9), ("sm", 8)], [("sm", 8)],
                        tag=P + f"gma{ic}_{tt}")

        self._scales_from_max(P + "h", self.dram["clip_down"].ap())

        for oc in range(8):
            self.TT(stats[:, 192 + oc:192 + oc + 1],
                    stats[:, 64 + oc:64 + oc + 1], sm[:, 5:6], ALU.mult,
                    [("stats", 64 + oc), ("sm", 5)], [("stats", 192 + oc)],
                    tag=P + f"descA{oc}")
            self.TS(stats[:, 192 + oc:192 + oc + 1],
                    stats[:, 192 + oc:192 + oc + 1], 0.5, None, ALU.mult, None,
                    [("stats", 192 + oc)], [("stats", 192 + oc)],
                    tag=P + f"descB{oc}")

        for tt in range(4):
            for ic in range(NIC):
                par = ic % 2
                gg = self.rf("gld0" if par == 0 else "gld1")
                ggk = ("arf", "gld", par)
                self.DMA(gg, self.dram["g_spill"].ap()[ic * 128:(ic + 1) * 128,
                                                       tt * 512:(tt + 1) * 512],
                         [ggk, ("gsp", ic, tt)], [ggk], tag=P + f"gld{tt}_{ic}")
                self.quant_round(gg, sm[:, 0:1], ("sm", 0),
                                 self.rb("hq", 512, ic * 512),
                                 ggk, ("hq", ic), P + f"hq{tt}_{ic}")
            for ph in range(2):
                bks = [0, 1, 2, 3] if ph == 0 else [4, 5, 7, 2]
                for ic in range(NIC):
                    par = ic % 2
                    dwl = self.rb("dwl0" if par == 0 else "dwl1", 512)
                    dwlk = ("arb", "dwl", par)
                    self.DMA(dwl, wsp[2 * NIC + ic, :, ph * 512:(ph + 1) * 512],
                             [dwlk] + [("wqd", ic, ph * 4 + j) for j in range(4)],
                             [dwlk], tag=P + f"dwld{tt}_{ph}_{ic}")
                    for j in range(4):
                        bk = bks[j]
                        pbj = pb[bk] if bk < 6 else self.ps["pb7"]
                        self.MM(pbj[:, :], dwl[:, j * 128:(j + 1) * 128],
                                self.rb("hq", 512, ic * 512),
                                ic == 0, ic == NIC - 1,
                                [("hq", ic), dwlk, ("psum", bk)], [("psum", bk)],
                                tag=P + f"dm{tt}_{ph}_{ic}_{j}")
                for j in range(4):
                    bk = bks[j]
                    pbj = pb[bk] if bk < 6 else self.ps["pb7"]
                    oc = ph * 4 + j
                    xsl = xs[:, oc * TOK + tt * 512: oc * TOK + tt * 512 + 512]
                    self.STT(xsl, pbj[:, :], stats[:, 192 + oc:192 + oc + 1],
                             xsl, ALU.mult, ALU.add,
                             [("psum", bk), ("stats", 192 + oc), ("x", oc)],
                             [("x", oc), ("psum", bk)], tag=P + f"dres{tt}_{oc}")



    def emit_attn_group(self, g):
        P = f"ag{g}"
        pb = [self.ps[f"pb{i}"] for i in range(6)]
        pb7 = self.ps["pb7"]
        ptb = self.ps["ptb"]
        lng = self.ab_("lng", 4096)
        lngk = ("arb", "lng")
        hgb = g // 4

        self.DMA(lng.rearrange("p (a m) -> p a m", a=8),
                 self.dram["ag_lnx_out"].ap()[hgb * D:(hgb + 1) * D,
                                              (g % 4) * 512:(g % 4) * 512 + 512]
                 .rearrange("(a p) m -> p a m", p=128),
                 [lngk, ("aglnx",)], [lngk], tag=P + "lngld")

        for which, kbase in (("q", 0), ("k", 4096)):
            for oc in range(4):
                bnk = 2 + (oc % 2)
                for kc in range(8):
                    self.MM(pb[bnk][:, :],
                            self.ab_("aw", 128, kbase + kc * 512 + oc * 128),
                            lng[:, kc * 512:(kc + 1) * 512],
                            kc == 0, kc == 7,
                            [lngk, ("aw", which, kc), ("psum", bnk)],
                            [("psum", bnk)], tag=P + f"{which}m{oc}_{kc}")
                t = self.af_("tmp", 512)
                tk = ("arf", "tmp")
                self.ACTV(t, pb[bnk][:, :], AF.Tanh,
                          [("psum", bnk)], [tk], tag=P + f"{which}th{oc}",
                          scale=0.5)
                self.TS(t, t, 0.5, 0.5, ALU.mult, ALU.add, [tk], [tk],
                        tag=P + f"{which}sg{oc}")
                dst = self.ab_(which, 512, oc * 512)
                self.TT(dst, t, pb[bnk][:, :], ALU.mult,
                        [tk, ("psum", bnk)],
                        [("arb", which, oc), ("psum", bnk)],
                        tag=P + f"{which}sl{oc}")
        if self.gcut < 2:
            return
        bo = self.sb["cs_blockones"]
        for oc in range(4):
            k2 = self.af_("tmp", 512)
            tk = ("arf", "tmp")
            kt = self.ab_("k", 512, oc * 512)
            k2b = self.rb("xb0", 512)
            self.TT(k2b, kt, kt, ALU.mult, [("arb", "k", oc)],
                    [("arb", "xb", 0)], tag=P + f"k2_{oc}")
            self.MM(pb7[:, :], bo[:], k2b, True, True,
                    [("arb", "xb", 0), ("c", "blockones"), ("psum", 7)],
                    [("psum", 7)], tag=P + f"k2m{oc}")
            self.ACTV(k2, pb7[:, :], AF.Sqrt, [("psum", 7)],
                      [tk, ("psum", 7)], tag=P + f"knrm{oc}")
            self.TS(k2, k2, 1e-12, None, ALU.max, None, [tk], [tk],
                    tag=P + f"knmx{oc}")
            self.RECIP(k2, k2, [tk], [tk], tag=P + f"knr{oc}")
            self.TT(kt, kt, k2, ALU.mult, [("arb", "k", oc), tk],
                    [("arb", "k", oc)], tag=P + f"knd{oc}")

        if self.gcut < 3:
            return
        for kc in range(8):
            self.DMA(self.ab_("wv", 512, kc * 512),
                     self.dram["Wv"].ap()[kc * 128:(kc + 1) * 128, :],
                     [("arb", "wv", kc)], [("arb", "wv", kc)],
                     tag=P + f"wvl{kc}", eng="pool")
        for c in range(8):
            bnk = 4 + (c % 2)
            for kc in range(8):
                self.MM(pb[bnk][0:64, :],
                        lng[:, kc * 512 + c * 64: kc * 512 + c * 64 + 64],
                        self.ab_("wv", 512, kc * 512), kc == 0, kc == 7,
                        [lngk, ("arb", "wv", kc), ("psum", bnk)],
                        [("psum", bnk)], tag=P + f"vm{c}_{kc}")
            self.COPY(self.af_("vg", 512, c * 512)[0:64, :], pb[bnk][0:64, :],
                      [("psum", bnk)], [("arf", "vg", c), ("psum", bnk)],
                      tag=P + f"ve{c}")
        self.DMA(self.sb["arf"][64:128, ATF["vg"]:ATF["vg"] + 4096],
                 self.sb["arf"][0:64, ATF["vg"]:ATF["vg"] + 4096],
                 [("arf", "vg", c) for c in range(8)] + [("arf", "vgd")],
                 [("arf", "vgd")], tag=P + "vdup")

        if self.gcut < 4:
            return
        for kc in range(8):
            self.MM(pb7[0:64, :], self.ab_("awf", 64, kc * 64),
                    lng[:, kc * 512:(kc + 1) * 512], kc == 0, kc == 7,
                    [lngk, ("aw", "f1", kc), ("psum", 7)], [("psum", 7)],
                    tag=P + f"f1m{kc}")
        xf1 = self.sb["arb"][0:64, ATB["u"]:ATB["u"] + 512]
        self.COPY(xf1, pb7[0:64, :], [("psum", 7)],
                  [("arb", "xf1"), ("psum", 7)], tag=P + "f1e")
        for oc in range(4):
            bnk = 2 + (oc % 2)
            self.MM(pb[bnk][:, :],
                    self.sb["arb"][0:64, ATB["awf"] + 512 + oc * 128:
                                   ATB["awf"] + 512 + (oc + 1) * 128],
                    xf1, True, True,
                    [("arb", "xf1"), ("aw", "f2"), ("psum", bnk)],
                    [("psum", bnk)], tag=P + f"f2m{oc}")
            lam = self.af_("zf", 512, oc * 512)
            self.ACTV(lam, pb[bnk][:, :], AF.Tanh, [("psum", bnk)],
                      [("arf", "zf", oc), ("psum", bnk)], tag=P + f"lth{oc}",
                      scale=0.5)
            self.TS(lam, lam, 0.5, 0.5, ALU.mult, ALU.add,
                    [("arf", "zf", oc)], [("arf", "zf", oc)], tag=P + f"lsg{oc}")

        for kc in range(8):
            self.MM(pb7[0:8, :], self.ab_("awf", 8, 1024 + kc * 8),
                    lng[:, kc * 512:(kc + 1) * 512], kc == 0, kc == 7,
                    [lngk, ("aw", "g", kc), ("psum", 7)], [("psum", 7)],
                    tag=P + f"gm{kc}")
        gamt = self.sb["arf"][0:8, ATF["tmp"]:ATF["tmp"] + 512]
        self.ACTV(gamt, pb7[0:8, :], AF.Tanh, [("psum", 7)],
                  [("arf", "tmp"), ("psum", 7)], tag=P + "gth", scale=0.5)
        self.TS(gamt, gamt, -0.5, -0.5, ALU.mult, ALU.add, [("arf", "tmp")],
                [("arf", "tmp")], tag=P + "gsg")
        gamb = self.sb["arb"][0:8, ATB["g8"]:ATB["g8"] + 512]
        self.COPY(gamb, gamt, [("arf", "tmp")], [("arb", "g8")], tag=P + "gbc")
        sel = self.sb["cs_sel8"]
        for pt in range(4):
            self.MM(pb7[:, :], sel[:, pt * 128:(pt + 1) * 128], gamb,
                    True, True,
                    [("arb", "g8"), ("c", "sel8"), ("psum", 7)],
                    [("psum", 7)], tag=P + f"grm{pt}")
            self.COPY(self.ab_("gr", 512, pt * 512), pb7[:, :], [("psum", 7)],
                      [("arb", "gr", pt), ("psum", 7)], tag=P + f"gre{pt}")

        if self.gcut < 5:
            return
        z64 = self.sb["cs_zero64"]
        for pt in range(4):
            for c in range(8):
                off = ATF["P"] + pt * 512 + c * 64
                zoff = ATF["zf"] + pt * 512 + c * 64
                self.SCAN(self.sb["arf"][:, off:off + 64],
                          self.sb["arf"][:, zoff:zoff + 64],
                          z64[:, 0:64], 1.0, ALU.mult, ALU.add,
                          [("arf", "zf", pt), ("c", "zero64")],
                          [("arf", "P", pt)], tag=P + f"scan{pt}_{c}")
            self.RECIP(self.af_("Pinv", 512, pt * 512),
                       self.af_("P", 512, pt * 512),
                       [("arf", "P", pt)], [("arf", "Pinv", pt)],
                       tag=P + f"pinv{pt}")

        if self.gcut < 6:
            return
        for pt in range(4):
            q = self.ab_("q", 512, pt * 512)
            k = self.ab_("k", 512, pt * 512)
            Pt = self.af_("P", 512, pt * 512)
            Pi = self.af_("Pinv", 512, pt * 512)
            lam = self.af_("zf", 512, pt * 512)
            t = self.af_("tmp", 512)
            tk = ("arf", "tmp")
            self.TT(self.ab_("qb", 512, pt * 512), q, Pt, ALU.mult,
                    [("arb", "q", pt), ("arf", "P", pt)], [("arb", "qb", pt)],
                    tag=P + f"qb{pt}")
            self.TT(self.ab_("kt", 512, pt * 512), k, Pi, ALU.mult,
                    [("arb", "k", pt), ("arf", "Pinv", pt)], [("arb", "kt", pt)],
                    tag=P + f"ktb{pt}")
            self.TT(t, lam, Pt, ALU.mult,
                    [("arf", "zf", pt), ("arf", "P", pt)], [tk],
                    tag=P + f"lp{pt}")
            self.TT(t, t, self.ab_("gr", 512, pt * 512), ALU.mult,
                    [tk, ("arb", "gr", pt)], [tk], tag=P + f"lpg{pt}")
            self.TT(self.ab_("ab", 512, pt * 512), k, t, ALU.mult,
                    [("arb", "k", pt), tk], [("arb", "ab", pt)],
                    tag=P + f"abt{pt}")
            for c in range(8):
                self.TS(self.ab_("k", 64, pt * 512 + c * 64),
                        self.ab_("kt", 64, pt * 512 + c * 64),
                        self.sb["arf"][:, ATF["P"] + pt * 512 + c * 64 + 63:
                                       ATF["P"] + pt * 512 + c * 64 + 64],
                        None, ALU.mult, None,
                        [("arb", "kt", pt), ("arf", "P", pt)],
                        [("arb", "k", pt)], tag=P + f"khf{pt}_{c}")

        if self.gcut < 7:
            return
        for pt in range(4):
            for cp in range(4):
                self.TPO(ptb[:, cp * 128:(cp + 1) * 128],
                         self.ab_("k", 128, pt * 512 + cp * 128),
                         [("arb", "k", pt)], [("ptb", cp)],
                         tag=P + f"khT{pt}_{cp}")
            self.COPY(self.ab_("kht", 512, pt * 512), ptb[:, 0:512],
                      [("ptb", cp) for cp in range(4)],
                      [("arb", "kht", pt)] + [("ptb", cp) for cp in range(4)],
                      tag=P + f"khE{pt}")

        if self.gcut < 8:
            return
        self.run_pairs(g)

        for pt in range(4):
            self.DMA(self.dram["o_spill"].ap()[pt * 128:(pt + 1) * 128,
                                               g * 512:(g + 1) * 512],
                     self.ab_("og", 512, pt * 512),
                     [("arb", "og", pt, c) for c in range(8)],
                     [("osp", g, pt)], tag=P + f"osp{pt}")

    PW = dict(T=0, W=512, sT=1024, sW=1536, N=2048, AT=2560)

    def pwsl(self, slot, j, c, pp):
        base = ATB[("pw", "pw2", "pw3")[pp]] + self.PW[slot] + 256 * j + 64 * (c // 2)
        r = 64 * (c % 2)
        return self.sb["arb"][r:r + 64, base:base + 64]

    def pwt(self, slot, pp):
        return self.ab_(("pw", "pw2", "pw3")[pp], 512, self.PW[slot])

    def _gsl(self, reg, h, c):
        pt = h // 2
        base = ATB[reg] + pt * 512 + c * 64
        r = 64 * (h % 2)
        return self.sb["arb"][r:r + 64, base:base + 64]

    def _vsl(self, h, c):
        base = ATF["vg"] + c * 512 + h * 64
        r = 64 * (c % 2)
        return self.sb["arf"][r:r + 64, base:base + 64]

    def _khsl(self, h, c):
        base = ATB["kht"] + (h // 2) * 512 + (c // 2) * 128 + 64 * (h % 2)
        r = 64 * (c % 2)
        return self.sb["arf"] if False else self.sb["arb"][r:r + 64, base:base + 64]

    def _ssl(self, h):
        base = ATB["s"] + 64 * (h // 2)
        r = 64 * (h % 2)
        return self.sb["arb"][r:r + 64, base:base + 64]

    def gen_precompute(self, g, hp):
        P = f"ag{g}p{hp}"
        pp = hp % 3
        pb = [self.ps[f"pb{i}"] for i in range(6)]
        pb7 = self.ps["pb7"]
        if hp % 2 == 0:
            bt, bw, kt_, kw_ = pb[2], pb[3], 2, 3
        else:
            bt, bw, kt_, kw_ = pb[1], pb7, 1, 7
        kkey = ("arb", "pw", pp)
        msl = self.sb["cs_mstril"]
        msu = self.sb["cs_mstriu"]
        mti = self.sb["cs_mtriuincl"]
        idb = self.sb["cs_identblk"]
        heads = (2 * hp, 2 * hp + 1)
        gsl = self._gsl
        arb = self.sb["arb"]

        def osl(j, c, bank):
            return bank[64 * (c % 2):64 * (c % 2) + 64,
                        256 * j + 64 * (c // 2):256 * j + 64 * (c // 2) + 64]

        def pwh(slot, r0, c0, n):
            base = ATB[("pw", "pw2", "pw3")[pp]] + self.PW[slot] + c0
            return arb[r0:r0 + 64, base:base + n]

        for j, h in enumerate(heads):
            for c in range(8):
                tp = (64 * (h % 2), 64 * (c % 2))
                self.MM(osl(j, c, bt), gsl("ab", h, c), gsl("kt", h, c),
                        True, True,
                        [("arb", "ab", h // 2), ("arb", "kt", h // 2),
                         ("psum", kt_)], [("psum", kt_)],
                        tag=P + f"Tb{j}_{c}", tp=tp)
                self.MM(osl(j, c, bw), gsl("kt", h, c), gsl("ab", h, c),
                        True, True,
                        [("arb", "ab", h // 2), ("arb", "kt", h // 2),
                         ("psum", kw_)], [("psum", kw_)],
                        tag=P + f"Wb{j}_{c}", tp=tp)
                if c % 4 == 3:
                    yield
        self.TT(self.pwt("T", pp), bt[:, :], msl[:], ALU.mult,
                [("psum", kt_), ("c", "mstril")],
                [kkey + ("T",), ("psum", kt_)], tag=P + "Te")
        self.TT(self.pwt("W", pp), bw[:, :], msu[:], ALU.mult,
                [("psum", kw_), ("c", "mstriu")],
                [kkey + ("W",), ("psum", kw_)], tag=P + "We")
        yield
        for j, h in enumerate(heads):
            for c in range(8):
                tp = (64 * (h % 2), 64 * (c % 2))
                self.MM(osl(j, c, bt), gsl("kt", h, c), gsl("qb", h, c),
                        True, True,
                        [("arb", "qb", h // 2), ("arb", "kt", h // 2),
                         ("psum", kt_)], [("psum", kt_)],
                        tag=P + f"ATb{j}_{c}", tp=tp)
                if c % 4 == 3:
                    yield
        self.TT(self.pwt("AT", pp), bt[:, :], mti[:], ALU.mult,
                [("psum", kt_), ("c", "mtriuincl")],
                [kkey + ("AT",), ("psum", kt_)], tag=P + "ATe")
        self.TT(self.pwt("N", pp), self.pwt("W", pp), idb[:], ALU.add,
                [kkey + ("W",), ("c", "identblk")], [kkey + ("N",)],
                tag=P + "Ni")
        yield

        cur_t, cur_w, nxt_t, nxt_w = "T", "W", "sT", "sW"
        corder = [0, 2, 4, 6, 1, 3, 5, 7]
        for lvl in range(4):
            for j, h in enumerate(heads):
                for c in corder:
                    tpp = (64 * (c % 2), 64 * (c % 2))
                    self.MM(osl(j, c, bt), self.pwsl(cur_w, j, c, pp),
                            self.pwsl(cur_t, j, c, pp), True, True,
                            [kkey + (cur_w,), kkey + (cur_t,),
                             ("psum", kt_)], [("psum", kt_)],
                            tag=P + f"sq{lvl}t{j}_{c}", tp=tpp)
                    self.MM(osl(j, c, bw), self.pwsl(cur_t, j, c, pp),
                            self.pwsl(cur_w, j, c, pp), True, True,
                            [kkey + (cur_w,), kkey + (cur_t,),
                             ("psum", kw_)], [("psum", kw_)],
                            tag=P + f"sq{lvl}w{j}_{c}", tp=tpp)
                    if c in (6, 7):
                        yield
            self.ACTV(self.pwt(nxt_t, pp), bt[:, :], AF.Copy,
                      [("psum", kt_)], [kkey + (nxt_t,), ("psum", kt_)],
                      tag=P + f"sqe{lvl}t")
            self.ACTV(self.pwt(nxt_w, pp), bw[:, :], AF.Copy,
                      [("psum", kw_)], [kkey + (nxt_w,), ("psum", kw_)],
                      tag=P + f"sqe{lvl}w")
            yield
            for j, h in enumerate(heads):
                for c in corder:
                    tpp = (64 * (c % 2), 64 * (c % 2))
                    self.MM(osl(j, c, bt), self.pwsl(nxt_t, j, c, pp),
                            self.pwsl("N", j, c, pp), True, True,
                            [kkey + (nxt_t,), kkey + ("N",),
                             ("psum", kt_)], [("psum", kt_)],
                            tag=P + f"nu{lvl}_{j}_{c}", tp=tpp)
                    if c in (6, 7):
                        yield
            self.TT(self.pwt("N", pp), bt[:, :], self.pwt("N", pp), ALU.add,
                    [("psum", kt_), kkey + ("N",)],
                    [kkey + ("N",), ("psum", kt_)], tag=P + f"nue{lvl}")
            yield
            cur_t, cur_w, nxt_t, nxt_w = nxt_t, nxt_w, cur_t, cur_w
        for j, h in enumerate(heads):
            for c in corder:
                tpp = (64 * (c % 2), 64 * (c % 2))
                self.MM(osl(j, c, bt), self.pwsl(cur_w, j, c, pp),
                        self.pwsl(cur_t, j, c, pp), True, True,
                        [kkey + (cur_w,), kkey + (cur_t,),
                         ("psum", kt_)], [("psum", kt_)],
                        tag=P + f"sq32t{j}_{c}", tp=tpp)
                if c in (6, 7):
                    yield
        self.ACTV(self.pwt(nxt_t, pp), bt[:, :], AF.Copy, [("psum", kt_)],
                  [kkey + (nxt_t,), ("psum", kt_)], tag=P + "sqe32t")
        yield
        for j, h in enumerate(heads):
            for c in corder:
                tpp = (64 * (c % 2), 64 * (c % 2))
                self.MM(osl(j, c, bt), self.pwsl(nxt_t, j, c, pp),
                        self.pwsl("N", j, c, pp), True, True,
                        [kkey + (nxt_t,), kkey + ("N",),
                         ("psum", kt_)], [("psum", kt_)],
                        tag=P + f"nu32_{j}_{c}", tp=tpp)
                if c in (6, 7):
                    yield
        self.TT(self.pwt("N", pp), bt[:, :], self.pwt("N", pp), ALU.add,
                [("psum", kt_), kkey + ("N",)], [kkey + ("N",), ("psum", kt_)],
                tag=P + "nue32")
        yield

    def gen_chain(self, g, hp):
        P = f"ag{g}c{hp}"
        pp = hp % 3
        pb = [self.ps[f"pb{i}"] for i in range(6)]
        kkey = ("arb", "pw", pp)
        heads = (2 * hp, 2 * hp + 1)
        gsl, vsl, khsl, ssl = self._gsl, self._vsl, self._khsl, self._ssl
        arb = self.sb["arb"]
        UC2 = ATB["uc2"]
        ucR = [arb[:, UC2 + 128 * j:UC2 + 128 * j + 64] for j in (0, 1)]
        ucU = [arb[:, UC2 + 128 * j + 64:UC2 + 128 * j + 128] for j in (0, 1)]
        cb = [pb[0], pb[4]]
        ck = [0, 4]
        for c in range(8):
            cpar = 64 * (c % 2)
            for j, h in enumerate(heads):
                hpar = 64 * (h % 2)
                self.MM(pb[5][hpar:hpar + 64, 0:64], ssl(h), gsl("qb", h, c),
                        True, True,
                        [("arb", "s", h), ("arb", "qb", h // 2), ("psum", 5)],
                        [("psum", 5)], tag=P + f"cO1{c}_{j}", tp=(hpar, hpar))
            for j, h in enumerate(heads):
                hpar = 64 * (h % 2)
                self.MM(cb[j][cpar:cpar + 64, 0:64], gsl("ab", h, c), ssl(h),
                        True, True,
                        [("arb", "ab", h // 2), ("arb", "s", h),
                         ("psum", ck[j])], [("psum", ck[j])],
                        tag=P + f"cR{c}_{j}", tp=(hpar, cpar))
            for j, h in enumerate(heads):
                self.TT(ucR[j][cpar:cpar + 64, :],
                        cb[j][cpar:cpar + 64, 0:64], vsl(h, c), ALU.add,
                        [("psum", ck[j]), ("arf", "vgd")],
                        [("arb", "ucR", j), ("psum", ck[j])],
                        tag=P + f"cRe{c}_{j}")
            for j, h in enumerate(heads):
                self.MM(cb[j][cpar:cpar + 64, 64:128],
                        self.pwsl("N", j, c, pp), ucR[j][cpar:cpar + 64, :],
                        True, True,
                        [kkey + ("N",), ("arb", "ucR", j), ("psum", ck[j])],
                        [("psum", ck[j])], tag=P + f"cU{c}_{j}",
                        tp=(cpar, cpar))
            for j, h in enumerate(heads):
                self.ACTV(ucU[j][cpar:cpar + 64, :],
                          cb[j][cpar:cpar + 64, 64:128], AF.Copy,
                          [("psum", ck[j])],
                          [("arb", "ucU", j), ("psum", ck[j])],
                          tag=P + f"cUe{c}_{j}")
            for j, h in enumerate(heads):
                hpar = 64 * (h % 2)
                self.MM(pb[5][hpar:hpar + 64, 64:128],
                        ucU[j][cpar:cpar + 64, :],
                        self.pwsl("AT", j, c, pp), True, True,
                        [("arb", "ucU", j), kkey + ("AT",), ("psum", 5)],
                        [("psum", 5)], tag=P + f"cO2{c}_{j}", tp=(cpar, hpar))
                self.MM(cb[j][hpar:hpar + 64, 128:192], khsl(h, c),
                        ucU[j][cpar:cpar + 64, :], True, True,
                        [("arb", "kht", h // 2), ("arb", "ucU", j),
                         ("psum", ck[j])], [("psum", ck[j])],
                        tag=P + f"cK{c}_{j}", tp=(cpar, hpar))
            for j, h in enumerate(heads):
                hpar = 64 * (h % 2)
                pcol = self.sb["arf"][hpar:hpar + 64,
                                      ATF["P"] + (h // 2) * 512 + c * 64 + 63:
                                      ATF["P"] + (h // 2) * 512 + c * 64 + 64]
                self.STT(ssl(h), ssl(h), pcol,
                         cb[j][hpar:hpar + 64, 128:192],
                         ALU.mult, ALU.add,
                         [("arb", "s", h), ("arf", "P", h // 2),
                          ("psum", ck[j])],
                         [("arb", "s", h), ("psum", ck[j])],
                         tag=P + f"cS{c}_{j}")
                yield
            self.COPY(self.ab_("og", 64, hp * 512 + c * 64), pb[5][:, 0:64],
                      [("psum", 5)], [("arb", "og", hp, c)], tag=P + f"cOc{c}")
            self.TT(self.ab_("og", 64, hp * 512 + c * 64),
                    self.ab_("og", 64, hp * 512 + c * 64), pb[5][:, 64:128],
                    ALU.add, [("arb", "og", hp, c), ("psum", 5)],
                    [("arb", "og", hp, c), ("psum", 5)], tag=P + f"cOe{c}")
            yield

    def run_pairs(self, g):
        pre = {hp: self.gen_precompute(g, hp) for hp in range(4)}
        chains = {}
        pre_done, ch_done = set(), set()
        active, next_pre = [], 0
        while len(ch_done) < 4:
            while (next_pre < 4 and len(active) < 2
                   and (next_pre < 3 or (next_pre - 3) in ch_done)):
                active.append(next_pre)
                next_pre += 1
            achain = None
            for hp in range(4):
                if hp in pre_done and hp not in ch_done:
                    achain = hp
                    break
            if achain is not None:
                gen = chains.get(achain)
                if gen is None:
                    gen = chains[achain] = self.gen_chain(g, achain)
                try:
                    next(gen)
                except StopIteration:
                    ch_done.add(achain)
            for hp in list(active):
                try:
                    next(pre[hp])
                    next(pre[hp])
                except StopIteration:
                    pre_done.add(hp)
                    active.remove(hp)
            if achain is None and not active and next_pre >= 4:
                break

    def emit_attn_output(self):
        P = "ao"
        pb = [self.ps[f"pb{i}"] for i in range(6)]
        pb7 = self.ps["pb7"]
        xs = self.sb["xs"]
        ones128 = self.sb["cs_ones128"]
        hgsel = ("hgsel",)

        for kc in range(8):
            self.DMA(self.ob_("wo", 1024, kc * 1024),
                     self.dram["Wo"].ap()[kc * 128:(kc + 1) * 128, :],
                     [], [("wo", kc)], tag=P + f"wol{kc}", eng="pool")
            self.DMA(self.ob_("lnxo1", 64, kc * 64),
                     self.dram["Wog1"].ap()[kc * 128:(kc + 1) * 128, :],
                     [], [("wog1", kc)], tag=P + f"wog1l{kc}", eng="pool")
        self.DMA(self.sb["arb"][0:64, OTB["lnxo1"] + 512:OTB["lnxo1"] + 1536],
                 self.dram["Wog2"].ap()[:, :], [], [("wog2",)],
                 tag=P + "wog2l", eng="pool")
        self.DMA(self.sb["small"][:, 24:32], self.dram["norm_w"].ap(),
                 [("sm", 24)], [("sm", 24)], tag=P + "nwl")

        for tt in range(4):
            par = tt % 2
            ot = self.ob_("ot0" if par == 0 else "ot1", 4096)
            otk = ("arb", "ot", par)
            lnxo = self.ob_("lnxo0", 4096)
            lnk = ("arb", "lnxo")
            own0 = (tt * 512)
            oth = self.ob_("ot0" if par == 1 else "ot1", 4096)
            othk = ("arb", "ot", 1 - par)
            self.DMA(ot.rearrange("p (a m) -> p a m", a=8),
                     self.dram["ag_o_out"].ap()[:, own0:own0 + 512]
                     .rearrange("(a p) m -> p a m", p=128),
                     [otk, ("agoout",)], [otk], tag=P + f"otl{tt}")
            self.DMA(oth.rearrange("p (a m) -> p a m", a=8),
                     self.dram["ag_o_out"].ap()[:, 2048 + own0:2048 + own0 + 512]
                     .rearrange("(a p) m -> p a m", p=128),
                     [othk, ("agoout",)], [othk], tag=P + f"oth{tt}")
            self.TT(oth, oth, ot, ALU.subtract, [otk, othk], [othk],
                    tag=P + f"osub{tt}")
            self.STT(ot, oth, self.sb["hgm"][:, 0:1], ot, ALU.mult, ALU.add,
                     [othk, ("hgm",), otk], [otk], tag=P + f"oblend{tt}")
            self.DMA(lnxo.rearrange("p (a m) -> p a m", a=8),
                     self.dram["ag_lnx_in"].ap()[:, own0:own0 + 512]
                     .rearrange("(a p) m -> p a m", p=128),
                     [lnk] + [("agin", tt, f) for f in range(8)], [lnk],
                     tag=P + f"lnxl{tt}")
            for kc in range(8):
                self.MM(pb7[0:64, :], self.ob_("lnxo1", 64, kc * 64),
                        lnxo[:, kc * 512:(kc + 1) * 512], kc == 0, kc == 7,
                        [lnk, ("wog1", kc), ("psum", 7)], [("psum", 7)],
                        tag=P + f"og1m{tt}_{kc}")
            og1 = self.sb["arb"][0:64, OTB["ogt"]:OTB["ogt"] + 512]
            self.COPY(og1, pb7[0:64, :], [("psum", 7)],
                      [("arb", "ogt"), ("psum", 7)], tag=P + f"og1e{tt}")
            for oc in range(8):
                bnk = 2 + (oc % 2)
                self.MM(pb[bnk][:, :],
                        self.sb["arb"][0:64, OTB["lnxo1"] + 512 + oc * 128:
                                       OTB["lnxo1"] + 512 + (oc + 1) * 128],
                        og1, True, True,
                        [("arb", "ogt"), ("wog2",), ("psum", bnk)],
                        [("psum", bnk)], tag=P + f"og2m{tt}_{oc}")
                t = self.af_("tmp", 512) if False else self.of_("mu", 512)
                tk = ("arf", "mu")
                self.ACTV(t, pb[bnk][:, :], AF.Tanh, [("psum", bnk)], [tk],
                          tag=P + f"ogth{tt}_{oc}", scale=0.5)
                self.TS(t, t, 0.5, 0.5, ALU.mult, ALU.add, [tk], [tk],
                        tag=P + f"ogsg{tt}_{oc}")
                ov = self.of_("ov", 512, oc * 512)
                self.TT(ov, ot[:, oc * 512:(oc + 1) * 512], t, ALU.mult,
                        [otk, tk], [("arf", "ov", oc)], tag=P + f"ov{tt}_{oc}")
            for oc in range(8):
                ovb = self.rb("xb0" if oc % 2 == 0 else "xb1", 512)
                ovbk = ("arb", "xb", oc % 2)
                ov = self.of_("ov", 512, oc * 512)
                self.COPY(ovb, ov, [("arf", "ov", oc)], [ovbk],
                          tag=P + f"ovb{tt}_{oc}")
                x2 = self.rb("x20" if oc % 2 == 0 else "x21", 512)
                x2k = ("arb", "x2", oc % 2)
                self.TT(x2, ovb, ovb, ALU.mult, [ovbk], [x2k],
                        tag=P + f"ovsq{tt}_{oc}")
                self.MM(pb[0][:, :], ones128[:], ovb, oc == 0, oc == 7,
                        [ovbk, ("c", "ones128"), ("psum", 0)], [("psum", 0)],
                        tag=P + f"osum{tt}_{oc}")
                self.MM(pb[1][:, :], ones128[:], x2, oc == 0, oc == 7,
                        [x2k, ("c", "ones128"), ("psum", 1)], [("psum", 1)],
                        tag=P + f"osumsq{tt}_{oc}")
            mu = self.of_("mu", 512)
            var = self.of_("var", 512)
            rstd = self.of_("rstd", 512)
            mu2 = self.of_("mu2", 512)
            self.TS(mu, pb[0][:, :], 1.0 / D, None, ALU.mult, None,
                    [("psum", 0)], [("arf", "mu"), ("psum", 0)], tag=P + f"mu{tt}")
            self.TS(var, pb[1][:, :], 1.0 / D, None, ALU.mult, None,
                    [("psum", 1)], [("arf", "var"), ("psum", 1)],
                    tag=P + f"msq{tt}")
            self.TT(mu2, mu, mu, ALU.mult, [("arf", "mu")], [("arf", "mu2")],
                    tag=P + f"mu2{tt}")
            self.TT(var, var, mu2, ALU.subtract,
                    [("arf", "var"), ("arf", "mu2")], [("arf", "var")],
                    tag=P + f"var{tt}")
            self.ACTV(rstd, var, AF.Sqrt, [("arf", "var"), ("c", "eps")],
                      [("arf", "rstd")], tag=P + f"sd{tt}",
                      bias=self.sb["cs_eps"][:, 0:1])
            self.RECIP(rstd, rstd, [("arf", "rstd")], [("arf", "rstd")],
                       tag=P + f"rstd{tt}")
            for oc in range(8):
                ov = self.of_("ov", 512, oc * 512)
                self.TT(ov, ov, mu, ALU.subtract,
                        [("arf", "ov", oc), ("arf", "mu")], [("arf", "ov", oc)],
                        tag=P + f"on1{tt}_{oc}")
                self.TT(ov, ov, rstd, ALU.mult,
                        [("arf", "ov", oc), ("arf", "rstd")],
                        [("arf", "ov", oc)], tag=P + f"on2{tt}_{oc}")
                oln = self.ob_("oln", 512, oc * 512)
                self.ACTV(oln, ov, AF.Copy,
                          [("arf", "ov", oc), ("sm", 24)],
                          [("arb", "oln", oc)], tag=P + f"onw{tt}_{oc}",
                          scale=self.sb["small"][:, 24 + oc:24 + oc + 1])
            for oc2 in range(8):
                bnk = 2 + (oc2 % 2)
                for kc in range(8):
                    self.MM(pb[bnk][:, :],
                            self.ob_("wo", 128, kc * 1024 + oc2 * 128),
                            self.ob_("oln", 512, kc * 512), kc == 0, kc == 7,
                            [("arb", "oln", kc), ("wo", kc), ("psum", bnk)],
                            [("psum", bnk)], tag=P + f"wom{tt}_{oc2}_{kc}")
                xsl = xs[:, oc2 * TOK + tt * 512: oc2 * TOK + tt * 512 + 512]
                self.TT(xsl, pb[bnk][:, :], xsl, ALU.add,
                        [("psum", bnk), ("x", oc2)],
                        [("x", oc2), ("psum", bnk)], tag=P + f"ores{tt}_{oc2}")
    def ab_(self, reg, width, off=0):
        return self.sb["arb"][:, ATB[reg] + off: ATB[reg] + off + width]

    def ob_(self, reg, width, off=0):
        return self.sb["arb"][:, OTB[reg] + off: OTB[reg] + off + width]

    def af_(self, reg, width, off=0):
        return self.sb["arf"][:, ATF[reg] + off: ATF[reg] + off + width]

    def of_(self, reg, width, off=0):
        return self.sb["arf"][:, OTF[reg] + off: OTF[reg] + off + width]

    def emit_attn(self):
        S = self.S
        xs = self.sb["xs"]
        arb = self.sb["arb"]
        arf = self.sb["arf"]
        sm = self.sb["small"]
        pb = [self.ps[f"pb{i}"] for i in range(6)]
        pb7 = self.ps["pb7"]
        ptb = self.ps["ptb"]
        ones128 = self.sb["cs_ones128"]
        hg_tokens = TOK

        S.barrier()
        P = "at"

        for tt in range(4):
            for f in range(8):
                xsl = xs[:, f * TOK + tt * 512: f * TOK + tt * 512 + 512]
                xb = self.rb("xb0" if f % 2 == 0 else "xb1", 512)
                x2 = self.rb("x20" if f % 2 == 0 else "x21", 512)
                xbk = ("arb", "xb", f % 2)
                x2k = ("arb", "x2", f % 2)
                self.COPY(xb, xsl, [("x", f)], [xbk], tag=P + f"cast{tt}_{f}")
                self.TT(x2, xb, xb, ALU.mult, [xbk], [x2k], tag=P + f"sq{tt}_{f}")
                self.MM(pb[0][:, :], ones128[:], xb, f == 0, f == 7,
                        [xbk, ("c", "ones128"), ("psum", 0)], [("psum", 0)],
                        tag=P + f"sums{tt}_{f}")
                self.MM(pb[1][:, :], ones128[:], x2, f == 0, f == 7,
                        [x2k, ("c", "ones128"), ("psum", 1)], [("psum", 1)],
                        tag=P + f"sumsq{tt}_{f}")
            mu, var, rstd, mu2 = (self.rf(r) for r in ("mu", "var", "rstd", "mu2"))
            self.TS(mu, pb[0][:, :], 1.0 / D, None, ALU.mult, None,
                    [("psum", 0)], [("arf", "mu"), ("psum", 0)], tag=P + f"mu{tt}")
            self.TS(var, pb[1][:, :], 1.0 / D, None, ALU.mult, None,
                    [("psum", 1)], [("arf", "var"), ("psum", 1)], tag=P + f"msq{tt}")
            self.TT(mu2, mu, mu, ALU.mult, [("arf", "mu")], [("arf", "mu2")],
                    tag=P + f"mu2{tt}")
            self.TT(var, var, mu2, ALU.subtract,
                    [("arf", "var"), ("arf", "mu2")], [("arf", "var")],
                    tag=P + f"var{tt}")
            self.ACTV(rstd, var, AF.Sqrt, [("arf", "var"), ("c", "eps")],
                      [("arf", "rstd")], tag=P + f"sd{tt}",
                      bias=self.sb["cs_eps"][:, 0:1])
            self.RECIP(rstd, rstd, [("arf", "rstd")], [("arf", "rstd")],
                       tag=P + f"rstd{tt}")
            for f in range(8):
                xsl = xs[:, f * TOK + tt * 512: f * TOK + tt * 512 + 512]
                lnt = self.rf("lnt0" if f % 2 == 0 else "lnt1")
                lk = ("arf", "lnt", f % 2)
                self.TT(lnt, xsl, mu, ALU.subtract, [("x", f), ("arf", "mu")],
                        [lk], tag=P + f"lsub{tt}_{f}")
                self.TT(lnt, lnt, rstd, ALU.mult, [lk, ("arf", "rstd")], [lk],
                        tag=P + f"lmul{tt}_{f}")
                self.DMA(self.dram["ag_lnx_in"].ap()[f * 128:(f + 1) * 128,
                                                     tt * 512:(tt + 1) * 512],
                         lnt, [lk], [("agin", tt, f), lk],
                         tag=P + f"lnst{tt}_{f}", eng="pool")

        if self.attn_cut < 1:
            return
        self.CC("AllGather", ALU.bypass, [[0, 1], [2, 3], [4, 5], [6, 7]],
                [self.dram["ag_lnx_in"].ap()], [self.dram["ag_lnx_out"].ap()],
                [("agin", tt, f) for tt in range(4) for f in range(8)],
                [("aglnx",)], tag=P + "aglnx")
        if self.attn_cut < 2:
            return

        for kc in range(8):
            self.DMA(self.ab_("aw", 512, kc * 512),
                     self.dram["Wq"].ap()[kc * 128:(kc + 1) * 128, :],
                     [], [("aw", "q", kc)], tag=P + f"wql{kc}", eng="pool")
            self.DMA(self.ab_("aw", 512, 4096 + kc * 512),
                     self.dram["Wk"].ap()[kc * 128:(kc + 1) * 128, :],
                     [], [("aw", "k", kc)], tag=P + f"wkl{kc}", eng="pool")
            self.DMA(self.ab_("awf", 64, kc * 64),
                     self.dram["Wf1"].ap()[kc * 128:(kc + 1) * 128, :],
                     [], [("aw", "f1", kc)], tag=P + f"wf1l{kc}", eng="pool")
            self.DMA(self.ab_("awf", 8, 1024 + kc * 8),
                     self.dram["Wgam"].ap()[kc * 128:(kc + 1) * 128, :],
                     [], [("aw", "g", kc)], tag=P + f"wgl{kc}", eng="pool")
        self.DMA(self.sb["arb"][0:64, ATB["awf"] + 512: ATB["awf"] + 1024],
                 self.dram["Wf2"].ap()[:, :], [], [("aw", "f2")],
                 tag=P + "wf2l", eng="pool")

        self.MEMSET(self.ab_("s", 256), 0.0,
                    [("arb", "s", h) for h in range(8)], tag=P + "s0")

        if self.attn_cut < 3:
            return
        for g in range(min(NG, self.attn_cut - 2)):
            self.emit_attn_group(g)
        if self.attn_cut < 11:
            return

        self.CC("AllGather", ALU.bypass, [[0, 1], [2, 3], [4, 5], [6, 7]],
                [self.dram["o_spill"].ap()], [self.dram["ag_o_out"].ap()],
                [("osp", g, pt) for g in range(NG) for pt in range(4)],
                [("agoout",)], tag=P + "ago")

        if self.attn_cut < 12:
            return
        S.barrier()
        self.emit_attn_output()
        S.barrier()



def _prep_core_inputs(c, inputs, const_arrays):
    b, hg = c // 2, c % 2
    x = inputs["L0_attn_x"]
    xslice = np.asarray(x[b, hg * TOK:(hg + 1) * TOK, :], dtype=np.float32)
    mycols = slice(hg * 512, hg * 512 + 512)
    m = {
        "x_fm": np.ascontiguousarray(xslice.T),
        "wg_nat": np.ascontiguousarray(inputs["L0_ffn_w_gate"], dtype=np.float32),
        "wu_nat": np.ascontiguousarray(inputs["L0_ffn_w_up"], dtype=np.float32),
        "wd_nat": np.ascontiguousarray(inputs["L0_ffn_w_down"], dtype=np.float32),
        "clip_x": np.asarray(inputs["L0_ffn_clip_x"], dtype=np.float32).reshape(1, 1),
        "clip_down": np.asarray(inputs["L0_ffn_clip_down"], dtype=np.float32).reshape(1, 1),
        "Wq": np.ascontiguousarray(np.asarray(inputs["L0_attn_Wq"])[:, mycols], dtype=np.float32),
        "Wk": np.ascontiguousarray(np.asarray(inputs["L0_attn_Wk"])[:, mycols], dtype=np.float32),
        "Wv": np.ascontiguousarray(np.asarray(inputs["L0_attn_Wv"])[:, mycols], dtype=np.float32),
        "Wgam": np.ascontiguousarray(np.asarray(inputs["L0_attn_Wgamma"])[:, hg * NH:(hg + 1) * NH], dtype=np.float32),
        "Wf1": np.ascontiguousarray(inputs["L0_attn_Wf1"], dtype=np.float32),
        "Wf2": np.ascontiguousarray(np.asarray(inputs["L0_attn_Wf2"])[:, mycols], dtype=np.float32),
        "Wog1": np.ascontiguousarray(inputs["L0_attn_Wog1"], dtype=np.float32),
        "Wog2": np.ascontiguousarray(inputs["L0_attn_Wog2"], dtype=np.float32),
        "Wo": np.ascontiguousarray(inputs["L0_attn_Wo"], dtype=np.float32),
        "norm_w": np.ascontiguousarray(np.asarray(inputs["L0_attn_norm_w"], dtype=np.float32).reshape(8, 128).T),
        "hgmask": np.full((128, 1), float(hg), dtype=np.float32),
    }
    m.update(const_arrays)
    return m


_BUILD_CACHE = {}


def _get_built(phases, dumps=(), attn_cut=99, gcut=99):
    key = (tuple(phases), tuple(n for n, _, _ in dumps), attn_cut, gcut)
    if key not in _BUILD_CACHE:
        b = Builder(phases=phases, dumps=dumps, attn_cut=attn_cut, gcut=gcut)
        b.build()
        with (b.ctx, b.nc.Block() as block):
            sems = {}
            es = ExitStack()
            for e in ENGINES:
                sems[e] = es.enter_context(b.nc.semaphore("s_" + e))
            dsems = [es.enter_context(b.nc.semaphore(f"dp{i}"))
                     for i in range(64)]
            csems = [es.enter_context(b.nc.semaphore(f"cc{i}"))
                     for i in range(8)]
            b.S.emit(block, sems, dsems, csems)
        _BUILD_CACHE[key] = b
    return _BUILD_CACHE[key]


def kernel(**inputs):
    b = _get_built(("ffn1", "attn", "ffn2"))
    in_maps = [_prep_core_inputs(c, inputs, b.const_arrays) for c in range(8)]
    res = run_bass_kernel_spmd(b.nc, in_maps, core_ids=list(range(8)))
    out = np.empty((4, TFULL, D), dtype=np.float32)
    for c in range(8):
        bb, hg = c // 2, c % 2
        out[bb, hg * TOK:(hg + 1) * TOK, :] = res.results[c]["out_fm"].T
    return out


# version 11 (final)
# speedup vs baseline: 1.1016x; 1.0593x over previous
import numpy as np
import ml_dtypes
from contextlib import ExitStack

import concourse.bass as bass
import concourse.mybir as mybir
from concourse.bass_utils import run_bass_kernel_spmd

F32 = mybir.dt.float32
BF16 = mybir.dt.bfloat16
I8 = mybir.dt.int8
AF = mybir.ActivationFunctionType
ALU = mybir.AluOpType
AX = mybir.AxisListType.X

D = 1024
TOK = 2048
TFULL = 4096
NH = 8
HD = 64
II = 2816
NIC = II // 128
CH = 64
G = 512
NG = TFULL // G
QMAX = 127.0

ENGINES = ("te", "dve", "act", "pool", "sp")
SELF_FENCE = True


class Sched:

    def __init__(self):
        self.steps = []
        self.eng_steps = {e: [] for e in ENGINES}
        self.writers = {}
        self.readers = {}

    def step(self, eng, fn, reads=(), writes=(), dma=False, tag=""):
        idx = len(self.steps)
        deps = set()
        for k in reads:
            w = self.writers.get(k)
            if w is not None:
                deps.add(w)
            self.readers.setdefault(k, []).append(idx)
        for k in writes:
            w = self.writers.get(k)
            if w is not None:
                deps.add(w)
            for r in self.readers.get(k, ()):
                deps.add(r)
            self.readers[k] = []
            self.writers[k] = idx
        deps.discard(idx)
        st = dict(idx=idx, eng=eng, fn=fn, deps=deps, dma=dma, tag=tag,
                  acc=False)
        self.steps.append(st)
        self.eng_steps[eng].append(st)
        return idx

    _bar_n = 0

    def barrier(self):
        n = Sched._bar_n = Sched._bar_n + 1
        for e in ENGINES:
            self.step(e, lambda en: en.nop(), reads=(),
                      writes=[("__barA__", n, e)], tag=f"barA{n}")
        for e in ENGINES:
            self.step(e, lambda en: en.nop(),
                      reads=[("__barA__", n, x) for x in ENGINES],
                      writes=[("__barB__", n, e)], tag=f"barB{n}")

    def emit(self, block, sems, dma_sems, cc_sems):
        lastmm = {}
        for st in self.steps:
            bank = st.get("mm_bank")
            if bank is None:
                continue
            prev = lastmm.get(bank)
            if prev is not None and prev["mm_rg"] != st["mm_rg"]:
                st["deps"].add(prev["idx"])
                st.setdefault("rgdeps", set()).add(prev["idx"])
            lastmm[bank] = st
        NP = len(dma_sems)
        ncc = 0
        for st in self.steps:
            st["sig"] = False
            if st["acc"]:
                st["cc_sem"] = ncc
                ncc += 1
        assert ncc <= len(cc_sems)
        for e in ENGINES:
            for i, st in enumerate(self.eng_steps[e]):
                st["epos"] = i
        for st in self.steps:
            rgd = st.get("rgdeps", ())
            best = {}
            for d in st["deps"]:
                dep = self.steps[d]
                if dep["dma"] or dep["acc"]:
                    continue
                if dep["eng"] == st["eng"] == "te" and d not in rgd:
                    continue
                b = best.get(dep["eng"])
                if b is None or dep["epos"] > b["epos"]:
                    best[dep["eng"]] = dep
            st["best"] = best
            for dep in best.values():
                dep["sig"] = True
        use = [0] * NP
        nsw = NP * 3 // 4
        ndma = {"sp": 0, "act": 0, "pool": 0, "te": 0, "dve": 0}
        for st in self.steps:
            if st["dma"]:
                if st["eng"] == "pool":
                    p = nsw + ndma["pool"] % (NP - nsw)
                else:
                    p = ndma[st["eng"]] % nsw
                ndma[st["eng"]] += 1
                st["dma_sem"] = p
                st["dma_fence"] = use[p] * 16 if use[p] else None
                use[p] += 1
                st["dma_val"] = use[p] * 16
                st["sig_val"] = None
        for e in ENGINES:
            n = 0
            for st in self.eng_steps[e]:
                if st["dma"] or st["acc"]:
                    st["sig_val"] = None
                    continue
                st["sig_val"] = (n := n + 1) if st["sig"] else None
        for e in ENGINES:
            know = {x: 0 for x in ENGINES}
            know_d = [0] * NP
            for st in self.eng_steps[e]:
                waits, dwaits, cwaits = {}, {}, {}
                for d in sorted(st["deps"]):
                    dep = self.steps[d]
                    if dep["dma"]:
                        p = dep["dma_sem"]
                        if dep["dma_val"] > know_d[p]:
                            dwaits[p] = max(dwaits.get(p, 0), dep["dma_val"])
                    elif dep["acc"]:
                        cwaits[dep["cc_sem"]] = 1
                for de, dep in st["best"].items():
                    if dep["sig_val"] > know[de]:
                        waits[de] = max(waits.get(de, 0), dep["sig_val"])
                st["waits"] = waits
                st["dwaits"] = dwaits
                st["cwaits"] = cwaits
                for de, v in waits.items():
                    know[de] = max(know[de], v)
                for p, v in dwaits.items():
                    know_d[p] = max(know_d[p], v)
                for d in st["deps"]:
                    dep = self.steps[d]
                    if dep["eng"] == e or dep["dma"]:
                        continue
                    for de, v in dep.get("waits", {}).items():
                        if de != e:
                            know[de] = max(know[de], v)
                    for p, v in dep.get("dwaits", {}).items():
                        know_d[p] = max(know_d[p], v)

        def emit_engine(e, eng):
            for st in self.eng_steps[e]:
                if (SELF_FENCE and st["sig"] and st["sig_val"]
                        and st["sig_val"] > 1):
                    eng.wait_ge(sems[e], st["sig_val"] - 1)
                for de, v in sorted(st["waits"].items()):
                    eng.wait_ge(sems[de], v)
                for p, v in sorted(st["dwaits"].items()):
                    eng.wait_ge(dma_sems[p], v)
                for p, v in sorted(st.get("cwaits", {}).items()):
                    eng.wait_ge(cc_sems[p], v)
                if st["dma"] and st["dma_fence"] is not None:
                    eng.wait_ge(dma_sems[st["dma_sem"]], st["dma_fence"])
                inst = st["fn"](eng)
                if st["dma"]:
                    assert inst is not None, st["tag"]
                    inst.then_inc(dma_sems[st["dma_sem"]], 16)
                elif st["acc"]:
                    assert inst is not None, st["tag"]
                    inst.then_inc(cc_sems[st["cc_sem"]], 1)
                elif st["sig"]:
                    if inst is None:
                        inst = eng.nop()
                    inst.then_inc(sems[e], 1)

        @block.tensor
        def _(eng):
            emit_engine("te", eng)

        @block.vector
        def _(eng):
            emit_engine("dve", eng)

        @block.scalar
        def _(eng):
            emit_engine("act", eng)

        @block.gpsimd
        def _(eng):
            emit_engine("pool", eng)

        @block.sync
        def _(eng):
            emit_engine("sp", eng)


def _bind(fn, *a):
    return lambda eng: fn(eng, *a)


ARF_COLS = 11264
ARF = dict(mu=0, var=512, rstd=1024, mu2=1536, lnt0=2048, lnt1=2560,
           sg0=3072, sg1=3584, gg0=4096, gg1=4608, gld0=5120, gld1=5632,
           natb0=6144, natb1=7168, dnat=8192)
ARB_COLS = 44800
ARB = dict(xq=0,
           ws=16384,
           qb0=20480, qb1=21504,
           dwl0=22528, dwl1=23040,
           hq=23552,
           xb0=34816, xb1=35328, x20=35840, x21=36352,
           free=36864)
ATB = dict(aw=0,
           awf=8192,
           wv=9344,
           lng=13440,
           q=17536,
           k=19584,
           kt=21632,
           ab=23680,
           qb=25728,
           gr=27776,
           pw=29824,
           kht=32896,
           u=34944,
           uc=35456,
           g8=35584,
           s=36096,
           og=36352,
           pw2=38400,
           pw3=41472,
           uc2=44544,
           end=44800)
OTB = dict(wo=0,
           ot0=8192, ot1=12288,
           lnxo0=16384,
           lnxo1=20480,
           ogt=24576,
           oln=25088)
ATF = dict(zf=0,
           P=2048,
           Pinv=4096,
           vg=6144,
           tmp=10240)
OTF = dict(ov=0,
           mu=4096, var=4608, rstd=5120, mu2=5632)


class Builder:
    def __init__(self, phases=("ffn1", "attn", "ffn2"), dumps=(), attn_cut=99,
                 gcut=99):
        self.attn_cut = attn_cut
        self.gcut = gcut
        self.phases = phases
        self.dumps = list(dumps)
        self.nc = bass.Bass("TRN2", target_bir_lowering=False, debug=False)
        self.S = Sched()
        self.ctx = ExitStack()
        self.sb = {}
        self.ps = {}
        self.dram = {}
        self.const_arrays = {}

    def sbuf(self, name, shape, dtype):
        t = self.ctx.enter_context(self.nc.sbuf_tensor(name, shape, dtype))
        self.sb[name] = t
        return t

    def psum(self, name, shape, dtype):
        t = self.ctx.enter_context(self.nc.psum_tensor(name, shape, dtype))
        self.ps[name] = t
        return t

    def din(self, name, shape, dtype):
        t = self.nc.dram_tensor(name, shape, dtype, kind="ExternalInput")
        self.dram[name] = t
        return t

    def dout(self, name, shape, dtype):
        t = self.nc.dram_tensor(name, shape, dtype, kind="ExternalOutput")
        self.dram[name] = t
        return t

    def dtmp(self, name, shape, dtype):
        t = self.nc.dram_tensor(name, shape, dtype)
        self.dram[name] = t
        return t

    def const(self, name, arr):
        if name not in self.const_arrays:
            dt = {np.dtype(np.float32): F32,
                  np.dtype(ml_dtypes.bfloat16): BF16}[arr.dtype]
            self.din("c_" + name, list(arr.shape), dt)
            self.const_arrays["c_" + name] = np.ascontiguousarray(arr)
            self.sbuf("cs_" + name, list(arr.shape), dt)
            self.DMA(self.sb["cs_" + name][:], self.dram["c_" + name].ap(),
                     [], [("c", name)], tag="ldc_" + name)
        return self.sb["cs_" + name]

    def rf(self, reg, width=512, off=0):
        c = ARF[reg] + off
        return self.sb["arf"][:, c:c + width]

    def rb(self, reg, width, off=0):
        c = ARB[reg] + off
        return self.sb["arb"][:, c:c + width]

    def TT(self, out, a, b, op, reads, writes, tag=""):
        self.S.step("dve", _bind(
            lambda e, out, a, b: e.tensor_tensor(out=out, in0=a, in1=b, op=op),
            out, a, b), reads, writes, tag=tag)

    def TS(self, out, a, s1, s2, op0, op1, reads, writes, tag="", eng="dve"):
        if s2 is None:
            self.S.step(eng, _bind(
                lambda e, out, a, s1: e.tensor_scalar(
                    out=out, in0=a, scalar1=s1, scalar2=None, op0=op0),
                out, a, s1), reads, writes, tag=tag)
        else:
            self.S.step(eng, _bind(
                lambda e, out, a, s1, s2: e.tensor_scalar(
                    out=out, in0=a, scalar1=s1, scalar2=s2, op0=op0, op1=op1),
                out, a, s1, s2), reads, writes, tag=tag)

    def STT(self, out, a, s, b, op0, op1, reads, writes, tag="", eng="dve"):
        self.S.step(eng, _bind(
            lambda e, out, a, s, b: e.scalar_tensor_tensor(
                out=out, in0=a, scalar=s, in1=b, op0=op0, op1=op1),
            out, a, s, b), reads, writes, tag=tag)

    def COPY(self, out, a, reads, writes, tag="", eng="dve"):
        self.S.step(eng, _bind(
            lambda e, out, a: e.tensor_copy(out=out, in_=a), out, a),
            reads, writes, tag=tag)

    def MEMSET(self, ap, v, writes, tag="", eng="dve"):
        self.S.step(eng, _bind(lambda e, ap, v: e.memset(ap, v), ap, v),
                    (), writes, tag=tag)

    def RED(self, out, a, op, reads, writes, tag="", absval=False):
        self.S.step("dve", _bind(
            lambda e, out, a: e.tensor_reduce(
                out=out, in_=a, axis=AX, op=op, apply_absolute_value=absval),
            out, a), reads, writes, tag=tag)

    def RECIP(self, out, a, reads, writes, tag=""):
        self.S.step("dve", _bind(
            lambda e, out, a: e.reciprocal(out=out, in_=a), out, a),
            reads, writes, tag=tag)

    def SCAN(self, out, d0, d1, init, op0, op1, reads, writes, tag=""):
        self.S.step("dve", _bind(
            lambda e, out, d0, d1: e.tensor_tensor_scan(
                out=out, data0=d0, data1=d1, initial=init, op0=op0, op1=op1),
            out, d0, d1), reads, writes, tag=tag)

    def ACTV(self, out, a, func, reads, writes, tag="", bias=0.0, scale=1.0):
        self.S.step("act", _bind(
            lambda e, out, a, bias, scale: e.activation(
                out=out, in_=a, func=func, bias=bias, scale=scale),
            out, a, bias, scale), reads, writes, tag=tag)

    def MM(self, out, lhsT, rhs, start, stop, reads, writes, tag="", tp=None):
        i = self.S.step("te", _bind(
            lambda e, out, lhsT, rhs: e.matmul(
                out, lhsT, rhs, start=start, stop=stop, tile_position=tp),
            out, lhsT, rhs), reads, writes, tag=tag)
        st = self.S.steps[i]
        st["mm_bank"] = out.tensor.name
        st["mm_rg"] = (tp[0] if tp else 0, lhsT.shape[0])

    def TPO(self, out, a, reads, writes, tag="", f32=False):
        nm = "identf" if f32 else "ident"
        ident = self.sb["cs_" + nm][:, :]
        i = self.S.step("te", _bind(
            lambda e, out, a, ident: e.transpose(out, a, ident),
            out, a, ident), list(reads) + [("c", nm)], writes, tag=tag)
        st = self.S.steps[i]
        st["mm_bank"] = out.tensor.name
        st["mm_rg"] = (0, a.shape[0])

    def DMA(self, out, a, reads, writes, tag="", eng="sp"):
        self.S.step(eng, _bind(
            lambda e, out, a: e.dma_start(out=out, in_=a), out, a),
            reads, writes, dma=True, tag=tag)

    def CC(self, kind, op, groups, ins, outs, reads, writes, tag=""):
        i = self.S.step("pool", _bind(
            lambda e, ins, outs: e.collective_compute(
                kind, op, replica_groups=groups, ins=ins, outs=outs),
            ins, outs), reads, writes, tag=tag)
        self.S.steps[i]["acc"] = True

    def build(self):
        self.din("x_fm", [D, TOK], F32)
        self.din("wg_nat", [II, D], F32)
        self.din("wu_nat", [II, D], F32)
        self.din("wd_nat", [D, II], F32)
        self.din("clip_x", [1, 1], F32)
        self.din("clip_down", [1, 1], F32)
        self.din("Wq", [D, D // 2], F32)
        self.din("Wk", [D, D // 2], F32)
        self.din("Wv", [D, D // 2], F32)
        self.din("Wgam", [D, NH], F32)
        self.din("Wf1", [D, HD], F32)
        self.din("Wf2", [HD, D // 2], F32)
        self.din("Wog1", [D, HD], F32)
        self.din("Wog2", [HD, D], F32)
        self.din("Wo", [D, D], F32)
        self.din("norm_w", [128, 8], F32)
        self.din("hgmask", [128, 1], F32)
        self.sbuf("hgm", [128, 1], F32)
        self.dout("out_fm", [D, TOK], F32)

        self.dtmp("ln_spill", [D, TOK], F32)
        self.dtmp("g_spill", [II, TOK], F32)
        self.dtmp("x1_spill", [D, TOK], F32)
        self.dtmp("wq_spill", [3 * NIC, 128, 1024], BF16)
        self.dtmp("o_spill", [D // 2, TFULL], BF16)
        self.dtmp("ar_in", [1, 128], F32)
        self.dtmp("ar_out", [1, 128], F32)
        self.dtmp("ag_lnx_in", [D, TOK], BF16)
        self.dtmp("ag_lnx_out", [2 * D, TOK], BF16)
        self.dtmp("ag_o_out", [D, TFULL], BF16)

        self.sbuf("xs", [128, 8 * TOK], F32)
        self.sbuf("arf", [128, ARF_COLS], F32)
        self.sbuf("arb", [128, ARB_COLS], BF16)
        self.sbuf("stats", [128, 256], F32)
        self.sbuf("small", [128, 64], F32)
        self.sbuf("srow", [1, 512], F32)

        for i in range(6):
            self.psum(f"pb{i}", [128, 512], F32)
        self.psum("ptb", [128, 1024], BF16)
        self.psum("pb7", [128, 512], F32)

        self.const("ones128", np.ones((128, 128), dtype=ml_dtypes.bfloat16))
        bo = np.zeros((128, 128), dtype=np.float32)
        bo[0:64, 0:64] = 1.0
        bo[64:128, 64:128] = 1.0
        self.const("blockones", bo.astype(ml_dtypes.bfloat16))
        sel = np.zeros((8, 512), dtype=np.float32)
        for h in range(8):
            for pcol in range(512):
                if h == 2 * (pcol // 128) + (pcol % 128) // 64:
                    sel[h, pcol] = 1.0
        self.const("sel8", sel.astype(ml_dtypes.bfloat16))
        pp, ff = np.meshgrid(np.arange(128), np.arange(512), indexing="ij")
        self.const("mstril", ((ff % 64) < (pp % 64)).astype(ml_dtypes.bfloat16))
        self.const("mstriu", ((ff % 64) > (pp % 64)).astype(ml_dtypes.bfloat16))
        self.const("mtriuincl", ((ff % 64) >= (pp % 64)).astype(ml_dtypes.bfloat16))
        self.const("identblk", ((ff % 64) == (pp % 64)).astype(ml_dtypes.bfloat16))
        self.const("zero64", np.zeros((128, 64), dtype=np.float32))
        self.const("ones1f", np.ones((1, 128), dtype=np.float32))
        self.const("ident", np.eye(128, dtype=np.float32).astype(ml_dtypes.bfloat16))
        self.const("identf", np.eye(128, dtype=np.float32))
        self.const("eps", np.full((128, 1), 1e-5, dtype=np.float32))

        xs = self.sb["xs"]
        xin = self.dram["x_fm"].ap()
        self.DMA(self.sb["hgm"][:], self.dram["hgmask"].ap(), [], [("hgm",)],
                 tag="ldhgm")
        for f in range(8):
            self.DMA(xs[:, f * TOK:(f + 1) * TOK], xin[f * 128:(f + 1) * 128, :],
                     [], [("x", f)], tag=f"ldx{f}")

        if "ffn1" in self.phases:
            self.emit_ffn(0)
        if "attn" in self.phases:
            self.emit_attn()
        if "ffn2" in self.phases:
            self.emit_ffn(1)

        outap = self.dram["out_fm"].ap()
        for f in range(8):
            self.DMA(outap[f * 128:(f + 1) * 128, :], xs[:, f * TOK:(f + 1) * TOK],
                     [("x", f)], [("ost", f)], tag=f"stx{f}")

        for name, src, reads in self.dumps:
            t = self.dout("dump_" + name, [src.shape[0], src.shape[1]], src.dtype)
            self.DMA(t.ap(), src, list(reads), [("dump", name)],
                     tag="dump_" + name)
        self.S.barrier()
        self.S.step("sp", lambda e: e.nop(),
                    reads=[("ost", f) for f in range(8)] +
                          [("dump", n) for n, _, _ in self.dumps],
                    tag="final_wait")
        return self

    def _scales_from_max(self, P, clip_ap):
        sm = self.sb["small"]
        self.TPO(self.ps["pb7"][0:1, 0:128], sm[:, 8:9], [("sm", 8)],
                 [("psum", 7)], tag=P + "mxT", f32=True)
        self.COPY(self.sb["srow"][0:1, 0:128], self.ps["pb7"][0:1, 0:128],
                  [("psum", 7)], [("srow",), ("psum", 7)], tag=P + "mxc")
        self.RED(sm[0:1, 10:11], self.sb["srow"][0:1, 0:128], ALU.max,
                 [("srow",)], [("sm", 10)], tag=P + "mxr")
        self.TS(self.sb["srow"][0:1, 128:256], self.sb["cs_ones1f"][0:1, :],
                sm[0:1, 10:11], None, ALU.mult, None,
                [("sm", 10), ("c", "ones1f")], [("srow2",)], tag=P + "mxbc")
        self.DMA(self.dram["ar_in"].ap()[0:1, :], self.sb["srow"][0:1, 128:256],
                 [("srow2",)], [("ar",)], tag=P + "arst")
        self.CC("AllReduce", ALU.max, [list(range(8))],
                [self.dram["ar_in"].ap()], [self.dram["ar_out"].ap()],
                [("ar",)], [("ar",)], tag=P + "ar")
        self.DMA(sm[0:1, 11:12], self.dram["ar_out"].ap()[0:1, 0:1],
                 [("ar",)], [("sm", 11)], tag=P + "arld")
        self.DMA(sm[0:1, 12:13], clip_ap, [], [("sm", 12)], tag=P + "clipld")
        self.ACTV(sm[0:1, 13:14], sm[0:1, 12:13], AF.Tanh,
                  [("sm", 12)], [("sm", 13)], tag=P + "tanh", scale=0.5)
        self.TS(sm[0:1, 13:14], sm[0:1, 13:14], 0.5, 0.5, ALU.mult, ALU.add,
                [("sm", 13)], [("sm", 13)], tag=P + "sig")
        self.TT(sm[0:1, 14:15], sm[0:1, 13:14], sm[0:1, 11:12], ALU.mult,
                [("sm", 13), ("sm", 11)], [("sm", 14)], tag=P + "sM")
        self.TS(sm[0:1, 15:16], sm[0:1, 14:15], 1.0 / QMAX, None, ALU.mult, None,
                [("sm", 14)], [("sm", 15)], tag=P + "s")
        self.RECIP(sm[0:1, 16:17], sm[0:1, 14:15], [("sm", 14)], [("sm", 16)],
                   tag=P + "rinv")
        self.TS(sm[0:1, 17:18], sm[0:1, 16:17], QMAX, None, ALU.mult, None,
                [("sm", 16)], [("sm", 17)], tag=P + "r")
        ones1f = self.sb["cs_ones1f"]
        self.MM(self.ps["pb7"][:, 0:1], ones1f[0:1, :], sm[0:1, 17:18],
                True, True, [("sm", 17), ("c", "ones1f"), ("psum", 7)],
                [("psum", 7)], tag=P + "bcr")
        self.COPY(sm[:, 0:1], self.ps["pb7"][:, 0:1], [("psum", 7)],
                  [("sm", 0), ("psum", 7)], tag=P + "bcrc")
        self.MM(self.ps["pb7"][:, 0:1], ones1f[0:1, :], sm[0:1, 15:16],
                True, True, [("sm", 15), ("c", "ones1f"), ("psum", 7)],
                [("psum", 7)], tag=P + "bcs")
        self.COPY(sm[:, 5:6], self.ps["pb7"][:, 0:1], [("psum", 7)],
                  [("sm", 5), ("psum", 7)], tag=P + "bcsc")

    def quant_round(self, src, rcol, rk, out, srck, outk, tag, fbuf=None,
                    fkey=None):
        if fbuf is None:
            fbuf = self.rf("var", src.shape[1])
            fkey = ("arf", "var")
        MAGIC = 12582912.0
        self.TS(src, src, rcol, MAGIC, ALU.mult, ALU.add, [srck, rk], [srck],
                tag=tag + "a")
        self.TS(src, src, -MAGIC, None, ALU.add, None, [srck], [srck],
                tag=tag + "b")
        self.TS(out, src, 127.0, -128.0, ALU.min, ALU.max, [srck], [outk],
                tag=tag + "d")

    def emit_ffn(self, kk):
        xs = self.sb["xs"]
        arb = self.sb["arb"]
        stats = self.sb["stats"]
        sm = self.sb["small"]
        pb = [self.ps[f"pb{i}"] for i in range(6)]
        ptb = self.ps["ptb"]
        ones128 = self.sb["cs_ones128"]
        P = f"f{kk}"
        wsp = self.dram["wq_spill"].ap()

        if kk == 0:
            for widx, nm in ((0, "wg_nat"), (1, "wu_nat")):
                nat = self.dram[nm].ap()
                for ic in range(NIC):
                    par = ic % 2
                    natb = self.rf("natb0" if par == 0 else "natb1", 1024)
                    natk = ("arf", "natb", par)
                    self.DMA(natb, nat[ic * 128:(ic + 1) * 128, :],
                             [natk], [natk], tag=f"natld{widx}_{ic}")
                    self.RED(sm[:, 20:21], natb, ALU.max, [natk], [("sm", 20)],
                             tag=f"wmax{widx}_{ic}", absval=True)
                    self.TS(stats[:, widx * 32 + ic: widx * 32 + ic + 1],
                            sm[:, 20:21], 1.0 / QMAX, None, ALU.mult, None,
                            [("sm", 20)], [("stats", widx * 32 + ic)],
                            tag=f"wsc{widx}_{ic}")
                    self.RECIP(sm[:, 21:22], sm[:, 20:21], [("sm", 20)],
                               [("sm", 21)], tag=f"wrc{widx}_{ic}")
                    self.TS(sm[:, 22:23], sm[:, 21:22], QMAX, None, ALU.mult,
                            None, [("sm", 21)], [("sm", 22)],
                            tag=f"wr2{widx}_{ic}")
                    qb = self.rb("qb0" if par == 0 else "qb1", 1024)
                    qbk = ("arb", "qb", par)
                    self.quant_round(natb, sm[:, 22:23], ("sm", 22), qb,
                                     natk, qbk, f"wq{widx}_{ic}",
                                     fbuf=self.rf("dnat", 1024),
                                     fkey=("arf", "dnat"))
                    for f in range(8):
                        self.TPO(ptb[:, f * 128:(f + 1) * 128],
                                 qb[:, f * 128:(f + 1) * 128],
                                 [qbk], [("ptb", f)], tag=f"wt{widx}_{ic}_{f}")
                    stt = self.rb("ws", 1024, par * 2048)
                    sttk = ("arb", "ws", par, 0)
                    self.COPY(stt, ptb[:, :], [("ptb", f) for f in range(8)],
                              [sttk] + [("ptb", f) for f in range(8)],
                              tag=f"wte{widx}_{ic}")
                    self.DMA(wsp[widx * NIC + ic, :, :], stt, [sttk],
                             [("wq", widx, ic)], tag=f"wsp{widx}_{ic}",
                             eng="act")
            nat = self.dram["wd_nat"].ap()
            for oc in range(8):
                natb = self.rf("dnat", 2816)
                natk = ("arf", "dnat")
                self.DMA(natb, nat[oc * 128:(oc + 1) * 128, :], [natk], [natk],
                         tag=f"dnatld{oc}", eng="act")
                self.RED(sm[:, 20:21], natb, ALU.max, [natk], [("sm", 20)],
                         tag=f"dmax{oc}", absval=True)
                self.TS(stats[:, 64 + oc: 64 + oc + 1], sm[:, 20:21],
                        1.0 / QMAX, None, ALU.mult, None, [("sm", 20)],
                        [("stats", 64 + oc)], tag=f"dsc{oc}")
                self.RECIP(sm[:, 21:22], sm[:, 20:21], [("sm", 20)],
                           [("sm", 21)], tag=f"drc{oc}")
                self.TS(sm[:, 22:23], sm[:, 21:22], QMAX, None, ALU.mult, None,
                        [("sm", 21)], [("sm", 22)], tag=f"dr2{oc}")
                for part, w in ((0, 1024), (1024, 1024), (2048, 768)):
                    qb = self.rb("qb0", 1024)
                    qbk = ("arb", "qb", 0)
                    self.quant_round(natb[:, part:part + w], sm[:, 22:23],
                                     ("sm", 22), qb[:, 0:w], natk, qbk,
                                     f"dq{oc}_{part}",
                                     fbuf=self.rf("natb0", w),
                                     fkey=("arf", "natb", 0))
                    for j in range(w // 128):
                        ic = (part + j * 128) // 128
                        self.TPO(ptb[:, j * 128:(j + 1) * 128],
                                 qb[:, j * 128:(j + 1) * 128],
                                 [qbk], [("ptb", j)], tag=f"dt{oc}_{ic}")
                    stt = self.rb("ws", 1024, 0)
                    sttk = ("arb", "ws", 0, 0)
                    self.COPY(stt[:, 0:w], ptb[:, 0:w],
                              [("ptb", j) for j in range(w // 128)],
                              [sttk] + [("ptb", j) for j in range(w // 128)],
                              tag=f"dte{oc}_{part}")
                    for j in range(w // 128):
                        ic = (part + j * 128) // 128
                        self.DMA(wsp[2 * NIC + ic, :, oc * 128:(oc + 1) * 128],
                                 stt[:, j * 128:(j + 1) * 128], [sttk],
                                 [("wqd", ic, oc)], tag=f"dsp{oc}_{ic}",
                                 eng="act")

        self.MEMSET(sm[:, 8:9], 0.0, [("sm", 8)], tag=P + "mx0")
        for tt in range(4):
            for f in range(8):
                xsl = xs[:, f * TOK + tt * 512: f * TOK + tt * 512 + 512]
                xb = self.rb("xb0" if f % 2 == 0 else "xb1", 512)
                x2 = self.rb("x20" if f % 2 == 0 else "x21", 512)
                xbk = ("arb", "xb", f % 2)
                x2k = ("arb", "x2", f % 2)
                self.COPY(xb, xsl, [("x", f)], [xbk], tag=P + f"cast{tt}_{f}")
                self.TT(x2, xb, xb, ALU.mult, [xbk], [x2k], tag=P + f"sq{tt}_{f}")
                self.MM(pb[0][:, :], ones128[:], xb, f == 0, f == 7,
                        [xbk, ("c", "ones128"), ("psum", 0)], [("psum", 0)],
                        tag=P + f"sums{tt}_{f}")
                self.MM(pb[1][:, :], ones128[:], x2, f == 0, f == 7,
                        [x2k, ("c", "ones128"), ("psum", 1)], [("psum", 1)],
                        tag=P + f"sumsq{tt}_{f}")
            mu, var, rstd, mu2 = (self.rf(r) for r in ("mu", "var", "rstd", "mu2"))
            self.TS(mu, pb[0][:, :], 1.0 / D, None, ALU.mult, None,
                    [("psum", 0)], [("arf", "mu"), ("psum", 0)], tag=P + f"mu{tt}")
            self.TS(var, pb[1][:, :], 1.0 / D, None, ALU.mult, None,
                    [("psum", 1)], [("arf", "var"), ("psum", 1)], tag=P + f"msq{tt}")
            self.TT(mu2, mu, mu, ALU.mult, [("arf", "mu")], [("arf", "mu2")],
                    tag=P + f"mu2{tt}")
            self.TT(var, var, mu2, ALU.subtract,
                    [("arf", "var"), ("arf", "mu2")], [("arf", "var")],
                    tag=P + f"var{tt}")
            self.ACTV(rstd, var, AF.Sqrt, [("arf", "var"), ("c", "eps")],
                      [("arf", "rstd")], tag=P + f"sd{tt}",
                      bias=self.sb["cs_eps"][:, 0:1])
            self.RECIP(rstd, rstd, [("arf", "rstd")], [("arf", "rstd")],
                       tag=P + f"rstd{tt}")
            for f in range(8):
                xsl = xs[:, f * TOK + tt * 512: f * TOK + tt * 512 + 512]
                lnt = self.rf("lnt0" if f % 2 == 0 else "lnt1")
                lk = ("arf", "lnt", f % 2)
                self.TT(lnt, xsl, mu, ALU.subtract, [("x", f), ("arf", "mu")],
                        [lk], tag=P + f"lsub{tt}_{f}")
                self.TT(lnt, lnt, rstd, ALU.mult, [lk, ("arf", "rstd")], [lk],
                        tag=P + f"lmul{tt}_{f}")
                self.DMA(self.dram["ln_spill"].ap()[f * 128:(f + 1) * 128,
                                                    tt * 512:(tt + 1) * 512],
                         lnt, [lk], [("lnsp", tt, f)], tag=P + f"lnsp{tt}_{f}")
                self.RED(sm[:, 9:10], lnt, ALU.max, [lk], [("sm", 9)],
                         tag=P + f"lmax{tt}_{f}", absval=True)
                self.TT(sm[:, 8:9], sm[:, 8:9], sm[:, 9:10], ALU.max,
                        [("sm", 9), ("sm", 8)], [("sm", 8)],
                        tag=P + f"mxa{tt}_{f}")

        self._scales_from_max(P, self.dram["clip_x"].ap())

        for f in range(8):
            for tt in range(4):
                lnt = self.rf("lnt0" if (f + tt) % 2 == 0 else "lnt1")
                lk = ("arf", "lnt", (f + tt) % 2)
                self.DMA(lnt, self.dram["ln_spill"].ap()[f * 128:(f + 1) * 128,
                                                         tt * 512:(tt + 1) * 512],
                         [("lnsp", tt, f), lk], [lk], tag=P + f"lnld{tt}_{f}")
                self.quant_round(lnt, sm[:, 0:1], ("sm", 0),
                                 arb[:, f * TOK + tt * 512: f * TOK + tt * 512 + 512],
                                 lk, ("xq", f, tt), P + f"q{tt}_{f}")

        for widx in range(2):
            for ic in range(NIC):
                self.TT(stats[:, 128 + widx * 32 + ic: 128 + widx * 32 + ic + 1],
                        stats[:, widx * 32 + ic: widx * 32 + ic + 1],
                        sm[:, 5:6], ALU.mult,
                        [("stats", widx * 32 + ic), ("sm", 5)],
                        [("stats", 128 + widx * 32 + ic)],
                        tag=P + f"esc{widx}_{ic}")

        self.MEMSET(sm[:, 8:9], 0.0, [("sm", 8)], tag=P + "gmx0")
        for ic in range(NIC):
            par = ic % 2
            gst = self.rb("ws", 1024, par * 2048)
            ust = self.rb("ws", 1024, par * 2048 + 1024)
            gstk = ("arb", "ws", par, 0)
            ustk = ("arb", "ws", par, 1)
            self.DMA(gst, wsp[ic, :, :], [gstk, ("wq", 0, ic)], [gstk],
                     tag=P + f"gwld{ic}", eng="act")
            self.DMA(ust, wsp[NIC + ic, :, :], [ustk, ("wq", 1, ic)], [ustk],
                     tag=P + f"uwld{ic}", eng="act")
            for tt in range(4):
                bg = 2 + (tt % 2)
                bu = 4 + (tt % 2)
                for f in range(8):
                    self.MM(pb[bg][:, :], gst[:, f * 128:(f + 1) * 128],
                            arb[:, f * TOK + tt * 512: f * TOK + tt * 512 + 512],
                            f == 0, f == 7,
                            [("xq", f, tt), gstk, ("psum", bg)], [("psum", bg)],
                            tag=P + f"gm{ic}_{tt}_{f}")
                pbu = pb[bu] if bu < 6 else self.ps["pb7"]
                for f in range(8):
                    self.MM(pbu[:, :], ust[:, f * 128:(f + 1) * 128],
                            arb[:, f * TOK + tt * 512: f * TOK + tt * 512 + 512],
                            f == 0, f == 7,
                            [("xq", f, tt), ustk, ("psum", bu)], [("psum", bu)],
                            tag=P + f"um{ic}_{tt}_{f}")
                pbu = pb[bu] if bu < 6 else self.ps["pb7"]
                sg = self.rf("sg0" if tt % 2 == 0 else "sg1")
                gg = self.rf("gg0" if tt % 2 == 0 else "gg1")
                sgk = ("arf", "sg", tt % 2)
                ggk = ("arf", "gg", tt % 2)
                self.ACTV(sg, pb[bg][:, :], AF.Silu,
                          [("psum", bg), ("stats", 128 + ic)],
                          [sgk, ("psum", bg)], tag=P + f"sge{ic}_{tt}",
                          scale=stats[:, 128 + ic: 128 + ic + 1])
                self.STT(gg, pbu[:, :], stats[:, 160 + ic:160 + ic + 1], sg,
                         ALU.mult, ALU.mult,
                         [("psum", bu), ("stats", 160 + ic), sgk],
                         [ggk, ("psum", bu)], tag=P + f"ge{ic}_{tt}")
                self.DMA(self.dram["g_spill"].ap()[ic * 128:(ic + 1) * 128,
                                                   tt * 512:(tt + 1) * 512],
                         gg, [ggk], [("gsp", ic, tt)], tag=P + f"gsp{ic}_{tt}",
                         eng="act")
                self.RED(sm[:, 9:10], gg, ALU.max, [ggk], [("sm", 9)],
                         tag=P + f"gmx{ic}_{tt}", absval=True)
                self.TT(sm[:, 8:9], sm[:, 8:9], sm[:, 9:10], ALU.max,
                        [("sm", 9), ("sm", 8)], [("sm", 8)],
                        tag=P + f"gma{ic}_{tt}")

        self._scales_from_max(P + "h", self.dram["clip_down"].ap())

        for oc in range(8):
            self.TT(stats[:, 192 + oc:192 + oc + 1],
                    stats[:, 64 + oc:64 + oc + 1], sm[:, 5:6], ALU.mult,
                    [("stats", 64 + oc), ("sm", 5)], [("stats", 192 + oc)],
                    tag=P + f"descA{oc}")
            self.TS(stats[:, 192 + oc:192 + oc + 1],
                    stats[:, 192 + oc:192 + oc + 1], 0.5, None, ALU.mult, None,
                    [("stats", 192 + oc)], [("stats", 192 + oc)],
                    tag=P + f"descB{oc}")

        for tt in range(4):
            for ic in range(NIC):
                par = ic % 2
                gg = self.rf("gld0" if par == 0 else "gld1")
                ggk = ("arf", "gld", par)
                self.DMA(gg, self.dram["g_spill"].ap()[ic * 128:(ic + 1) * 128,
                                                       tt * 512:(tt + 1) * 512],
                         [ggk, ("gsp", ic, tt)], [ggk], tag=P + f"gld{tt}_{ic}",
                         eng="sp")
                self.quant_round(gg, sm[:, 0:1], ("sm", 0),
                                 self.rb("hq", 512, ic * 512),
                                 ggk, ("hq", ic), P + f"hq{tt}_{ic}")
            for ph in range(2):
                bks = [0, 1, 2, 3] if ph == 0 else [4, 5, 7, 2]
                for ic in range(NIC):
                    par = ic % 2
                    dwl = self.rb("dwl0" if par == 0 else "dwl1", 512)
                    dwlk = ("arb", "dwl", par)
                    self.DMA(dwl, wsp[2 * NIC + ic, :, ph * 512:(ph + 1) * 512],
                             [dwlk] + [("wqd", ic, ph * 4 + j) for j in range(4)],
                             [dwlk], tag=P + f"dwld{tt}_{ph}_{ic}", eng="act")
                    for j in range(4):
                        bk = bks[j]
                        pbj = pb[bk] if bk < 6 else self.ps["pb7"]
                        self.MM(pbj[:, :], dwl[:, j * 128:(j + 1) * 128],
                                self.rb("hq", 512, ic * 512),
                                ic == 0, ic == NIC - 1,
                                [("hq", ic), dwlk, ("psum", bk)], [("psum", bk)],
                                tag=P + f"dm{tt}_{ph}_{ic}_{j}")
                for j in range(4):
                    bk = bks[j]
                    pbj = pb[bk] if bk < 6 else self.ps["pb7"]
                    oc = ph * 4 + j
                    xsl = xs[:, oc * TOK + tt * 512: oc * TOK + tt * 512 + 512]
                    self.STT(xsl, pbj[:, :], stats[:, 192 + oc:192 + oc + 1],
                             xsl, ALU.mult, ALU.add,
                             [("psum", bk), ("stats", 192 + oc), ("x", oc)],
                             [("x", oc), ("psum", bk)], tag=P + f"dres{tt}_{oc}")



    def emit_attn_group(self, g):
        P = f"ag{g}"
        pb = [self.ps[f"pb{i}"] for i in range(6)]
        pb7 = self.ps["pb7"]
        ptb = self.ps["ptb"]
        lng = self.ab_("lng", 4096)
        lngk = ("arb", "lng")
        hgb = g // 4

        self.DMA(lng.rearrange("p (a m) -> p a m", a=8),
                 self.dram["ag_lnx_out"].ap()[hgb * D:(hgb + 1) * D,
                                              (g % 4) * 512:(g % 4) * 512 + 512]
                 .rearrange("(a p) m -> p a m", p=128),
                 [lngk, ("aglnx",)], [lngk], tag=P + "lngld")

        for which, kbase in (("q", 0), ("k", 4096)):
            for oc in range(4):
                bnk = 2 + (oc % 2)
                for kc in range(8):
                    self.MM(pb[bnk][:, :],
                            self.ab_("aw", 128, kbase + kc * 512 + oc * 128),
                            lng[:, kc * 512:(kc + 1) * 512],
                            kc == 0, kc == 7,
                            [lngk, ("aw", which, kc), ("psum", bnk)],
                            [("psum", bnk)], tag=P + f"{which}m{oc}_{kc}")
                t = self.af_("tmp", 512)
                tk = ("arf", "tmp")
                self.ACTV(t, pb[bnk][:, :], AF.Tanh,
                          [("psum", bnk)], [tk], tag=P + f"{which}th{oc}",
                          scale=0.5)
                self.TS(t, t, 0.5, 0.5, ALU.mult, ALU.add, [tk], [tk],
                        tag=P + f"{which}sg{oc}")
                dst = self.ab_(which, 512, oc * 512)
                self.TT(dst, t, pb[bnk][:, :], ALU.mult,
                        [tk, ("psum", bnk)],
                        [("arb", which, oc), ("psum", bnk)],
                        tag=P + f"{which}sl{oc}")
        if self.gcut < 2:
            return
        bo = self.sb["cs_blockones"]
        for oc in range(4):
            k2 = self.af_("tmp", 512)
            tk = ("arf", "tmp")
            kt = self.ab_("k", 512, oc * 512)
            k2b = self.rb("xb0", 512)
            self.TT(k2b, kt, kt, ALU.mult, [("arb", "k", oc)],
                    [("arb", "xb", 0)], tag=P + f"k2_{oc}")
            self.MM(pb7[:, :], bo[:], k2b, True, True,
                    [("arb", "xb", 0), ("c", "blockones"), ("psum", 7)],
                    [("psum", 7)], tag=P + f"k2m{oc}")
            self.ACTV(k2, pb7[:, :], AF.Sqrt, [("psum", 7)],
                      [tk, ("psum", 7)], tag=P + f"knrm{oc}")
            self.TS(k2, k2, 1e-12, None, ALU.max, None, [tk], [tk],
                    tag=P + f"knmx{oc}")
            self.RECIP(k2, k2, [tk], [tk], tag=P + f"knr{oc}")
            self.TT(kt, kt, k2, ALU.mult, [("arb", "k", oc), tk],
                    [("arb", "k", oc)], tag=P + f"knd{oc}")

        if self.gcut < 3:
            return
        for kc in range(8):
            self.DMA(self.ab_("wv", 512, kc * 512),
                     self.dram["Wv"].ap()[kc * 128:(kc + 1) * 128, :],
                     [("arb", "wv", kc)], [("arb", "wv", kc)],
                     tag=P + f"wvl{kc}", eng="pool")
        for c in range(8):
            bnk = 4 + (c % 2)
            for kc in range(8):
                self.MM(pb[bnk][0:64, :],
                        lng[:, kc * 512 + c * 64: kc * 512 + c * 64 + 64],
                        self.ab_("wv", 512, kc * 512), kc == 0, kc == 7,
                        [lngk, ("arb", "wv", kc), ("psum", bnk)],
                        [("psum", bnk)], tag=P + f"vm{c}_{kc}")
            self.COPY(self.af_("vg", 512, c * 512)[0:64, :], pb[bnk][0:64, :],
                      [("psum", bnk)], [("arf", "vg", c), ("psum", bnk)],
                      tag=P + f"ve{c}")
        self.DMA(self.sb["arf"][64:128, ATF["vg"]:ATF["vg"] + 4096],
                 self.sb["arf"][0:64, ATF["vg"]:ATF["vg"] + 4096],
                 [("arf", "vg", c) for c in range(8)] + [("arf", "vgd")],
                 [("arf", "vgd")], tag=P + "vdup")

        if self.gcut < 4:
            return
        for kc in range(8):
            self.MM(pb7[0:64, :], self.ab_("awf", 64, kc * 64),
                    lng[:, kc * 512:(kc + 1) * 512], kc == 0, kc == 7,
                    [lngk, ("aw", "f1", kc), ("psum", 7)], [("psum", 7)],
                    tag=P + f"f1m{kc}")
        xf1 = self.sb["arb"][0:64, ATB["u"]:ATB["u"] + 512]
        self.COPY(xf1, pb7[0:64, :], [("psum", 7)],
                  [("arb", "xf1"), ("psum", 7)], tag=P + "f1e")
        for oc in range(4):
            bnk = 2 + (oc % 2)
            self.MM(pb[bnk][:, :],
                    self.sb["arb"][0:64, ATB["awf"] + 512 + oc * 128:
                                   ATB["awf"] + 512 + (oc + 1) * 128],
                    xf1, True, True,
                    [("arb", "xf1"), ("aw", "f2"), ("psum", bnk)],
                    [("psum", bnk)], tag=P + f"f2m{oc}")
            lam = self.af_("zf", 512, oc * 512)
            self.ACTV(lam, pb[bnk][:, :], AF.Tanh, [("psum", bnk)],
                      [("arf", "zf", oc), ("psum", bnk)], tag=P + f"lth{oc}",
                      scale=0.5)
            self.TS(lam, lam, 0.5, 0.5, ALU.mult, ALU.add,
                    [("arf", "zf", oc)], [("arf", "zf", oc)], tag=P + f"lsg{oc}")

        for kc in range(8):
            self.MM(pb7[0:8, :], self.ab_("awf", 8, 1024 + kc * 8),
                    lng[:, kc * 512:(kc + 1) * 512], kc == 0, kc == 7,
                    [lngk, ("aw", "g", kc), ("psum", 7)], [("psum", 7)],
                    tag=P + f"gm{kc}")
        gamt = self.sb["arf"][0:8, ATF["tmp"]:ATF["tmp"] + 512]
        self.ACTV(gamt, pb7[0:8, :], AF.Tanh, [("psum", 7)],
                  [("arf", "tmp"), ("psum", 7)], tag=P + "gth", scale=0.5)
        self.TS(gamt, gamt, -0.5, -0.5, ALU.mult, ALU.add, [("arf", "tmp")],
                [("arf", "tmp")], tag=P + "gsg")
        gamb = self.sb["arb"][0:8, ATB["g8"]:ATB["g8"] + 512]
        self.COPY(gamb, gamt, [("arf", "tmp")], [("arb", "g8")], tag=P + "gbc")
        sel = self.sb["cs_sel8"]
        for pt in range(4):
            self.MM(pb7[:, :], sel[:, pt * 128:(pt + 1) * 128], gamb,
                    True, True,
                    [("arb", "g8"), ("c", "sel8"), ("psum", 7)],
                    [("psum", 7)], tag=P + f"grm{pt}")
            self.COPY(self.ab_("gr", 512, pt * 512), pb7[:, :], [("psum", 7)],
                      [("arb", "gr", pt), ("psum", 7)], tag=P + f"gre{pt}")

        if self.gcut < 5:
            return
        z64 = self.sb["cs_zero64"]
        for pt in range(4):
            for c in range(8):
                off = ATF["P"] + pt * 512 + c * 64
                zoff = ATF["zf"] + pt * 512 + c * 64
                self.SCAN(self.sb["arf"][:, off:off + 64],
                          self.sb["arf"][:, zoff:zoff + 64],
                          z64[:, 0:64], 1.0, ALU.mult, ALU.add,
                          [("arf", "zf", pt), ("c", "zero64")],
                          [("arf", "P", pt)], tag=P + f"scan{pt}_{c}")
            self.RECIP(self.af_("Pinv", 512, pt * 512),
                       self.af_("P", 512, pt * 512),
                       [("arf", "P", pt)], [("arf", "Pinv", pt)],
                       tag=P + f"pinv{pt}")

        if self.gcut < 6:
            return
        for pt in range(4):
            q = self.ab_("q", 512, pt * 512)
            k = self.ab_("k", 512, pt * 512)
            Pt = self.af_("P", 512, pt * 512)
            Pi = self.af_("Pinv", 512, pt * 512)
            lam = self.af_("zf", 512, pt * 512)
            t = self.af_("tmp", 512)
            tk = ("arf", "tmp")
            self.TT(self.ab_("qb", 512, pt * 512), q, Pt, ALU.mult,
                    [("arb", "q", pt), ("arf", "P", pt)], [("arb", "qb", pt)],
                    tag=P + f"qb{pt}")
            self.TT(self.ab_("kt", 512, pt * 512), k, Pi, ALU.mult,
                    [("arb", "k", pt), ("arf", "Pinv", pt)], [("arb", "kt", pt)],
                    tag=P + f"ktb{pt}")
            self.TT(t, lam, Pt, ALU.mult,
                    [("arf", "zf", pt), ("arf", "P", pt)], [tk],
                    tag=P + f"lp{pt}")
            self.TT(t, t, self.ab_("gr", 512, pt * 512), ALU.mult,
                    [tk, ("arb", "gr", pt)], [tk], tag=P + f"lpg{pt}")
            self.TT(self.ab_("ab", 512, pt * 512), k, t, ALU.mult,
                    [("arb", "k", pt), tk], [("arb", "ab", pt)],
                    tag=P + f"abt{pt}")
            for c in range(8):
                self.TS(self.ab_("k", 64, pt * 512 + c * 64),
                        self.ab_("kt", 64, pt * 512 + c * 64),
                        self.sb["arf"][:, ATF["P"] + pt * 512 + c * 64 + 63:
                                       ATF["P"] + pt * 512 + c * 64 + 64],
                        None, ALU.mult, None,
                        [("arb", "kt", pt), ("arf", "P", pt)],
                        [("arb", "k", pt)], tag=P + f"khf{pt}_{c}")

        if self.gcut < 7:
            return
        for pt in range(4):
            for cp in range(4):
                self.TPO(ptb[:, cp * 128:(cp + 1) * 128],
                         self.ab_("k", 128, pt * 512 + cp * 128),
                         [("arb", "k", pt)], [("ptb", cp)],
                         tag=P + f"khT{pt}_{cp}")
            self.COPY(self.ab_("kht", 512, pt * 512), ptb[:, 0:512],
                      [("ptb", cp) for cp in range(4)],
                      [("arb", "kht", pt)] + [("ptb", cp) for cp in range(4)],
                      tag=P + f"khE{pt}")

        if self.gcut < 8:
            return
        self.run_pairs(g)

        for pt in range(4):
            self.DMA(self.dram["o_spill"].ap()[pt * 128:(pt + 1) * 128,
                                               g * 512:(g + 1) * 512],
                     self.ab_("og", 512, pt * 512),
                     [("arb", "og", pt, c) for c in range(8)],
                     [("osp", g, pt)], tag=P + f"osp{pt}")

    PW = dict(T=0, W=512, sT=1024, sW=1536, N=2048, AT=2560)

    def pwsl(self, slot, j, c, pp):
        base = ATB[("pw", "pw2", "pw3")[pp]] + self.PW[slot] + 256 * j + 64 * (c // 2)
        r = 64 * (c % 2)
        return self.sb["arb"][r:r + 64, base:base + 64]

    def pwt(self, slot, pp):
        return self.ab_(("pw", "pw2", "pw3")[pp], 512, self.PW[slot])

    def _gsl(self, reg, h, c):
        pt = h // 2
        base = ATB[reg] + pt * 512 + c * 64
        r = 64 * (h % 2)
        return self.sb["arb"][r:r + 64, base:base + 64]

    def _vsl(self, h, c):
        base = ATF["vg"] + c * 512 + h * 64
        r = 64 * (c % 2)
        return self.sb["arf"][r:r + 64, base:base + 64]

    def _khsl(self, h, c):
        base = ATB["kht"] + (h // 2) * 512 + (c // 2) * 128 + 64 * (h % 2)
        r = 64 * (c % 2)
        return self.sb["arf"] if False else self.sb["arb"][r:r + 64, base:base + 64]

    def _ssl(self, h):
        base = ATB["s"] + 64 * (h // 2)
        r = 64 * (h % 2)
        return self.sb["arb"][r:r + 64, base:base + 64]

    def gen_precompute(self, g, hp):
        P = f"ag{g}p{hp}"
        pp = hp % 3
        pb = [self.ps[f"pb{i}"] for i in range(6)]
        pb7 = self.ps["pb7"]
        if hp % 2 == 0:
            bt, bw, kt_, kw_ = pb[2], pb[3], 2, 3
        else:
            bt, bw, kt_, kw_ = pb[1], pb7, 1, 7
        kkey = ("arb", "pw", pp)
        msl = self.sb["cs_mstril"]
        msu = self.sb["cs_mstriu"]
        mti = self.sb["cs_mtriuincl"]
        idb = self.sb["cs_identblk"]
        heads = (2 * hp, 2 * hp + 1)
        gsl = self._gsl
        arb = self.sb["arb"]

        def osl(j, c, bank):
            return bank[64 * (c % 2):64 * (c % 2) + 64,
                        256 * j + 64 * (c // 2):256 * j + 64 * (c // 2) + 64]

        def pwh(slot, r0, c0, n):
            base = ATB[("pw", "pw2", "pw3")[pp]] + self.PW[slot] + c0
            return arb[r0:r0 + 64, base:base + n]

        for j, h in enumerate(heads):
            for c in range(8):
                tp = (64 * (h % 2), 64 * (c % 2))
                self.MM(osl(j, c, bt), gsl("ab", h, c), gsl("kt", h, c),
                        True, True,
                        [("arb", "ab", h // 2), ("arb", "kt", h // 2),
                         ("psum", kt_)], [("psum", kt_)],
                        tag=P + f"Tb{j}_{c}", tp=tp)
                self.MM(osl(j, c, bw), gsl("kt", h, c), gsl("ab", h, c),
                        True, True,
                        [("arb", "ab", h // 2), ("arb", "kt", h // 2),
                         ("psum", kw_)], [("psum", kw_)],
                        tag=P + f"Wb{j}_{c}", tp=tp)
                if c % 4 == 3:
                    yield
        self.TT(self.pwt("T", pp), bt[:, :], msl[:], ALU.mult,
                [("psum", kt_), ("c", "mstril")],
                [kkey + ("T",), ("psum", kt_)], tag=P + "Te")
        self.TT(self.pwt("W", pp), bw[:, :], msu[:], ALU.mult,
                [("psum", kw_), ("c", "mstriu")],
                [kkey + ("W",), ("psum", kw_)], tag=P + "We")
        yield
        for j, h in enumerate(heads):
            for c in range(8):
                tp = (64 * (h % 2), 64 * (c % 2))
                self.MM(osl(j, c, bt), gsl("kt", h, c), gsl("qb", h, c),
                        True, True,
                        [("arb", "qb", h // 2), ("arb", "kt", h // 2),
                         ("psum", kt_)], [("psum", kt_)],
                        tag=P + f"ATb{j}_{c}", tp=tp)
                if c % 4 == 3:
                    yield
        self.TT(self.pwt("AT", pp), bt[:, :], mti[:], ALU.mult,
                [("psum", kt_), ("c", "mtriuincl")],
                [kkey + ("AT",), ("psum", kt_)], tag=P + "ATe")
        self.TT(self.pwt("N", pp), self.pwt("W", pp), idb[:], ALU.add,
                [kkey + ("W",), ("c", "identblk")], [kkey + ("N",)],
                tag=P + "Ni")
        yield

        cur_t, cur_w, nxt_t, nxt_w = "T", "W", "sT", "sW"
        corder = [0, 2, 4, 6, 1, 3, 5, 7]
        for lvl in range(4):
            for j, h in enumerate(heads):
                for c in corder:
                    tpp = (64 * (c % 2), 64 * (c % 2))
                    self.MM(osl(j, c, bt), self.pwsl(cur_w, j, c, pp),
                            self.pwsl(cur_t, j, c, pp), True, True,
                            [kkey + (cur_w,), kkey + (cur_t,),
                             ("psum", kt_)], [("psum", kt_)],
                            tag=P + f"sq{lvl}t{j}_{c}", tp=tpp)
                    self.MM(osl(j, c, bw), self.pwsl(cur_t, j, c, pp),
                            self.pwsl(cur_w, j, c, pp), True, True,
                            [kkey + (cur_w,), kkey + (cur_t,),
                             ("psum", kw_)], [("psum", kw_)],
                            tag=P + f"sq{lvl}w{j}_{c}", tp=tpp)
                    if c in (6, 7):
                        yield
            self.ACTV(self.pwt(nxt_t, pp), bt[:, :], AF.Copy,
                      [("psum", kt_)], [kkey + (nxt_t,), ("psum", kt_)],
                      tag=P + f"sqe{lvl}t")
            self.ACTV(self.pwt(nxt_w, pp), bw[:, :], AF.Copy,
                      [("psum", kw_)], [kkey + (nxt_w,), ("psum", kw_)],
                      tag=P + f"sqe{lvl}w")
            yield
            for j, h in enumerate(heads):
                for c in corder:
                    tpp = (64 * (c % 2), 64 * (c % 2))
                    self.MM(osl(j, c, bt), self.pwsl(nxt_t, j, c, pp),
                            self.pwsl("N", j, c, pp), True, True,
                            [kkey + (nxt_t,), kkey + ("N",),
                             ("psum", kt_)], [("psum", kt_)],
                            tag=P + f"nu{lvl}_{j}_{c}", tp=tpp)
                    if c in (6, 7):
                        yield
            self.TT(self.pwt("N", pp), bt[:, :], self.pwt("N", pp), ALU.add,
                    [("psum", kt_), kkey + ("N",)],
                    [kkey + ("N",), ("psum", kt_)], tag=P + f"nue{lvl}")
            yield
            cur_t, cur_w, nxt_t, nxt_w = nxt_t, nxt_w, cur_t, cur_w
        for j, h in enumerate(heads):
            for c in corder:
                tpp = (64 * (c % 2), 64 * (c % 2))
                self.MM(osl(j, c, bt), self.pwsl(cur_w, j, c, pp),
                        self.pwsl(cur_t, j, c, pp), True, True,
                        [kkey + (cur_w,), kkey + (cur_t,),
                         ("psum", kt_)], [("psum", kt_)],
                        tag=P + f"sq32t{j}_{c}", tp=tpp)
                if c in (6, 7):
                    yield
        self.ACTV(self.pwt(nxt_t, pp), bt[:, :], AF.Copy, [("psum", kt_)],
                  [kkey + (nxt_t,), ("psum", kt_)], tag=P + "sqe32t")
        yield
        for j, h in enumerate(heads):
            for c in corder:
                tpp = (64 * (c % 2), 64 * (c % 2))
                self.MM(osl(j, c, bt), self.pwsl(nxt_t, j, c, pp),
                        self.pwsl("N", j, c, pp), True, True,
                        [kkey + (nxt_t,), kkey + ("N",),
                         ("psum", kt_)], [("psum", kt_)],
                        tag=P + f"nu32_{j}_{c}", tp=tpp)
                if c in (6, 7):
                    yield
        self.TT(self.pwt("N", pp), bt[:, :], self.pwt("N", pp), ALU.add,
                [("psum", kt_), kkey + ("N",)], [kkey + ("N",), ("psum", kt_)],
                tag=P + "nue32")
        yield

    def gen_chain(self, g, hp):
        P = f"ag{g}c{hp}"
        pp = hp % 3
        pb = [self.ps[f"pb{i}"] for i in range(6)]
        kkey = ("arb", "pw", pp)
        heads = (2 * hp, 2 * hp + 1)
        gsl, vsl, khsl, ssl = self._gsl, self._vsl, self._khsl, self._ssl
        arb = self.sb["arb"]
        UC2 = ATB["uc2"]
        ucR = [arb[:, UC2 + 128 * j:UC2 + 128 * j + 64] for j in (0, 1)]
        ucU = [arb[:, UC2 + 128 * j + 64:UC2 + 128 * j + 128] for j in (0, 1)]
        cb = [pb[0], pb[4]]
        ck = [0, 4]
        for c in range(8):
            cpar = 64 * (c % 2)
            for j, h in enumerate(heads):
                hpar = 64 * (h % 2)
                self.MM(pb[5][hpar:hpar + 64, 0:64], ssl(h), gsl("qb", h, c),
                        True, True,
                        [("arb", "s", h), ("arb", "qb", h // 2), ("psum", 5)],
                        [("psum", 5)], tag=P + f"cO1{c}_{j}", tp=(hpar, hpar))
            for j, h in enumerate(heads):
                hpar = 64 * (h % 2)
                self.MM(cb[j][cpar:cpar + 64, 0:64], gsl("ab", h, c), ssl(h),
                        True, True,
                        [("arb", "ab", h // 2), ("arb", "s", h),
                         ("psum", ck[j])], [("psum", ck[j])],
                        tag=P + f"cR{c}_{j}", tp=(hpar, cpar))
            for j, h in enumerate(heads):
                self.TT(ucR[j][cpar:cpar + 64, :],
                        cb[j][cpar:cpar + 64, 0:64], vsl(h, c), ALU.add,
                        [("psum", ck[j]), ("arf", "vgd")],
                        [("arb", "ucR", j), ("psum", ck[j])],
                        tag=P + f"cRe{c}_{j}")
            for j, h in enumerate(heads):
                self.MM(cb[j][cpar:cpar + 64, 64:128],
                        self.pwsl("N", j, c, pp), ucR[j][cpar:cpar + 64, :],
                        True, True,
                        [kkey + ("N",), ("arb", "ucR", j), ("psum", ck[j])],
                        [("psum", ck[j])], tag=P + f"cU{c}_{j}",
                        tp=(cpar, cpar))
            for j, h in enumerate(heads):
                self.ACTV(ucU[j][cpar:cpar + 64, :],
                          cb[j][cpar:cpar + 64, 64:128], AF.Copy,
                          [("psum", ck[j])],
                          [("arb", "ucU", j), ("psum", ck[j])],
                          tag=P + f"cUe{c}_{j}")
            for j, h in enumerate(heads):
                hpar = 64 * (h % 2)
                self.MM(pb[5][hpar:hpar + 64, 64:128],
                        ucU[j][cpar:cpar + 64, :],
                        self.pwsl("AT", j, c, pp), True, True,
                        [("arb", "ucU", j), kkey + ("AT",), ("psum", 5)],
                        [("psum", 5)], tag=P + f"cO2{c}_{j}", tp=(cpar, hpar))
                self.MM(cb[j][hpar:hpar + 64, 128:192], khsl(h, c),
                        ucU[j][cpar:cpar + 64, :], True, True,
                        [("arb", "kht", h // 2), ("arb", "ucU", j),
                         ("psum", ck[j])], [("psum", ck[j])],
                        tag=P + f"cK{c}_{j}", tp=(cpar, hpar))
            for j, h in enumerate(heads):
                hpar = 64 * (h % 2)
                pcol = self.sb["arf"][hpar:hpar + 64,
                                      ATF["P"] + (h // 2) * 512 + c * 64 + 63:
                                      ATF["P"] + (h // 2) * 512 + c * 64 + 64]
                self.STT(ssl(h), ssl(h), pcol,
                         cb[j][hpar:hpar + 64, 128:192],
                         ALU.mult, ALU.add,
                         [("arb", "s", h), ("arf", "P", h // 2),
                          ("psum", ck[j])],
                         [("arb", "s", h), ("psum", ck[j])],
                         tag=P + f"cS{c}_{j}")
                yield
            self.COPY(self.ab_("og", 64, hp * 512 + c * 64), pb[5][:, 0:64],
                      [("psum", 5)], [("arb", "og", hp, c)], tag=P + f"cOc{c}")
            self.TT(self.ab_("og", 64, hp * 512 + c * 64),
                    self.ab_("og", 64, hp * 512 + c * 64), pb[5][:, 64:128],
                    ALU.add, [("arb", "og", hp, c), ("psum", 5)],
                    [("arb", "og", hp, c), ("psum", 5)], tag=P + f"cOe{c}")
            yield

    def run_pairs(self, g):
        pre = {hp: self.gen_precompute(g, hp) for hp in range(4)}
        chains = {}
        pre_done, ch_done = set(), set()
        active, next_pre = [], 0
        while len(ch_done) < 4:
            while (next_pre < 4 and len(active) < 2
                   and (next_pre < 3 or (next_pre - 3) in ch_done)):
                active.append(next_pre)
                next_pre += 1
            achain = None
            for hp in range(4):
                if hp in pre_done and hp not in ch_done:
                    achain = hp
                    break
            if achain is not None:
                gen = chains.get(achain)
                if gen is None:
                    gen = chains[achain] = self.gen_chain(g, achain)
                try:
                    next(gen)
                except StopIteration:
                    ch_done.add(achain)
            for hp in list(active):
                try:
                    next(pre[hp])
                    next(pre[hp])
                except StopIteration:
                    pre_done.add(hp)
                    active.remove(hp)
            if achain is None and not active and next_pre >= 4:
                break

    def emit_attn_output(self):
        P = "ao"
        pb = [self.ps[f"pb{i}"] for i in range(6)]
        pb7 = self.ps["pb7"]
        xs = self.sb["xs"]
        ones128 = self.sb["cs_ones128"]
        hgsel = ("hgsel",)

        for kc in range(8):
            self.DMA(self.ob_("wo", 1024, kc * 1024),
                     self.dram["Wo"].ap()[kc * 128:(kc + 1) * 128, :],
                     [], [("wo", kc)], tag=P + f"wol{kc}", eng="pool")
            self.DMA(self.ob_("lnxo1", 64, kc * 64),
                     self.dram["Wog1"].ap()[kc * 128:(kc + 1) * 128, :],
                     [], [("wog1", kc)], tag=P + f"wog1l{kc}", eng="pool")
        self.DMA(self.sb["arb"][0:64, OTB["lnxo1"] + 512:OTB["lnxo1"] + 1536],
                 self.dram["Wog2"].ap()[:, :], [], [("wog2",)],
                 tag=P + "wog2l", eng="pool")
        self.DMA(self.sb["small"][:, 24:32], self.dram["norm_w"].ap(),
                 [("sm", 24)], [("sm", 24)], tag=P + "nwl")

        for tt in range(4):
            par = tt % 2
            ot = self.ob_("ot0" if par == 0 else "ot1", 4096)
            otk = ("arb", "ot", par)
            lnxo = self.ob_("lnxo0", 4096)
            lnk = ("arb", "lnxo")
            own0 = (tt * 512)
            oth = self.ob_("ot0" if par == 1 else "ot1", 4096)
            othk = ("arb", "ot", 1 - par)
            self.DMA(ot.rearrange("p (a m) -> p a m", a=8),
                     self.dram["ag_o_out"].ap()[:, own0:own0 + 512]
                     .rearrange("(a p) m -> p a m", p=128),
                     [otk, ("agoout",)], [otk], tag=P + f"otl{tt}")
            self.DMA(oth.rearrange("p (a m) -> p a m", a=8),
                     self.dram["ag_o_out"].ap()[:, 2048 + own0:2048 + own0 + 512]
                     .rearrange("(a p) m -> p a m", p=128),
                     [othk, ("agoout",)], [othk], tag=P + f"oth{tt}")
            self.TT(oth, oth, ot, ALU.subtract, [otk, othk], [othk],
                    tag=P + f"osub{tt}")
            self.STT(ot, oth, self.sb["hgm"][:, 0:1], ot, ALU.mult, ALU.add,
                     [othk, ("hgm",), otk], [otk], tag=P + f"oblend{tt}")
            self.DMA(lnxo.rearrange("p (a m) -> p a m", a=8),
                     self.dram["ag_lnx_in"].ap()[:, own0:own0 + 512]
                     .rearrange("(a p) m -> p a m", p=128),
                     [lnk] + [("agin", tt, f) for f in range(8)], [lnk],
                     tag=P + f"lnxl{tt}")
            for kc in range(8):
                self.MM(pb7[0:64, :], self.ob_("lnxo1", 64, kc * 64),
                        lnxo[:, kc * 512:(kc + 1) * 512], kc == 0, kc == 7,
                        [lnk, ("wog1", kc), ("psum", 7)], [("psum", 7)],
                        tag=P + f"og1m{tt}_{kc}")
            og1 = self.sb["arb"][0:64, OTB["ogt"]:OTB["ogt"] + 512]
            self.COPY(og1, pb7[0:64, :], [("psum", 7)],
                      [("arb", "ogt"), ("psum", 7)], tag=P + f"og1e{tt}")
            for oc in range(8):
                bnk = 2 + (oc % 2)
                self.MM(pb[bnk][:, :],
                        self.sb["arb"][0:64, OTB["lnxo1"] + 512 + oc * 128:
                                       OTB["lnxo1"] + 512 + (oc + 1) * 128],
                        og1, True, True,
                        [("arb", "ogt"), ("wog2",), ("psum", bnk)],
                        [("psum", bnk)], tag=P + f"og2m{tt}_{oc}")
                t = self.af_("tmp", 512) if False else self.of_("mu", 512)
                tk = ("arf", "mu")
                self.ACTV(t, pb[bnk][:, :], AF.Tanh, [("psum", bnk)], [tk],
                          tag=P + f"ogth{tt}_{oc}", scale=0.5)
                self.TS(t, t, 0.5, 0.5, ALU.mult, ALU.add, [tk], [tk],
                        tag=P + f"ogsg{tt}_{oc}")
                ov = self.of_("ov", 512, oc * 512)
                self.TT(ov, ot[:, oc * 512:(oc + 1) * 512], t, ALU.mult,
                        [otk, tk], [("arf", "ov", oc)], tag=P + f"ov{tt}_{oc}")
            for oc in range(8):
                ovb = self.rb("xb0" if oc % 2 == 0 else "xb1", 512)
                ovbk = ("arb", "xb", oc % 2)
                ov = self.of_("ov", 512, oc * 512)
                self.COPY(ovb, ov, [("arf", "ov", oc)], [ovbk],
                          tag=P + f"ovb{tt}_{oc}")
                x2 = self.rb("x20" if oc % 2 == 0 else "x21", 512)
                x2k = ("arb", "x2", oc % 2)
                self.TT(x2, ovb, ovb, ALU.mult, [ovbk], [x2k],
                        tag=P + f"ovsq{tt}_{oc}")
                self.MM(pb[0][:, :], ones128[:], ovb, oc == 0, oc == 7,
                        [ovbk, ("c", "ones128"), ("psum", 0)], [("psum", 0)],
                        tag=P + f"osum{tt}_{oc}")
                self.MM(pb[1][:, :], ones128[:], x2, oc == 0, oc == 7,
                        [x2k, ("c", "ones128"), ("psum", 1)], [("psum", 1)],
                        tag=P + f"osumsq{tt}_{oc}")
            mu = self.of_("mu", 512)
            var = self.of_("var", 512)
            rstd = self.of_("rstd", 512)
            mu2 = self.of_("mu2", 512)
            self.TS(mu, pb[0][:, :], 1.0 / D, None, ALU.mult, None,
                    [("psum", 0)], [("arf", "mu"), ("psum", 0)], tag=P + f"mu{tt}")
            self.TS(var, pb[1][:, :], 1.0 / D, None, ALU.mult, None,
                    [("psum", 1)], [("arf", "var"), ("psum", 1)],
                    tag=P + f"msq{tt}")
            self.TT(mu2, mu, mu, ALU.mult, [("arf", "mu")], [("arf", "mu2")],
                    tag=P + f"mu2{tt}")
            self.TT(var, var, mu2, ALU.subtract,
                    [("arf", "var"), ("arf", "mu2")], [("arf", "var")],
                    tag=P + f"var{tt}")
            self.ACTV(rstd, var, AF.Sqrt, [("arf", "var"), ("c", "eps")],
                      [("arf", "rstd")], tag=P + f"sd{tt}",
                      bias=self.sb["cs_eps"][:, 0:1])
            self.RECIP(rstd, rstd, [("arf", "rstd")], [("arf", "rstd")],
                       tag=P + f"rstd{tt}")
            for oc in range(8):
                ov = self.of_("ov", 512, oc * 512)
                self.TT(ov, ov, mu, ALU.subtract,
                        [("arf", "ov", oc), ("arf", "mu")], [("arf", "ov", oc)],
                        tag=P + f"on1{tt}_{oc}")
                self.TT(ov, ov, rstd, ALU.mult,
                        [("arf", "ov", oc), ("arf", "rstd")],
                        [("arf", "ov", oc)], tag=P + f"on2{tt}_{oc}")
                oln = self.ob_("oln", 512, oc * 512)
                self.ACTV(oln, ov, AF.Copy,
                          [("arf", "ov", oc), ("sm", 24)],
                          [("arb", "oln", oc)], tag=P + f"onw{tt}_{oc}",
                          scale=self.sb["small"][:, 24 + oc:24 + oc + 1])
            for oc2 in range(8):
                bnk = 2 + (oc2 % 2)
                for kc in range(8):
                    self.MM(pb[bnk][:, :],
                            self.ob_("wo", 128, kc * 1024 + oc2 * 128),
                            self.ob_("oln", 512, kc * 512), kc == 0, kc == 7,
                            [("arb", "oln", kc), ("wo", kc), ("psum", bnk)],
                            [("psum", bnk)], tag=P + f"wom{tt}_{oc2}_{kc}")
                xsl = xs[:, oc2 * TOK + tt * 512: oc2 * TOK + tt * 512 + 512]
                self.TT(xsl, pb[bnk][:, :], xsl, ALU.add,
                        [("psum", bnk), ("x", oc2)],
                        [("x", oc2), ("psum", bnk)], tag=P + f"ores{tt}_{oc2}")
    def ab_(self, reg, width, off=0):
        return self.sb["arb"][:, ATB[reg] + off: ATB[reg] + off + width]

    def ob_(self, reg, width, off=0):
        return self.sb["arb"][:, OTB[reg] + off: OTB[reg] + off + width]

    def af_(self, reg, width, off=0):
        return self.sb["arf"][:, ATF[reg] + off: ATF[reg] + off + width]

    def of_(self, reg, width, off=0):
        return self.sb["arf"][:, OTF[reg] + off: OTF[reg] + off + width]

    def emit_attn(self):
        S = self.S
        xs = self.sb["xs"]
        arb = self.sb["arb"]
        arf = self.sb["arf"]
        sm = self.sb["small"]
        pb = [self.ps[f"pb{i}"] for i in range(6)]
        pb7 = self.ps["pb7"]
        ptb = self.ps["ptb"]
        ones128 = self.sb["cs_ones128"]
        hg_tokens = TOK

        S.barrier()
        P = "at"

        for tt in range(4):
            for f in range(8):
                xsl = xs[:, f * TOK + tt * 512: f * TOK + tt * 512 + 512]
                xb = self.rb("xb0" if f % 2 == 0 else "xb1", 512)
                x2 = self.rb("x20" if f % 2 == 0 else "x21", 512)
                xbk = ("arb", "xb", f % 2)
                x2k = ("arb", "x2", f % 2)
                self.COPY(xb, xsl, [("x", f)], [xbk], tag=P + f"cast{tt}_{f}")
                self.TT(x2, xb, xb, ALU.mult, [xbk], [x2k], tag=P + f"sq{tt}_{f}")
                self.MM(pb[0][:, :], ones128[:], xb, f == 0, f == 7,
                        [xbk, ("c", "ones128"), ("psum", 0)], [("psum", 0)],
                        tag=P + f"sums{tt}_{f}")
                self.MM(pb[1][:, :], ones128[:], x2, f == 0, f == 7,
                        [x2k, ("c", "ones128"), ("psum", 1)], [("psum", 1)],
                        tag=P + f"sumsq{tt}_{f}")
            mu, var, rstd, mu2 = (self.rf(r) for r in ("mu", "var", "rstd", "mu2"))
            self.TS(mu, pb[0][:, :], 1.0 / D, None, ALU.mult, None,
                    [("psum", 0)], [("arf", "mu"), ("psum", 0)], tag=P + f"mu{tt}")
            self.TS(var, pb[1][:, :], 1.0 / D, None, ALU.mult, None,
                    [("psum", 1)], [("arf", "var"), ("psum", 1)], tag=P + f"msq{tt}")
            self.TT(mu2, mu, mu, ALU.mult, [("arf", "mu")], [("arf", "mu2")],
                    tag=P + f"mu2{tt}")
            self.TT(var, var, mu2, ALU.subtract,
                    [("arf", "var"), ("arf", "mu2")], [("arf", "var")],
                    tag=P + f"var{tt}")
            self.ACTV(rstd, var, AF.Sqrt, [("arf", "var"), ("c", "eps")],
                      [("arf", "rstd")], tag=P + f"sd{tt}",
                      bias=self.sb["cs_eps"][:, 0:1])
            self.RECIP(rstd, rstd, [("arf", "rstd")], [("arf", "rstd")],
                       tag=P + f"rstd{tt}")
            for f in range(8):
                xsl = xs[:, f * TOK + tt * 512: f * TOK + tt * 512 + 512]
                lnt = self.rf("lnt0" if f % 2 == 0 else "lnt1")
                lk = ("arf", "lnt", f % 2)
                self.TT(lnt, xsl, mu, ALU.subtract, [("x", f), ("arf", "mu")],
                        [lk], tag=P + f"lsub{tt}_{f}")
                self.TT(lnt, lnt, rstd, ALU.mult, [lk, ("arf", "rstd")], [lk],
                        tag=P + f"lmul{tt}_{f}")
                self.DMA(self.dram["ag_lnx_in"].ap()[f * 128:(f + 1) * 128,
                                                     tt * 512:(tt + 1) * 512],
                         lnt, [lk], [("agin", tt, f), lk],
                         tag=P + f"lnst{tt}_{f}", eng="pool")

        if self.attn_cut < 1:
            return
        self.CC("AllGather", ALU.bypass, [[0, 1], [2, 3], [4, 5], [6, 7]],
                [self.dram["ag_lnx_in"].ap()], [self.dram["ag_lnx_out"].ap()],
                [("agin", tt, f) for tt in range(4) for f in range(8)],
                [("aglnx",)], tag=P + "aglnx")
        if self.attn_cut < 2:
            return

        for kc in range(8):
            self.DMA(self.ab_("aw", 512, kc * 512),
                     self.dram["Wq"].ap()[kc * 128:(kc + 1) * 128, :],
                     [], [("aw", "q", kc)], tag=P + f"wql{kc}", eng="pool")
            self.DMA(self.ab_("aw", 512, 4096 + kc * 512),
                     self.dram["Wk"].ap()[kc * 128:(kc + 1) * 128, :],
                     [], [("aw", "k", kc)], tag=P + f"wkl{kc}", eng="pool")
            self.DMA(self.ab_("awf", 64, kc * 64),
                     self.dram["Wf1"].ap()[kc * 128:(kc + 1) * 128, :],
                     [], [("aw", "f1", kc)], tag=P + f"wf1l{kc}", eng="pool")
            self.DMA(self.ab_("awf", 8, 1024 + kc * 8),
                     self.dram["Wgam"].ap()[kc * 128:(kc + 1) * 128, :],
                     [], [("aw", "g", kc)], tag=P + f"wgl{kc}", eng="pool")
        self.DMA(self.sb["arb"][0:64, ATB["awf"] + 512: ATB["awf"] + 1024],
                 self.dram["Wf2"].ap()[:, :], [], [("aw", "f2")],
                 tag=P + "wf2l", eng="pool")

        self.MEMSET(self.ab_("s", 256), 0.0,
                    [("arb", "s", h) for h in range(8)], tag=P + "s0")

        if self.attn_cut < 3:
            return
        for g in range(min(NG, self.attn_cut - 2)):
            self.emit_attn_group(g)
        if self.attn_cut < 11:
            return

        self.CC("AllGather", ALU.bypass, [[0, 1], [2, 3], [4, 5], [6, 7]],
                [self.dram["o_spill"].ap()], [self.dram["ag_o_out"].ap()],
                [("osp", g, pt) for g in range(NG) for pt in range(4)],
                [("agoout",)], tag=P + "ago")

        if self.attn_cut < 12:
            return
        S.barrier()
        self.emit_attn_output()
        S.barrier()



def _prep_core_inputs(c, inputs, const_arrays):
    b, hg = c // 2, c % 2
    x = inputs["L0_attn_x"]
    xslice = np.asarray(x[b, hg * TOK:(hg + 1) * TOK, :], dtype=np.float32)
    mycols = slice(hg * 512, hg * 512 + 512)
    m = {
        "x_fm": np.ascontiguousarray(xslice.T),
        "wg_nat": np.ascontiguousarray(inputs["L0_ffn_w_gate"], dtype=np.float32),
        "wu_nat": np.ascontiguousarray(inputs["L0_ffn_w_up"], dtype=np.float32),
        "wd_nat": np.ascontiguousarray(inputs["L0_ffn_w_down"], dtype=np.float32),
        "clip_x": np.asarray(inputs["L0_ffn_clip_x"], dtype=np.float32).reshape(1, 1),
        "clip_down": np.asarray(inputs["L0_ffn_clip_down"], dtype=np.float32).reshape(1, 1),
        "Wq": np.ascontiguousarray(np.asarray(inputs["L0_attn_Wq"])[:, mycols], dtype=np.float32),
        "Wk": np.ascontiguousarray(np.asarray(inputs["L0_attn_Wk"])[:, mycols], dtype=np.float32),
        "Wv": np.ascontiguousarray(np.asarray(inputs["L0_attn_Wv"])[:, mycols], dtype=np.float32),
        "Wgam": np.ascontiguousarray(np.asarray(inputs["L0_attn_Wgamma"])[:, hg * NH:(hg + 1) * NH], dtype=np.float32),
        "Wf1": np.ascontiguousarray(inputs["L0_attn_Wf1"], dtype=np.float32),
        "Wf2": np.ascontiguousarray(np.asarray(inputs["L0_attn_Wf2"])[:, mycols], dtype=np.float32),
        "Wog1": np.ascontiguousarray(inputs["L0_attn_Wog1"], dtype=np.float32),
        "Wog2": np.ascontiguousarray(inputs["L0_attn_Wog2"], dtype=np.float32),
        "Wo": np.ascontiguousarray(inputs["L0_attn_Wo"], dtype=np.float32),
        "norm_w": np.ascontiguousarray(np.asarray(inputs["L0_attn_norm_w"], dtype=np.float32).reshape(8, 128).T),
        "hgmask": np.full((128, 1), float(hg), dtype=np.float32),
    }
    m.update(const_arrays)
    return m


_BUILD_CACHE = {}


def _get_built(phases, dumps=(), attn_cut=99, gcut=99):
    key = (tuple(phases), tuple(n for n, _, _ in dumps), attn_cut, gcut)
    if key not in _BUILD_CACHE:
        b = Builder(phases=phases, dumps=dumps, attn_cut=attn_cut, gcut=gcut)
        b.build()
        with (b.ctx, b.nc.Block() as block):
            sems = {}
            es = ExitStack()
            for e in ENGINES:
                sems[e] = es.enter_context(b.nc.semaphore("s_" + e))
            dsems = [es.enter_context(b.nc.semaphore(f"dp{i}"))
                     for i in range(64)]
            csems = [es.enter_context(b.nc.semaphore(f"cc{i}"))
                     for i in range(8)]
            b.S.emit(block, sems, dsems, csems)
        _BUILD_CACHE[key] = b
    return _BUILD_CACHE[key]


def kernel(**inputs):
    b = _get_built(("ffn1", "attn", "ffn2"))
    in_maps = [_prep_core_inputs(c, inputs, b.const_arrays) for c in range(8)]
    res = run_bass_kernel_spmd(b.nc, in_maps, core_ids=list(range(8)))
    out = np.empty((4, TFULL, D), dtype=np.float32)
    for c in range(8):
        bb, hg = c // 2, c % 2
        out[bb, hg * TOK:(hg + 1) * TOK, :] = res.results[c]["out_fm"].T
    return out


# version 12
# speedup vs baseline: 1.1079x; 1.0057x over previous
import numpy as np
import ml_dtypes
from contextlib import ExitStack

import concourse.bass as bass
import concourse.mybir as mybir
from concourse.bass_utils import run_bass_kernel_spmd

F32 = mybir.dt.float32
BF16 = mybir.dt.bfloat16
I8 = mybir.dt.int8
AF = mybir.ActivationFunctionType
ALU = mybir.AluOpType
AX = mybir.AxisListType.X

D = 1024
TOK = 2048
TFULL = 4096
NH = 8
HD = 64
II = 2816
NIC = II // 128
CH = 64
G = 512
NG = TFULL // G
QMAX = 127.0

ENGINES = ("te", "dve", "act", "pool", "sp")
SELF_FENCE = True


class Sched:

    def __init__(self):
        self.steps = []
        self.eng_steps = {e: [] for e in ENGINES}
        self.writers = {}
        self.readers = {}

    def step(self, eng, fn, reads=(), writes=(), dma=False, tag=""):
        idx = len(self.steps)
        deps = set()
        for k in reads:
            w = self.writers.get(k)
            if w is not None:
                deps.add(w)
            self.readers.setdefault(k, []).append(idx)
        for k in writes:
            w = self.writers.get(k)
            if w is not None:
                deps.add(w)
            for r in self.readers.get(k, ()):
                deps.add(r)
            self.readers[k] = []
            self.writers[k] = idx
        deps.discard(idx)
        st = dict(idx=idx, eng=eng, fn=fn, deps=deps, dma=dma, tag=tag,
                  acc=False)
        self.steps.append(st)
        self.eng_steps[eng].append(st)
        return idx

    _bar_n = 0

    def barrier(self):
        n = Sched._bar_n = Sched._bar_n + 1
        for e in ENGINES:
            self.step(e, lambda en: en.nop(), reads=(),
                      writes=[("__barA__", n, e)], tag=f"barA{n}")
        for e in ENGINES:
            self.step(e, lambda en: en.nop(),
                      reads=[("__barA__", n, x) for x in ENGINES],
                      writes=[("__barB__", n, e)], tag=f"barB{n}")

    def emit(self, block, sems, dma_sems, cc_sems):
        lastmm = {}
        for st in self.steps:
            bank = st.get("mm_bank")
            if bank is None:
                continue
            prev = lastmm.get(bank)
            if prev is not None and prev["mm_rg"] != st["mm_rg"]:
                st["deps"].add(prev["idx"])
                st.setdefault("rgdeps", set()).add(prev["idx"])
            lastmm[bank] = st
        NP = len(dma_sems)
        ncc = 0
        for st in self.steps:
            st["sig"] = False
            if st["acc"]:
                st["cc_sem"] = ncc
                ncc += 1
        assert ncc <= len(cc_sems)
        for e in ENGINES:
            for i, st in enumerate(self.eng_steps[e]):
                st["epos"] = i
        for st in self.steps:
            rgd = st.get("rgdeps", ())
            best = {}
            for d in st["deps"]:
                dep = self.steps[d]
                if dep["dma"] or dep["acc"]:
                    continue
                if dep["eng"] == st["eng"] == "te" and d not in rgd:
                    continue
                b = best.get(dep["eng"])
                if b is None or dep["epos"] > b["epos"]:
                    best[dep["eng"]] = dep
            st["best"] = best
            for dep in best.values():
                dep["sig"] = True
        use = [0] * NP
        nsw = NP * 3 // 4
        ndma = {"sp": 0, "act": 0, "pool": 0, "te": 0, "dve": 0}
        for st in self.steps:
            if st["dma"]:
                if st["eng"] == "pool":
                    p = nsw + ndma["pool"] % (NP - nsw)
                else:
                    p = ndma[st["eng"]] % nsw
                ndma[st["eng"]] += 1
                st["dma_sem"] = p
                st["dma_fence"] = use[p] * 16 if use[p] else None
                use[p] += 1
                st["dma_val"] = use[p] * 16
                st["sig_val"] = None
        for e in ENGINES:
            n = 0
            for st in self.eng_steps[e]:
                if st["dma"] or st["acc"]:
                    st["sig_val"] = None
                    continue
                st["sig_val"] = (n := n + 1) if st["sig"] else None
        for e in ENGINES:
            know = {x: 0 for x in ENGINES}
            know_d = [0] * NP
            for st in self.eng_steps[e]:
                waits, dwaits, cwaits = {}, {}, {}
                for d in sorted(st["deps"]):
                    dep = self.steps[d]
                    if dep["dma"]:
                        p = dep["dma_sem"]
                        if dep["dma_val"] > know_d[p]:
                            dwaits[p] = max(dwaits.get(p, 0), dep["dma_val"])
                    elif dep["acc"]:
                        cwaits[dep["cc_sem"]] = 1
                for de, dep in st["best"].items():
                    if dep["sig_val"] > know[de]:
                        waits[de] = max(waits.get(de, 0), dep["sig_val"])
                st["waits"] = waits
                st["dwaits"] = dwaits
                st["cwaits"] = cwaits
                for de, v in waits.items():
                    know[de] = max(know[de], v)
                for p, v in dwaits.items():
                    know_d[p] = max(know_d[p], v)
                for d in st["deps"]:
                    dep = self.steps[d]
                    if dep["eng"] == e or dep["dma"]:
                        continue
                    for de, v in dep.get("waits", {}).items():
                        if de != e:
                            know[de] = max(know[de], v)
                    for p, v in dep.get("dwaits", {}).items():
                        know_d[p] = max(know_d[p], v)

        def emit_engine(e, eng):
            for st in self.eng_steps[e]:
                if (SELF_FENCE and st["sig"] and st["sig_val"]
                        and st["sig_val"] > 1):
                    eng.wait_ge(sems[e], st["sig_val"] - 1)
                for de, v in sorted(st["waits"].items()):
                    eng.wait_ge(sems[de], v)
                for p, v in sorted(st["dwaits"].items()):
                    eng.wait_ge(dma_sems[p], v)
                for p, v in sorted(st.get("cwaits", {}).items()):
                    eng.wait_ge(cc_sems[p], v)
                if st["dma"] and st["dma_fence"] is not None:
                    eng.wait_ge(dma_sems[st["dma_sem"]], st["dma_fence"])
                inst = st["fn"](eng)
                if st["dma"]:
                    assert inst is not None, st["tag"]
                    inst.then_inc(dma_sems[st["dma_sem"]], 16)
                elif st["acc"]:
                    assert inst is not None, st["tag"]
                    inst.then_inc(cc_sems[st["cc_sem"]], 1)
                elif st["sig"]:
                    if inst is None:
                        inst = eng.nop()
                    inst.then_inc(sems[e], 1)

        @block.tensor
        def _(eng):
            emit_engine("te", eng)

        @block.vector
        def _(eng):
            emit_engine("dve", eng)

        @block.scalar
        def _(eng):
            emit_engine("act", eng)

        @block.gpsimd
        def _(eng):
            emit_engine("pool", eng)

        @block.sync
        def _(eng):
            emit_engine("sp", eng)


def _bind(fn, *a):
    return lambda eng: fn(eng, *a)


ARF_COLS = 11264
ARF = dict(mu=0, var=512, rstd=1024, mu2=1536, lnt0=2048, lnt1=2560,
           sg0=3072, sg1=3584, gg0=4096, gg1=4608, gld0=5120, gld1=5632,
           natb0=6144, natb1=7168, dnat=8192)
ARB_COLS = 44800
ARB = dict(xq=0,
           ws=16384,
           qb0=20480, qb1=21504,
           dwl0=22528, dwl1=23040,
           hq=23552,
           xb0=34816, xb1=35328, x20=35840, x21=36352,
           free=36864)
ATB = dict(aw=0,
           awf=8192,
           wv=9344,
           lng=13440,
           q=17536,
           k=19584,
           kt=21632,
           ab=23680,
           qb=25728,
           gr=27776,
           pw=29824,
           kht=32896,
           u=34944,
           uc=35456,
           g8=35584,
           s=36096,
           og=36352,
           pw2=38400,
           pw3=41472,
           uc2=44544,
           end=44800)
OTB = dict(wo=0,
           ot0=8192, ot1=12288,
           lnxo0=16384,
           lnxo1=20480,
           ogt=24576,
           oln=25088)
ATF = dict(zf=0,
           P=2048,
           Pinv=4096,
           vg=6144,
           tmp=10240)
OTF = dict(ov=0,
           mu=4096, var=4608, rstd=5120, mu2=5632)


class Builder:
    def __init__(self, phases=("ffn1", "attn", "ffn2"), dumps=(), attn_cut=99,
                 gcut=99):
        self.attn_cut = attn_cut
        self.gcut = gcut
        self.phases = phases
        self.dumps = list(dumps)
        self.nc = bass.Bass("TRN2", target_bir_lowering=False, debug=False)
        self.S = Sched()
        self.ctx = ExitStack()
        self.sb = {}
        self.ps = {}
        self.dram = {}
        self.const_arrays = {}

    def sbuf(self, name, shape, dtype):
        t = self.ctx.enter_context(self.nc.sbuf_tensor(name, shape, dtype))
        self.sb[name] = t
        return t

    def psum(self, name, shape, dtype):
        t = self.ctx.enter_context(self.nc.psum_tensor(name, shape, dtype))
        self.ps[name] = t
        return t

    def din(self, name, shape, dtype):
        t = self.nc.dram_tensor(name, shape, dtype, kind="ExternalInput")
        self.dram[name] = t
        return t

    def dout(self, name, shape, dtype):
        t = self.nc.dram_tensor(name, shape, dtype, kind="ExternalOutput")
        self.dram[name] = t
        return t

    def dtmp(self, name, shape, dtype):
        t = self.nc.dram_tensor(name, shape, dtype)
        self.dram[name] = t
        return t

    def const(self, name, arr):
        if name not in self.const_arrays:
            dt = {np.dtype(np.float32): F32,
                  np.dtype(ml_dtypes.bfloat16): BF16}[arr.dtype]
            self.din("c_" + name, list(arr.shape), dt)
            self.const_arrays["c_" + name] = np.ascontiguousarray(arr)
            self.sbuf("cs_" + name, list(arr.shape), dt)
            self.DMA(self.sb["cs_" + name][:], self.dram["c_" + name].ap(),
                     [], [("c", name)], tag="ldc_" + name)
        return self.sb["cs_" + name]

    def rf(self, reg, width=512, off=0):
        c = ARF[reg] + off
        return self.sb["arf"][:, c:c + width]

    def rb(self, reg, width, off=0):
        c = ARB[reg] + off
        return self.sb["arb"][:, c:c + width]

    def TT(self, out, a, b, op, reads, writes, tag=""):
        self.S.step("dve", _bind(
            lambda e, out, a, b: e.tensor_tensor(out=out, in0=a, in1=b, op=op),
            out, a, b), reads, writes, tag=tag)

    def TS(self, out, a, s1, s2, op0, op1, reads, writes, tag="", eng="dve"):
        if s2 is None:
            self.S.step(eng, _bind(
                lambda e, out, a, s1: e.tensor_scalar(
                    out=out, in0=a, scalar1=s1, scalar2=None, op0=op0),
                out, a, s1), reads, writes, tag=tag)
        else:
            self.S.step(eng, _bind(
                lambda e, out, a, s1, s2: e.tensor_scalar(
                    out=out, in0=a, scalar1=s1, scalar2=s2, op0=op0, op1=op1),
                out, a, s1, s2), reads, writes, tag=tag)

    def STT(self, out, a, s, b, op0, op1, reads, writes, tag="", eng="dve"):
        self.S.step(eng, _bind(
            lambda e, out, a, s, b: e.scalar_tensor_tensor(
                out=out, in0=a, scalar=s, in1=b, op0=op0, op1=op1),
            out, a, s, b), reads, writes, tag=tag)

    def COPY(self, out, a, reads, writes, tag="", eng="dve"):
        self.S.step(eng, _bind(
            lambda e, out, a: e.tensor_copy(out=out, in_=a), out, a),
            reads, writes, tag=tag)

    def MEMSET(self, ap, v, writes, tag="", eng="dve"):
        self.S.step(eng, _bind(lambda e, ap, v: e.memset(ap, v), ap, v),
                    (), writes, tag=tag)

    def RED(self, out, a, op, reads, writes, tag="", absval=False):
        self.S.step("dve", _bind(
            lambda e, out, a: e.tensor_reduce(
                out=out, in_=a, axis=AX, op=op, apply_absolute_value=absval),
            out, a), reads, writes, tag=tag)

    def RECIP(self, out, a, reads, writes, tag=""):
        self.S.step("dve", _bind(
            lambda e, out, a: e.reciprocal(out=out, in_=a), out, a),
            reads, writes, tag=tag)

    def SCAN(self, out, d0, d1, init, op0, op1, reads, writes, tag=""):
        self.S.step("dve", _bind(
            lambda e, out, d0, d1: e.tensor_tensor_scan(
                out=out, data0=d0, data1=d1, initial=init, op0=op0, op1=op1),
            out, d0, d1), reads, writes, tag=tag)

    def ACTV(self, out, a, func, reads, writes, tag="", bias=0.0, scale=1.0):
        self.S.step("act", _bind(
            lambda e, out, a, bias, scale: e.activation(
                out=out, in_=a, func=func, bias=bias, scale=scale),
            out, a, bias, scale), reads, writes, tag=tag)

    def MM(self, out, lhsT, rhs, start, stop, reads, writes, tag="", tp=None):
        i = self.S.step("te", _bind(
            lambda e, out, lhsT, rhs: e.matmul(
                out, lhsT, rhs, start=start, stop=stop, tile_position=tp),
            out, lhsT, rhs), reads, writes, tag=tag)
        st = self.S.steps[i]
        st["mm_bank"] = out.tensor.name
        st["mm_rg"] = (tp[0] if tp else 0, lhsT.shape[0])

    def TPO(self, out, a, reads, writes, tag="", f32=False):
        nm = "identf" if f32 else "ident"
        ident = self.sb["cs_" + nm][:, :]
        i = self.S.step("te", _bind(
            lambda e, out, a, ident: e.transpose(out, a, ident),
            out, a, ident), list(reads) + [("c", nm)], writes, tag=tag)
        st = self.S.steps[i]
        st["mm_bank"] = out.tensor.name
        st["mm_rg"] = (0, a.shape[0])

    def DMA(self, out, a, reads, writes, tag="", eng="sp"):
        self.S.step(eng, _bind(
            lambda e, out, a: e.dma_start(out=out, in_=a), out, a),
            reads, writes, dma=True, tag=tag)

    def CC(self, kind, op, groups, ins, outs, reads, writes, tag=""):
        i = self.S.step("pool", _bind(
            lambda e, ins, outs: e.collective_compute(
                kind, op, replica_groups=groups, ins=ins, outs=outs),
            ins, outs), reads, writes, tag=tag)
        self.S.steps[i]["acc"] = True

    def build(self):
        self.din("x_fm", [D, TOK], F32)
        self.din("wg_nat", [II, D], F32)
        self.din("wu_nat", [II, D], F32)
        self.din("wd_nat", [D, II], F32)
        self.din("clip_x", [1, 1], F32)
        self.din("clip_down", [1, 1], F32)
        self.din("Wq", [D, D // 2], F32)
        self.din("Wk", [D, D // 2], F32)
        self.din("Wv", [D, D // 2], F32)
        self.din("Wgam", [D, NH], F32)
        self.din("Wf1", [D, HD], F32)
        self.din("Wf2", [HD, D // 2], F32)
        self.din("Wog1", [D, HD], F32)
        self.din("Wog2", [HD, D], F32)
        self.din("Wo", [D, D], F32)
        self.din("norm_w", [128, 8], F32)
        self.din("hgmask", [128, 1], F32)
        self.sbuf("hgm", [128, 1], F32)
        self.dout("out_fm", [D, TOK], F32)

        self.dtmp("ln_spill", [D, TOK], F32)
        self.dtmp("g_spill", [II, TOK], F32)
        self.dtmp("x1_spill", [D, TOK], F32)
        self.dtmp("wq_spill", [3 * NIC, 128, 1024], BF16)
        self.dtmp("o_spill", [D // 2, TFULL], BF16)
        self.dtmp("ar_in", [1, 128], F32)
        self.dtmp("ar_out", [1, 128], F32)
        self.dtmp("ag_lnx_in", [D, TOK], BF16)
        self.dtmp("ag_lnx_out", [2 * D, TOK], BF16)
        self.dtmp("ag_o_out", [D, TFULL], BF16)

        self.sbuf("xs", [128, 8 * TOK], F32)
        self.sbuf("arf", [128, ARF_COLS], F32)
        self.sbuf("arb", [128, ARB_COLS], BF16)
        self.sbuf("stats", [128, 256], F32)
        self.sbuf("small", [128, 64], F32)
        self.sbuf("srow", [1, 512], F32)

        for i in range(6):
            self.psum(f"pb{i}", [128, 512], F32)
        self.psum("ptb", [128, 1024], BF16)
        self.psum("pb7", [128, 512], F32)

        self.const("ones128", np.ones((128, 128), dtype=ml_dtypes.bfloat16))
        bo = np.zeros((128, 128), dtype=np.float32)
        bo[0:64, 0:64] = 1.0
        bo[64:128, 64:128] = 1.0
        self.const("blockones", bo.astype(ml_dtypes.bfloat16))
        sel = np.zeros((8, 512), dtype=np.float32)
        for h in range(8):
            for pcol in range(512):
                if h == 2 * (pcol // 128) + (pcol % 128) // 64:
                    sel[h, pcol] = 1.0
        self.const("sel8", sel.astype(ml_dtypes.bfloat16))
        pp, ff = np.meshgrid(np.arange(128), np.arange(512), indexing="ij")
        self.const("mstril", ((ff % 64) < (pp % 64)).astype(ml_dtypes.bfloat16))
        self.const("mstriu", ((ff % 64) > (pp % 64)).astype(ml_dtypes.bfloat16))
        self.const("mtriuincl", ((ff % 64) >= (pp % 64)).astype(ml_dtypes.bfloat16))
        self.const("identblk", ((ff % 64) == (pp % 64)).astype(ml_dtypes.bfloat16))
        self.const("zero64", np.zeros((128, 64), dtype=np.float32))
        self.const("ones1f", np.ones((1, 128), dtype=np.float32))
        self.const("ident", np.eye(128, dtype=np.float32).astype(ml_dtypes.bfloat16))
        self.const("identf", np.eye(128, dtype=np.float32))
        self.const("eps", np.full((128, 1), 1e-5, dtype=np.float32))

        xs = self.sb["xs"]
        xin = self.dram["x_fm"].ap()
        self.DMA(self.sb["hgm"][:], self.dram["hgmask"].ap(), [], [("hgm",)],
                 tag="ldhgm")
        for f in range(8):
            self.DMA(xs[:, f * TOK:(f + 1) * TOK], xin[f * 128:(f + 1) * 128, :],
                     [], [("x", f)], tag=f"ldx{f}")

        if "ffn1" in self.phases:
            self.emit_ffn(0)
        if "attn" in self.phases:
            self.emit_attn()
        if "ffn2" in self.phases:
            self.emit_ffn(1)

        outap = self.dram["out_fm"].ap()
        for f in range(8):
            self.DMA(outap[f * 128:(f + 1) * 128, :], xs[:, f * TOK:(f + 1) * TOK],
                     [("x", f)], [("ost", f)], tag=f"stx{f}")

        for name, src, reads in self.dumps:
            t = self.dout("dump_" + name, [src.shape[0], src.shape[1]], src.dtype)
            self.DMA(t.ap(), src, list(reads), [("dump", name)],
                     tag="dump_" + name)
        self.S.barrier()
        self.S.step("sp", lambda e: e.nop(),
                    reads=[("ost", f) for f in range(8)] +
                          [("dump", n) for n, _, _ in self.dumps],
                    tag="final_wait")
        return self

    def _scales_from_max(self, P, clip_ap):
        sm = self.sb["small"]
        self.TPO(self.ps["pb7"][0:1, 0:128], sm[:, 8:9], [("sm", 8)],
                 [("psum", 7)], tag=P + "mxT", f32=True)
        self.COPY(self.sb["srow"][0:1, 0:128], self.ps["pb7"][0:1, 0:128],
                  [("psum", 7)], [("srow",), ("psum", 7)], tag=P + "mxc")
        self.RED(sm[0:1, 10:11], self.sb["srow"][0:1, 0:128], ALU.max,
                 [("srow",)], [("sm", 10)], tag=P + "mxr")
        self.TS(self.sb["srow"][0:1, 128:256], self.sb["cs_ones1f"][0:1, :],
                sm[0:1, 10:11], None, ALU.mult, None,
                [("sm", 10), ("c", "ones1f")], [("srow2",)], tag=P + "mxbc")
        self.DMA(self.dram["ar_in"].ap()[0:1, :], self.sb["srow"][0:1, 128:256],
                 [("srow2",)], [("ar",)], tag=P + "arst")
        self.CC("AllReduce", ALU.max, [list(range(8))],
                [self.dram["ar_in"].ap()], [self.dram["ar_out"].ap()],
                [("ar",)], [("ar",)], tag=P + "ar")
        self.DMA(sm[0:1, 11:12], self.dram["ar_out"].ap()[0:1, 0:1],
                 [("ar",)], [("sm", 11)], tag=P + "arld")
        self.DMA(sm[0:1, 12:13], clip_ap, [], [("sm", 12)], tag=P + "clipld")
        self.ACTV(sm[0:1, 13:14], sm[0:1, 12:13], AF.Tanh,
                  [("sm", 12)], [("sm", 13)], tag=P + "tanh", scale=0.5)
        self.TS(sm[0:1, 13:14], sm[0:1, 13:14], 0.5, 0.5, ALU.mult, ALU.add,
                [("sm", 13)], [("sm", 13)], tag=P + "sig")
        self.TT(sm[0:1, 14:15], sm[0:1, 13:14], sm[0:1, 11:12], ALU.mult,
                [("sm", 13), ("sm", 11)], [("sm", 14)], tag=P + "sM")
        self.TS(sm[0:1, 15:16], sm[0:1, 14:15], 1.0 / QMAX, None, ALU.mult, None,
                [("sm", 14)], [("sm", 15)], tag=P + "s")
        self.RECIP(sm[0:1, 16:17], sm[0:1, 14:15], [("sm", 14)], [("sm", 16)],
                   tag=P + "rinv")
        self.TS(sm[0:1, 17:18], sm[0:1, 16:17], QMAX, None, ALU.mult, None,
                [("sm", 16)], [("sm", 17)], tag=P + "r")
        ones1f = self.sb["cs_ones1f"]
        self.MM(self.ps["pb7"][:, 0:1], ones1f[0:1, :], sm[0:1, 17:18],
                True, True, [("sm", 17), ("c", "ones1f"), ("psum", 7)],
                [("psum", 7)], tag=P + "bcr")
        self.COPY(sm[:, 0:1], self.ps["pb7"][:, 0:1], [("psum", 7)],
                  [("sm", 0), ("psum", 7)], tag=P + "bcrc")
        self.MM(self.ps["pb7"][:, 0:1], ones1f[0:1, :], sm[0:1, 15:16],
                True, True, [("sm", 15), ("c", "ones1f"), ("psum", 7)],
                [("psum", 7)], tag=P + "bcs")
        self.COPY(sm[:, 5:6], self.ps["pb7"][:, 0:1], [("psum", 7)],
                  [("sm", 5), ("psum", 7)], tag=P + "bcsc")

    def quant_round(self, src, rcol, rk, out, srck, outk, tag, fbuf=None,
                    fkey=None):
        if fbuf is None:
            fbuf = self.rf("var", src.shape[1])
            fkey = ("arf", "var")
        MAGIC = 12582912.0
        self.TS(src, src, rcol, MAGIC, ALU.mult, ALU.add, [srck, rk], [srck],
                tag=tag + "a")
        self.TS(src, src, -MAGIC, None, ALU.add, None, [srck], [srck],
                tag=tag + "b")
        self.TS(out, src, 127.0, -128.0, ALU.min, ALU.max, [srck], [outk],
                tag=tag + "d")

    def emit_ffn(self, kk):
        xs = self.sb["xs"]
        arb = self.sb["arb"]
        stats = self.sb["stats"]
        sm = self.sb["small"]
        pb = [self.ps[f"pb{i}"] for i in range(6)]
        ptb = self.ps["ptb"]
        ones128 = self.sb["cs_ones128"]
        P = f"f{kk}"
        wsp = self.dram["wq_spill"].ap()

        if kk == 0:
            for widx, nm in ((0, "wg_nat"), (1, "wu_nat")):
                nat = self.dram[nm].ap()
                for ic in range(NIC):
                    par = ic % 2
                    natb = self.rf("natb0" if par == 0 else "natb1", 1024)
                    natk = ("arf", "natb", par)
                    self.DMA(natb, nat[ic * 128:(ic + 1) * 128, :],
                             [natk], [natk], tag=f"natld{widx}_{ic}")
                    self.RED(sm[:, 20:21], natb, ALU.max, [natk], [("sm", 20)],
                             tag=f"wmax{widx}_{ic}", absval=True)
                    self.TS(stats[:, widx * 32 + ic: widx * 32 + ic + 1],
                            sm[:, 20:21], 1.0 / QMAX, None, ALU.mult, None,
                            [("sm", 20)], [("stats", widx * 32 + ic)],
                            tag=f"wsc{widx}_{ic}")
                    self.RECIP(sm[:, 21:22], sm[:, 20:21], [("sm", 20)],
                               [("sm", 21)], tag=f"wrc{widx}_{ic}")
                    self.TS(sm[:, 22:23], sm[:, 21:22], QMAX, None, ALU.mult,
                            None, [("sm", 21)], [("sm", 22)],
                            tag=f"wr2{widx}_{ic}")
                    qb = self.rb("qb0" if par == 0 else "qb1", 1024)
                    qbk = ("arb", "qb", par)
                    self.quant_round(natb, sm[:, 22:23], ("sm", 22), qb,
                                     natk, qbk, f"wq{widx}_{ic}",
                                     fbuf=self.rf("dnat", 1024),
                                     fkey=("arf", "dnat"))
                    for f in range(8):
                        self.TPO(ptb[:, f * 128:(f + 1) * 128],
                                 qb[:, f * 128:(f + 1) * 128],
                                 [qbk], [("ptb", f)], tag=f"wt{widx}_{ic}_{f}")
                    stt = self.rb("ws", 1024, par * 2048)
                    sttk = ("arb", "ws", par, 0)
                    self.COPY(stt, ptb[:, :], [("ptb", f) for f in range(8)],
                              [sttk] + [("ptb", f) for f in range(8)],
                              tag=f"wte{widx}_{ic}")
                    self.DMA(wsp[widx * NIC + ic, :, :], stt, [sttk],
                             [("wq", widx, ic)], tag=f"wsp{widx}_{ic}",
                             eng="act")
            nat = self.dram["wd_nat"].ap()
            for oc in range(8):
                natb = self.rf("dnat", 2816)
                natk = ("arf", "dnat")
                self.DMA(natb, nat[oc * 128:(oc + 1) * 128, :], [natk], [natk],
                         tag=f"dnatld{oc}", eng="act")
                self.RED(sm[:, 20:21], natb, ALU.max, [natk], [("sm", 20)],
                         tag=f"dmax{oc}", absval=True)
                self.TS(stats[:, 64 + oc: 64 + oc + 1], sm[:, 20:21],
                        1.0 / QMAX, None, ALU.mult, None, [("sm", 20)],
                        [("stats", 64 + oc)], tag=f"dsc{oc}")
                self.RECIP(sm[:, 21:22], sm[:, 20:21], [("sm", 20)],
                           [("sm", 21)], tag=f"drc{oc}")
                self.TS(sm[:, 22:23], sm[:, 21:22], QMAX, None, ALU.mult, None,
                        [("sm", 21)], [("sm", 22)], tag=f"dr2{oc}")
                for part, w in ((0, 1024), (1024, 1024), (2048, 768)):
                    qb = self.rb("qb0", 1024)
                    qbk = ("arb", "qb", 0)
                    self.quant_round(natb[:, part:part + w], sm[:, 22:23],
                                     ("sm", 22), qb[:, 0:w], natk, qbk,
                                     f"dq{oc}_{part}",
                                     fbuf=self.rf("natb0", w),
                                     fkey=("arf", "natb", 0))
                    for j in range(w // 128):
                        ic = (part + j * 128) // 128
                        self.TPO(ptb[:, j * 128:(j + 1) * 128],
                                 qb[:, j * 128:(j + 1) * 128],
                                 [qbk], [("ptb", j)], tag=f"dt{oc}_{ic}")
                    stt = self.rb("ws", 1024, 0)
                    sttk = ("arb", "ws", 0, 0)
                    self.COPY(stt[:, 0:w], ptb[:, 0:w],
                              [("ptb", j) for j in range(w // 128)],
                              [sttk] + [("ptb", j) for j in range(w // 128)],
                              tag=f"dte{oc}_{part}")
                    for j in range(w // 128):
                        ic = (part + j * 128) // 128
                        self.DMA(wsp[2 * NIC + ic, :, oc * 128:(oc + 1) * 128],
                                 stt[:, j * 128:(j + 1) * 128], [sttk],
                                 [("wqd", ic, oc)], tag=f"dsp{oc}_{ic}",
                                 eng="act")

        self.MEMSET(sm[:, 8:9], 0.0, [("sm", 8)], tag=P + "mx0")
        for tt in range(4):
            for f in range(8):
                xsl = xs[:, f * TOK + tt * 512: f * TOK + tt * 512 + 512]
                xb = self.rb("xb0" if f % 2 == 0 else "xb1", 512)
                x2 = self.rb("x20" if f % 2 == 0 else "x21", 512)
                xbk = ("arb", "xb", f % 2)
                x2k = ("arb", "x2", f % 2)
                self.COPY(xb, xsl, [("x", f)], [xbk], tag=P + f"cast{tt}_{f}")
                self.TT(x2, xb, xb, ALU.mult, [xbk], [x2k], tag=P + f"sq{tt}_{f}")
                self.MM(pb[0][:, :], ones128[:], xb, f == 0, f == 7,
                        [xbk, ("c", "ones128"), ("psum", 0)], [("psum", 0)],
                        tag=P + f"sums{tt}_{f}")
                self.MM(pb[1][:, :], ones128[:], x2, f == 0, f == 7,
                        [x2k, ("c", "ones128"), ("psum", 1)], [("psum", 1)],
                        tag=P + f"sumsq{tt}_{f}")
            mu, var, rstd, mu2 = (self.rf(r) for r in ("mu", "var", "rstd", "mu2"))
            self.TS(mu, pb[0][:, :], 1.0 / D, None, ALU.mult, None,
                    [("psum", 0)], [("arf", "mu"), ("psum", 0)], tag=P + f"mu{tt}")
            self.TS(var, pb[1][:, :], 1.0 / D, None, ALU.mult, None,
                    [("psum", 1)], [("arf", "var"), ("psum", 1)], tag=P + f"msq{tt}")
            self.TT(mu2, mu, mu, ALU.mult, [("arf", "mu")], [("arf", "mu2")],
                    tag=P + f"mu2{tt}")
            self.TT(var, var, mu2, ALU.subtract,
                    [("arf", "var"), ("arf", "mu2")], [("arf", "var")],
                    tag=P + f"var{tt}")
            self.ACTV(rstd, var, AF.Sqrt, [("arf", "var"), ("c", "eps")],
                      [("arf", "rstd")], tag=P + f"sd{tt}",
                      bias=self.sb["cs_eps"][:, 0:1])
            self.RECIP(rstd, rstd, [("arf", "rstd")], [("arf", "rstd")],
                       tag=P + f"rstd{tt}")
            for f in range(8):
                xsl = xs[:, f * TOK + tt * 512: f * TOK + tt * 512 + 512]
                lnt = self.rf("lnt0" if f % 2 == 0 else "lnt1")
                lk = ("arf", "lnt", f % 2)
                self.TT(lnt, xsl, mu, ALU.subtract, [("x", f), ("arf", "mu")],
                        [lk], tag=P + f"lsub{tt}_{f}")
                self.TT(lnt, lnt, rstd, ALU.mult, [lk, ("arf", "rstd")], [lk],
                        tag=P + f"lmul{tt}_{f}")
                self.DMA(self.dram["ln_spill"].ap()[f * 128:(f + 1) * 128,
                                                    tt * 512:(tt + 1) * 512],
                         lnt, [lk], [("lnsp", tt, f)], tag=P + f"lnsp{tt}_{f}")
                self.RED(sm[:, 9:10], lnt, ALU.max, [lk], [("sm", 9)],
                         tag=P + f"lmax{tt}_{f}", absval=True)
                self.TT(sm[:, 8:9], sm[:, 8:9], sm[:, 9:10], ALU.max,
                        [("sm", 9), ("sm", 8)], [("sm", 8)],
                        tag=P + f"mxa{tt}_{f}")

        self._scales_from_max(P, self.dram["clip_x"].ap())

        for f in range(8):
            for tt in range(4):
                lnt = self.rf("lnt0" if (f + tt) % 2 == 0 else "lnt1")
                lk = ("arf", "lnt", (f + tt) % 2)
                self.DMA(lnt, self.dram["ln_spill"].ap()[f * 128:(f + 1) * 128,
                                                         tt * 512:(tt + 1) * 512],
                         [("lnsp", tt, f), lk], [lk], tag=P + f"lnld{tt}_{f}")
                self.quant_round(lnt, sm[:, 0:1], ("sm", 0),
                                 arb[:, f * TOK + tt * 512: f * TOK + tt * 512 + 512],
                                 lk, ("xq", f, tt), P + f"q{tt}_{f}")

        for widx in range(2):
            for ic in range(NIC):
                self.TT(stats[:, 128 + widx * 32 + ic: 128 + widx * 32 + ic + 1],
                        stats[:, widx * 32 + ic: widx * 32 + ic + 1],
                        sm[:, 5:6], ALU.mult,
                        [("stats", widx * 32 + ic), ("sm", 5)],
                        [("stats", 128 + widx * 32 + ic)],
                        tag=P + f"esc{widx}_{ic}")

        self.MEMSET(sm[:, 8:9], 0.0, [("sm", 8)], tag=P + "gmx0")
        for ic in range(NIC):
            par = ic % 2
            gst = self.rb("ws", 1024, par * 2048)
            ust = self.rb("ws", 1024, par * 2048 + 1024)
            gstk = ("arb", "ws", par, 0)
            ustk = ("arb", "ws", par, 1)
            self.DMA(gst, wsp[ic, :, :], [gstk, ("wq", 0, ic)], [gstk],
                     tag=P + f"gwld{ic}", eng="act")
            self.DMA(ust, wsp[NIC + ic, :, :], [ustk, ("wq", 1, ic)], [ustk],
                     tag=P + f"uwld{ic}", eng="act")
            for tt in range(4):
                bg = 2 + (tt % 2)
                bu = 4 + (tt % 2)
                for f in range(8):
                    self.MM(pb[bg][:, :], gst[:, f * 128:(f + 1) * 128],
                            arb[:, f * TOK + tt * 512: f * TOK + tt * 512 + 512],
                            f == 0, f == 7,
                            [("xq", f, tt), gstk, ("psum", bg)], [("psum", bg)],
                            tag=P + f"gm{ic}_{tt}_{f}")
                pbu = pb[bu] if bu < 6 else self.ps["pb7"]
                for f in range(8):
                    self.MM(pbu[:, :], ust[:, f * 128:(f + 1) * 128],
                            arb[:, f * TOK + tt * 512: f * TOK + tt * 512 + 512],
                            f == 0, f == 7,
                            [("xq", f, tt), ustk, ("psum", bu)], [("psum", bu)],
                            tag=P + f"um{ic}_{tt}_{f}")
                pbu = pb[bu] if bu < 6 else self.ps["pb7"]
                sg = self.rf("sg0" if tt % 2 == 0 else "sg1")
                gg = self.rf("gg0" if tt % 2 == 0 else "gg1")
                sgk = ("arf", "sg", tt % 2)
                ggk = ("arf", "gg", tt % 2)
                self.ACTV(sg, pb[bg][:, :], AF.Silu,
                          [("psum", bg), ("stats", 128 + ic)],
                          [sgk, ("psum", bg)], tag=P + f"sge{ic}_{tt}",
                          scale=stats[:, 128 + ic: 128 + ic + 1])
                self.STT(gg, pbu[:, :], stats[:, 160 + ic:160 + ic + 1], sg,
                         ALU.mult, ALU.mult,
                         [("psum", bu), ("stats", 160 + ic), sgk],
                         [ggk, ("psum", bu)], tag=P + f"ge{ic}_{tt}")
                self.DMA(self.dram["g_spill"].ap()[ic * 128:(ic + 1) * 128,
                                                   tt * 512:(tt + 1) * 512],
                         gg, [ggk], [("gsp", ic, tt)], tag=P + f"gsp{ic}_{tt}",
                         eng="act")
                self.RED(sm[:, 9:10], gg, ALU.max, [ggk], [("sm", 9)],
                         tag=P + f"gmx{ic}_{tt}", absval=True)
                self.TT(sm[:, 8:9], sm[:, 8:9], sm[:, 9:10], ALU.max,
                        [("sm", 9), ("sm", 8)], [("sm", 8)],
                        tag=P + f"gma{ic}_{tt}")

        self._scales_from_max(P + "h", self.dram["clip_down"].ap())

        for oc in range(8):
            self.TT(stats[:, 192 + oc:192 + oc + 1],
                    stats[:, 64 + oc:64 + oc + 1], sm[:, 5:6], ALU.mult,
                    [("stats", 64 + oc), ("sm", 5)], [("stats", 192 + oc)],
                    tag=P + f"descA{oc}")
            self.TS(stats[:, 192 + oc:192 + oc + 1],
                    stats[:, 192 + oc:192 + oc + 1], 0.5, None, ALU.mult, None,
                    [("stats", 192 + oc)], [("stats", 192 + oc)],
                    tag=P + f"descB{oc}")

        for tt in range(4):
            for ic in range(NIC):
                par = ic % 2
                gg = self.rf("gld0" if par == 0 else "gld1")
                ggk = ("arf", "gld", par)
                self.DMA(gg, self.dram["g_spill"].ap()[ic * 128:(ic + 1) * 128,
                                                       tt * 512:(tt + 1) * 512],
                         [ggk, ("gsp", ic, tt)], [ggk], tag=P + f"gld{tt}_{ic}",
                         eng="sp")
                self.quant_round(gg, sm[:, 0:1], ("sm", 0),
                                 self.rb("hq", 512, ic * 512),
                                 ggk, ("hq", ic), P + f"hq{tt}_{ic}")
            for ph in range(2):
                bks = [0, 1, 2, 3] if ph == 0 else [4, 5, 7, 2]
                for ic in range(NIC):
                    par = ic % 2
                    dwl = self.rb("dwl0" if par == 0 else "dwl1", 512)
                    dwlk = ("arb", "dwl", par)
                    self.DMA(dwl, wsp[2 * NIC + ic, :, ph * 512:(ph + 1) * 512],
                             [dwlk] + [("wqd", ic, ph * 4 + j) for j in range(4)],
                             [dwlk], tag=P + f"dwld{tt}_{ph}_{ic}", eng="act")
                    for j in range(4):
                        bk = bks[j]
                        pbj = pb[bk] if bk < 6 else self.ps["pb7"]
                        self.MM(pbj[:, :], dwl[:, j * 128:(j + 1) * 128],
                                self.rb("hq", 512, ic * 512),
                                ic == 0, ic == NIC - 1,
                                [("hq", ic), dwlk, ("psum", bk)], [("psum", bk)],
                                tag=P + f"dm{tt}_{ph}_{ic}_{j}")
                for j in range(4):
                    bk = bks[j]
                    pbj = pb[bk] if bk < 6 else self.ps["pb7"]
                    oc = ph * 4 + j
                    xsl = xs[:, oc * TOK + tt * 512: oc * TOK + tt * 512 + 512]
                    self.STT(xsl, pbj[:, :], stats[:, 192 + oc:192 + oc + 1],
                             xsl, ALU.mult, ALU.add,
                             [("psum", bk), ("stats", 192 + oc), ("x", oc)],
                             [("x", oc), ("psum", bk)], tag=P + f"dres{tt}_{oc}")



    def emit_attn_group(self, g):
        P = f"ag{g}"
        pb = [self.ps[f"pb{i}"] for i in range(6)]
        pb7 = self.ps["pb7"]
        ptb = self.ps["ptb"]
        lng = self.ab_("lng", 4096)
        lngk = ("arb", "lng")
        hgb = g // 4

        self.DMA(lng.rearrange("p (a m) -> p a m", a=8),
                 self.dram["ag_lnx_out"].ap()[hgb * D:(hgb + 1) * D,
                                              (g % 4) * 512:(g % 4) * 512 + 512]
                 .rearrange("(a p) m -> p a m", p=128),
                 [lngk, ("aglnx",)], [lngk], tag=P + "lngld")

        for which, kbase in (("q", 0), ("k", 4096)):
            for oc in range(4):
                bnk = 2 + (oc % 2)
                for kc in range(8):
                    self.MM(pb[bnk][:, :],
                            self.ab_("aw", 128, kbase + kc * 512 + oc * 128),
                            lng[:, kc * 512:(kc + 1) * 512],
                            kc == 0, kc == 7,
                            [lngk, ("aw", which, kc), ("psum", bnk)],
                            [("psum", bnk)], tag=P + f"{which}m{oc}_{kc}")
                t = self.af_("tmp", 512)
                tk = ("arf", "tmp")
                self.ACTV(t, pb[bnk][:, :], AF.Tanh,
                          [("psum", bnk)], [tk], tag=P + f"{which}th{oc}",
                          scale=0.5)
                self.TS(t, t, 0.5, 0.5, ALU.mult, ALU.add, [tk], [tk],
                        tag=P + f"{which}sg{oc}")
                dst = self.ab_(which, 512, oc * 512)
                self.TT(dst, t, pb[bnk][:, :], ALU.mult,
                        [tk, ("psum", bnk)],
                        [("arb", which, oc), ("psum", bnk)],
                        tag=P + f"{which}sl{oc}")
        if self.gcut < 2:
            return
        bo = self.sb["cs_blockones"]
        for oc in range(4):
            k2 = self.af_("tmp", 512)
            tk = ("arf", "tmp")
            kt = self.ab_("k", 512, oc * 512)
            k2b = self.rb("xb0", 512)
            self.TT(k2b, kt, kt, ALU.mult, [("arb", "k", oc)],
                    [("arb", "xb", 0)], tag=P + f"k2_{oc}")
            self.MM(pb7[:, :], bo[:], k2b, True, True,
                    [("arb", "xb", 0), ("c", "blockones"), ("psum", 7)],
                    [("psum", 7)], tag=P + f"k2m{oc}")
            self.ACTV(k2, pb7[:, :], AF.Sqrt, [("psum", 7)],
                      [tk, ("psum", 7)], tag=P + f"knrm{oc}")
            self.TS(k2, k2, 1e-12, None, ALU.max, None, [tk], [tk],
                    tag=P + f"knmx{oc}")
            self.RECIP(k2, k2, [tk], [tk], tag=P + f"knr{oc}")
            self.TT(kt, kt, k2, ALU.mult, [("arb", "k", oc), tk],
                    [("arb", "k", oc)], tag=P + f"knd{oc}")

        if self.gcut < 3:
            return
        for kc in range(8):
            self.DMA(self.ab_("wv", 512, kc * 512),
                     self.dram["Wv"].ap()[kc * 128:(kc + 1) * 128, :],
                     [("arb", "wv", kc)], [("arb", "wv", kc)],
                     tag=P + f"wvl{kc}", eng="pool")
        for c in range(8):
            bnk = 2 + (c % 2)
            for kc in range(8):
                self.MM(pb[bnk][0:64, :],
                        lng[:, kc * 512 + c * 64: kc * 512 + c * 64 + 64],
                        self.ab_("wv", 512, kc * 512), kc == 0, kc == 7,
                        [lngk, ("arb", "wv", kc), ("psum", bnk)],
                        [("psum", bnk)], tag=P + f"vm{c}_{kc}")
            self.COPY(self.af_("vg", 512, c * 512)[0:64, :], pb[bnk][0:64, :],
                      [("psum", bnk)], [("arf", "vg", c), ("psum", bnk)],
                      tag=P + f"ve{c}")
        self.DMA(self.sb["arf"][64:128, ATF["vg"]:ATF["vg"] + 4096],
                 self.sb["arf"][0:64, ATF["vg"]:ATF["vg"] + 4096],
                 [("arf", "vg", c) for c in range(8)] + [("arf", "vgd")],
                 [("arf", "vgd")], tag=P + "vdup")

        if self.gcut < 4:
            return
        for kc in range(8):
            self.MM(pb7[0:64, :], self.ab_("awf", 64, kc * 64),
                    lng[:, kc * 512:(kc + 1) * 512], kc == 0, kc == 7,
                    [lngk, ("aw", "f1", kc), ("psum", 7)], [("psum", 7)],
                    tag=P + f"f1m{kc}")
        xf1 = self.sb["arb"][0:64, ATB["u"]:ATB["u"] + 512]
        self.COPY(xf1, pb7[0:64, :], [("psum", 7)],
                  [("arb", "xf1"), ("psum", 7)], tag=P + "f1e")
        for oc in range(4):
            bnk = 2 + (oc % 2)
            self.MM(pb[bnk][:, :],
                    self.sb["arb"][0:64, ATB["awf"] + 512 + oc * 128:
                                   ATB["awf"] + 512 + (oc + 1) * 128],
                    xf1, True, True,
                    [("arb", "xf1"), ("aw", "f2"), ("psum", bnk)],
                    [("psum", bnk)], tag=P + f"f2m{oc}")
            lam = self.af_("zf", 512, oc * 512)
            self.ACTV(lam, pb[bnk][:, :], AF.Tanh, [("psum", bnk)],
                      [("arf", "zf", oc), ("psum", bnk)], tag=P + f"lth{oc}",
                      scale=0.5)
            self.TS(lam, lam, 0.5, 0.5, ALU.mult, ALU.add,
                    [("arf", "zf", oc)], [("arf", "zf", oc)], tag=P + f"lsg{oc}")

        for kc in range(8):
            self.MM(pb7[0:8, :], self.ab_("awf", 8, 1024 + kc * 8),
                    lng[:, kc * 512:(kc + 1) * 512], kc == 0, kc == 7,
                    [lngk, ("aw", "g", kc), ("psum", 7)], [("psum", 7)],
                    tag=P + f"gm{kc}")
        gamt = self.sb["arf"][0:8, ATF["tmp"]:ATF["tmp"] + 512]
        self.ACTV(gamt, pb7[0:8, :], AF.Tanh, [("psum", 7)],
                  [("arf", "tmp"), ("psum", 7)], tag=P + "gth", scale=0.5)
        self.TS(gamt, gamt, -0.5, -0.5, ALU.mult, ALU.add, [("arf", "tmp")],
                [("arf", "tmp")], tag=P + "gsg")
        gamb = self.sb["arb"][0:8, ATB["g8"]:ATB["g8"] + 512]
        self.COPY(gamb, gamt, [("arf", "tmp")], [("arb", "g8")], tag=P + "gbc")
        sel = self.sb["cs_sel8"]
        for pt in range(4):
            self.MM(pb7[:, :], sel[:, pt * 128:(pt + 1) * 128], gamb,
                    True, True,
                    [("arb", "g8"), ("c", "sel8"), ("psum", 7)],
                    [("psum", 7)], tag=P + f"grm{pt}")
            self.COPY(self.ab_("gr", 512, pt * 512), pb7[:, :], [("psum", 7)],
                      [("arb", "gr", pt), ("psum", 7)], tag=P + f"gre{pt}")

        if self.gcut < 5:
            return
        z64 = self.sb["cs_zero64"]
        for pt in range(4):
            for c in range(8):
                off = ATF["P"] + pt * 512 + c * 64
                zoff = ATF["zf"] + pt * 512 + c * 64
                self.SCAN(self.sb["arf"][:, off:off + 64],
                          self.sb["arf"][:, zoff:zoff + 64],
                          z64[:, 0:64], 1.0, ALU.mult, ALU.add,
                          [("arf", "zf", pt), ("c", "zero64")],
                          [("arf", "P", pt)], tag=P + f"scan{pt}_{c}")
            self.RECIP(self.af_("Pinv", 512, pt * 512),
                       self.af_("P", 512, pt * 512),
                       [("arf", "P", pt)], [("arf", "Pinv", pt)],
                       tag=P + f"pinv{pt}")

        if self.gcut < 6:
            return
        for pt in range(4):
            q = self.ab_("q", 512, pt * 512)
            k = self.ab_("k", 512, pt * 512)
            Pt = self.af_("P", 512, pt * 512)
            Pi = self.af_("Pinv", 512, pt * 512)
            lam = self.af_("zf", 512, pt * 512)
            t = self.af_("tmp", 512)
            tk = ("arf", "tmp")
            self.TT(self.ab_("qb", 512, pt * 512), q, Pt, ALU.mult,
                    [("arb", "q", pt), ("arf", "P", pt)], [("arb", "qb", pt)],
                    tag=P + f"qb{pt}")
            self.TT(self.ab_("kt", 512, pt * 512), k, Pi, ALU.mult,
                    [("arb", "k", pt), ("arf", "Pinv", pt)], [("arb", "kt", pt)],
                    tag=P + f"ktb{pt}")
            self.TT(t, lam, Pt, ALU.mult,
                    [("arf", "zf", pt), ("arf", "P", pt)], [tk],
                    tag=P + f"lp{pt}")
            self.TT(t, t, self.ab_("gr", 512, pt * 512), ALU.mult,
                    [tk, ("arb", "gr", pt)], [tk], tag=P + f"lpg{pt}")
            self.TT(self.ab_("ab", 512, pt * 512), k, t, ALU.mult,
                    [("arb", "k", pt), tk], [("arb", "ab", pt)],
                    tag=P + f"abt{pt}")
            for c in range(8):
                self.TS(self.ab_("k", 64, pt * 512 + c * 64),
                        self.ab_("kt", 64, pt * 512 + c * 64),
                        self.sb["arf"][:, ATF["P"] + pt * 512 + c * 64 + 63:
                                       ATF["P"] + pt * 512 + c * 64 + 64],
                        None, ALU.mult, None,
                        [("arb", "kt", pt), ("arf", "P", pt)],
                        [("arb", "k", pt)], tag=P + f"khf{pt}_{c}")

        if self.gcut < 7:
            return
        for pt in range(4):
            for cp in range(4):
                self.TPO(ptb[:, cp * 128:(cp + 1) * 128],
                         self.ab_("k", 128, pt * 512 + cp * 128),
                         [("arb", "k", pt)], [("ptb", cp)],
                         tag=P + f"khT{pt}_{cp}")
            self.COPY(self.ab_("kht", 512, pt * 512), ptb[:, 0:512],
                      [("ptb", cp) for cp in range(4)],
                      [("arb", "kht", pt)] + [("ptb", cp) for cp in range(4)],
                      tag=P + f"khE{pt}")

        if self.gcut < 8:
            return
        self.run_pairs(g)

        for pt in range(4):
            self.DMA(self.dram["o_spill"].ap()[pt * 128:(pt + 1) * 128,
                                               g * 512:(g + 1) * 512],
                     self.ab_("og", 512, pt * 512),
                     [("arb", "og", pt, c) for c in range(8)],
                     [("osp", g, pt)], tag=P + f"osp{pt}")

    PW = dict(T=0, W=512, sT=1024, sW=1536, N=2048, AT=2560)

    def pwsl(self, slot, j, c, pp):
        base = ATB[("pw", "pw2", "pw3")[pp]] + self.PW[slot] + 256 * j + 64 * (c // 2)
        r = 64 * (c % 2)
        return self.sb["arb"][r:r + 64, base:base + 64]

    def pwt(self, slot, pp):
        return self.ab_(("pw", "pw2", "pw3")[pp], 512, self.PW[slot])

    def _gsl(self, reg, h, c):
        pt = h // 2
        base = ATB[reg] + pt * 512 + c * 64
        r = 64 * (h % 2)
        return self.sb["arb"][r:r + 64, base:base + 64]

    def _vsl(self, h, c):
        base = ATF["vg"] + c * 512 + h * 64
        r = 64 * (c % 2)
        return self.sb["arf"][r:r + 64, base:base + 64]

    def _khsl(self, h, c):
        base = ATB["kht"] + (h // 2) * 512 + (c // 2) * 128 + 64 * (h % 2)
        r = 64 * (c % 2)
        return self.sb["arf"] if False else self.sb["arb"][r:r + 64, base:base + 64]

    def _ssl(self, h):
        base = ATB["s"] + 64 * (h // 2)
        r = 64 * (h % 2)
        return self.sb["arb"][r:r + 64, base:base + 64]

    def gen_precompute(self, g, hp):
        P = f"ag{g}p{hp}"
        pp = hp % 3
        pb = [self.ps[f"pb{i}"] for i in range(6)]
        pb7 = self.ps["pb7"]
        if hp % 2 == 0:
            bt, bw, kt_, kw_ = pb[2], pb[3], 2, 3
        else:
            bt, bw, kt_, kw_ = pb[1], pb7, 1, 7
        kkey = ("arb", "pw", pp)
        msl = self.sb["cs_mstril"]
        msu = self.sb["cs_mstriu"]
        mti = self.sb["cs_mtriuincl"]
        idb = self.sb["cs_identblk"]
        heads = (2 * hp, 2 * hp + 1)
        gsl = self._gsl
        arb = self.sb["arb"]

        def osl(j, c, bank):
            return bank[64 * (c % 2):64 * (c % 2) + 64,
                        256 * j + 64 * (c // 2):256 * j + 64 * (c // 2) + 64]

        def pwh(slot, r0, c0, n):
            base = ATB[("pw", "pw2", "pw3")[pp]] + self.PW[slot] + c0
            return arb[r0:r0 + 64, base:base + n]

        for j, h in enumerate(heads):
            for c in range(8):
                tp = (64 * (h % 2), 64 * (c % 2))
                self.MM(osl(j, c, bt), gsl("ab", h, c), gsl("kt", h, c),
                        True, True,
                        [("arb", "ab", h // 2), ("arb", "kt", h // 2),
                         ("psum", kt_)], [("psum", kt_)],
                        tag=P + f"Tb{j}_{c}", tp=tp)
                self.MM(osl(j, c, bw), gsl("kt", h, c), gsl("ab", h, c),
                        True, True,
                        [("arb", "ab", h // 2), ("arb", "kt", h // 2),
                         ("psum", kw_)], [("psum", kw_)],
                        tag=P + f"Wb{j}_{c}", tp=tp)
                if c % 4 == 3:
                    yield
        self.TT(self.pwt("T", pp), bt[:, :], msl[:], ALU.mult,
                [("psum", kt_), ("c", "mstril")],
                [kkey + ("T",), ("psum", kt_)], tag=P + "Te")
        self.TT(self.pwt("W", pp), bw[:, :], msu[:], ALU.mult,
                [("psum", kw_), ("c", "mstriu")],
                [kkey + ("W",), ("psum", kw_)], tag=P + "We")
        yield
        for j, h in enumerate(heads):
            for c in range(8):
                tp = (64 * (h % 2), 64 * (c % 2))
                self.MM(osl(j, c, bt), gsl("kt", h, c), gsl("qb", h, c),
                        True, True,
                        [("arb", "qb", h // 2), ("arb", "kt", h // 2),
                         ("psum", kt_)], [("psum", kt_)],
                        tag=P + f"ATb{j}_{c}", tp=tp)
                if c % 4 == 3:
                    yield
        self.TT(self.pwt("AT", pp), bt[:, :], mti[:], ALU.mult,
                [("psum", kt_), ("c", "mtriuincl")],
                [kkey + ("AT",), ("psum", kt_)], tag=P + "ATe")
        self.TT(self.pwt("N", pp), self.pwt("W", pp), idb[:], ALU.add,
                [kkey + ("W",), ("c", "identblk")], [kkey + ("N",)],
                tag=P + "Ni")
        yield

        cur_t, cur_w, nxt_t, nxt_w = "T", "W", "sT", "sW"
        corder = [0, 2, 4, 6, 1, 3, 5, 7]
        for lvl in range(4):
            for j, h in enumerate(heads):
                for c in corder:
                    tpp = (64 * (c % 2), 64 * (c % 2))
                    self.MM(osl(j, c, bt), self.pwsl(cur_w, j, c, pp),
                            self.pwsl(cur_t, j, c, pp), True, True,
                            [kkey + (cur_w,), kkey + (cur_t,),
                             ("psum", kt_)], [("psum", kt_)],
                            tag=P + f"sq{lvl}t{j}_{c}", tp=tpp)
                    self.MM(osl(j, c, bw), self.pwsl(cur_t, j, c, pp),
                            self.pwsl(cur_w, j, c, pp), True, True,
                            [kkey + (cur_w,), kkey + (cur_t,),
                             ("psum", kw_)], [("psum", kw_)],
                            tag=P + f"sq{lvl}w{j}_{c}", tp=tpp)
                    if c in (6, 7):
                        yield
            self.ACTV(self.pwt(nxt_t, pp), bt[:, :], AF.Copy,
                      [("psum", kt_)], [kkey + (nxt_t,), ("psum", kt_)],
                      tag=P + f"sqe{lvl}t")
            self.ACTV(self.pwt(nxt_w, pp), bw[:, :], AF.Copy,
                      [("psum", kw_)], [kkey + (nxt_w,), ("psum", kw_)],
                      tag=P + f"sqe{lvl}w")
            yield
            for j, h in enumerate(heads):
                for c in corder:
                    tpp = (64 * (c % 2), 64 * (c % 2))
                    self.MM(osl(j, c, bt), self.pwsl(nxt_t, j, c, pp),
                            self.pwsl("N", j, c, pp), True, True,
                            [kkey + (nxt_t,), kkey + ("N",),
                             ("psum", kt_)], [("psum", kt_)],
                            tag=P + f"nu{lvl}_{j}_{c}", tp=tpp)
                    if c in (6, 7):
                        yield
            self.TT(self.pwt("N", pp), bt[:, :], self.pwt("N", pp), ALU.add,
                    [("psum", kt_), kkey + ("N",)],
                    [kkey + ("N",), ("psum", kt_)], tag=P + f"nue{lvl}")
            yield
            cur_t, cur_w, nxt_t, nxt_w = nxt_t, nxt_w, cur_t, cur_w
        for j, h in enumerate(heads):
            for c in corder:
                tpp = (64 * (c % 2), 64 * (c % 2))
                self.MM(osl(j, c, bt), self.pwsl(cur_w, j, c, pp),
                        self.pwsl(cur_t, j, c, pp), True, True,
                        [kkey + (cur_w,), kkey + (cur_t,),
                         ("psum", kt_)], [("psum", kt_)],
                        tag=P + f"sq32t{j}_{c}", tp=tpp)
                if c in (6, 7):
                    yield
        self.ACTV(self.pwt(nxt_t, pp), bt[:, :], AF.Copy, [("psum", kt_)],
                  [kkey + (nxt_t,), ("psum", kt_)], tag=P + "sqe32t")
        yield
        for j, h in enumerate(heads):
            for c in corder:
                tpp = (64 * (c % 2), 64 * (c % 2))
                self.MM(osl(j, c, bt), self.pwsl(nxt_t, j, c, pp),
                        self.pwsl("N", j, c, pp), True, True,
                        [kkey + (nxt_t,), kkey + ("N",),
                         ("psum", kt_)], [("psum", kt_)],
                        tag=P + f"nu32_{j}_{c}", tp=tpp)
                if c in (6, 7):
                    yield
        self.TT(self.pwt("N", pp), bt[:, :], self.pwt("N", pp), ALU.add,
                [("psum", kt_), kkey + ("N",)], [kkey + ("N",), ("psum", kt_)],
                tag=P + "nue32")
        yield

    def gen_chain(self, g, hp):
        P = f"ag{g}c{hp}"
        pp = hp % 3
        pb = [self.ps[f"pb{i}"] for i in range(6)]
        kkey = ("arb", "pw", pp)
        heads = (2 * hp, 2 * hp + 1)
        gsl, vsl, khsl, ssl = self._gsl, self._vsl, self._khsl, self._ssl
        arb = self.sb["arb"]
        UC2 = ATB["uc2"]
        ucR = [arb[:, UC2 + 128 * j:UC2 + 128 * j + 64] for j in (0, 1)]
        ucU = [arb[:, UC2 + 128 * j + 64:UC2 + 128 * j + 128] for j in (0, 1)]
        cb = [pb[0], pb[4]]
        ck = [0, 4]
        for c in range(8):
            cpar = 64 * (c % 2)
            for j, h in enumerate(heads):
                hpar = 64 * (h % 2)
                self.MM(pb[5][hpar:hpar + 64, 0:64], ssl(h), gsl("qb", h, c),
                        True, True,
                        [("arb", "s", h), ("arb", "qb", h // 2), ("psum", 5)],
                        [("psum", 5)], tag=P + f"cO1{c}_{j}", tp=(hpar, hpar))
            for j, h in enumerate(heads):
                hpar = 64 * (h % 2)
                self.MM(cb[j][cpar:cpar + 64, 0:64], gsl("ab", h, c), ssl(h),
                        True, True,
                        [("arb", "ab", h // 2), ("arb", "s", h),
                         ("psum", ck[j])], [("psum", ck[j])],
                        tag=P + f"cR{c}_{j}", tp=(hpar, cpar))
            for j, h in enumerate(heads):
                self.TT(ucR[j][cpar:cpar + 64, :],
                        cb[j][cpar:cpar + 64, 0:64], vsl(h, c), ALU.add,
                        [("psum", ck[j]), ("arf", "vgd"), ("arf", "vg", c)],
                        [("arb", "ucR", j), ("psum", ck[j])],
                        tag=P + f"cRe{c}_{j}")
            for j, h in enumerate(heads):
                self.MM(cb[j][cpar:cpar + 64, 64:128],
                        self.pwsl("N", j, c, pp), ucR[j][cpar:cpar + 64, :],
                        True, True,
                        [kkey + ("N",), ("arb", "ucR", j), ("psum", ck[j])],
                        [("psum", ck[j])], tag=P + f"cU{c}_{j}",
                        tp=(cpar, cpar))
            for j, h in enumerate(heads):
                self.ACTV(ucU[j][cpar:cpar + 64, :],
                          cb[j][cpar:cpar + 64, 64:128], AF.Copy,
                          [("psum", ck[j])],
                          [("arb", "ucU", j), ("psum", ck[j])],
                          tag=P + f"cUe{c}_{j}")
            for j, h in enumerate(heads):
                hpar = 64 * (h % 2)
                self.MM(pb[5][hpar:hpar + 64, 64:128],
                        ucU[j][cpar:cpar + 64, :],
                        self.pwsl("AT", j, c, pp), True, True,
                        [("arb", "ucU", j), kkey + ("AT",), ("psum", 5)],
                        [("psum", 5)], tag=P + f"cO2{c}_{j}", tp=(cpar, hpar))
                self.MM(cb[j][hpar:hpar + 64, 128:192], khsl(h, c),
                        ucU[j][cpar:cpar + 64, :], True, True,
                        [("arb", "kht", h // 2), ("arb", "ucU", j),
                         ("psum", ck[j])], [("psum", ck[j])],
                        tag=P + f"cK{c}_{j}", tp=(cpar, hpar))
            for j, h in enumerate(heads):
                hpar = 64 * (h % 2)
                pcol = self.sb["arf"][hpar:hpar + 64,
                                      ATF["P"] + (h // 2) * 512 + c * 64 + 63:
                                      ATF["P"] + (h // 2) * 512 + c * 64 + 64]
                self.STT(ssl(h), ssl(h), pcol,
                         cb[j][hpar:hpar + 64, 128:192],
                         ALU.mult, ALU.add,
                         [("arb", "s", h), ("arf", "P", h // 2),
                          ("psum", ck[j])],
                         [("arb", "s", h), ("psum", ck[j])],
                         tag=P + f"cS{c}_{j}")
                yield
            self.COPY(self.ab_("og", 64, hp * 512 + c * 64), pb[5][:, 0:64],
                      [("psum", 5)], [("arb", "og", hp, c)], tag=P + f"cOc{c}")
            self.TT(self.ab_("og", 64, hp * 512 + c * 64),
                    self.ab_("og", 64, hp * 512 + c * 64), pb[5][:, 64:128],
                    ALU.add, [("arb", "og", hp, c), ("psum", 5)],
                    [("arb", "og", hp, c), ("psum", 5)], tag=P + f"cOe{c}")
            yield

    def run_pairs(self, g):
        pre = {hp: self.gen_precompute(g, hp) for hp in range(4)}
        chains = {}
        pre_done, ch_done = set(), set()
        active, next_pre = [], 0
        while len(ch_done) < 4:
            while (next_pre < 4 and len(active) < 2
                   and (next_pre < 3 or (next_pre - 3) in ch_done)):
                active.append(next_pre)
                next_pre += 1
            achain = None
            for hp in range(4):
                if hp in pre_done and hp not in ch_done:
                    achain = hp
                    break
            if achain is not None:
                gen = chains.get(achain)
                if gen is None:
                    gen = chains[achain] = self.gen_chain(g, achain)
                try:
                    next(gen)
                except StopIteration:
                    ch_done.add(achain)
            for hp in list(active):
                try:
                    next(pre[hp])
                    next(pre[hp])
                except StopIteration:
                    pre_done.add(hp)
                    active.remove(hp)
            if achain is None and not active and next_pre >= 4:
                break

    def emit_attn_output(self):
        P = "ao"
        pb = [self.ps[f"pb{i}"] for i in range(6)]
        pb7 = self.ps["pb7"]
        xs = self.sb["xs"]
        ones128 = self.sb["cs_ones128"]
        hgsel = ("hgsel",)

        for kc in range(8):
            self.DMA(self.ob_("wo", 1024, kc * 1024),
                     self.dram["Wo"].ap()[kc * 128:(kc + 1) * 128, :],
                     [], [("wo", kc)], tag=P + f"wol{kc}", eng="pool")
            self.DMA(self.ob_("lnxo1", 64, kc * 64),
                     self.dram["Wog1"].ap()[kc * 128:(kc + 1) * 128, :],
                     [], [("wog1", kc)], tag=P + f"wog1l{kc}", eng="pool")
        self.DMA(self.sb["arb"][0:64, OTB["lnxo1"] + 512:OTB["lnxo1"] + 1536],
                 self.dram["Wog2"].ap()[:, :], [], [("wog2",)],
                 tag=P + "wog2l", eng="pool")
        self.DMA(self.sb["small"][:, 24:32], self.dram["norm_w"].ap(),
                 [("sm", 24)], [("sm", 24)], tag=P + "nwl")

        for tt in range(4):
            par = tt % 2
            ot = self.ob_("ot0" if par == 0 else "ot1", 4096)
            otk = ("arb", "ot", par)
            lnxo = self.ob_("lnxo0", 4096)
            lnk = ("arb", "lnxo")
            own0 = (tt * 512)
            oth = self.ob_("ot0" if par == 1 else "ot1", 4096)
            othk = ("arb", "ot", 1 - par)
            self.DMA(ot.rearrange("p (a m) -> p a m", a=8),
                     self.dram["ag_o_out"].ap()[:, own0:own0 + 512]
                     .rearrange("(a p) m -> p a m", p=128),
                     [otk, ("agoout",)], [otk], tag=P + f"otl{tt}")
            self.DMA(oth.rearrange("p (a m) -> p a m", a=8),
                     self.dram["ag_o_out"].ap()[:, 2048 + own0:2048 + own0 + 512]
                     .rearrange("(a p) m -> p a m", p=128),
                     [othk, ("agoout",)], [othk], tag=P + f"oth{tt}")
            self.TT(oth, oth, ot, ALU.subtract, [otk, othk], [othk],
                    tag=P + f"osub{tt}")
            self.STT(ot, oth, self.sb["hgm"][:, 0:1], ot, ALU.mult, ALU.add,
                     [othk, ("hgm",), otk], [otk], tag=P + f"oblend{tt}")
            self.DMA(lnxo.rearrange("p (a m) -> p a m", a=8),
                     self.dram["ag_lnx_in"].ap()[:, own0:own0 + 512]
                     .rearrange("(a p) m -> p a m", p=128),
                     [lnk] + [("agin", tt, f) for f in range(8)], [lnk],
                     tag=P + f"lnxl{tt}")
            for kc in range(8):
                self.MM(pb7[0:64, :], self.ob_("lnxo1", 64, kc * 64),
                        lnxo[:, kc * 512:(kc + 1) * 512], kc == 0, kc == 7,
                        [lnk, ("wog1", kc), ("psum", 7)], [("psum", 7)],
                        tag=P + f"og1m{tt}_{kc}")
            og1 = self.sb["arb"][0:64, OTB["ogt"]:OTB["ogt"] + 512]
            self.COPY(og1, pb7[0:64, :], [("psum", 7)],
                      [("arb", "ogt"), ("psum", 7)], tag=P + f"og1e{tt}")
            for oc in range(8):
                bnk = 2 + (oc % 2)
                self.MM(pb[bnk][:, :],
                        self.sb["arb"][0:64, OTB["lnxo1"] + 512 + oc * 128:
                                       OTB["lnxo1"] + 512 + (oc + 1) * 128],
                        og1, True, True,
                        [("arb", "ogt"), ("wog2",), ("psum", bnk)],
                        [("psum", bnk)], tag=P + f"og2m{tt}_{oc}")
                t = self.af_("tmp", 512) if False else self.of_("mu", 512)
                tk = ("arf", "mu")
                self.ACTV(t, pb[bnk][:, :], AF.Tanh, [("psum", bnk)], [tk],
                          tag=P + f"ogth{tt}_{oc}", scale=0.5)
                self.TS(t, t, 0.5, 0.5, ALU.mult, ALU.add, [tk], [tk],
                        tag=P + f"ogsg{tt}_{oc}")
                ov = self.of_("ov", 512, oc * 512)
                self.TT(ov, ot[:, oc * 512:(oc + 1) * 512], t, ALU.mult,
                        [otk, tk], [("arf", "ov", oc)], tag=P + f"ov{tt}_{oc}")
            for oc in range(8):
                ovb = self.rb("xb0" if oc % 2 == 0 else "xb1", 512)
                ovbk = ("arb", "xb", oc % 2)
                ov = self.of_("ov", 512, oc * 512)
                self.COPY(ovb, ov, [("arf", "ov", oc)], [ovbk],
                          tag=P + f"ovb{tt}_{oc}")
                x2 = self.rb("x20" if oc % 2 == 0 else "x21", 512)
                x2k = ("arb", "x2", oc % 2)
                self.TT(x2, ovb, ovb, ALU.mult, [ovbk], [x2k],
                        tag=P + f"ovsq{tt}_{oc}")
                self.MM(pb[0][:, :], ones128[:], ovb, oc == 0, oc == 7,
                        [ovbk, ("c", "ones128"), ("psum", 0)], [("psum", 0)],
                        tag=P + f"osum{tt}_{oc}")
                self.MM(pb[1][:, :], ones128[:], x2, oc == 0, oc == 7,
                        [x2k, ("c", "ones128"), ("psum", 1)], [("psum", 1)],
                        tag=P + f"osumsq{tt}_{oc}")
            mu = self.of_("mu", 512)
            var = self.of_("var", 512)
            rstd = self.of_("rstd", 512)
            mu2 = self.of_("mu2", 512)
            self.TS(mu, pb[0][:, :], 1.0 / D, None, ALU.mult, None,
                    [("psum", 0)], [("arf", "mu"), ("psum", 0)], tag=P + f"mu{tt}")
            self.TS(var, pb[1][:, :], 1.0 / D, None, ALU.mult, None,
                    [("psum", 1)], [("arf", "var"), ("psum", 1)],
                    tag=P + f"msq{tt}")
            self.TT(mu2, mu, mu, ALU.mult, [("arf", "mu")], [("arf", "mu2")],
                    tag=P + f"mu2{tt}")
            self.TT(var, var, mu2, ALU.subtract,
                    [("arf", "var"), ("arf", "mu2")], [("arf", "var")],
                    tag=P + f"var{tt}")
            self.ACTV(rstd, var, AF.Sqrt, [("arf", "var"), ("c", "eps")],
                      [("arf", "rstd")], tag=P + f"sd{tt}",
                      bias=self.sb["cs_eps"][:, 0:1])
            self.RECIP(rstd, rstd, [("arf", "rstd")], [("arf", "rstd")],
                       tag=P + f"rstd{tt}")
            for oc in range(8):
                ov = self.of_("ov", 512, oc * 512)
                self.TT(ov, ov, mu, ALU.subtract,
                        [("arf", "ov", oc), ("arf", "mu")], [("arf", "ov", oc)],
                        tag=P + f"on1{tt}_{oc}")
                self.TT(ov, ov, rstd, ALU.mult,
                        [("arf", "ov", oc), ("arf", "rstd")],
                        [("arf", "ov", oc)], tag=P + f"on2{tt}_{oc}")
                oln = self.ob_("oln", 512, oc * 512)
                self.ACTV(oln, ov, AF.Copy,
                          [("arf", "ov", oc), ("sm", 24)],
                          [("arb", "oln", oc)], tag=P + f"onw{tt}_{oc}",
                          scale=self.sb["small"][:, 24 + oc:24 + oc + 1])
            for oc2 in range(8):
                bnk = 2 + (oc2 % 2)
                for kc in range(8):
                    self.MM(pb[bnk][:, :],
                            self.ob_("wo", 128, kc * 1024 + oc2 * 128),
                            self.ob_("oln", 512, kc * 512), kc == 0, kc == 7,
                            [("arb", "oln", kc), ("wo", kc), ("psum", bnk)],
                            [("psum", bnk)], tag=P + f"wom{tt}_{oc2}_{kc}")
                xsl = xs[:, oc2 * TOK + tt * 512: oc2 * TOK + tt * 512 + 512]
                self.TT(xsl, pb[bnk][:, :], xsl, ALU.add,
                        [("psum", bnk), ("x", oc2)],
                        [("x", oc2), ("psum", bnk)], tag=P + f"ores{tt}_{oc2}")
    def ab_(self, reg, width, off=0):
        return self.sb["arb"][:, ATB[reg] + off: ATB[reg] + off + width]

    def ob_(self, reg, width, off=0):
        return self.sb["arb"][:, OTB[reg] + off: OTB[reg] + off + width]

    def af_(self, reg, width, off=0):
        return self.sb["arf"][:, ATF[reg] + off: ATF[reg] + off + width]

    def of_(self, reg, width, off=0):
        return self.sb["arf"][:, OTF[reg] + off: OTF[reg] + off + width]

    def emit_attn(self):
        S = self.S
        xs = self.sb["xs"]
        arb = self.sb["arb"]
        arf = self.sb["arf"]
        sm = self.sb["small"]
        pb = [self.ps[f"pb{i}"] for i in range(6)]
        pb7 = self.ps["pb7"]
        ptb = self.ps["ptb"]
        ones128 = self.sb["cs_ones128"]
        hg_tokens = TOK

        S.barrier()
        P = "at"

        for tt in range(4):
            for f in range(8):
                xsl = xs[:, f * TOK + tt * 512: f * TOK + tt * 512 + 512]
                xb = self.rb("xb0" if f % 2 == 0 else "xb1", 512)
                x2 = self.rb("x20" if f % 2 == 0 else "x21", 512)
                xbk = ("arb", "xb", f % 2)
                x2k = ("arb", "x2", f % 2)
                self.COPY(xb, xsl, [("x", f)], [xbk], tag=P + f"cast{tt}_{f}")
                self.TT(x2, xb, xb, ALU.mult, [xbk], [x2k], tag=P + f"sq{tt}_{f}")
                self.MM(pb[0][:, :], ones128[:], xb, f == 0, f == 7,
                        [xbk, ("c", "ones128"), ("psum", 0)], [("psum", 0)],
                        tag=P + f"sums{tt}_{f}")
                self.MM(pb[1][:, :], ones128[:], x2, f == 0, f == 7,
                        [x2k, ("c", "ones128"), ("psum", 1)], [("psum", 1)],
                        tag=P + f"sumsq{tt}_{f}")
            mu, var, rstd, mu2 = (self.rf(r) for r in ("mu", "var", "rstd", "mu2"))
            self.TS(mu, pb[0][:, :], 1.0 / D, None, ALU.mult, None,
                    [("psum", 0)], [("arf", "mu"), ("psum", 0)], tag=P + f"mu{tt}")
            self.TS(var, pb[1][:, :], 1.0 / D, None, ALU.mult, None,
                    [("psum", 1)], [("arf", "var"), ("psum", 1)], tag=P + f"msq{tt}")
            self.TT(mu2, mu, mu, ALU.mult, [("arf", "mu")], [("arf", "mu2")],
                    tag=P + f"mu2{tt}")
            self.TT(var, var, mu2, ALU.subtract,
                    [("arf", "var"), ("arf", "mu2")], [("arf", "var")],
                    tag=P + f"var{tt}")
            self.ACTV(rstd, var, AF.Sqrt, [("arf", "var"), ("c", "eps")],
                      [("arf", "rstd")], tag=P + f"sd{tt}",
                      bias=self.sb["cs_eps"][:, 0:1])
            self.RECIP(rstd, rstd, [("arf", "rstd")], [("arf", "rstd")],
                       tag=P + f"rstd{tt}")
            for f in range(8):
                xsl = xs[:, f * TOK + tt * 512: f * TOK + tt * 512 + 512]
                lnt = self.rf("lnt0" if f % 2 == 0 else "lnt1")
                lk = ("arf", "lnt", f % 2)
                self.TT(lnt, xsl, mu, ALU.subtract, [("x", f), ("arf", "mu")],
                        [lk], tag=P + f"lsub{tt}_{f}")
                self.TT(lnt, lnt, rstd, ALU.mult, [lk, ("arf", "rstd")], [lk],
                        tag=P + f"lmul{tt}_{f}")
                self.DMA(self.dram["ag_lnx_in"].ap()[f * 128:(f + 1) * 128,
                                                     tt * 512:(tt + 1) * 512],
                         lnt, [lk], [("agin", tt, f), lk],
                         tag=P + f"lnst{tt}_{f}", eng="pool")

        if self.attn_cut < 1:
            return
        self.CC("AllGather", ALU.bypass, [[0, 1], [2, 3], [4, 5], [6, 7]],
                [self.dram["ag_lnx_in"].ap()], [self.dram["ag_lnx_out"].ap()],
                [("agin", tt, f) for tt in range(4) for f in range(8)],
                [("aglnx",)], tag=P + "aglnx")
        if self.attn_cut < 2:
            return

        for kc in range(8):
            self.DMA(self.ab_("aw", 512, kc * 512),
                     self.dram["Wq"].ap()[kc * 128:(kc + 1) * 128, :],
                     [], [("aw", "q", kc)], tag=P + f"wql{kc}", eng="pool")
            self.DMA(self.ab_("aw", 512, 4096 + kc * 512),
                     self.dram["Wk"].ap()[kc * 128:(kc + 1) * 128, :],
                     [], [("aw", "k", kc)], tag=P + f"wkl{kc}", eng="pool")
            self.DMA(self.ab_("awf", 64, kc * 64),
                     self.dram["Wf1"].ap()[kc * 128:(kc + 1) * 128, :],
                     [], [("aw", "f1", kc)], tag=P + f"wf1l{kc}", eng="pool")
            self.DMA(self.ab_("awf", 8, 1024 + kc * 8),
                     self.dram["Wgam"].ap()[kc * 128:(kc + 1) * 128, :],
                     [], [("aw", "g", kc)], tag=P + f"wgl{kc}", eng="pool")
        self.DMA(self.sb["arb"][0:64, ATB["awf"] + 512: ATB["awf"] + 1024],
                 self.dram["Wf2"].ap()[:, :], [], [("aw", "f2")],
                 tag=P + "wf2l", eng="pool")

        self.MEMSET(self.ab_("s", 256), 0.0,
                    [("arb", "s", h) for h in range(8)], tag=P + "s0")

        if self.attn_cut < 3:
            return
        for g in range(min(NG, self.attn_cut - 2)):
            self.emit_attn_group(g)
        if self.attn_cut < 11:
            return

        self.CC("AllGather", ALU.bypass, [[0, 1], [2, 3], [4, 5], [6, 7]],
                [self.dram["o_spill"].ap()], [self.dram["ag_o_out"].ap()],
                [("osp", g, pt) for g in range(NG) for pt in range(4)],
                [("agoout",)], tag=P + "ago")

        if self.attn_cut < 12:
            return
        S.barrier()
        self.emit_attn_output()
        S.barrier()



def _prep_core_inputs(c, inputs, const_arrays):
    b, hg = c // 2, c % 2
    x = inputs["L0_attn_x"]
    xslice = np.asarray(x[b, hg * TOK:(hg + 1) * TOK, :], dtype=np.float32)
    mycols = slice(hg * 512, hg * 512 + 512)
    m = {
        "x_fm": np.ascontiguousarray(xslice.T),
        "wg_nat": np.ascontiguousarray(inputs["L0_ffn_w_gate"], dtype=np.float32),
        "wu_nat": np.ascontiguousarray(inputs["L0_ffn_w_up"], dtype=np.float32),
        "wd_nat": np.ascontiguousarray(inputs["L0_ffn_w_down"], dtype=np.float32),
        "clip_x": np.asarray(inputs["L0_ffn_clip_x"], dtype=np.float32).reshape(1, 1),
        "clip_down": np.asarray(inputs["L0_ffn_clip_down"], dtype=np.float32).reshape(1, 1),
        "Wq": np.ascontiguousarray(np.asarray(inputs["L0_attn_Wq"])[:, mycols], dtype=np.float32),
        "Wk": np.ascontiguousarray(np.asarray(inputs["L0_attn_Wk"])[:, mycols], dtype=np.float32),
        "Wv": np.ascontiguousarray(np.asarray(inputs["L0_attn_Wv"])[:, mycols], dtype=np.float32),
        "Wgam": np.ascontiguousarray(np.asarray(inputs["L0_attn_Wgamma"])[:, hg * NH:(hg + 1) * NH], dtype=np.float32),
        "Wf1": np.ascontiguousarray(inputs["L0_attn_Wf1"], dtype=np.float32),
        "Wf2": np.ascontiguousarray(np.asarray(inputs["L0_attn_Wf2"])[:, mycols], dtype=np.float32),
        "Wog1": np.ascontiguousarray(inputs["L0_attn_Wog1"], dtype=np.float32),
        "Wog2": np.ascontiguousarray(inputs["L0_attn_Wog2"], dtype=np.float32),
        "Wo": np.ascontiguousarray(inputs["L0_attn_Wo"], dtype=np.float32),
        "norm_w": np.ascontiguousarray(np.asarray(inputs["L0_attn_norm_w"], dtype=np.float32).reshape(8, 128).T),
        "hgmask": np.full((128, 1), float(hg), dtype=np.float32),
    }
    m.update(const_arrays)
    return m


_BUILD_CACHE = {}


def _get_built(phases, dumps=(), attn_cut=99, gcut=99):
    key = (tuple(phases), tuple(n for n, _, _ in dumps), attn_cut, gcut)
    if key not in _BUILD_CACHE:
        b = Builder(phases=phases, dumps=dumps, attn_cut=attn_cut, gcut=gcut)
        b.build()
        with (b.ctx, b.nc.Block() as block):
            sems = {}
            es = ExitStack()
            for e in ENGINES:
                sems[e] = es.enter_context(b.nc.semaphore("s_" + e))
            dsems = [es.enter_context(b.nc.semaphore(f"dp{i}"))
                     for i in range(64)]
            csems = [es.enter_context(b.nc.semaphore(f"cc{i}"))
                     for i in range(8)]
            b.S.emit(block, sems, dsems, csems)
        _BUILD_CACHE[key] = b
    return _BUILD_CACHE[key]


def kernel(**inputs):
    b = _get_built(("ffn1", "attn", "ffn2"))
    in_maps = [_prep_core_inputs(c, inputs, b.const_arrays) for c in range(8)]
    res = run_bass_kernel_spmd(b.nc, in_maps, core_ids=list(range(8)))
    out = np.empty((4, TFULL, D), dtype=np.float32)
    for c in range(8):
        bb, hg = c // 2, c % 2
        out[bb, hg * TOK:(hg + 1) * TOK, :] = res.results[c]["out_fm"].T
    return out


# version 13
# speedup vs baseline: 1.1157x; 1.0071x over previous
import numpy as np
import ml_dtypes
from contextlib import ExitStack

import concourse.bass as bass
import concourse.mybir as mybir
from concourse.bass_utils import run_bass_kernel_spmd

F32 = mybir.dt.float32
BF16 = mybir.dt.bfloat16
I8 = mybir.dt.int8
AF = mybir.ActivationFunctionType
ALU = mybir.AluOpType
AX = mybir.AxisListType.X

D = 1024
TOK = 2048
TFULL = 4096
NH = 8
HD = 64
II = 2816
NIC = II // 128
CH = 64
G = 512
NG = TFULL // G
QMAX = 127.0

ENGINES = ("te", "dve", "act", "pool", "sp")
SELF_FENCE = True


class Sched:

    def __init__(self):
        self.steps = []
        self.eng_steps = {e: [] for e in ENGINES}
        self.writers = {}
        self.readers = {}

    def step(self, eng, fn, reads=(), writes=(), dma=False, tag=""):
        idx = len(self.steps)
        deps = set()
        for k in reads:
            w = self.writers.get(k)
            if w is not None:
                deps.add(w)
            self.readers.setdefault(k, []).append(idx)
        for k in writes:
            w = self.writers.get(k)
            if w is not None:
                deps.add(w)
            for r in self.readers.get(k, ()):
                deps.add(r)
            self.readers[k] = []
            self.writers[k] = idx
        deps.discard(idx)
        st = dict(idx=idx, eng=eng, fn=fn, deps=deps, dma=dma, tag=tag,
                  acc=False)
        self.steps.append(st)
        self.eng_steps[eng].append(st)
        return idx

    _bar_n = 0

    def barrier(self):
        n = Sched._bar_n = Sched._bar_n + 1
        for e in ENGINES:
            self.step(e, lambda en: en.nop(), reads=(),
                      writes=[("__barA__", n, e)], tag=f"barA{n}")
        for e in ENGINES:
            self.step(e, lambda en: en.nop(),
                      reads=[("__barA__", n, x) for x in ENGINES],
                      writes=[("__barB__", n, e)], tag=f"barB{n}")

    def emit(self, block, sems, dma_sems, cc_sems):
        lastmm = {}
        for st in self.steps:
            bank = st.get("mm_bank")
            if bank is None:
                continue
            prev = lastmm.get(bank)
            if prev is not None and prev["mm_rg"] != st["mm_rg"]:
                st["deps"].add(prev["idx"])
                st.setdefault("rgdeps", set()).add(prev["idx"])
            lastmm[bank] = st
        NP = len(dma_sems)
        ncc = 0
        for st in self.steps:
            st["sig"] = False
            if st["acc"]:
                st["cc_sem"] = ncc
                ncc += 1
        assert ncc <= len(cc_sems)
        for e in ENGINES:
            for i, st in enumerate(self.eng_steps[e]):
                st["epos"] = i
        for st in self.steps:
            rgd = st.get("rgdeps", ())
            best = {}
            for d in st["deps"]:
                dep = self.steps[d]
                if dep["dma"] or dep["acc"]:
                    continue
                if dep["eng"] == st["eng"] == "te" and d not in rgd:
                    continue
                b = best.get(dep["eng"])
                if b is None or dep["epos"] > b["epos"]:
                    best[dep["eng"]] = dep
            st["best"] = best
            for dep in best.values():
                dep["sig"] = True
        use = [0] * NP
        nsw = NP * 3 // 4
        ndma = {"sp": 0, "act": 0, "pool": 0, "te": 0, "dve": 0}
        for st in self.steps:
            if st["dma"]:
                if st["eng"] == "pool":
                    p = nsw + ndma["pool"] % (NP - nsw)
                else:
                    p = ndma[st["eng"]] % nsw
                ndma[st["eng"]] += 1
                st["dma_sem"] = p
                st["dma_fence"] = use[p] * 16 if use[p] else None
                use[p] += 1
                st["dma_val"] = use[p] * 16
                st["sig_val"] = None
        for e in ENGINES:
            n = 0
            for st in self.eng_steps[e]:
                if st["dma"] or st["acc"]:
                    st["sig_val"] = None
                    continue
                st["sig_val"] = (n := n + 1) if st["sig"] else None
        for e in ENGINES:
            know = {x: 0 for x in ENGINES}
            know_d = [0] * NP
            for st in self.eng_steps[e]:
                waits, dwaits, cwaits = {}, {}, {}
                for d in sorted(st["deps"]):
                    dep = self.steps[d]
                    if dep["dma"]:
                        p = dep["dma_sem"]
                        if dep["dma_val"] > know_d[p]:
                            dwaits[p] = max(dwaits.get(p, 0), dep["dma_val"])
                    elif dep["acc"]:
                        cwaits[dep["cc_sem"]] = 1
                for de, dep in st["best"].items():
                    if dep["sig_val"] > know[de]:
                        waits[de] = max(waits.get(de, 0), dep["sig_val"])
                st["waits"] = waits
                st["dwaits"] = dwaits
                st["cwaits"] = cwaits
                for de, v in waits.items():
                    know[de] = max(know[de], v)
                for p, v in dwaits.items():
                    know_d[p] = max(know_d[p], v)
                for d in st["deps"]:
                    dep = self.steps[d]
                    if dep["eng"] == e or dep["dma"]:
                        continue
                    for de, v in dep.get("waits", {}).items():
                        if de != e:
                            know[de] = max(know[de], v)
                    for p, v in dep.get("dwaits", {}).items():
                        know_d[p] = max(know_d[p], v)

        def emit_engine(e, eng):
            for st in self.eng_steps[e]:
                if (SELF_FENCE and st["sig"] and st["sig_val"]
                        and st["sig_val"] > 1):
                    eng.wait_ge(sems[e], st["sig_val"] - 1)
                for de, v in sorted(st["waits"].items()):
                    eng.wait_ge(sems[de], v)
                for p, v in sorted(st["dwaits"].items()):
                    eng.wait_ge(dma_sems[p], v)
                for p, v in sorted(st.get("cwaits", {}).items()):
                    eng.wait_ge(cc_sems[p], v)
                if st["dma"] and st["dma_fence"] is not None:
                    eng.wait_ge(dma_sems[st["dma_sem"]], st["dma_fence"])
                inst = st["fn"](eng)
                if st["dma"]:
                    assert inst is not None, st["tag"]
                    inst.then_inc(dma_sems[st["dma_sem"]], 16)
                elif st["acc"]:
                    assert inst is not None, st["tag"]
                    inst.then_inc(cc_sems[st["cc_sem"]], 1)
                elif st["sig"]:
                    if inst is None:
                        inst = eng.nop()
                    inst.then_inc(sems[e], 1)

        @block.tensor
        def _(eng):
            emit_engine("te", eng)

        @block.vector
        def _(eng):
            emit_engine("dve", eng)

        @block.scalar
        def _(eng):
            emit_engine("act", eng)

        @block.gpsimd
        def _(eng):
            emit_engine("pool", eng)

        @block.sync
        def _(eng):
            emit_engine("sp", eng)


def _bind(fn, *a):
    return lambda eng: fn(eng, *a)


ARF_COLS = 11264
ARF = dict(mu=0, var=512, rstd=1024, mu2=1536, lnt0=2048, lnt1=2560,
           sg0=3072, sg1=3584, gg0=4096, gg1=4608, gld0=5120, gld1=5632,
           natb0=6144, natb1=7168, dnat=8192)
ARB_COLS = 44800
ARB = dict(xq=0,
           ws=16384,
           qb0=20480, qb1=21504,
           dwl0=22528, dwl1=23040,
           hq=23552,
           xb0=34816, xb1=35328, x20=35840, x21=36352,
           free=36864)
ATB = dict(aw=0,
           awf=8192,
           wv=9344,
           lng=13440,
           q=17536,
           k=19584,
           kt=21632,
           ab=23680,
           qb=25728,
           gr=27776,
           pw=29824,
           kht=32896,
           u=34944,
           uc=35456,
           g8=35584,
           s=36096,
           og=36352,
           pw2=38400,
           pw3=41472,
           uc2=44544,
           end=44800)
OTB = dict(wo=0,
           ot0=8192, ot1=12288,
           lnxo0=16384,
           lnxo1=20480,
           ogt=24576,
           oln=25088)
ATF = dict(zf=0,
           P=2048,
           Pinv=4096,
           vg=6144,
           tmp=10240)
OTF = dict(ov=0,
           mu=4096, var=4608, rstd=5120, mu2=5632)


class Builder:
    def __init__(self, phases=("ffn1", "attn", "ffn2"), dumps=(), attn_cut=99,
                 gcut=99):
        self.attn_cut = attn_cut
        self.gcut = gcut
        self.phases = phases
        self.dumps = list(dumps)
        self.nc = bass.Bass("TRN2", target_bir_lowering=False, debug=False)
        self.S = Sched()
        self.ctx = ExitStack()
        self.sb = {}
        self.ps = {}
        self.dram = {}
        self.const_arrays = {}

    def sbuf(self, name, shape, dtype):
        t = self.ctx.enter_context(self.nc.sbuf_tensor(name, shape, dtype))
        self.sb[name] = t
        return t

    def psum(self, name, shape, dtype):
        t = self.ctx.enter_context(self.nc.psum_tensor(name, shape, dtype))
        self.ps[name] = t
        return t

    def din(self, name, shape, dtype):
        t = self.nc.dram_tensor(name, shape, dtype, kind="ExternalInput")
        self.dram[name] = t
        return t

    def dout(self, name, shape, dtype):
        t = self.nc.dram_tensor(name, shape, dtype, kind="ExternalOutput")
        self.dram[name] = t
        return t

    def dtmp(self, name, shape, dtype):
        t = self.nc.dram_tensor(name, shape, dtype)
        self.dram[name] = t
        return t

    def const(self, name, arr):
        if name not in self.const_arrays:
            dt = {np.dtype(np.float32): F32,
                  np.dtype(ml_dtypes.bfloat16): BF16}[arr.dtype]
            self.din("c_" + name, list(arr.shape), dt)
            self.const_arrays["c_" + name] = np.ascontiguousarray(arr)
            self.sbuf("cs_" + name, list(arr.shape), dt)
            self.DMA(self.sb["cs_" + name][:], self.dram["c_" + name].ap(),
                     [], [("c", name)], tag="ldc_" + name)
        return self.sb["cs_" + name]

    def rf(self, reg, width=512, off=0):
        c = ARF[reg] + off
        return self.sb["arf"][:, c:c + width]

    def rb(self, reg, width, off=0):
        c = ARB[reg] + off
        return self.sb["arb"][:, c:c + width]

    def TT(self, out, a, b, op, reads, writes, tag=""):
        self.S.step("dve", _bind(
            lambda e, out, a, b: e.tensor_tensor(out=out, in0=a, in1=b, op=op),
            out, a, b), reads, writes, tag=tag)

    def TS(self, out, a, s1, s2, op0, op1, reads, writes, tag="", eng="dve"):
        if s2 is None:
            self.S.step(eng, _bind(
                lambda e, out, a, s1: e.tensor_scalar(
                    out=out, in0=a, scalar1=s1, scalar2=None, op0=op0),
                out, a, s1), reads, writes, tag=tag)
        else:
            self.S.step(eng, _bind(
                lambda e, out, a, s1, s2: e.tensor_scalar(
                    out=out, in0=a, scalar1=s1, scalar2=s2, op0=op0, op1=op1),
                out, a, s1, s2), reads, writes, tag=tag)

    def STT(self, out, a, s, b, op0, op1, reads, writes, tag="", eng="dve"):
        self.S.step(eng, _bind(
            lambda e, out, a, s, b: e.scalar_tensor_tensor(
                out=out, in0=a, scalar=s, in1=b, op0=op0, op1=op1),
            out, a, s, b), reads, writes, tag=tag)

    def COPY(self, out, a, reads, writes, tag="", eng="dve"):
        self.S.step(eng, _bind(
            lambda e, out, a: e.tensor_copy(out=out, in_=a), out, a),
            reads, writes, tag=tag)

    def MEMSET(self, ap, v, writes, tag="", eng="dve"):
        self.S.step(eng, _bind(lambda e, ap, v: e.memset(ap, v), ap, v),
                    (), writes, tag=tag)

    def RED(self, out, a, op, reads, writes, tag="", absval=False):
        self.S.step("dve", _bind(
            lambda e, out, a: e.tensor_reduce(
                out=out, in_=a, axis=AX, op=op, apply_absolute_value=absval),
            out, a), reads, writes, tag=tag)

    def RECIP(self, out, a, reads, writes, tag=""):
        self.S.step("dve", _bind(
            lambda e, out, a: e.reciprocal(out=out, in_=a), out, a),
            reads, writes, tag=tag)

    def SCAN(self, out, d0, d1, init, op0, op1, reads, writes, tag=""):
        self.S.step("dve", _bind(
            lambda e, out, d0, d1: e.tensor_tensor_scan(
                out=out, data0=d0, data1=d1, initial=init, op0=op0, op1=op1),
            out, d0, d1), reads, writes, tag=tag)

    def ACTV(self, out, a, func, reads, writes, tag="", bias=0.0, scale=1.0):
        self.S.step("act", _bind(
            lambda e, out, a, bias, scale: e.activation(
                out=out, in_=a, func=func, bias=bias, scale=scale),
            out, a, bias, scale), reads, writes, tag=tag)

    def MM(self, out, lhsT, rhs, start, stop, reads, writes, tag="", tp=None):
        i = self.S.step("te", _bind(
            lambda e, out, lhsT, rhs: e.matmul(
                out, lhsT, rhs, start=start, stop=stop, tile_position=tp),
            out, lhsT, rhs), reads, writes, tag=tag)
        st = self.S.steps[i]
        st["mm_bank"] = out.tensor.name
        st["mm_rg"] = (tp[0] if tp else 0, lhsT.shape[0])

    def TPO(self, out, a, reads, writes, tag="", f32=False):
        nm = "identf" if f32 else "ident"
        ident = self.sb["cs_" + nm][:, :]
        i = self.S.step("te", _bind(
            lambda e, out, a, ident: e.transpose(out, a, ident),
            out, a, ident), list(reads) + [("c", nm)], writes, tag=tag)
        st = self.S.steps[i]
        st["mm_bank"] = out.tensor.name
        st["mm_rg"] = (0, a.shape[0])

    def DMA(self, out, a, reads, writes, tag="", eng="sp"):
        self.S.step(eng, _bind(
            lambda e, out, a: e.dma_start(out=out, in_=a), out, a),
            reads, writes, dma=True, tag=tag)

    def CC(self, kind, op, groups, ins, outs, reads, writes, tag=""):
        i = self.S.step("pool", _bind(
            lambda e, ins, outs: e.collective_compute(
                kind, op, replica_groups=groups, ins=ins, outs=outs),
            ins, outs), reads, writes, tag=tag)
        self.S.steps[i]["acc"] = True

    def build(self):
        self.din("x_fm", [D, TOK], F32)
        self.din("wg_nat", [II, D], F32)
        self.din("wu_nat", [II, D], F32)
        self.din("wd_nat", [D, II], F32)
        self.din("clip_x", [1, 1], F32)
        self.din("clip_down", [1, 1], F32)
        self.din("Wq", [D, D // 2], F32)
        self.din("Wk", [D, D // 2], F32)
        self.din("Wv", [D, D // 2], F32)
        self.din("Wgam", [D, NH], F32)
        self.din("Wf1", [D, HD], F32)
        self.din("Wf2", [HD, D // 2], F32)
        self.din("Wog1", [D, HD], F32)
        self.din("Wog2", [HD, D], F32)
        self.din("Wo", [D, D], F32)
        self.din("norm_w", [128, 8], F32)
        self.din("hgmask", [128, 1], F32)
        self.sbuf("hgm", [128, 1], F32)
        self.dout("out_fm", [D, TOK], F32)

        self.dtmp("ln_spill", [D, TOK], F32)
        self.dtmp("g_spill", [II, TOK], F32)
        self.dtmp("x1_spill", [D, TOK], F32)
        self.dtmp("wq_spill", [3 * NIC, 128, 1024], BF16)
        self.dtmp("o_spill", [D // 2, TFULL], BF16)
        self.dtmp("ar_in", [1, 128], F32)
        self.dtmp("ar_out", [1, 128], F32)
        self.dtmp("ag_lnx_in", [D, TOK], BF16)
        self.dtmp("ag_lnx_out", [2 * D, TOK], BF16)
        self.dtmp("ag_o_out", [D, TFULL], BF16)

        self.sbuf("xs", [128, 8 * TOK], F32)
        self.sbuf("arf", [128, ARF_COLS], F32)
        self.sbuf("arb", [128, ARB_COLS], BF16)
        self.sbuf("stats", [128, 256], F32)
        self.sbuf("small", [128, 64], F32)
        self.sbuf("srow", [1, 512], F32)

        for i in range(6):
            self.psum(f"pb{i}", [128, 512], F32)
        self.psum("ptb", [128, 1024], BF16)
        self.psum("pb7", [128, 512], F32)

        self.const("ones128", np.ones((128, 128), dtype=ml_dtypes.bfloat16))
        bo = np.zeros((128, 128), dtype=np.float32)
        bo[0:64, 0:64] = 1.0
        bo[64:128, 64:128] = 1.0
        self.const("blockones", bo.astype(ml_dtypes.bfloat16))
        sel = np.zeros((8, 512), dtype=np.float32)
        for h in range(8):
            for pcol in range(512):
                if h == 2 * (pcol // 128) + (pcol % 128) // 64:
                    sel[h, pcol] = 1.0
        self.const("sel8", sel.astype(ml_dtypes.bfloat16))
        pp, ff = np.meshgrid(np.arange(128), np.arange(512), indexing="ij")
        self.const("mstril", ((ff % 64) < (pp % 64)).astype(ml_dtypes.bfloat16))
        self.const("mstriu", ((ff % 64) > (pp % 64)).astype(ml_dtypes.bfloat16))
        self.const("mtriuincl", ((ff % 64) >= (pp % 64)).astype(ml_dtypes.bfloat16))
        self.const("identblk", ((ff % 64) == (pp % 64)).astype(ml_dtypes.bfloat16))
        self.const("zero64", np.zeros((128, 64), dtype=np.float32))
        self.const("ones1f", np.ones((1, 128), dtype=np.float32))
        self.const("ident", np.eye(128, dtype=np.float32).astype(ml_dtypes.bfloat16))
        self.const("identf", np.eye(128, dtype=np.float32))
        self.const("eps", np.full((128, 1), 1e-5, dtype=np.float32))

        xs = self.sb["xs"]
        xin = self.dram["x_fm"].ap()
        self.DMA(self.sb["hgm"][:], self.dram["hgmask"].ap(), [], [("hgm",)],
                 tag="ldhgm")
        for f in range(8):
            self.DMA(xs[:, f * TOK:(f + 1) * TOK], xin[f * 128:(f + 1) * 128, :],
                     [], [("x", f)], tag=f"ldx{f}")

        if "ffn1" in self.phases:
            self.emit_ffn(0)
        if "attn" in self.phases:
            self.emit_attn()
        if "ffn2" in self.phases:
            self.emit_ffn(1)

        outap = self.dram["out_fm"].ap()
        for f in range(8):
            self.DMA(outap[f * 128:(f + 1) * 128, :], xs[:, f * TOK:(f + 1) * TOK],
                     [("x", f)], [("ost", f)], tag=f"stx{f}")

        for name, src, reads in self.dumps:
            t = self.dout("dump_" + name, [src.shape[0], src.shape[1]], src.dtype)
            self.DMA(t.ap(), src, list(reads), [("dump", name)],
                     tag="dump_" + name)
        self.S.barrier()
        self.S.step("sp", lambda e: e.nop(),
                    reads=[("ost", f) for f in range(8)] +
                          [("dump", n) for n, _, _ in self.dumps],
                    tag="final_wait")
        return self

    def _scales_from_max(self, P, clip_ap):
        sm = self.sb["small"]
        self.TPO(self.ps["pb7"][0:1, 0:128], sm[:, 8:9], [("sm", 8)],
                 [("psum", 7)], tag=P + "mxT", f32=True)
        self.COPY(self.sb["srow"][0:1, 0:128], self.ps["pb7"][0:1, 0:128],
                  [("psum", 7)], [("srow",), ("psum", 7)], tag=P + "mxc")
        self.RED(sm[0:1, 10:11], self.sb["srow"][0:1, 0:128], ALU.max,
                 [("srow",)], [("sm", 10)], tag=P + "mxr")
        self.TS(self.sb["srow"][0:1, 128:256], self.sb["cs_ones1f"][0:1, :],
                sm[0:1, 10:11], None, ALU.mult, None,
                [("sm", 10), ("c", "ones1f")], [("srow2",)], tag=P + "mxbc")
        self.DMA(self.dram["ar_in"].ap()[0:1, :], self.sb["srow"][0:1, 128:256],
                 [("srow2",)], [("ar",)], tag=P + "arst")
        self.CC("AllReduce", ALU.max, [list(range(8))],
                [self.dram["ar_in"].ap()], [self.dram["ar_out"].ap()],
                [("ar",)], [("ar",)], tag=P + "ar")
        self.DMA(sm[0:1, 11:12], self.dram["ar_out"].ap()[0:1, 0:1],
                 [("ar",)], [("sm", 11)], tag=P + "arld")
        self.DMA(sm[0:1, 12:13], clip_ap, [], [("sm", 12)], tag=P + "clipld")
        self.ACTV(sm[0:1, 13:14], sm[0:1, 12:13], AF.Tanh,
                  [("sm", 12)], [("sm", 13)], tag=P + "tanh", scale=0.5)
        self.TS(sm[0:1, 13:14], sm[0:1, 13:14], 0.5, 0.5, ALU.mult, ALU.add,
                [("sm", 13)], [("sm", 13)], tag=P + "sig")
        self.TT(sm[0:1, 14:15], sm[0:1, 13:14], sm[0:1, 11:12], ALU.mult,
                [("sm", 13), ("sm", 11)], [("sm", 14)], tag=P + "sM")
        self.TS(sm[0:1, 15:16], sm[0:1, 14:15], 1.0 / QMAX, None, ALU.mult, None,
                [("sm", 14)], [("sm", 15)], tag=P + "s")
        self.RECIP(sm[0:1, 16:17], sm[0:1, 14:15], [("sm", 14)], [("sm", 16)],
                   tag=P + "rinv")
        self.TS(sm[0:1, 17:18], sm[0:1, 16:17], QMAX, None, ALU.mult, None,
                [("sm", 16)], [("sm", 17)], tag=P + "r")
        ones1f = self.sb["cs_ones1f"]
        self.MM(self.ps["pb7"][:, 0:1], ones1f[0:1, :], sm[0:1, 17:18],
                True, True, [("sm", 17), ("c", "ones1f"), ("psum", 7)],
                [("psum", 7)], tag=P + "bcr")
        self.COPY(sm[:, 0:1], self.ps["pb7"][:, 0:1], [("psum", 7)],
                  [("sm", 0), ("psum", 7)], tag=P + "bcrc")
        self.MM(self.ps["pb7"][:, 0:1], ones1f[0:1, :], sm[0:1, 15:16],
                True, True, [("sm", 15), ("c", "ones1f"), ("psum", 7)],
                [("psum", 7)], tag=P + "bcs")
        self.COPY(sm[:, 5:6], self.ps["pb7"][:, 0:1], [("psum", 7)],
                  [("sm", 5), ("psum", 7)], tag=P + "bcsc")

    def quant_round(self, src, rcol, rk, out, srck, outk, tag, fbuf=None,
                    fkey=None):
        if fbuf is None:
            fbuf = self.rf("var", src.shape[1])
            fkey = ("arf", "var")
        MAGIC = 12582912.0
        self.TS(src, src, rcol, MAGIC, ALU.mult, ALU.add, [srck, rk], [srck],
                tag=tag + "a")
        self.TS(src, src, -MAGIC, None, ALU.add, None, [srck], [srck],
                tag=tag + "b")
        self.TS(out, src, 127.0, -128.0, ALU.min, ALU.max, [srck], [outk],
                tag=tag + "d")

    def emit_ffn(self, kk):
        xs = self.sb["xs"]
        arb = self.sb["arb"]
        stats = self.sb["stats"]
        sm = self.sb["small"]
        pb = [self.ps[f"pb{i}"] for i in range(6)]
        ptb = self.ps["ptb"]
        ones128 = self.sb["cs_ones128"]
        P = f"f{kk}"
        wsp = self.dram["wq_spill"].ap()

        if kk == 0:
            for widx, nm in ((0, "wg_nat"), (1, "wu_nat")):
                nat = self.dram[nm].ap()
                for ic in range(NIC):
                    par = ic % 2
                    natb = self.rf("natb0" if par == 0 else "natb1", 1024)
                    natk = ("arf", "natb", par)
                    self.DMA(natb, nat[ic * 128:(ic + 1) * 128, :],
                             [natk], [natk], tag=f"natld{widx}_{ic}")
                    self.RED(sm[:, 20:21], natb, ALU.max, [natk], [("sm", 20)],
                             tag=f"wmax{widx}_{ic}", absval=True)
                    self.TS(stats[:, widx * 32 + ic: widx * 32 + ic + 1],
                            sm[:, 20:21], 1.0 / QMAX, None, ALU.mult, None,
                            [("sm", 20)], [("stats", widx * 32 + ic)],
                            tag=f"wsc{widx}_{ic}")
                    self.RECIP(sm[:, 21:22], sm[:, 20:21], [("sm", 20)],
                               [("sm", 21)], tag=f"wrc{widx}_{ic}")
                    self.TS(sm[:, 22:23], sm[:, 21:22], QMAX, None, ALU.mult,
                            None, [("sm", 21)], [("sm", 22)],
                            tag=f"wr2{widx}_{ic}")
                    qb = self.rb("qb0" if par == 0 else "qb1", 1024)
                    qbk = ("arb", "qb", par)
                    self.quant_round(natb, sm[:, 22:23], ("sm", 22), qb,
                                     natk, qbk, f"wq{widx}_{ic}",
                                     fbuf=self.rf("dnat", 1024),
                                     fkey=("arf", "dnat"))
                    for f in range(8):
                        self.TPO(ptb[:, f * 128:(f + 1) * 128],
                                 qb[:, f * 128:(f + 1) * 128],
                                 [qbk], [("ptb", f)], tag=f"wt{widx}_{ic}_{f}")
                    stt = self.rb("ws", 1024, par * 2048)
                    sttk = ("arb", "ws", par, 0)
                    self.COPY(stt, ptb[:, :], [("ptb", f) for f in range(8)],
                              [sttk] + [("ptb", f) for f in range(8)],
                              tag=f"wte{widx}_{ic}")
                    self.DMA(wsp[widx * NIC + ic, :, :], stt, [sttk],
                             [("wq", widx, ic)], tag=f"wsp{widx}_{ic}",
                             eng="act")
            nat = self.dram["wd_nat"].ap()
            for oc in range(8):
                natb = self.rf("dnat", 2816)
                natk = ("arf", "dnat")
                self.DMA(natb, nat[oc * 128:(oc + 1) * 128, :], [natk], [natk],
                         tag=f"dnatld{oc}", eng="act")
                self.RED(sm[:, 20:21], natb, ALU.max, [natk], [("sm", 20)],
                         tag=f"dmax{oc}", absval=True)
                self.TS(stats[:, 64 + oc: 64 + oc + 1], sm[:, 20:21],
                        1.0 / QMAX, None, ALU.mult, None, [("sm", 20)],
                        [("stats", 64 + oc)], tag=f"dsc{oc}")
                self.RECIP(sm[:, 21:22], sm[:, 20:21], [("sm", 20)],
                           [("sm", 21)], tag=f"drc{oc}")
                self.TS(sm[:, 22:23], sm[:, 21:22], QMAX, None, ALU.mult, None,
                        [("sm", 21)], [("sm", 22)], tag=f"dr2{oc}")
                for part, w in ((0, 1024), (1024, 1024), (2048, 768)):
                    qb = self.rb("qb0", 1024)
                    qbk = ("arb", "qb", 0)
                    self.quant_round(natb[:, part:part + w], sm[:, 22:23],
                                     ("sm", 22), qb[:, 0:w], natk, qbk,
                                     f"dq{oc}_{part}",
                                     fbuf=self.rf("natb0", w),
                                     fkey=("arf", "natb", 0))
                    for j in range(w // 128):
                        ic = (part + j * 128) // 128
                        self.TPO(ptb[:, j * 128:(j + 1) * 128],
                                 qb[:, j * 128:(j + 1) * 128],
                                 [qbk], [("ptb", j)], tag=f"dt{oc}_{ic}")
                    stt = self.rb("ws", 1024, 0)
                    sttk = ("arb", "ws", 0, 0)
                    self.COPY(stt[:, 0:w], ptb[:, 0:w],
                              [("ptb", j) for j in range(w // 128)],
                              [sttk] + [("ptb", j) for j in range(w // 128)],
                              tag=f"dte{oc}_{part}")
                    for j in range(w // 128):
                        ic = (part + j * 128) // 128
                        self.DMA(wsp[2 * NIC + ic, :, oc * 128:(oc + 1) * 128],
                                 stt[:, j * 128:(j + 1) * 128], [sttk],
                                 [("wqd", ic, oc)], tag=f"dsp{oc}_{ic}",
                                 eng="act")

        self.MEMSET(sm[:, 8:9], 0.0, [("sm", 8)], tag=P + "mx0")
        for tt in range(4):
            for f in range(8):
                xsl = xs[:, f * TOK + tt * 512: f * TOK + tt * 512 + 512]
                xb = self.rb("xb0" if f % 2 == 0 else "xb1", 512)
                x2 = self.rb("x20" if f % 2 == 0 else "x21", 512)
                xbk = ("arb", "xb", f % 2)
                x2k = ("arb", "x2", f % 2)
                self.COPY(xb, xsl, [("x", f)], [xbk], tag=P + f"cast{tt}_{f}")
                self.TT(x2, xb, xb, ALU.mult, [xbk], [x2k], tag=P + f"sq{tt}_{f}")
                self.MM(pb[0][:, :], ones128[:], xb, f == 0, f == 7,
                        [xbk, ("c", "ones128"), ("psum", 0)], [("psum", 0)],
                        tag=P + f"sums{tt}_{f}")
                self.MM(pb[1][:, :], ones128[:], x2, f == 0, f == 7,
                        [x2k, ("c", "ones128"), ("psum", 1)], [("psum", 1)],
                        tag=P + f"sumsq{tt}_{f}")
            mu, var, rstd, mu2 = (self.rf(r) for r in ("mu", "var", "rstd", "mu2"))
            self.TS(mu, pb[0][:, :], 1.0 / D, None, ALU.mult, None,
                    [("psum", 0)], [("arf", "mu"), ("psum", 0)], tag=P + f"mu{tt}")
            self.TS(var, pb[1][:, :], 1.0 / D, None, ALU.mult, None,
                    [("psum", 1)], [("arf", "var"), ("psum", 1)], tag=P + f"msq{tt}")
            self.TT(mu2, mu, mu, ALU.mult, [("arf", "mu")], [("arf", "mu2")],
                    tag=P + f"mu2{tt}")
            self.TT(var, var, mu2, ALU.subtract,
                    [("arf", "var"), ("arf", "mu2")], [("arf", "var")],
                    tag=P + f"var{tt}")
            self.ACTV(rstd, var, AF.Sqrt, [("arf", "var"), ("c", "eps")],
                      [("arf", "rstd")], tag=P + f"sd{tt}",
                      bias=self.sb["cs_eps"][:, 0:1])
            self.RECIP(rstd, rstd, [("arf", "rstd")], [("arf", "rstd")],
                       tag=P + f"rstd{tt}")
            for f in range(8):
                xsl = xs[:, f * TOK + tt * 512: f * TOK + tt * 512 + 512]
                lnt = self.rf("lnt0" if f % 2 == 0 else "lnt1")
                lk = ("arf", "lnt", f % 2)
                self.TT(lnt, xsl, mu, ALU.subtract, [("x", f), ("arf", "mu")],
                        [lk], tag=P + f"lsub{tt}_{f}")
                self.TT(lnt, lnt, rstd, ALU.mult, [lk, ("arf", "rstd")], [lk],
                        tag=P + f"lmul{tt}_{f}")
                self.DMA(self.dram["ln_spill"].ap()[f * 128:(f + 1) * 128,
                                                    tt * 512:(tt + 1) * 512],
                         lnt, [lk], [("lnsp", tt, f)], tag=P + f"lnsp{tt}_{f}")
                self.RED(sm[:, 9:10], lnt, ALU.max, [lk], [("sm", 9)],
                         tag=P + f"lmax{tt}_{f}", absval=True)
                self.TT(sm[:, 8:9], sm[:, 8:9], sm[:, 9:10], ALU.max,
                        [("sm", 9), ("sm", 8)], [("sm", 8)],
                        tag=P + f"mxa{tt}_{f}")

        self._scales_from_max(P, self.dram["clip_x"].ap())

        for f in range(8):
            for tt in range(4):
                lnt = self.rf("lnt0" if (f + tt) % 2 == 0 else "lnt1")
                lk = ("arf", "lnt", (f + tt) % 2)
                self.DMA(lnt, self.dram["ln_spill"].ap()[f * 128:(f + 1) * 128,
                                                         tt * 512:(tt + 1) * 512],
                         [("lnsp", tt, f), lk], [lk], tag=P + f"lnld{tt}_{f}")
                self.quant_round(lnt, sm[:, 0:1], ("sm", 0),
                                 arb[:, f * TOK + tt * 512: f * TOK + tt * 512 + 512],
                                 lk, ("xq", f, tt), P + f"q{tt}_{f}")

        for widx in range(2):
            for ic in range(NIC):
                self.TT(stats[:, 128 + widx * 32 + ic: 128 + widx * 32 + ic + 1],
                        stats[:, widx * 32 + ic: widx * 32 + ic + 1],
                        sm[:, 5:6], ALU.mult,
                        [("stats", widx * 32 + ic), ("sm", 5)],
                        [("stats", 128 + widx * 32 + ic)],
                        tag=P + f"esc{widx}_{ic}")

        self.MEMSET(sm[:, 8:9], 0.0, [("sm", 8)], tag=P + "gmx0")
        for ic in range(NIC):
            par = ic % 2
            gst = self.rb("ws", 1024, par * 2048)
            ust = self.rb("ws", 1024, par * 2048 + 1024)
            gstk = ("arb", "ws", par, 0)
            ustk = ("arb", "ws", par, 1)
            self.DMA(gst, wsp[ic, :, :], [gstk, ("wq", 0, ic)], [gstk],
                     tag=P + f"gwld{ic}", eng="act")
            self.DMA(ust, wsp[NIC + ic, :, :], [ustk, ("wq", 1, ic)], [ustk],
                     tag=P + f"uwld{ic}", eng="act")
            for tt in range(4):
                bg = 2 + (tt % 2)
                bu = 4 + (tt % 2)
                for f in range(8):
                    self.MM(pb[bg][:, :], gst[:, f * 128:(f + 1) * 128],
                            arb[:, f * TOK + tt * 512: f * TOK + tt * 512 + 512],
                            f == 0, f == 7,
                            [("xq", f, tt), gstk, ("psum", bg)], [("psum", bg)],
                            tag=P + f"gm{ic}_{tt}_{f}")
                pbu = pb[bu] if bu < 6 else self.ps["pb7"]
                for f in range(8):
                    self.MM(pbu[:, :], ust[:, f * 128:(f + 1) * 128],
                            arb[:, f * TOK + tt * 512: f * TOK + tt * 512 + 512],
                            f == 0, f == 7,
                            [("xq", f, tt), ustk, ("psum", bu)], [("psum", bu)],
                            tag=P + f"um{ic}_{tt}_{f}")
                pbu = pb[bu] if bu < 6 else self.ps["pb7"]
                sg = self.rf("sg0" if tt % 2 == 0 else "sg1")
                gg = self.rf("gg0" if tt % 2 == 0 else "gg1")
                sgk = ("arf", "sg", tt % 2)
                ggk = ("arf", "gg", tt % 2)
                self.ACTV(sg, pb[bg][:, :], AF.Silu,
                          [("psum", bg), ("stats", 128 + ic)],
                          [sgk, ("psum", bg)], tag=P + f"sge{ic}_{tt}",
                          scale=stats[:, 128 + ic: 128 + ic + 1])
                self.STT(gg, pbu[:, :], stats[:, 160 + ic:160 + ic + 1], sg,
                         ALU.mult, ALU.mult,
                         [("psum", bu), ("stats", 160 + ic), sgk],
                         [ggk, ("psum", bu)], tag=P + f"ge{ic}_{tt}")
                self.DMA(self.dram["g_spill"].ap()[ic * 128:(ic + 1) * 128,
                                                   tt * 512:(tt + 1) * 512],
                         gg, [ggk], [("gsp", ic, tt)], tag=P + f"gsp{ic}_{tt}",
                         eng="act")
                self.RED(sm[:, 9:10], gg, ALU.max, [ggk], [("sm", 9)],
                         tag=P + f"gmx{ic}_{tt}", absval=True)
                self.TT(sm[:, 8:9], sm[:, 8:9], sm[:, 9:10], ALU.max,
                        [("sm", 9), ("sm", 8)], [("sm", 8)],
                        tag=P + f"gma{ic}_{tt}")

        self._scales_from_max(P + "h", self.dram["clip_down"].ap())

        for oc in range(8):
            self.TT(stats[:, 192 + oc:192 + oc + 1],
                    stats[:, 64 + oc:64 + oc + 1], sm[:, 5:6], ALU.mult,
                    [("stats", 64 + oc), ("sm", 5)], [("stats", 192 + oc)],
                    tag=P + f"descA{oc}")
            self.TS(stats[:, 192 + oc:192 + oc + 1],
                    stats[:, 192 + oc:192 + oc + 1], 0.5, None, ALU.mult, None,
                    [("stats", 192 + oc)], [("stats", 192 + oc)],
                    tag=P + f"descB{oc}")

        for tt in range(4):
            for ic in range(NIC):
                par = ic % 2
                gg = self.rf("gld0" if par == 0 else "gld1")
                ggk = ("arf", "gld", par)
                self.DMA(gg, self.dram["g_spill"].ap()[ic * 128:(ic + 1) * 128,
                                                       tt * 512:(tt + 1) * 512],
                         [ggk, ("gsp", ic, tt)], [ggk], tag=P + f"gld{tt}_{ic}",
                         eng="sp")
                self.quant_round(gg, sm[:, 0:1], ("sm", 0),
                                 self.rb("hq", 512, ic * 512),
                                 ggk, ("hq", ic), P + f"hq{tt}_{ic}")
            for ph in range(2):
                bks = [0, 1, 2, 3] if ph == 0 else [4, 5, 7, 2]
                for ic in range(NIC):
                    par = ic % 2
                    dwl = self.rb("dwl0" if par == 0 else "dwl1", 512)
                    dwlk = ("arb", "dwl", par)
                    self.DMA(dwl, wsp[2 * NIC + ic, :, ph * 512:(ph + 1) * 512],
                             [dwlk] + [("wqd", ic, ph * 4 + j) for j in range(4)],
                             [dwlk], tag=P + f"dwld{tt}_{ph}_{ic}", eng="act")
                    for j in range(4):
                        bk = bks[j]
                        pbj = pb[bk] if bk < 6 else self.ps["pb7"]
                        self.MM(pbj[:, :], dwl[:, j * 128:(j + 1) * 128],
                                self.rb("hq", 512, ic * 512),
                                ic == 0, ic == NIC - 1,
                                [("hq", ic), dwlk, ("psum", bk)], [("psum", bk)],
                                tag=P + f"dm{tt}_{ph}_{ic}_{j}")
                for j in range(4):
                    bk = bks[j]
                    pbj = pb[bk] if bk < 6 else self.ps["pb7"]
                    oc = ph * 4 + j
                    xsl = xs[:, oc * TOK + tt * 512: oc * TOK + tt * 512 + 512]
                    self.STT(xsl, pbj[:, :], stats[:, 192 + oc:192 + oc + 1],
                             xsl, ALU.mult, ALU.add,
                             [("psum", bk), ("stats", 192 + oc), ("x", oc)],
                             [("x", oc), ("psum", bk)], tag=P + f"dres{tt}_{oc}")



    def emit_attn_group(self, g):
        P = f"ag{g}"
        pb = [self.ps[f"pb{i}"] for i in range(6)]
        pb7 = self.ps["pb7"]
        ptb = self.ps["ptb"]
        lng = self.ab_("lng", 4096)
        lngk = ("arb", "lng")
        hgb = g // 4

        self.DMA(lng.rearrange("p (a m) -> p a m", a=8),
                 self.dram["ag_lnx_out"].ap()[hgb * D:(hgb + 1) * D,
                                              (g % 4) * 512:(g % 4) * 512 + 512]
                 .rearrange("(a p) m -> p a m", p=128),
                 [lngk, ("aglnx",)], [lngk], tag=P + "lngld")

        for which, kbase in (("q", 0), ("k", 4096)):
            for oc in range(4):
                bnk = 2 + (oc % 2)
                for kc in range(8):
                    self.MM(pb[bnk][:, :],
                            self.ab_("aw", 128, kbase + kc * 512 + oc * 128),
                            lng[:, kc * 512:(kc + 1) * 512],
                            kc == 0, kc == 7,
                            [lngk, ("aw", which, kc), ("psum", bnk)],
                            [("psum", bnk)], tag=P + f"{which}m{oc}_{kc}")
                t = self.af_("tmp", 512)
                tk = ("arf", "tmp")
                self.ACTV(t, pb[bnk][:, :], AF.Tanh,
                          [("psum", bnk)], [tk], tag=P + f"{which}th{oc}",
                          scale=0.5)
                self.TS(t, t, 0.5, 0.5, ALU.mult, ALU.add, [tk], [tk],
                        tag=P + f"{which}sg{oc}")
                dst = self.ab_(which, 512, oc * 512)
                self.TT(dst, t, pb[bnk][:, :], ALU.mult,
                        [tk, ("psum", bnk)],
                        [("arb", which, oc), ("psum", bnk)],
                        tag=P + f"{which}sl{oc}")
        if self.gcut < 2:
            return
        bo = self.sb["cs_blockones"]
        for oc in range(4):
            k2 = self.af_("tmp", 512)
            tk = ("arf", "tmp")
            kt = self.ab_("k", 512, oc * 512)
            k2b = self.rb("xb0", 512)
            self.TT(k2b, kt, kt, ALU.mult, [("arb", "k", oc)],
                    [("arb", "xb", 0)], tag=P + f"k2_{oc}")
            self.MM(pb7[:, :], bo[:], k2b, True, True,
                    [("arb", "xb", 0), ("c", "blockones"), ("psum", 7)],
                    [("psum", 7)], tag=P + f"k2m{oc}")
            self.ACTV(k2, pb7[:, :], AF.Sqrt, [("psum", 7)],
                      [tk, ("psum", 7)], tag=P + f"knrm{oc}")
            self.TS(k2, k2, 1e-12, None, ALU.max, None, [tk], [tk],
                    tag=P + f"knmx{oc}")
            self.RECIP(k2, k2, [tk], [tk], tag=P + f"knr{oc}")
            self.TT(kt, kt, k2, ALU.mult, [("arb", "k", oc), tk],
                    [("arb", "k", oc)], tag=P + f"knd{oc}")

        if self.gcut < 3:
            return
        for kc in range(8):
            self.DMA(self.ab_("wv", 512, kc * 512),
                     self.dram["Wv"].ap()[kc * 128:(kc + 1) * 128, :],
                     [("arb", "wv", kc)], [("arb", "wv", kc)],
                     tag=P + f"wvl{kc}", eng="pool")
        for c in range(8):
            bnk = 2 + (c % 2)
            for kc in range(8):
                self.MM(pb[bnk][0:64, :],
                        lng[:, kc * 512 + c * 64: kc * 512 + c * 64 + 64],
                        self.ab_("wv", 512, kc * 512), kc == 0, kc == 7,
                        [lngk, ("arb", "wv", kc), ("psum", bnk)],
                        [("psum", bnk)], tag=P + f"vm{c}_{kc}")
            self.ACTV(self.af_("vg", 512, c * 512)[0:64, :], pb[bnk][0:64, :],
                      AF.Copy, [("psum", bnk)], [("arf", "vg", c), ("psum", bnk)],
                      tag=P + f"ve{c}")
        self.DMA(self.sb["arf"][64:128, ATF["vg"]:ATF["vg"] + 4096],
                 self.sb["arf"][0:64, ATF["vg"]:ATF["vg"] + 4096],
                 [("arf", "vg", c) for c in range(8)] + [("arf", "vgd")],
                 [("arf", "vgd")], tag=P + "vdup")

        if self.gcut < 4:
            return
        for kc in range(8):
            self.MM(pb7[0:64, :], self.ab_("awf", 64, kc * 64),
                    lng[:, kc * 512:(kc + 1) * 512], kc == 0, kc == 7,
                    [lngk, ("aw", "f1", kc), ("psum", 7)], [("psum", 7)],
                    tag=P + f"f1m{kc}")
        xf1 = self.sb["arb"][0:64, ATB["u"]:ATB["u"] + 512]
        self.COPY(xf1, pb7[0:64, :], [("psum", 7)],
                  [("arb", "xf1"), ("psum", 7)], tag=P + "f1e")
        for oc in range(4):
            bnk = 2 + (oc % 2)
            self.MM(pb[bnk][:, :],
                    self.sb["arb"][0:64, ATB["awf"] + 512 + oc * 128:
                                   ATB["awf"] + 512 + (oc + 1) * 128],
                    xf1, True, True,
                    [("arb", "xf1"), ("aw", "f2"), ("psum", bnk)],
                    [("psum", bnk)], tag=P + f"f2m{oc}")
            lam = self.af_("zf", 512, oc * 512)
            self.ACTV(lam, pb[bnk][:, :], AF.Tanh, [("psum", bnk)],
                      [("arf", "zf", oc), ("psum", bnk)], tag=P + f"lth{oc}",
                      scale=0.5)
            self.TS(lam, lam, 0.5, 0.5, ALU.mult, ALU.add,
                    [("arf", "zf", oc)], [("arf", "zf", oc)], tag=P + f"lsg{oc}")

        for kc in range(8):
            self.MM(pb7[0:8, :], self.ab_("awf", 8, 1024 + kc * 8),
                    lng[:, kc * 512:(kc + 1) * 512], kc == 0, kc == 7,
                    [lngk, ("aw", "g", kc), ("psum", 7)], [("psum", 7)],
                    tag=P + f"gm{kc}")
        gamt = self.sb["arf"][0:8, ATF["tmp"]:ATF["tmp"] + 512]
        self.ACTV(gamt, pb7[0:8, :], AF.Tanh, [("psum", 7)],
                  [("arf", "tmp"), ("psum", 7)], tag=P + "gth", scale=0.5)
        self.TS(gamt, gamt, -0.5, -0.5, ALU.mult, ALU.add, [("arf", "tmp")],
                [("arf", "tmp")], tag=P + "gsg")
        gamb = self.sb["arb"][0:8, ATB["g8"]:ATB["g8"] + 512]
        self.COPY(gamb, gamt, [("arf", "tmp")], [("arb", "g8")], tag=P + "gbc")
        sel = self.sb["cs_sel8"]
        for pt in range(4):
            self.MM(pb7[:, :], sel[:, pt * 128:(pt + 1) * 128], gamb,
                    True, True,
                    [("arb", "g8"), ("c", "sel8"), ("psum", 7)],
                    [("psum", 7)], tag=P + f"grm{pt}")
            self.COPY(self.ab_("gr", 512, pt * 512), pb7[:, :], [("psum", 7)],
                      [("arb", "gr", pt), ("psum", 7)], tag=P + f"gre{pt}")

        if self.gcut < 5:
            return
        z64 = self.sb["cs_zero64"]
        for pt in range(4):
            for c in range(8):
                off = ATF["P"] + pt * 512 + c * 64
                zoff = ATF["zf"] + pt * 512 + c * 64
                self.SCAN(self.sb["arf"][:, off:off + 64],
                          self.sb["arf"][:, zoff:zoff + 64],
                          z64[:, 0:64], 1.0, ALU.mult, ALU.add,
                          [("arf", "zf", pt), ("c", "zero64")],
                          [("arf", "P", pt)], tag=P + f"scan{pt}_{c}")
            self.RECIP(self.af_("Pinv", 512, pt * 512),
                       self.af_("P", 512, pt * 512),
                       [("arf", "P", pt)], [("arf", "Pinv", pt)],
                       tag=P + f"pinv{pt}")

        if self.gcut < 6:
            return
        for pt in range(4):
            q = self.ab_("q", 512, pt * 512)
            k = self.ab_("k", 512, pt * 512)
            Pt = self.af_("P", 512, pt * 512)
            Pi = self.af_("Pinv", 512, pt * 512)
            lam = self.af_("zf", 512, pt * 512)
            t = self.af_("tmp", 512)
            tk = ("arf", "tmp")
            self.TT(self.ab_("qb", 512, pt * 512), q, Pt, ALU.mult,
                    [("arb", "q", pt), ("arf", "P", pt)], [("arb", "qb", pt)],
                    tag=P + f"qb{pt}")
            self.TT(self.ab_("kt", 512, pt * 512), k, Pi, ALU.mult,
                    [("arb", "k", pt), ("arf", "Pinv", pt)], [("arb", "kt", pt)],
                    tag=P + f"ktb{pt}")
            self.TT(t, lam, Pt, ALU.mult,
                    [("arf", "zf", pt), ("arf", "P", pt)], [tk],
                    tag=P + f"lp{pt}")
            self.TT(t, t, self.ab_("gr", 512, pt * 512), ALU.mult,
                    [tk, ("arb", "gr", pt)], [tk], tag=P + f"lpg{pt}")
            self.TT(self.ab_("ab", 512, pt * 512), k, t, ALU.mult,
                    [("arb", "k", pt), tk], [("arb", "ab", pt)],
                    tag=P + f"abt{pt}")
            for c in range(8):
                self.TS(self.ab_("k", 64, pt * 512 + c * 64),
                        self.ab_("kt", 64, pt * 512 + c * 64),
                        self.sb["arf"][:, ATF["P"] + pt * 512 + c * 64 + 63:
                                       ATF["P"] + pt * 512 + c * 64 + 64],
                        None, ALU.mult, None,
                        [("arb", "kt", pt), ("arf", "P", pt)],
                        [("arb", "k", pt)], tag=P + f"khf{pt}_{c}")

        if self.gcut < 7:
            return
        for pt in range(4):
            for cp in range(4):
                self.TPO(ptb[:, cp * 128:(cp + 1) * 128],
                         self.ab_("k", 128, pt * 512 + cp * 128),
                         [("arb", "k", pt)], [("ptb", cp)],
                         tag=P + f"khT{pt}_{cp}")
            self.COPY(self.ab_("kht", 512, pt * 512), ptb[:, 0:512],
                      [("ptb", cp) for cp in range(4)],
                      [("arb", "kht", pt)] + [("ptb", cp) for cp in range(4)],
                      tag=P + f"khE{pt}")

        if self.gcut < 8:
            return
        self.run_pairs(g)

        for pt in range(4):
            self.DMA(self.dram["o_spill"].ap()[pt * 128:(pt + 1) * 128,
                                               g * 512:(g + 1) * 512],
                     self.ab_("og", 512, pt * 512),
                     [("arb", "og", pt, c) for c in range(8)],
                     [("osp", g, pt)], tag=P + f"osp{pt}")

    PW = dict(T=0, W=512, sT=1024, sW=1536, N=2048, AT=2560)

    def pwsl(self, slot, j, c, pp):
        base = ATB[("pw", "pw2", "pw3")[pp]] + self.PW[slot] + 256 * j + 64 * (c // 2)
        r = 64 * (c % 2)
        return self.sb["arb"][r:r + 64, base:base + 64]

    def pwt(self, slot, pp):
        return self.ab_(("pw", "pw2", "pw3")[pp], 512, self.PW[slot])

    def _gsl(self, reg, h, c):
        pt = h // 2
        base = ATB[reg] + pt * 512 + c * 64
        r = 64 * (h % 2)
        return self.sb["arb"][r:r + 64, base:base + 64]

    def _vsl(self, h, c):
        base = ATF["vg"] + c * 512 + h * 64
        r = 64 * (c % 2)
        return self.sb["arf"][r:r + 64, base:base + 64]

    def _khsl(self, h, c):
        base = ATB["kht"] + (h // 2) * 512 + (c // 2) * 128 + 64 * (h % 2)
        r = 64 * (c % 2)
        return self.sb["arf"] if False else self.sb["arb"][r:r + 64, base:base + 64]

    def _ssl(self, h):
        base = ATB["s"] + 64 * (h // 2)
        r = 64 * (h % 2)
        return self.sb["arb"][r:r + 64, base:base + 64]

    def gen_precompute(self, g, hp):
        P = f"ag{g}p{hp}"
        pp = hp % 3
        pb = [self.ps[f"pb{i}"] for i in range(6)]
        pb7 = self.ps["pb7"]
        if hp % 2 == 0:
            bt, bw, kt_, kw_ = pb[2], pb[3], 2, 3
        else:
            bt, bw, kt_, kw_ = pb[1], pb7, 1, 7
        kkey = ("arb", "pw", pp)
        msl = self.sb["cs_mstril"]
        msu = self.sb["cs_mstriu"]
        mti = self.sb["cs_mtriuincl"]
        idb = self.sb["cs_identblk"]
        heads = (2 * hp, 2 * hp + 1)
        gsl = self._gsl
        arb = self.sb["arb"]

        def osl(j, c, bank):
            return bank[64 * (c % 2):64 * (c % 2) + 64,
                        256 * j + 64 * (c // 2):256 * j + 64 * (c // 2) + 64]

        def pwh(slot, r0, c0, n):
            base = ATB[("pw", "pw2", "pw3")[pp]] + self.PW[slot] + c0
            return arb[r0:r0 + 64, base:base + n]

        for j, h in enumerate(heads):
            for c in range(8):
                tp = (64 * (h % 2), 64 * (c % 2))
                self.MM(osl(j, c, bt), gsl("ab", h, c), gsl("kt", h, c),
                        True, True,
                        [("arb", "ab", h // 2), ("arb", "kt", h // 2),
                         ("psum", kt_)], [("psum", kt_)],
                        tag=P + f"Tb{j}_{c}", tp=tp)
                self.MM(osl(j, c, bw), gsl("kt", h, c), gsl("ab", h, c),
                        True, True,
                        [("arb", "ab", h // 2), ("arb", "kt", h // 2),
                         ("psum", kw_)], [("psum", kw_)],
                        tag=P + f"Wb{j}_{c}", tp=tp)
                if c % 4 == 3:
                    yield
        self.TT(self.pwt("T", pp), bt[:, :], msl[:], ALU.mult,
                [("psum", kt_), ("c", "mstril")],
                [kkey + ("T",), ("psum", kt_)], tag=P + "Te")
        self.TT(self.pwt("W", pp), bw[:, :], msu[:], ALU.mult,
                [("psum", kw_), ("c", "mstriu")],
                [kkey + ("W",), ("psum", kw_)], tag=P + "We")
        yield
        for j, h in enumerate(heads):
            for c in range(8):
                tp = (64 * (h % 2), 64 * (c % 2))
                self.MM(osl(j, c, bt), gsl("kt", h, c), gsl("qb", h, c),
                        True, True,
                        [("arb", "qb", h // 2), ("arb", "kt", h // 2),
                         ("psum", kt_)], [("psum", kt_)],
                        tag=P + f"ATb{j}_{c}", tp=tp)
                if c % 4 == 3:
                    yield
        self.TT(self.pwt("AT", pp), bt[:, :], mti[:], ALU.mult,
                [("psum", kt_), ("c", "mtriuincl")],
                [kkey + ("AT",), ("psum", kt_)], tag=P + "ATe")
        self.TT(self.pwt("N", pp), self.pwt("W", pp), idb[:], ALU.add,
                [kkey + ("W",), ("c", "identblk")], [kkey + ("N",)],
                tag=P + "Ni")
        yield

        cur_t, cur_w, nxt_t, nxt_w = "T", "W", "sT", "sW"
        corder = [0, 2, 4, 6, 1, 3, 5, 7]
        for lvl in range(4):
            for j, h in enumerate(heads):
                for c in corder:
                    tpp = (64 * (c % 2), 64 * (c % 2))
                    self.MM(osl(j, c, bt), self.pwsl(cur_w, j, c, pp),
                            self.pwsl(cur_t, j, c, pp), True, True,
                            [kkey + (cur_w,), kkey + (cur_t,),
                             ("psum", kt_)], [("psum", kt_)],
                            tag=P + f"sq{lvl}t{j}_{c}", tp=tpp)
                    self.MM(osl(j, c, bw), self.pwsl(cur_t, j, c, pp),
                            self.pwsl(cur_w, j, c, pp), True, True,
                            [kkey + (cur_w,), kkey + (cur_t,),
                             ("psum", kw_)], [("psum", kw_)],
                            tag=P + f"sq{lvl}w{j}_{c}", tp=tpp)
                    if c in (6, 7):
                        yield
            self.ACTV(self.pwt(nxt_t, pp), bt[:, :], AF.Copy,
                      [("psum", kt_)], [kkey + (nxt_t,), ("psum", kt_)],
                      tag=P + f"sqe{lvl}t")
            self.ACTV(self.pwt(nxt_w, pp), bw[:, :], AF.Copy,
                      [("psum", kw_)], [kkey + (nxt_w,), ("psum", kw_)],
                      tag=P + f"sqe{lvl}w")
            yield
            for j, h in enumerate(heads):
                for c in corder:
                    tpp = (64 * (c % 2), 64 * (c % 2))
                    self.MM(osl(j, c, bt), self.pwsl(nxt_t, j, c, pp),
                            self.pwsl("N", j, c, pp), True, True,
                            [kkey + (nxt_t,), kkey + ("N",),
                             ("psum", kt_)], [("psum", kt_)],
                            tag=P + f"nu{lvl}_{j}_{c}", tp=tpp)
                    if c in (6, 7):
                        yield
            self.TT(self.pwt("N", pp), bt[:, :], self.pwt("N", pp), ALU.add,
                    [("psum", kt_), kkey + ("N",)],
                    [kkey + ("N",), ("psum", kt_)], tag=P + f"nue{lvl}")
            yield
            cur_t, cur_w, nxt_t, nxt_w = nxt_t, nxt_w, cur_t, cur_w
        for j, h in enumerate(heads):
            for c in corder:
                tpp = (64 * (c % 2), 64 * (c % 2))
                self.MM(osl(j, c, bt), self.pwsl(cur_w, j, c, pp),
                        self.pwsl(cur_t, j, c, pp), True, True,
                        [kkey + (cur_w,), kkey + (cur_t,),
                         ("psum", kt_)], [("psum", kt_)],
                        tag=P + f"sq32t{j}_{c}", tp=tpp)
                if c in (6, 7):
                    yield
        self.ACTV(self.pwt(nxt_t, pp), bt[:, :], AF.Copy, [("psum", kt_)],
                  [kkey + (nxt_t,), ("psum", kt_)], tag=P + "sqe32t")
        yield
        for j, h in enumerate(heads):
            for c in corder:
                tpp = (64 * (c % 2), 64 * (c % 2))
                self.MM(osl(j, c, bt), self.pwsl(nxt_t, j, c, pp),
                        self.pwsl("N", j, c, pp), True, True,
                        [kkey + (nxt_t,), kkey + ("N",),
                         ("psum", kt_)], [("psum", kt_)],
                        tag=P + f"nu32_{j}_{c}", tp=tpp)
                if c in (6, 7):
                    yield
        self.TT(self.pwt("N", pp), bt[:, :], self.pwt("N", pp), ALU.add,
                [("psum", kt_), kkey + ("N",)], [kkey + ("N",), ("psum", kt_)],
                tag=P + "nue32")
        yield

    def gen_chain(self, g, hp):
        P = f"ag{g}c{hp}"
        pp = hp % 3
        pb = [self.ps[f"pb{i}"] for i in range(6)]
        kkey = ("arb", "pw", pp)
        heads = (2 * hp, 2 * hp + 1)
        gsl, vsl, khsl, ssl = self._gsl, self._vsl, self._khsl, self._ssl
        arb = self.sb["arb"]
        UC2 = ATB["uc2"]
        ucR = [arb[:, UC2 + 128 * j:UC2 + 128 * j + 64] for j in (0, 1)]
        ucU = [arb[:, UC2 + 128 * j + 64:UC2 + 128 * j + 128] for j in (0, 1)]
        cb = [pb[0], pb[4]]
        ck = [0, 4]
        for c in range(8):
            cpar = 64 * (c % 2)
            for j, h in enumerate(heads):
                hpar = 64 * (h % 2)
                self.MM(pb[5][hpar:hpar + 64, 0:64], ssl(h), gsl("qb", h, c),
                        True, True,
                        [("arb", "s", h), ("arb", "qb", h // 2), ("psum", 5)],
                        [("psum", 5)], tag=P + f"cO1{c}_{j}", tp=(hpar, hpar))
            for j, h in enumerate(heads):
                hpar = 64 * (h % 2)
                self.MM(cb[j][cpar:cpar + 64, 0:64], gsl("ab", h, c), ssl(h),
                        True, True,
                        [("arb", "ab", h // 2), ("arb", "s", h),
                         ("psum", ck[j])], [("psum", ck[j])],
                        tag=P + f"cR{c}_{j}", tp=(hpar, cpar))
            for j, h in enumerate(heads):
                self.TT(ucR[j][cpar:cpar + 64, :],
                        cb[j][cpar:cpar + 64, 0:64], vsl(h, c), ALU.add,
                        [("psum", ck[j]), ("arf", "vgd"), ("arf", "vg", c)],
                        [("arb", "ucR", j), ("psum", ck[j])],
                        tag=P + f"cRe{c}_{j}")
            for j, h in enumerate(heads):
                self.MM(cb[j][cpar:cpar + 64, 64:128],
                        self.pwsl("N", j, c, pp), ucR[j][cpar:cpar + 64, :],
                        True, True,
                        [kkey + ("N",), ("arb", "ucR", j), ("psum", ck[j])],
                        [("psum", ck[j])], tag=P + f"cU{c}_{j}",
                        tp=(cpar, cpar))
            for j, h in enumerate(heads):
                self.ACTV(ucU[j][cpar:cpar + 64, :],
                          cb[j][cpar:cpar + 64, 64:128], AF.Copy,
                          [("psum", ck[j])],
                          [("arb", "ucU", j), ("psum", ck[j])],
                          tag=P + f"cUe{c}_{j}")
            for j, h in enumerate(heads):
                hpar = 64 * (h % 2)
                self.MM(pb[5][hpar:hpar + 64, 64:128],
                        ucU[j][cpar:cpar + 64, :],
                        self.pwsl("AT", j, c, pp), True, True,
                        [("arb", "ucU", j), kkey + ("AT",), ("psum", 5)],
                        [("psum", 5)], tag=P + f"cO2{c}_{j}", tp=(cpar, hpar))
                self.MM(cb[j][hpar:hpar + 64, 128:192], khsl(h, c),
                        ucU[j][cpar:cpar + 64, :], True, True,
                        [("arb", "kht", h // 2), ("arb", "ucU", j),
                         ("psum", ck[j])], [("psum", ck[j])],
                        tag=P + f"cK{c}_{j}", tp=(cpar, hpar))
            for j, h in enumerate(heads):
                hpar = 64 * (h % 2)
                pcol = self.sb["arf"][hpar:hpar + 64,
                                      ATF["P"] + (h // 2) * 512 + c * 64 + 63:
                                      ATF["P"] + (h // 2) * 512 + c * 64 + 64]
                self.STT(ssl(h), ssl(h), pcol,
                         cb[j][hpar:hpar + 64, 128:192],
                         ALU.mult, ALU.add,
                         [("arb", "s", h), ("arf", "P", h // 2),
                          ("psum", ck[j])],
                         [("arb", "s", h), ("psum", ck[j])],
                         tag=P + f"cS{c}_{j}")
                yield
            self.ACTV(self.ab_("og", 64, hp * 512 + c * 64), pb[5][:, 0:64],
                      AF.Copy, [("psum", 5)], [("arb", "og", hp, c)],
                      tag=P + f"cOc{c}")
            self.TT(self.ab_("og", 64, hp * 512 + c * 64),
                    self.ab_("og", 64, hp * 512 + c * 64), pb[5][:, 64:128],
                    ALU.add, [("arb", "og", hp, c), ("psum", 5)],
                    [("arb", "og", hp, c), ("psum", 5)], tag=P + f"cOe{c}")
            yield

    def run_pairs(self, g):
        pre = {hp: self.gen_precompute(g, hp) for hp in range(4)}
        chains = {}
        pre_done, ch_done = set(), set()
        active, next_pre = [], 0
        while len(ch_done) < 4:
            while (next_pre < 4 and len(active) < 2
                   and (next_pre < 3 or (next_pre - 3) in ch_done)):
                active.append(next_pre)
                next_pre += 1
            achain = None
            for hp in range(4):
                if hp in pre_done and hp not in ch_done:
                    achain = hp
                    break
            if achain is not None:
                gen = chains.get(achain)
                if gen is None:
                    gen = chains[achain] = self.gen_chain(g, achain)
                try:
                    next(gen)
                except StopIteration:
                    ch_done.add(achain)
            for hp in list(active):
                try:
                    next(pre[hp])
                    next(pre[hp])
                except StopIteration:
                    pre_done.add(hp)
                    active.remove(hp)
            if achain is None and not active and next_pre >= 4:
                break

    def emit_attn_output(self):
        P = "ao"
        pb = [self.ps[f"pb{i}"] for i in range(6)]
        pb7 = self.ps["pb7"]
        xs = self.sb["xs"]
        ones128 = self.sb["cs_ones128"]
        hgsel = ("hgsel",)

        for kc in range(8):
            self.DMA(self.ob_("wo", 1024, kc * 1024),
                     self.dram["Wo"].ap()[kc * 128:(kc + 1) * 128, :],
                     [], [("wo", kc)], tag=P + f"wol{kc}", eng="pool")
            self.DMA(self.ob_("lnxo1", 64, kc * 64),
                     self.dram["Wog1"].ap()[kc * 128:(kc + 1) * 128, :],
                     [], [("wog1", kc)], tag=P + f"wog1l{kc}", eng="pool")
        self.DMA(self.sb["arb"][0:64, OTB["lnxo1"] + 512:OTB["lnxo1"] + 1536],
                 self.dram["Wog2"].ap()[:, :], [], [("wog2",)],
                 tag=P + "wog2l", eng="pool")
        self.DMA(self.sb["small"][:, 24:32], self.dram["norm_w"].ap(),
                 [("sm", 24)], [("sm", 24)], tag=P + "nwl")

        for tt in range(4):
            par = tt % 2
            ot = self.ob_("ot0" if par == 0 else "ot1", 4096)
            otk = ("arb", "ot", par)
            lnxo = self.ob_("lnxo0", 4096)
            lnk = ("arb", "lnxo")
            own0 = (tt * 512)
            oth = self.ob_("ot0" if par == 1 else "ot1", 4096)
            othk = ("arb", "ot", 1 - par)
            self.DMA(ot.rearrange("p (a m) -> p a m", a=8),
                     self.dram["ag_o_out"].ap()[:, own0:own0 + 512]
                     .rearrange("(a p) m -> p a m", p=128),
                     [otk, ("agoout",)], [otk], tag=P + f"otl{tt}")
            self.DMA(oth.rearrange("p (a m) -> p a m", a=8),
                     self.dram["ag_o_out"].ap()[:, 2048 + own0:2048 + own0 + 512]
                     .rearrange("(a p) m -> p a m", p=128),
                     [othk, ("agoout",)], [othk], tag=P + f"oth{tt}")
            self.TT(oth, oth, ot, ALU.subtract, [otk, othk], [othk],
                    tag=P + f"osub{tt}")
            self.STT(ot, oth, self.sb["hgm"][:, 0:1], ot, ALU.mult, ALU.add,
                     [othk, ("hgm",), otk], [otk], tag=P + f"oblend{tt}")
            self.DMA(lnxo.rearrange("p (a m) -> p a m", a=8),
                     self.dram["ag_lnx_in"].ap()[:, own0:own0 + 512]
                     .rearrange("(a p) m -> p a m", p=128),
                     [lnk] + [("agin", tt, f) for f in range(8)], [lnk],
                     tag=P + f"lnxl{tt}")
            for kc in range(8):
                self.MM(pb7[0:64, :], self.ob_("lnxo1", 64, kc * 64),
                        lnxo[:, kc * 512:(kc + 1) * 512], kc == 0, kc == 7,
                        [lnk, ("wog1", kc), ("psum", 7)], [("psum", 7)],
                        tag=P + f"og1m{tt}_{kc}")
            og1 = self.sb["arb"][0:64, OTB["ogt"]:OTB["ogt"] + 512]
            self.COPY(og1, pb7[0:64, :], [("psum", 7)],
                      [("arb", "ogt"), ("psum", 7)], tag=P + f"og1e{tt}")
            for oc in range(8):
                bnk = 2 + (oc % 2)
                self.MM(pb[bnk][:, :],
                        self.sb["arb"][0:64, OTB["lnxo1"] + 512 + oc * 128:
                                       OTB["lnxo1"] + 512 + (oc + 1) * 128],
                        og1, True, True,
                        [("arb", "ogt"), ("wog2",), ("psum", bnk)],
                        [("psum", bnk)], tag=P + f"og2m{tt}_{oc}")
                t = self.af_("tmp", 512) if False else self.of_("mu", 512)
                tk = ("arf", "mu")
                self.ACTV(t, pb[bnk][:, :], AF.Tanh, [("psum", bnk)], [tk],
                          tag=P + f"ogth{tt}_{oc}", scale=0.5)
                self.TS(t, t, 0.5, 0.5, ALU.mult, ALU.add, [tk], [tk],
                        tag=P + f"ogsg{tt}_{oc}")
                ov = self.of_("ov", 512, oc * 512)
                self.TT(ov, ot[:, oc * 512:(oc + 1) * 512], t, ALU.mult,
                        [otk, tk], [("arf", "ov", oc)], tag=P + f"ov{tt}_{oc}")
            for oc in range(8):
                ovb = self.rb("xb0" if oc % 2 == 0 else "xb1", 512)
                ovbk = ("arb", "xb", oc % 2)
                ov = self.of_("ov", 512, oc * 512)
                self.COPY(ovb, ov, [("arf", "ov", oc)], [ovbk],
                          tag=P + f"ovb{tt}_{oc}")
                x2 = self.rb("x20" if oc % 2 == 0 else "x21", 512)
                x2k = ("arb", "x2", oc % 2)
                self.TT(x2, ovb, ovb, ALU.mult, [ovbk], [x2k],
                        tag=P + f"ovsq{tt}_{oc}")
                self.MM(pb[0][:, :], ones128[:], ovb, oc == 0, oc == 7,
                        [ovbk, ("c", "ones128"), ("psum", 0)], [("psum", 0)],
                        tag=P + f"osum{tt}_{oc}")
                self.MM(pb[1][:, :], ones128[:], x2, oc == 0, oc == 7,
                        [x2k, ("c", "ones128"), ("psum", 1)], [("psum", 1)],
                        tag=P + f"osumsq{tt}_{oc}")
            mu = self.of_("mu", 512)
            var = self.of_("var", 512)
            rstd = self.of_("rstd", 512)
            mu2 = self.of_("mu2", 512)
            self.TS(mu, pb[0][:, :], 1.0 / D, None, ALU.mult, None,
                    [("psum", 0)], [("arf", "mu"), ("psum", 0)], tag=P + f"mu{tt}")
            self.TS(var, pb[1][:, :], 1.0 / D, None, ALU.mult, None,
                    [("psum", 1)], [("arf", "var"), ("psum", 1)],
                    tag=P + f"msq{tt}")
            self.TT(mu2, mu, mu, ALU.mult, [("arf", "mu")], [("arf", "mu2")],
                    tag=P + f"mu2{tt}")
            self.TT(var, var, mu2, ALU.subtract,
                    [("arf", "var"), ("arf", "mu2")], [("arf", "var")],
                    tag=P + f"var{tt}")
            self.ACTV(rstd, var, AF.Sqrt, [("arf", "var"), ("c", "eps")],
                      [("arf", "rstd")], tag=P + f"sd{tt}",
                      bias=self.sb["cs_eps"][:, 0:1])
            self.RECIP(rstd, rstd, [("arf", "rstd")], [("arf", "rstd")],
                       tag=P + f"rstd{tt}")
            for oc in range(8):
                ov = self.of_("ov", 512, oc * 512)
                self.TT(ov, ov, mu, ALU.subtract,
                        [("arf", "ov", oc), ("arf", "mu")], [("arf", "ov", oc)],
                        tag=P + f"on1{tt}_{oc}")
                self.TT(ov, ov, rstd, ALU.mult,
                        [("arf", "ov", oc), ("arf", "rstd")],
                        [("arf", "ov", oc)], tag=P + f"on2{tt}_{oc}")
                oln = self.ob_("oln", 512, oc * 512)
                self.ACTV(oln, ov, AF.Copy,
                          [("arf", "ov", oc), ("sm", 24)],
                          [("arb", "oln", oc)], tag=P + f"onw{tt}_{oc}",
                          scale=self.sb["small"][:, 24 + oc:24 + oc + 1])
            for oc2 in range(8):
                bnk = 2 + (oc2 % 2)
                for kc in range(8):
                    self.MM(pb[bnk][:, :],
                            self.ob_("wo", 128, kc * 1024 + oc2 * 128),
                            self.ob_("oln", 512, kc * 512), kc == 0, kc == 7,
                            [("arb", "oln", kc), ("wo", kc), ("psum", bnk)],
                            [("psum", bnk)], tag=P + f"wom{tt}_{oc2}_{kc}")
                xsl = xs[:, oc2 * TOK + tt * 512: oc2 * TOK + tt * 512 + 512]
                self.TT(xsl, pb[bnk][:, :], xsl, ALU.add,
                        [("psum", bnk), ("x", oc2)],
                        [("x", oc2), ("psum", bnk)], tag=P + f"ores{tt}_{oc2}")
    def ab_(self, reg, width, off=0):
        return self.sb["arb"][:, ATB[reg] + off: ATB[reg] + off + width]

    def ob_(self, reg, width, off=0):
        return self.sb["arb"][:, OTB[reg] + off: OTB[reg] + off + width]

    def af_(self, reg, width, off=0):
        return self.sb["arf"][:, ATF[reg] + off: ATF[reg] + off + width]

    def of_(self, reg, width, off=0):
        return self.sb["arf"][:, OTF[reg] + off: OTF[reg] + off + width]

    def emit_attn(self):
        S = self.S
        xs = self.sb["xs"]
        arb = self.sb["arb"]
        arf = self.sb["arf"]
        sm = self.sb["small"]
        pb = [self.ps[f"pb{i}"] for i in range(6)]
        pb7 = self.ps["pb7"]
        ptb = self.ps["ptb"]
        ones128 = self.sb["cs_ones128"]
        hg_tokens = TOK

        S.barrier()
        P = "at"

        for tt in range(4):
            for f in range(8):
                xsl = xs[:, f * TOK + tt * 512: f * TOK + tt * 512 + 512]
                xb = self.rb("xb0" if f % 2 == 0 else "xb1", 512)
                x2 = self.rb("x20" if f % 2 == 0 else "x21", 512)
                xbk = ("arb", "xb", f % 2)
                x2k = ("arb", "x2", f % 2)
                self.COPY(xb, xsl, [("x", f)], [xbk], tag=P + f"cast{tt}_{f}")
                self.TT(x2, xb, xb, ALU.mult, [xbk], [x2k], tag=P + f"sq{tt}_{f}")
                self.MM(pb[0][:, :], ones128[:], xb, f == 0, f == 7,
                        [xbk, ("c", "ones128"), ("psum", 0)], [("psum", 0)],
                        tag=P + f"sums{tt}_{f}")
                self.MM(pb[1][:, :], ones128[:], x2, f == 0, f == 7,
                        [x2k, ("c", "ones128"), ("psum", 1)], [("psum", 1)],
                        tag=P + f"sumsq{tt}_{f}")
            mu, var, rstd, mu2 = (self.rf(r) for r in ("mu", "var", "rstd", "mu2"))
            self.TS(mu, pb[0][:, :], 1.0 / D, None, ALU.mult, None,
                    [("psum", 0)], [("arf", "mu"), ("psum", 0)], tag=P + f"mu{tt}")
            self.TS(var, pb[1][:, :], 1.0 / D, None, ALU.mult, None,
                    [("psum", 1)], [("arf", "var"), ("psum", 1)], tag=P + f"msq{tt}")
            self.TT(mu2, mu, mu, ALU.mult, [("arf", "mu")], [("arf", "mu2")],
                    tag=P + f"mu2{tt}")
            self.TT(var, var, mu2, ALU.subtract,
                    [("arf", "var"), ("arf", "mu2")], [("arf", "var")],
                    tag=P + f"var{tt}")
            self.ACTV(rstd, var, AF.Sqrt, [("arf", "var"), ("c", "eps")],
                      [("arf", "rstd")], tag=P + f"sd{tt}",
                      bias=self.sb["cs_eps"][:, 0:1])
            self.RECIP(rstd, rstd, [("arf", "rstd")], [("arf", "rstd")],
                       tag=P + f"rstd{tt}")
            for f in range(8):
                xsl = xs[:, f * TOK + tt * 512: f * TOK + tt * 512 + 512]
                lnt = self.rf("lnt0" if f % 2 == 0 else "lnt1")
                lk = ("arf", "lnt", f % 2)
                self.TT(lnt, xsl, mu, ALU.subtract, [("x", f), ("arf", "mu")],
                        [lk], tag=P + f"lsub{tt}_{f}")
                self.TT(lnt, lnt, rstd, ALU.mult, [lk, ("arf", "rstd")], [lk],
                        tag=P + f"lmul{tt}_{f}")
                self.DMA(self.dram["ag_lnx_in"].ap()[f * 128:(f + 1) * 128,
                                                     tt * 512:(tt + 1) * 512],
                         lnt, [lk], [("agin", tt, f), lk],
                         tag=P + f"lnst{tt}_{f}", eng="pool")

        if self.attn_cut < 1:
            return
        self.CC("AllGather", ALU.bypass, [[0, 1], [2, 3], [4, 5], [6, 7]],
                [self.dram["ag_lnx_in"].ap()], [self.dram["ag_lnx_out"].ap()],
                [("agin", tt, f) for tt in range(4) for f in range(8)],
                [("aglnx",)], tag=P + "aglnx")
        if self.attn_cut < 2:
            return

        for kc in range(8):
            self.DMA(self.ab_("aw", 512, kc * 512),
                     self.dram["Wq"].ap()[kc * 128:(kc + 1) * 128, :],
                     [], [("aw", "q", kc)], tag=P + f"wql{kc}", eng="pool")
            self.DMA(self.ab_("aw", 512, 4096 + kc * 512),
                     self.dram["Wk"].ap()[kc * 128:(kc + 1) * 128, :],
                     [], [("aw", "k", kc)], tag=P + f"wkl{kc}", eng="pool")
            self.DMA(self.ab_("awf", 64, kc * 64),
                     self.dram["Wf1"].ap()[kc * 128:(kc + 1) * 128, :],
                     [], [("aw", "f1", kc)], tag=P + f"wf1l{kc}", eng="pool")
            self.DMA(self.ab_("awf", 8, 1024 + kc * 8),
                     self.dram["Wgam"].ap()[kc * 128:(kc + 1) * 128, :],
                     [], [("aw", "g", kc)], tag=P + f"wgl{kc}", eng="pool")
        self.DMA(self.sb["arb"][0:64, ATB["awf"] + 512: ATB["awf"] + 1024],
                 self.dram["Wf2"].ap()[:, :], [], [("aw", "f2")],
                 tag=P + "wf2l", eng="pool")

        self.MEMSET(self.ab_("s", 256), 0.0,
                    [("arb", "s", h) for h in range(8)], tag=P + "s0")

        if self.attn_cut < 3:
            return
        for g in range(min(NG, self.attn_cut - 2)):
            self.emit_attn_group(g)
        if self.attn_cut < 11:
            return

        self.CC("AllGather", ALU.bypass, [[0, 1], [2, 3], [4, 5], [6, 7]],
                [self.dram["o_spill"].ap()], [self.dram["ag_o_out"].ap()],
                [("osp", g, pt) for g in range(NG) for pt in range(4)],
                [("agoout",)], tag=P + "ago")

        if self.attn_cut < 12:
            return
        S.barrier()
        self.emit_attn_output()
        S.barrier()



def _prep_core_inputs(c, inputs, const_arrays):
    b, hg = c // 2, c % 2
    x = inputs["L0_attn_x"]
    xslice = np.asarray(x[b, hg * TOK:(hg + 1) * TOK, :], dtype=np.float32)
    mycols = slice(hg * 512, hg * 512 + 512)
    m = {
        "x_fm": np.ascontiguousarray(xslice.T),
        "wg_nat": np.ascontiguousarray(inputs["L0_ffn_w_gate"], dtype=np.float32),
        "wu_nat": np.ascontiguousarray(inputs["L0_ffn_w_up"], dtype=np.float32),
        "wd_nat": np.ascontiguousarray(inputs["L0_ffn_w_down"], dtype=np.float32),
        "clip_x": np.asarray(inputs["L0_ffn_clip_x"], dtype=np.float32).reshape(1, 1),
        "clip_down": np.asarray(inputs["L0_ffn_clip_down"], dtype=np.float32).reshape(1, 1),
        "Wq": np.ascontiguousarray(np.asarray(inputs["L0_attn_Wq"])[:, mycols], dtype=np.float32),
        "Wk": np.ascontiguousarray(np.asarray(inputs["L0_attn_Wk"])[:, mycols], dtype=np.float32),
        "Wv": np.ascontiguousarray(np.asarray(inputs["L0_attn_Wv"])[:, mycols], dtype=np.float32),
        "Wgam": np.ascontiguousarray(np.asarray(inputs["L0_attn_Wgamma"])[:, hg * NH:(hg + 1) * NH], dtype=np.float32),
        "Wf1": np.ascontiguousarray(inputs["L0_attn_Wf1"], dtype=np.float32),
        "Wf2": np.ascontiguousarray(np.asarray(inputs["L0_attn_Wf2"])[:, mycols], dtype=np.float32),
        "Wog1": np.ascontiguousarray(inputs["L0_attn_Wog1"], dtype=np.float32),
        "Wog2": np.ascontiguousarray(inputs["L0_attn_Wog2"], dtype=np.float32),
        "Wo": np.ascontiguousarray(inputs["L0_attn_Wo"], dtype=np.float32),
        "norm_w": np.ascontiguousarray(np.asarray(inputs["L0_attn_norm_w"], dtype=np.float32).reshape(8, 128).T),
        "hgmask": np.full((128, 1), float(hg), dtype=np.float32),
    }
    m.update(const_arrays)
    return m


_BUILD_CACHE = {}


def _get_built(phases, dumps=(), attn_cut=99, gcut=99):
    key = (tuple(phases), tuple(n for n, _, _ in dumps), attn_cut, gcut)
    if key not in _BUILD_CACHE:
        b = Builder(phases=phases, dumps=dumps, attn_cut=attn_cut, gcut=gcut)
        b.build()
        with (b.ctx, b.nc.Block() as block):
            sems = {}
            es = ExitStack()
            for e in ENGINES:
                sems[e] = es.enter_context(b.nc.semaphore("s_" + e))
            dsems = [es.enter_context(b.nc.semaphore(f"dp{i}"))
                     for i in range(64)]
            csems = [es.enter_context(b.nc.semaphore(f"cc{i}"))
                     for i in range(8)]
            b.S.emit(block, sems, dsems, csems)
        _BUILD_CACHE[key] = b
    return _BUILD_CACHE[key]


def kernel(**inputs):
    b = _get_built(("ffn1", "attn", "ffn2"))
    in_maps = [_prep_core_inputs(c, inputs, b.const_arrays) for c in range(8)]
    res = run_bass_kernel_spmd(b.nc, in_maps, core_ids=list(range(8)))
    out = np.empty((4, TFULL, D), dtype=np.float32)
    for c in range(8):
        bb, hg = c // 2, c % 2
        out[bb, hg * TOK:(hg + 1) * TOK, :] = res.results[c]["out_fm"].T
    return out
